# version 07
# baseline (speedup 1.0000x reference)
import numpy as np
import ml_dtypes
import concourse.bass as bass
import concourse.mybir as mybir
from concourse.bass_utils import run_bass_kernel_spmd

F32 = mybir.dt.float32
BF16 = mybir.dt.bfloat16
U8 = mybir.dt.uint8
AF = mybir.ActivationFunctionType
ALU = mybir.AluOpType
AX = mybir.AxisListType

D = 1024
S = 2048
NT = 8
NCH = 4
NEG = -1e30


def _rope_tables_l0():
    half = np.arange(0, 64, 2, dtype=np.float64)
    inv_freq = 1.0 / 10000.0 ** (half / 64.0)
    t = np.arange(S, dtype=np.float64)
    freqs = t[:, None] * inv_freq[None, :]
    base_scale = (half + 0.4 * 64) / (1.4 * 64)
    power = (t - S // 2) / (256 // 2)
    scale = base_scale[None, :] ** power[:, None]
    cos, sin = np.cos(freqs), np.sin(freqs)
    def tile128(m):
        r = np.concatenate([m.T, m.T], 0)
        return np.concatenate([r, r], 0)
    return (tile128(cos * scale), tile128(sin * scale),
            tile128(1.0 / (scale * scale)))

def _rope_tables_l1():
    half = np.arange(0, 64, 2, dtype=np.float64)
    inv_freq = 1.0 / 10000.0 ** (half / 64.0)
    t = np.arange(S, dtype=np.float64)
    freqs = t[:, None] * inv_freq[None, :]
    def tile128(m):
        r = np.concatenate([m.T, m.T], 0)
        return np.concatenate([r, r], 0)
    return tile128(np.cos(freqs)), tile128(np.sin(freqs))

def _local_masks():
    r = np.arange(128)[:, None]
    c = np.arange(512)[None, :]
    mk = []
    for off in (0, 128):
        ok = (c >= r + off) & (c <= r + off + 256)
        mk.append(np.where(ok, 0.0, NEG))
    return mk

def _l1_diag_mask():
    r = np.arange(128)[:, None]
    c = np.arange(128)[None, :]
    return np.where(c <= r, 0.0, NEG)

def _rt128():
    R = np.zeros((64, 64))
    R[0:32, 32:64] = -np.eye(32)
    R[32:64, 0:32] = np.eye(32)
    M = np.zeros((128, 128))
    M[0:64, 0:64] = R.T
    M[64:128, 64:128] = R.T
    return M

def _bd64():
    M = np.zeros((128, 128))
    M[0:64, 0:64] = 1.0
    M[64:128, 64:128] = 1.0
    return M

def _perm_wqb():
    perm = np.zeros(1024, dtype=np.int64)
    for h in range(16):
        for r in range(64):
            d = 2 * r if r < 32 else 2 * (r - 32) + 1
            perm[h * 64 + r] = h * 64 + d
    return perm

def _perm_wkvb():
    perm = np.zeros(512, dtype=np.int64)
    for g in range(4):
        for r in range(128):
            if r < 32:
                c = 4 * r
            elif r < 64:
                c = 4 * (r - 32) + 2
            else:
                c = 2 * (r - 64) + 1
            perm[g * 128 + r] = g * 128 + c
    return perm

def _consts():
    mke, mko = _local_masks()
    sel8 = np.zeros((8, 8 * 128))
    for e in range(8):
        sel8[e, e * 128:(e + 1) * 128] = 1.0
    scal = np.zeros((128, 8))
    scal[:, 0] = 1e-5; scal[:, 1] = 1.0; scal[:, 2] = -1.0
    scal[:, 3] = -0.5; scal[:, 4] = 8.0; scal[:, 5] = 1.0 / 1024.0
    d = {
        "c_scal": scal,
        "c_identf": np.eye(128), "c_identb": np.eye(128),
        "c_ones": np.ones((128, 128)), "c_bd64": _bd64(), "c_rt128": _rt128(),
        "c_mke": mke, "c_mko": mko, "c_diag": _l1_diag_mask(), "c_sel8": sel8,
    }
    cs, sn, i2 = _rope_tables_l0()
    c1, s1 = _rope_tables_l1()
    tbl = {"t_cs": cs, "t_sn": sn, "t_i2": i2, "t_r1c": c1, "t_r1s": s1}
    return ({k: v.astype(np.float32) for k, v in d.items()},
            {k: v.astype(np.float32) for k, v in tbl.items()})

_CONST_F32 = {"c_identf", "c_scal"}

def _tile_vec(v, ntile):
    out = np.zeros((128, ntile), dtype=np.float32)
    for t in range(ntile):
        seg = v[t * 128:(t + 1) * 128]
        out[:seg.shape[0], t] = seg
    return out


ENGS = ("t", "v", "s", "g", "y")

NDSEM = 12

class Prog:
    def __init__(self):
        self.nc = bass.Bass("TRN2", target_bir_lowering=False)
        self.ops = {e: [] for e in ENGS}
        self.cnt = {e: 0 for e in ENGS}
        self.dcnt = {"g": 0, "y": 0}

    def op(self, eng, fn, waits=(), inc=False):
        waits = tuple(w for w in waits if w is not None)
        self.ops[eng].append(("op", fn, waits, inc))
        if inc:
            self.cnt[eng] += 1
            return (eng, self.cnt[eng])
        return None

    def dma(self, queue, out, in_, waits=(), accum=False):
        waits = tuple(w for w in waits if w is not None)
        idx = self.dcnt[queue]
        slot = idx % NDSEM
        rnd = idx // NDSEM
        if rnd > 0:
            waits = waits + ((queue + "d%d" % slot, rnd * 16),)
        self.ops[queue].append(("dma", (out, in_, accum, slot), waits, True))
        self.dcnt[queue] += 1
        return (queue + "d%d" % slot, (rnd + 1) * 16)

    def barrier(self):
        snap = [(e, self.cnt[e]) for e in ENGS if self.cnt[e] > 0]
        for q in ("g", "y"):
            n = self.dcnt[q]
            for slot in range(min(n, NDSEM)):
                cnt = n // NDSEM + (1 if slot < n % NDSEM else 0)
                snap.append((q + "d%d" % slot, cnt * 16))
        for e in ENGS:
            ws = tuple(snap)
            if ws:
                self.ops[e].append(("wait", None, ws, False))

    def now(self, eng):
        return (eng, self.cnt[eng])

    @staticmethod
    def sync(ms):
        return (ms[0], ms[1], "f") if ms is not None else None

    def emit(self):
        nc = self.nc
        self.sems = {}
        for e in ENGS:
            self.sems[e] = nc.semaphore("sem_" + e).__enter__()
        for q in ("g", "y"):
            for slot in range(NDSEM):
                key = q + "d%d" % slot
                self.sems[key] = nc.semaphore("sem_" + key).__enter__()
        lastw = {e: {} for e in ENGS}

        def run(engobj, ename):
            lw = lastw[ename]
            for kind, fn, waits, inc in self.ops[ename]:
                for ms in waits:
                    sk, val = ms[0], ms[1]
                    if lw.get(sk, -1) >= val:
                        continue
                    engobj.wait_ge(self.sems[sk], val)
                    lw[sk] = val
                if kind == "op":
                    inst = fn(engobj)
                    if inc:
                        inst.then_inc(self.sems[ename], 1)
                elif kind == "dma":
                    out, in_, accum, slot = fn
                    key = ename + "d%d" % slot
                    if accum:
                        engobj.dma_start(out=out, in_=in_,
                                         accum_op=ALU.add).then_inc(
                            self.sems[key], 16)
                    else:
                        engobj.dma_start(out=out, in_=in_).then_inc(
                            self.sems[key], 16)

        with nc.Block() as block:
            block.tensor(lambda e: run(e, "t"))
            block.vector(lambda e: run(e, "v"))
            block.scalar(lambda e: run(e, "s"))
            block.gpsimd(lambda e: run(e, "g"))
            block.sync(lambda e: run(e, "y"))


class PSlots:
    def __init__(self, aps):
        self.aps = aps
        self.free_ms = [None] * len(aps)
        self.i = 0

    def next(self):
        i = self.i
        self.i = (self.i + 1) % len(self.aps)
        return i, self.aps[i], self.free_ms[i]

    def consumed(self, i, ms):
        self.free_ms[i] = ms


class WRing:
    def __init__(self, P, ware, cols):
        self.P, self.ware, self.cols = P, ware, cols
        self.head = 0
        self.live = []

    def load(self, dram_ap, ncols, queue="g"):
        if self.head + ncols > self.cols:
            self.head = 0
        start, end = self.head, self.head + ncols
        waits, keep = [], []
        for ent in self.live:
            if ent[0] < end and ent[1] > start:
                waits.append(ent[2][0])
            else:
                keep.append(ent)
        self.live = keep
        ap = self.ware[:, start:end]
        ms = self.P.dma(queue, ap, dram_ap, waits=waits)
        fh = [None]
        self.live.append((start, end, fh))
        self.head = end
        return ap, ms, fh

    @staticmethod
    def free(fh, ms):
        fh[0] = ms



class MK:
    def __init__(self, phases=("A", "B", "C", "D"), l0_layers=6, dbg=None):
        self.P = Prog()
        self.phases = phases
        self.l0_layers = l0_layers
        self.dbg = dbg
        nc = self.P.nc
        self.dram = {}

        def din(name, shape):
            self.dram[name] = nc.dram_tensor(name, list(shape), F32,
                                             kind="ExternalInput")
        din("x_in", (S, D))
        din("w_zh", (2, D, D))
        din("w_l0qkv", (6, D, 3072)); din("w_l0o", (6, D, D))
        din("w_l0w1p", (6, D, 5632)); din("w_l0w2p", (6, 2816, D))
        din("w_1qa", (D, 64)); din("w_1qb", (64, D))
        din("w_1kva", (D, 32))
        din("w_1kvb_k2", (32, 512)); din("w_1kvb_v", (32, 256))
        din("w_1o", (D, D))
        din("w_moe1", (8, D, 4096)); din("w_moe2", (8, 4096, D))
        din("w_g", (D, 8))
        self.pshapes = {
            "p_bz": (128, 8), "p_bh": (128, 8), "p_l0qs": (128, 6),
            "p_l0ks": (128, 6), "p_l0g": (128, 48), "p_l0b": (128, 48),
            "p_l0b1": (128, 44 * 6), "p_l0b2": (128, 48),
            "p_bout": (128, 8), "p_moeb1": (128, 256),
            "p_bg": (128, 1)}
        for pn, shp in self.pshapes.items():
            din(pn, shp)
        din("p_moeb2t", (8, 1024))
        cst, tbl = _consts()
        self._cst_np, self._tbl_np = cst, tbl
        for cn, cv in cst.items():
            din(cn, cv.shape)
        for tn, tv in tbl.items():
            din(tn, tv.shape)
        self.dram["x_out"] = nc.dram_tensor("x_out", [S, D], F32,
                                            kind="ExternalOutput")
        self.dram["xstate"] = nc.dram_tensor("xstate", [128, NT * S], F32,
                                             kind="Internal")
        self._alloc_sbuf()

    def _alloc_sbuf(self):
        nc = self.P.nc
        def sb(name, shape, dt):
            return nc.alloc_sbuf_tensor(name, list(shape), dt)
        self.XN = sb("XN", [128, NT * S], BF16)
        self.OUT = sb("OUT", [128, NT * S], BF16)
        self.Y = sb("Y", [128, NT * S], F32)
        self.WARE = sb("WARE", [128, 6 * 1024], BF16)
        self.XTC = sb("XTC", [128, 12 * 1024], U8)
        self.PLC = sb("PLC", [128, 46 * 1024], U8)
        xt = nc.lookup_mloc(self.XTC).addr
        pl = nc.lookup_mloc(self.PLC).addr
        yb = nc.lookup_mloc(self.Y).addr

        def at(name, shape, dt, base, off):
            return nc.alloc_sbuf_tensor_at(name, list(shape), dt,
                                           offset=base + off)
        self.TA = [at("TA0", [128, 1024], F32, xt, 0),
                   at("TA1", [128, 1024], F32, xt, 4096),
                   at("TA2", [128, 1024], F32, xt, 8192)]
        self.XS0 = at("XS0", [128, S], F32, xt, 0)
        self.TCS = at("TCS", [128, S], BF16, xt, 0)
        self.TSN = at("TSN", [128, S], BF16, xt, 4096)
        self.TI2 = at("TI2", [128, S], BF16, xt, 8192)
        self.T1C = at("T1C", [128, S], BF16, xt, 0)
        self.T1S = at("T1S", [128, S], BF16, xt, 4096)
        self.MU = at("MU", [128, S], F32, pl, 0)
        self.RS = at("RS", [128, S], F32, pl, 8192)
        self.Q1 = at("Q1", [128, S], F32, pl, 16384)
        self.Q2 = at("Q2", [128, S], F32, pl, 24576)
        self.BA = at("BA", [128, S], BF16, pl, 32768)
        self.BB = at("BB", [128, S], BF16, pl, 36864)
        self.BC = at("BC", [128, S], BF16, pl, 40960)
        self.SMALL = at("SMALL", [128, 512], F32, pl, 45056)
        self.BD = at("BD", [128, S], BF16, pl, 16384)
        self.BE = at("BE", [128, S], BF16, pl, 20480)
        self.SMF = at("SMF", [128, S], F32, pl, 24576)
        self.QARE = at("QARE", [128, 16 * 1024], BF16, yb, 0)
        self.KARE = at("KARE", [128, 8 * 1024], BF16, yb, 32768)
        self.VNARE = at("VNARE", [128, 8 * 1024], BF16, yb, 49152)
        self.HHARE = at("HHARE", [128, 32 * 1024], BF16, yb, 0)
        cst, tbl = self._cst_np, self._tbl_np
        self.C = {}
        for cn, cv in cst.items():
            dt = F32 if cn in _CONST_F32 else BF16
            self.C[cn] = sb("sb_" + cn, list(cv.shape), dt)
        for pn, shp in self.pshapes.items():
            self.C[pn] = sb("sb_" + pn, list(shp), F32)
        self.C["p_nbz"] = sb("sb_p_nbz", [128, 8], F32)
        self.psA = nc.alloc_psum_tensor("psA", [128, 4 * 512], F32)
        self.psB = nc.alloc_psum_tensor("psB", [128, 2 * 512], F32)
        self.psT = nc.alloc_psum_tensor("psT", [128, 1024], BF16)
        self.psF = nc.alloc_psum_tensor("psF", [128, 512], F32)
        self.slA = PSlots([self.psA[:, i * 512:(i + 1) * 512]
                           for i in range(4)])
        self.slB = PSlots([self.psB[:, i * 512:(i + 1) * 512]
                           for i in range(2)])
        self.wring = WRing(self.P, self.WARE, 6 * 1024)

    def load_consts(self):
        P = self.P
        for cn in self._cst_np:
            q = "y" if cn in _CONST_F32 else "g"
            P.dma(q, self.C[cn][:, :], self.dram[cn][:, :])
        for pn in self.pshapes:
            P.dma("y", self.C[pn][:, :], self.dram[pn][:, :])
        P.barrier()
        P.op("v", lambda e: e.tensor_scalar(
            out=self.C["p_nbz"][:, :], in0=self.C["p_bz"][:, :],
            scalar1=-1.0, scalar2=None, op0=ALU.mult), inc=True)
        P.barrier()

    def load_l0_tables(self):
        P = self.P
        for tn, h in (("t_cs", self.TCS), ("t_sn", self.TSN),
                      ("t_i2", self.TI2)):
            P.dma("g", h[:, :], self.dram[tn][:, :])

    def load_l1_tables(self):
        P = self.P
        for tn, h in (("t_r1c", self.T1C), ("t_r1s", self.T1S)):
            P.dma("g", h[:, :], self.dram[tn][:, :])

    def xs(self, t, c0=0, c1=S):
        return self.dram["xstate"][:, t * S + c0: t * S + c1]

    def xs3d(self, c0, n):
        return self.dram["xstate"][:, :].rearrange(
            "p (t s) -> p t s", t=NT)[:, :, c0:c0 + n]

    def phase_A(self):
        P = self.P
        cI = self.C["c_identf"]
        X = self.Y
        prev = [None, None]
        for sb_ in range(16):
            half = sb_ % 2
            stg = self.TA[half]
            mload = P.dma("y", stg[:, :],
                          self.dram["x_in"][sb_ * 128:(sb_ + 1) * 128, :],
                          waits=[prev[half]])
            mt = None
            for t in range(NT):
                src = stg[:, t * 128:(t + 1) * 128]
                pb = self.psA[:, half * 1024 + t * 128:
                              half * 1024 + t * 128 + 128]
                mt = P.op("t", lambda e, s_=src, p_=pb: e.transpose(
                    p_, s_, cI[:, :]),
                    waits=[mload, prev[half]], inc=(t == NT - 1))
            mv = P.op("v", lambda e, h_=half, s_=sb_: e.tensor_copy(
                out=X[:, :].rearrange("p (t s) -> p t s", t=NT)
                    [:, :, s_ * 128:s_ * 128 + 128],
                in_=self.psA[:, h_ * 1024:(h_ + 1) * 1024]
                    .rearrange("p (t s) -> p t s", s=128)),
                waits=[mt], inc=True)
            prev[half] = mv
        P.barrier()
        self.emit_ln(src_sbuf=X, dst=self.XN)
        P.barrier()
        self.emit_mingru(X)
        for t in range(NT):
            P.dma("y", self.xs(t), X[:, t * S:(t + 1) * S])
        P.barrier()

    def phase_E(self):
        P = self.P
        cI = self.C["c_identf"]
        prev = [None, None]
        prev_out = None
        for sb_ in range(16):
            half = sb_ % 2
            stg = self.TA[half]
            mload = P.dma("y", stg[:, :].rearrange("p (t s) -> p t s", s=128),
                          self.xs3d(sb_ * 128, 128), waits=[prev[half]])
            mt = None
            for t in range(NT):
                src = stg[:, t * 128:(t + 1) * 128]
                pb = self.psA[:, half * 1024 + t * 128:
                              half * 1024 + t * 128 + 128]
                mt = P.op("t", lambda e, s_=src, p_=pb: e.transpose(
                    p_, s_, cI[:, :]),
                    waits=[mload, prev[half]], inc=(t == NT - 1))
            mv = P.op("v", lambda e, h_=half: e.tensor_copy(
                out=self.TA[2][:, :],
                in_=self.psA[:, h_ * 1024:(h_ + 1) * 1024]),
                waits=[mt, prev_out], inc=True)
            prev[half] = mv
            prev_out = P.dma("y",
                             self.dram["x_out"][sb_ * 128:(sb_ + 1) * 128, :],
                             self.TA[2][:, :], waits=[mv])
        P.barrier()

    def emit_ln(self, src_sbuf, dst, shift=False, gain=False, gcol=0):
        P = self.P
        ones = self.C["c_ones"]
        RAW = self.OUT
        raw_ms = []
        for t in range(NT):
            if src_sbuf is not None:
                if shift and t >= 4:
                    m1 = P.op("s", lambda e, t_=t: e.copy(
                        out=RAW[:, t_ * S + 1:(t_ + 1) * S],
                        in_=src_sbuf[:, t_ * S:(t_ + 1) * S - 1]), inc=True)
                    m2 = P.op("v", lambda e, t_=t: e.memset(
                        RAW[:, t_ * S: t_ * S + 1], 0.0), inc=True)
                    raw_ms.append((m1, m2))
                else:
                    m1 = P.op("s", lambda e, t_=t: e.copy(
                        out=RAW[:, t_ * S:(t_ + 1) * S],
                        in_=src_sbuf[:, t_ * S:(t_ + 1) * S]), inc=True)
                    raw_ms.append((m1,))
            else:
                m1 = P.dma("g", RAW[:, t * S:(t + 1) * S], self.xs(t))
                raw_ms.append((m1,))
        for t in range(NT):
            for c in range(NCH):
                P.op("t", lambda e, t_=t, c_=c: e.matmul(
                    self.psA[:, c_ * 512:(c_ + 1) * 512], ones[:, :],
                    RAW[:, t_ * S + c_ * 512: t_ * S + (c_ + 1) * 512],
                    start=(t_ == 0), stop=(t_ == NT - 1)),
                    waits=list(raw_ms[t]),
                    inc=(t == NT - 1 and c == NCH - 1))
        msum = P.now("t")
        mmu = None
        for c in range(NCH):
            mmu = P.op("v", lambda e, c_=c: e.tensor_scalar(
                out=self.MU[:, c_ * 512:(c_ + 1) * 512],
                in0=self.psA[:, c_ * 512:(c_ + 1) * 512],
                scalar1=1.0 / 1024.0, scalar2=None, op0=ALU.mult),
                waits=[msum], inc=(c == NCH - 1))
        sqfree = [None, None]
        for t in range(NT):
            buf = self.BA if t % 2 == 0 else self.BB
            m = P.op("s", lambda e, t_=t, b_=buf: e.activation(
                out=b_[:, :], in_=RAW[:, t_ * S:(t_ + 1) * S],
                func=AF.Square),
                waits=list(raw_ms[t]) + [sqfree[t % 2]], inc=True)
            last = None
            for c in range(NCH):
                last = P.op("t", lambda e, t_=t, c_=c, b_=buf: e.matmul(
                    self.psA[:, c_ * 512:(c_ + 1) * 512], ones[:, :],
                    b_[:, c_ * 512:(c_ + 1) * 512],
                    start=(t_ == 0), stop=(t_ == NT - 1)),
                    waits=[m] + ([mmu] if t == 0 else []),
                    inc=(c == NCH - 1))
            sqfree[t % 2] = last
        msq = P.now("t")
        mq = P.op("s", lambda e: e.activation(
            out=self.Q1[:, :], in_=self.MU[:, :], func=AF.Square),
            waits=[mmu], inc=True)
        mv = None
        for c in range(NCH):
            mv = P.op("v", lambda e, c_=c: e.scalar_tensor_tensor(
                out=self.RS[:, c_ * 512:(c_ + 1) * 512],
                in0=self.psA[:, c_ * 512:(c_ + 1) * 512],
                scalar=1.0 / 1024.0,
                in1=self.Q1[:, c_ * 512:(c_ + 1) * 512],
                op0=ALU.mult, op1=ALU.subtract),
                waits=[msq, mq], inc=(c == NCH - 1))
        m1 = P.op("s", lambda e: e.activation(
            out=self.RS[:, :], in_=self.RS[:, :], func=AF.Ln,
            bias=self.C["c_scal"][:, 0:1]),
            waits=[mv], inc=True)
        m2 = P.op("s", lambda e: e.activation(
            out=self.RS[:, :], in_=self.RS[:, :], func=AF.Exp, scale=-0.5),
            waits=[m1], inc=True)
        ap_ms = []
        xt_free = [None, None]
        prev_mo = None
        for t in range(NT):
            if src_sbuf is not None:
                sap, sw = src_sbuf[:, t * S:(t + 1) * S], None
            else:
                buf = self.XS0 if t % 2 == 0 else self.Q2
                sw = P.dma("g", buf[:, :], self.xs(t), waits=[xt_free[t % 2]])
                sap = buf[:, :]
            if shift and t >= 4:
                ma = P.op("v", lambda e, s_=sap: e.tensor_sub(
                    out=self.Q1[:, 1:S], in0=s_[:, 0:S - 1],
                    in1=self.MU[:, 1:S]), waits=[mmu, sw, mq, mv, prev_mo], inc=True)
                mb = P.op("v", lambda e: e.tensor_scalar(
                    out=self.Q1[:, 0:1], in0=self.MU[:, 0:1], scalar1=-1.0,
                    scalar2=None, op0=ALU.mult), waits=[mmu], inc=True)
                tw = [ma, Prog.sync(mb)]
            else:
                ma = P.op("v", lambda e, s_=sap: e.tensor_sub(
                    out=self.Q1[:, :], in0=s_, in1=self.MU[:, :]),
                    waits=[mmu, sw, mq, mv, prev_mo], inc=True)
                tw = [ma]
            if not gain:
                mo = P.op("v", lambda e, t_=t: e.tensor_mul(
                    out=dst[:, t_ * S:(t_ + 1) * S], in0=self.Q1[:, :],
                    in1=self.RS[:, :]), waits=tw + [m2], inc=True)
            else:
                mt2 = P.op("v", lambda e: e.tensor_mul(
                    out=self.Q1[:, :], in0=self.Q1[:, :], in1=self.RS[:, :]),
                    waits=tw + [m2], inc=True)
                g_ap = self.C["p_l0g"][:, gcol * 8 + t: gcol * 8 + t + 1]
                b_ap = self.C["p_l0b"][:, gcol * 8 + t: gcol * 8 + t + 1]
                mo = P.op("v", lambda e, t_=t, g_=g_ap, b_=b_ap: e.tensor_scalar(
                    out=dst[:, t_ * S:(t_ + 1) * S], in0=self.Q1[:, :],
                    scalar1=g_, scalar2=b_, op0=ALU.mult, op1=ALU.add),
                    waits=[mt2], inc=True)
            ap_ms.append(mo)
            prev_mo = mo
            if src_sbuf is None:
                xt_free[t % 2] = mo
        self._ln_done = ap_ms
        return ap_ms

    def emit_mingru(self, X):
        P = self.P
        q1_free = [None] * NCH
        mu_free = None
        for et in range(NT):
            wz_ap, wz_ms, wz_fh = self.wring.load(
                self.dram["w_zh"][0].rearrange("(k p) e -> p k e", p=128)
                [:, :, et * 128:(et + 1) * 128], 1024)
            wh_ap, wh_ms, wh_fh = self.wring.load(
                self.dram["w_zh"][1].rearrange("(k p) e -> p k e", p=128)
                [:, :, et * 128:(et + 1) * 128], 1024)
            bz = self.C["p_bz"][:, et:et + 1]
            bh = self.C["p_bh"][:, et:et + 1]
            nbz = self.C["p_nbz"][:, et:et + 1]
            zms = []
            for c in range(NCH):
                iz, pz, fz = self.slA.next()
                mmz = None
                for kt in range(NT):
                    mmz = P.op("t", lambda e, k_=kt, c_=c, p_=pz, w_=wz_ap:
                               e.matmul(p_, w_[:, k_ * 128:(k_ + 1) * 128],
                                        self.XN[:, k_ * S + c_ * 512:
                                                k_ * S + (c_ + 1) * 512],
                                        start=(k_ == 0), stop=(k_ == NT - 1)),
                               waits=([wz_ms, fz] + list(self._ln_done)
                                      if kt == 0 else []),
                               inc=(kt == NT - 1))
                ih, ph, fh2 = self.slB.next()
                mmh = None
                for kt in range(NT):
                    mmh = P.op("t", lambda e, k_=kt, c_=c, p_=ph, w_=wh_ap:
                               e.matmul(p_, w_[:, k_ * 128:(k_ + 1) * 128],
                                        self.XN[:, k_ * S + c_ * 512:
                                                k_ * S + (c_ + 1) * 512],
                                        start=(k_ == 0), stop=(k_ == NT - 1)),
                               waits=[wh_ms, fh2] if kt == 0 else [],
                               inc=(kt == NT - 1))
                mz = P.op("s", lambda e, c_=c, p_=pz, b_=bz: e.activation(
                    out=self.Q1[:, c_ * 512:(c_ + 1) * 512], in_=p_,
                    func=AF.Sigmoid, bias=b_),
                    waits=[mmz, q1_free[c]], inc=True)
                mb = P.op("v", lambda e, c_=c, p_=ph, b_=bh:
                          e.scalar_tensor_tensor(
                              out=self.Q2[:, c_ * 512:(c_ + 1) * 512],
                              in0=p_, scalar=b_,
                              in1=self.Q1[:, c_ * 512:(c_ + 1) * 512],
                              op0=ALU.add, op1=ALU.mult),
                          waits=[mmh, mz], inc=True)
                self.slB.consumed(ih, mb)
                q1_free[c] = mb
                ma = P.op("s", lambda e, c_=c, p_=pz, nb_=nbz: e.activation(
                    out=self.MU[:, c_ * 512:(c_ + 1) * 512],
                    in_=p_, func=AF.Sigmoid, scale=-1.0, bias=nb_),
                    waits=[mmz, mu_free], inc=True)
                self.slA.consumed(iz, ma)
                zms += [mb, ma]
            WRing.free(wz_fh, P.now("t"))
            WRing.free(wh_fh, P.now("t"))
            msc = P.op("v", lambda e: e.tensor_tensor_scan(
                out=self.RS[:, :], data0=self.MU[:, :], data1=self.Q2[:, :],
                initial=0.0, op0=ALU.mult, op1=ALU.add), waits=zms, inc=True)
            mu_free = msc
            P.op("v", lambda e, t_=et: e.tensor_add(
                out=X[:, t_ * S:(t_ + 1) * S],
                in0=X[:, t_ * S:(t_ + 1) * S], in1=self.RS[:, :]),
                waits=[msc], inc=True)
        P.barrier()

    def build(self):
        P = self.P
        self._ln_done = []
        self.load_consts()
        if "A" in self.phases:
            self.phase_A()
        if "B" in self.phases:
            self.phase_B()
        if "C" in self.phases:
            self.phase_C()
        if "D" in self.phases:
            self.phase_D()
        self.phase_E()
        P.emit()
        return P.nc

    def mm_proj_chunk(self, ps, wap, c, shift, w0, src=None, nk=NT):
        P = self.P
        src = self.XN if src is None else src
        last = None
        for kt in range(nk):
            sh = shift and kt >= nk // 2
            if not sh:
                rhs = src[:, kt * S + c * 512: kt * S + (c + 1) * 512]
                out = ps
            elif c > 0:
                rhs = src[:, kt * S + c * 512 - 1: kt * S + (c + 1) * 512 - 1]
                out = ps
            else:
                rhs = src[:, kt * S: kt * S + 511]
                out = ps[:, 1:512]
            last = P.op("t", lambda e, k_=kt, r_=rhs, o_=out, w_=wap:
                        e.matmul(o_, w_[:, k_ * 128:(k_ + 1) * 128], r_,
                                 start=(k_ == 0), stop=(k_ == nk - 1)),
                        waits=(w0 if kt == 0 else ()), inc=(kt == nk - 1))
        return last

    def _qk_pipeline(self, lay, wap, w_ms, w_fh, dst, scale_ap, use_i2,
                     cast_now):
        P = self.P
        mev_all = []
        mln_all = []
        for c in range(NCH):
            islot, ps, fz = self.slA.next()
            mm = self.mm_proj_chunk(ps, wap, c, True, [w_ms, fz, cast_now])
            mev = P.op("v", lambda e, c_=c, p_=ps: e.tensor_copy(
                out=dst[:, c_ * 512:(c_ + 1) * 512], in_=p_),
                waits=[mm, self._dst_free.get(id(dst))], inc=True)
            self.slA.consumed(islot, mev)
            mev_all.append(mev)
            msq = P.op("s", lambda e, c_=c: e.activation(
                out=self.BA[:, c_ * 512:(c_ + 1) * 512],
                in_=dst[:, c_ * 512:(c_ + 1) * 512], func=AF.Square),
                waits=[mev, self._ba_free, self._ba_read], inc=True)
            mss = P.op("t", lambda e, c_=c: e.matmul(
                self.psF[:, :], self.C["c_bd64"][:, :],
                self.BA[:, c_ * 512:(c_ + 1) * 512], start=True, stop=True),
                waits=[msq, self._psF_free], inc=True)
            self._ba_free = mss
            mln = P.op("s", lambda e, c_=c: e.activation(
                out=self.MU[:, c_ * 512:(c_ + 1) * 512], in_=self.psF[:, :],
                func=AF.Ln, bias=self.C["c_scal"][:, 6:7]),
                waits=[mss, self._mu_free], inc=True)
            self._psF_free = mln
            mln_all.append(mln)
        mex = P.op("s", lambda e: e.activation(
            out=self.MU[:, :], in_=self.MU[:, :], func=AF.Exp, scale=-0.5),
            waits=mln_all, inc=True)
        msc = P.op("s", lambda e, s_=scale_ap: e.mul(
            out=self.MU[:, :], in_=self.MU[:, :], mul=s_),
            waits=[mex], inc=True)
        mqn = P.op("v", lambda e: e.tensor_mul(
            out=dst[:, :], in0=dst[:, :], in1=self.MU[:, :]),
            waits=[msc] + mev_all, inc=True)
        self._mu_free = mqn
        msm_all = []
        mro = None
        for c in range(NCH):
            islot, ps, fz = self.slB.next()
            mro = P.op("t", lambda e, c_=c, p_=ps: e.matmul(
                p_, self.C["c_rt128"][:, :],
                dst[:, c_ * 512:(c_ + 1) * 512], start=True, stop=True),
                waits=[mqn, fz], inc=True)
            msm = P.op("v", lambda e, c_=c, p_=ps: e.tensor_mul(
                out=self.SMF[:, c_ * 512:(c_ + 1) * 512], in0=p_,
                in1=self.TSN[:, c_ * 512:(c_ + 1) * 512]),
                waits=[mro, self._sm_free[0], self._sm_free[1]], inc=True)
            self.slB.consumed(islot, msm)
            msm_all.append(msm)
        mtc = P.op("v", lambda e: e.tensor_mul(
            out=dst[:, :], in0=dst[:, :], in1=self.TCS[:, :]),
            waits=[mro], inc=True)
        mad = P.op("v", lambda e: e.tensor_add(
            out=dst[:, :], in0=dst[:, :], in1=self.SMF[:, :]),
            waits=[mtc] + msm_all, inc=True)
        self._smf_read = mad
        if use_i2:
            mad = P.op("v", lambda e: e.tensor_mul(
                out=dst[:, :], in0=dst[:, :], in1=self.TI2[:, :]),
                waits=[mad], inc=True)
        WRing.free(w_fh, P.now("t"))
        return mad

    def l0_attn_hg(self, lay, hg, cast_now):
        P = self.P
        def colpan(sel, c0):
            return self.dram["w_l0qkv"][lay].rearrange(
                "(k p) e -> p k e", p=128)[:, :, c0 + hg * 128:
                                           c0 + hg * 128 + 128]
        wq, wq_ms, wq_fh = self.wring.load(colpan(0, 0), 1024)
        wk, wk_ms, wk_fh = self.wring.load(colpan(1, 1024), 1024)
        wv, wv_ms, wv_fh = self.wring.load(colpan(2, 2048), 1024)
        vt_ms = []
        for c in range(NCH):
            islot, ps, fz = self.slA.next()
            mm = self.mm_proj_chunk(ps, wv, c, True, [wv_ms, fz, cast_now])
            mev = P.op("s", lambda e, c_=c, p_=ps: e.copy(
                out=self.BC[:, c_ * 512:(c_ + 1) * 512], in_=p_),
                waits=[mm, self._bc_free], inc=True)
            self.slA.consumed(islot, mev)
            vt_ms.append(mev)
        WRing.free(wv_fh, P.now("t"))
        qs_ap = self.C["p_l0qs"][:, lay:lay + 1]
        ks_ap = self.C["p_l0ks"][:, lay:lay + 1]
        mq = self._qk_pipeline(lay, wq, wq_ms, wq_fh, self.BD, qs_ap,
                               False, cast_now)
        mk = self._qk_pipeline(lay, wk, wk_ms, wk_fh, self.BE, ks_ap,
                               True, cast_now)
        vn_ms = []
        for hh in (0, 1):
            mtr = None
            for jb in range(16):
                mtr = P.op("t", lambda e, h_=hh, j_=jb: e.transpose(
                    self.psT[:, j_ * 64:(j_ + 1) * 64],
                    self.BC[h_ * 64:h_ * 64 + 64, j_ * 128:(j_ + 1) * 128],
                    self.C["c_identb"][h_ * 64:h_ * 64 + 64,
                                       h_ * 64:h_ * 64 + 64]),
                    waits=vt_ms + [self._psT_free], inc=(jb == 15))
            mvn = P.op("v", lambda e, h_=hh: e.tensor_copy(
                out=self.BA[:, :].rearrange("p (b c) -> p b c", b=16)
                    [:, :, h_ * 64:h_ * 64 + 64],
                in_=self.psT[:, :].rearrange("p (b c) -> p b c", c=64)),
                waits=[mtr, self._ba_free], inc=True)
            self._psT_free = mvn
            vn_ms.append(mvn)
        if self.dbg == "qk0" and hg == 0:
            P.barrier()
            for src_, ti in ((self.BD, 0), (self.BE, 1), (self.BC, 2),
                             (self.BA, 3)):
                P.op("v", lambda e, s_=src_, t_=ti: e.tensor_copy(
                    out=self.Y[:, t_ * S:(t_ + 1) * S], in_=s_[:, :]),
                    inc=True)
            self._dump_y()
            self._dbg_stop = True
            return []
        out_ms = []
        for hh in (0, 1):
            for w in range(8):
                out_ms.append(self._l0_attn_win(hg, hh, w, vn_ms,
                                                [mq, mk]))
                if self._dbg_stop:
                    return []
        tnow = P.now("t")
        self._ba_read = tnow
        self._bc_free = tnow
        self._dst_free[id(self.BD)] = tnow
        self._dst_free[id(self.BE)] = tnow
        return out_ms

    def _l0_attn_win(self, hg, hh, w, vn_ms, qk_ms):
        P = self.P
        hr = 64 * hh
        width = 512 if w > 0 else 256
        j0 = (w * 256 - 256) if w > 0 else 0
        mpp_l = []
        tr_last = None
        for ib in (0, 1):
            i0 = w * 256 + ib * 128
            mask = self.C["c_mke"] if ib == 0 else self.C["c_mko"]
            mask_ap = mask[:, 0:width] if w > 0 else mask[:, 256:512]
            islot, ps, fz = self.slA.next()
            msmm = P.op("t", lambda e, p_=ps, i0_=i0, hr_=hr, j0_=j0,
                        wd_=width: e.matmul(
                p_[:, 0:wd_],
                self.BD[hr_:hr_ + 64, i0_:i0_ + 128],
                self.BE[hr_:hr_ + 64, j0_:j0_ + wd_],
                start=True, stop=True),
                waits=qk_ms + [fz], inc=True)
            msm = P.op("v", lambda e, p_=ps, m_=mask_ap, ib_=ib,
                       wd_=width: e.scalar_tensor_tensor(
                           out=self.SMF[:, ib_ * 512: ib_ * 512 + wd_],
                           in0=p_[:, 0:wd_], scalar=8.0, in1=m_,
                           op0=ALU.mult, op1=ALU.add),
                       waits=[msmm, self._sm_free[ib]], inc=True)
            self.slA.consumed(islot, msm)
            mnm = P.op("v", lambda e, ib_=ib, wd_=width: e.reduce_max(
                out=self.SMALL[:, ib_:ib_ + 1],
                in_=self.SMF[:, ib_ * 512: ib_ * 512 + wd_],
                axis=AX.X, negate=True),
                waits=[msm, self._stat_free[ib]], inc=True)
            p0 = self.BB[:, 1024 + ib * 512: 1024 + ib * 512 + width]
            mz0 = P.op("v", lambda e, ib_=ib: e.memset(
                self.SMALL[:, 2 + ib_:3 + ib_], 0.0),
                waits=[self._stat_free[ib]], inc=True)
            mex = P.op("s", lambda e, p0_=p0, ib_=ib, wd_=width:
                       e.activation(
                out=p0_, in_=self.SMF[:, ib_ * 512: ib_ * 512 + wd_],
                func=AF.Exp, bias=self.SMALL[:, ib_:ib_ + 1],
                accum_out=self.SMALL[:, 2 + ib_:3 + ib_]),
                waits=[mnm, self._p0_free[ib], mz0], inc=True)
            self._sm_free[ib] = mex
            miz = P.op("v", lambda e, ib_=ib: e.reciprocal(
                out=self.SMALL[:, 4 + ib_:5 + ib_],
                in_=self.SMALL[:, 2 + ib_:3 + ib_]), waits=[mex], inc=True)
            mpp = P.op("v", lambda e, p0_=p0, ib_=ib: e.tensor_scalar(
                out=p0_, in0=p0_, scalar1=self.SMALL[:, 4 + ib_:5 + ib_],
                scalar2=None, op0=ALU.mult), waits=[Prog.sync(miz)], inc=True)
            self._stat_free[ib] = mpp
            mpp_l.append(mpp)
            njb = 4 if w > 0 else 2
            for jb in range(njb):
                jbg = jb + (2 if w == 0 else 0)
                tr_last = P.op("t", lambda e, p0_=p0, j_=jb, jg_=jbg,
                               ib_=ib: e.transpose(
                    self.psT[:, jg_ * 256 + ib_ * 128:
                             jg_ * 256 + ib_ * 128 + 128],
                    p0_[:, j_ * 128:(j_ + 1) * 128],
                    self.C["c_identb"][:, :]),
                    waits=[mpp, self._psT_free],
                    inc=(ib == 1 and jb == njb - 1))
        mpt = P.op("v", lambda e: e.tensor_copy(
            out=self.BB[:, 0:1024], in_=self.psT[:, :]),
            waits=[tr_last, self._pt_read], inc=True)
        if self.dbg == "pw" and hg == 0 and hh == 0 and w == 2:
            P.barrier()
            for src_, ti in ((self.SMF, 0), (self.BB, 2), (self.BA, 4)):
                P.op("v", lambda e, s_=src_, t_=ti: e.tensor_copy(
                    out=self.Y[:, t_ * S:(t_ + 1) * S], in_=s_[:, :]),
                    inc=True)
            P.op("v", lambda e: e.tensor_copy(
                out=self.Y[:, 5 * S: 5 * S + 512],
                in_=self.SMALL[:, :]), inc=True)
            self._dump_y()
            self._dbg_stop = True
            return None
        self._psT_free = mpt
        for ib in (0, 1):
            self._p0_free[ib] = mpt
        jlist = list(range(2, 4)) if w == 0 else list(range(4))
        mav = None
        for n_, jbg in enumerate(jlist):
            blk = 2 * w - 2 + jbg
            mav = P.op("t", lambda e, j_=jbg, b_=blk,
                       st_=(n_ == 0), sp_=(n_ == len(jlist) - 1): e.matmul(
                self.psF[:, 0:256],
                self.BA[:, b_ * 128: b_ * 128 + 128],
                self.BB[:, j_ * 256: j_ * 256 + 256],
                start=st_, stop=sp_),
                waits=[mpt] + vn_ms + [self._psF_free],
                inc=(n_ == len(jlist) - 1))
        self._pt_read = mav
        mo = P.op("s", lambda e, hr_=hr, hg_=hg, w_=w: e.copy(
            out=self.OUT[hr_:hr_ + 64,
                         hg_ * S + w_ * 256: hg_ * S + w_ * 256 + 256],
            in_=self.psF[hr_:hr_ + 64, 0:256]), waits=[mav], inc=True)
        self._psF_free = mo
        return mo

    def l0_wo(self, lay):
        P = self.P
        for dt in range(NT):
            slab, w_ms, fh = self.wring.load(
                self.dram["w_l0o"][lay].rearrange("(k p) e -> p k e", p=128)
                [:, :, dt * 128:(dt + 1) * 128], 1024)
            for c in range(NCH):
                islot, ps, fz = self.slA.next()
                mm = self.mm_proj_chunk(ps, slab, c, False, [w_ms, fz],
                                        src=self.OUT)
                mev = P.op("v", lambda e, p_=ps, d_=dt, c_=c: e.tensor_add(
                    out=self.Y[:, d_ * S + c_ * 512: d_ * S + (c_ + 1) * 512],
                    in0=p_,
                    in1=self.Y[:, d_ * S + c_ * 512: d_ * S + (c_ + 1) * 512]),
                    waits=[mm], inc=True)
                self.slA.consumed(islot, mev)
            WRing.free(fh, P.now("t"))

    def l0_ffn(self, lay):
        P = self.P
        qsplit = [(0, 6), (6, 12), (12, 18), (18, 22)]
        g_free = [None] * NCH
        hh_free = None
        for qr, (f0, f1) in enumerate(qsplit):
            for fa in range(f0, f1):
                wa, wa_ms, wa_fh = self.wring.load(
                    self.dram["w_l0w1p"][lay].rearrange(
                        "(k p) e -> p k e", p=128)
                    [:, :, fa * 128:(fa + 1) * 128], 1024)
                wg, wg_ms, wg_fh = self.wring.load(
                    self.dram["w_l0w1p"][lay].rearrange(
                        "(k p) e -> p k e", p=128)
                    [:, :, 2816 + fa * 128: 2816 + (fa + 1) * 128], 1024)
                b1a = self.C["p_l0b1"][:, lay * 44 + fa: lay * 44 + fa + 1]
                b1g = self.C["p_l0b1"][:, lay * 44 + 22 + fa:
                                       lay * 44 + 22 + fa + 1]
                hhofs = (fa - f0) * S
                for c in range(NCH):
                    ia, pa, fza = self.slA.next()
                    mma = self.mm_proj_chunk(pa, wa, c, False, [wa_ms, fza])
                    ig, pg, fzg = self.slB.next()
                    mmg = self.mm_proj_chunk(pg, wg, c, False, [wg_ms, fzg])
                    mg = P.op("s", lambda e, p_=pg, c_=c, b_=b1g:
                              e.activation(
                                  out=self.BA[:, c_ * 512:(c_ + 1) * 512],
                                  in_=p_, func=AF.Gelu, bias=b_),
                              waits=[mmg, g_free[c]], inc=True)
                    self.slB.consumed(ig, mg)
                    mh = P.op("v", lambda e, p_=pa, c_=c, b_=b1a, o_=hhofs:
                              e.scalar_tensor_tensor(
                                  out=self.OUT[:, o_ + c_ * 512:
                                               o_ + (c_ + 1) * 512],
                                  in0=p_, scalar=b_,
                                  in1=self.BA[:, c_ * 512:(c_ + 1) * 512],
                                  op0=ALU.add, op1=ALU.mult),
                              waits=[mma, mg, hh_free], inc=True)
                    self.slA.consumed(ia, mh)
                    g_free[c] = mh
                WRing.free(wa_fh, P.now("t"))
                WRing.free(wg_fh, P.now("t"))
            hh_ready = P.now("v")
            nk = f1 - f0
            for dt in range(NT):
                slab, w_ms, fh = self.wring.load(
                    self.dram["w_l0w2p"][lay].rearrange(
                        "(k p) d -> p k d", p=128)
                    [:, f0:f1, dt * 128:(dt + 1) * 128], nk * 128)
                b2 = self.C["p_l0b2"][:, lay * 8 + dt: lay * 8 + dt + 1]
                for c in range(NCH):
                    islot, ps, fz = self.slA.next()
                    mm = self.mm_proj_chunk(ps, slab, c, False,
                                            [w_ms, fz, hh_ready],
                                            src=self.OUT, nk=nk)
                    if qr == 0:
                        mev = P.op("v", lambda e, p_=ps, d_=dt, c_=c, b_=b2:
                                   e.scalar_tensor_tensor(
                                       out=self.Y[:, d_ * S + c_ * 512:
                                                  d_ * S + (c_ + 1) * 512],
                                       in0=p_, scalar=b_,
                                       in1=self.Y[:, d_ * S + c_ * 512:
                                                  d_ * S + (c_ + 1) * 512],
                                       op0=ALU.add, op1=ALU.add),
                                   waits=[mm], inc=True)
                    else:
                        mev = P.op("v", lambda e, p_=ps, d_=dt, c_=c:
                                   e.tensor_add(
                                       out=self.Y[:, d_ * S + c_ * 512:
                                                  d_ * S + (c_ + 1) * 512],
                                       in0=p_,
                                       in1=self.Y[:, d_ * S + c_ * 512:
                                                  d_ * S + (c_ + 1) * 512]),
                                   waits=[mm], inc=True)
                    self.slA.consumed(islot, mev)
                WRing.free(fh, P.now("t"))
            hh_free = P.now("t")

    def _dump_y(self):
        P = self.P
        P.barrier()
        for t in range(NT):
            P.dma("y", self.xs(t), self.Y[:, t * S:(t + 1) * S])
        P.barrier()

    def phase_B(self):
        P = self.P
        P.barrier()
        self.emit_ln(src_sbuf=None, dst=self.Y)
        P.barrier()
        if self.dbg == "ln":
            self._dump_y()
            return
        self.load_l0_tables()
        P.barrier()
        for lay in range(self.l0_layers):
            for t in range(NT):
                P.op("s", lambda e, t_=t: e.copy(
                    out=self.XN[:, t_ * S:(t_ + 1) * S],
                    in_=self.Y[:, t_ * S:(t_ + 1) * S]),
                    inc=(t == NT - 1))
            cast_now = P.now("s")
            self._reset_attn_tracking()
            self._dbg_stop = False
            for hg in range(NT):
                self.l0_attn_hg(lay, hg, cast_now)
                if self._dbg_stop:
                    return
            P.barrier()
            if self.dbg == "out0" and lay == 0:
                for t in range(NT):
                    P.op("v", lambda e, t_=t: e.tensor_copy(
                        out=self.Y[:, t_ * S:(t_ + 1) * S],
                        in_=self.OUT[:, t_ * S:(t_ + 1) * S]), inc=True)
                self._dump_y()
                return
            self.l0_wo(lay)
            P.barrier()
            if self.dbg == "attn" and lay == 0:
                self._dump_y()
                return
            self.emit_ln(src_sbuf=self.Y, dst=self.XN, shift=True,
                         gain=True, gcol=lay)
            P.barrier()
            if self.dbg == "ffnln" and lay == 0:
                for t in range(NT):
                    P.op("v", lambda e, t_=t: e.tensor_copy(
                        out=self.Y[:, t_ * S:(t_ + 1) * S],
                        in_=self.XN[:, t_ * S:(t_ + 1) * S]), inc=True)
                self._dump_y()
                return
            self.l0_ffn(lay)
            P.barrier()
        prev = [None, None]
        for t in range(NT):
            buf = self.XS0 if t % 2 == 0 else self.Q2
            ml = P.dma("y", buf[:, :], self.xs(t), waits=[prev[t % 2]])
            ma = P.op("v", lambda e, t_=t, b_=buf: e.tensor_add(
                out=self.Y[:, t_ * S:(t_ + 1) * S],
                in0=self.Y[:, t_ * S:(t_ + 1) * S], in1=b_[:, :]),
                waits=[ml], inc=True)
            prev[t % 2] = ma
            P.dma("y", self.xs(t), self.Y[:, t * S:(t + 1) * S], waits=[ma])
        P.barrier()

    def _reset_attn_tracking(self):
        self._psA_free = None
        self._vtt_free = None
        self._hh_free = None
        self._wbc_read = None
        self._wbc_ms = {}
        self._sidx = 0
        self._dst_free = {}
        self._ba_free = None
        self._ba_read = None
        self._bc_free = None
        self._mu_free = None
        self._smf_free = None
        self._smf_read = None
        self._psF_free = None
        self._psT_free = None
        self._pt_read = None
        self._sm_free = [None, None]
        self._p0_free = [None, None]
        self._stat_free = [None, None]

    def _l1_rope(self, tile_ap, rot_rows, ready):
        P = self.P
        msm_all = []
        mro = None
        for c in range(NCH):
            islot, ps, fz = self.slB.next()
            mro = P.op("t", lambda e, c_=c, p_=ps, rr_=rot_rows: e.matmul(
                p_[0:rr_, :], self.C["c_rt128"][0:rr_, 0:rr_],
                tile_ap[0:rr_, c_ * 512:(c_ + 1) * 512],
                start=True, stop=True),
                waits=[ready, fz], inc=True)
            msm = P.op("v", lambda e, c_=c, p_=ps, rr_=rot_rows:
                       e.tensor_mul(
                           out=self.SMF[0:rr_, c_ * 512:(c_ + 1) * 512],
                           in0=p_[0:rr_, :],
                           in1=self.T1S[0:rr_, c_ * 512:(c_ + 1) * 512]),
                       waits=[mro, self._smf_read], inc=True)
            self.slB.consumed(islot, msm)
            msm_all.append(msm)
        mtc = P.op("v", lambda e, rr_=rot_rows: e.tensor_mul(
            out=tile_ap[0:rr_, :], in0=tile_ap[0:rr_, :],
            in1=self.T1C[0:rr_, :]), waits=[mro], inc=True)
        mad = P.op("v", lambda e, rr_=rot_rows: e.tensor_add(
            out=tile_ap[0:rr_, :], in0=tile_ap[0:rr_, :],
            in1=self.SMF[0:rr_, :]), waits=[mtc] + msm_all, inc=True)
        self._smf_read = mad
        return mad

    def phase_C(self):
        P = self.P
        self.PT1 = self.BE
        self.VTT = self.BC
        P.barrier()
        self.emit_ln(src_sbuf=None, dst=self.XN)
        P.barrier()
        self.load_l1_tables()
        P.barrier()
        self._reset_attn_tracking()
        self._smf_read = None
        wqa, wqa_ms, wqa_fh = self.wring.load(
            self.dram["w_1qa"].rearrange("(k p) e -> p k e", p=128), 512)
        t1_ms = []
        for c in range(NCH):
            islot, ps, fz = self.slB.next()
            mm = None
            for kt in range(NT):
                mm = P.op("t", lambda e, k_=kt, c_=c, p_=ps, w_=wqa:
                          e.matmul(p_[0:64, :], w_[:, k_ * 64:(k_ + 1) * 64],
                                   self.XN[:, k_ * S + c_ * 512:
                                           k_ * S + (c_ + 1) * 512],
                                   start=(k_ == 0), stop=(k_ == NT - 1)),
                          waits=[wqa_ms, fz] if kt == 0 else (),
                          inc=(kt == NT - 1))
            mev = P.op("s", lambda e, c_=c, p_=ps: e.copy(
                out=self.BA[0:64, c_ * 512:(c_ + 1) * 512], in_=p_[0:64, :]),
                waits=[mm], inc=True)
            self.slB.consumed(islot, mev)
            t1_ms.append(mev)
        WRing.free(wqa_fh, P.now("t"))
        wqb = self.BB[0:64, 0:1024]
        wqb_ms = P.dma("g", wqb, self.dram["w_1qb"][:, :])
        q_ms = []
        for et in range(NT):
            evs = []
            for c in range(NCH):
                islot, ps, fz = self.slA.next()
                mm = P.op("t", lambda e, c_=c, p_=ps, e_=et, w_=wqb:
                          e.matmul(p_, w_[:, e_ * 128:(e_ + 1) * 128],
                                   self.BA[0:64, c_ * 512:(c_ + 1) * 512],
                                   start=True, stop=True),
                          waits=[wqb_ms] + t1_ms + [fz], inc=True)
                mev = P.op("v", lambda e, c_=c, p_=ps, e_=et: e.tensor_copy(
                    out=self.QARE[:, e_ * S + c_ * 512:
                                  e_ * S + (c_ + 1) * 512], in_=p_),
                    waits=[mm], inc=True)
                self.slA.consumed(islot, mev)
                evs.append(mev)
            mq = self._l1_rope(self.QARE[:, et * S:(et + 1) * S], 128,
                               evs[-1])
            q_ms.append(mq)
        wkva, wkva_ms, wkva_fh = self.wring.load(
            self.dram["w_1kva"].rearrange("(k p) e -> p k e", p=128), 256)
        t2_ms = []
        for c in range(NCH):
            islot, ps, fz = self.slB.next()
            mm = None
            for kt in range(NT):
                mm = P.op("t", lambda e, k_=kt, c_=c, p_=ps, w_=wkva:
                          e.matmul(p_[0:32, :], w_[:, k_ * 32:(k_ + 1) * 32],
                                   self.XN[:, k_ * S + c_ * 512:
                                           k_ * S + (c_ + 1) * 512],
                                   start=(k_ == 0), stop=(k_ == NT - 1)),
                          waits=[wkva_ms, fz] + t1_ms if kt == 0 else (),
                          inc=(kt == NT - 1))
            mev = P.op("s", lambda e, c_=c, p_=ps: e.copy(
                out=self.BA[0:32, c_ * 512:(c_ + 1) * 512], in_=p_[0:32, :]),
                waits=[mm] + q_ms, inc=True)
            self.slB.consumed(islot, mev)
            t2_ms.append(mev)
        WRing.free(wkva_fh, P.now("t"))
        wk2 = self.BB[0:32, 1024:1536]
        wk2_ms = P.dma("g", wk2, self.dram["w_1kvb_k2"][:, :])
        k_ms = []
        for g in range(4):
            evs = []
            for c in range(NCH):
                islot, ps, fz = self.slA.next()
                mm = P.op("t", lambda e, c_=c, p_=ps, g_=g, w_=wk2:
                          e.matmul(p_, w_[:, g_ * 128:(g_ + 1) * 128],
                                   self.BA[0:32, c_ * 512:(c_ + 1) * 512],
                                   start=True, stop=True),
                          waits=[wk2_ms] + t2_ms + [fz], inc=True)
                mev = P.op("v", lambda e, c_=c, p_=ps, g_=g: e.tensor_copy(
                    out=self.KARE[:, g_ * S + c_ * 512:
                                  g_ * S + (c_ + 1) * 512], in_=p_),
                    waits=[mm], inc=True)
                self.slA.consumed(islot, mev)
                evs.append(mev)
            mk = self._l1_rope(self.KARE[:, g * S:(g + 1) * S], 128,
                               evs[-1])
            k_ms.append(mk)
        wv = self.BB[0:32, 1536:1792]
        wv_ms = P.dma("g", wv, self.dram["w_1kvb_v"][:, :])
        vn_ms = []
        for g in range(4):
            evs = []
            for c in range(NCH):
                islot, ps, fz = self.slB.next()
                mm = P.op("t", lambda e, c_=c, p_=ps, g_=g, w_=wv:
                          e.matmul(p_[0:64, :], w_[:, g_ * 64:(g_ + 1) * 64],
                                   self.BA[0:32, c_ * 512:(c_ + 1) * 512],
                                   start=True, stop=True),
                          waits=[wv_ms] + t2_ms + [fz, self._vtt_free],
                          inc=True)
                mev = P.op("s", lambda e, c_=c, p_=ps: e.copy(
                    out=self.VTT[0:64, c_ * 512:(c_ + 1) * 512],
                    in_=p_[0:64, :]), waits=[mm], inc=True)
                self.slB.consumed(islot, mev)
                evs.append(mev)
            mtr = None
            for jb in range(16):
                mtr = P.op("t", lambda e, j_=jb: e.transpose(
                    self.psT[:, j_ * 64:(j_ + 1) * 64],
                    self.VTT[0:64, j_ * 128:(j_ + 1) * 128],
                    self.C["c_identb"][0:64, 0:64]),
                    waits=evs + [self._psT_free], inc=(jb == 15))
            self._vtt_free = mtr
            mv1 = P.op("v", lambda e, g_=g: e.tensor_copy(
                out=self.VNARE[:, g_ * 2048: (g_ + 1) * 2048].rearrange(
                    "p (b c) -> p b c", c=128)[:, :, 0:64],
                in_=self.psT[:, :].rearrange("p (b c) -> p b c", c=64)),
                waits=[mtr], inc=True)
            mv2 = P.op("v", lambda e, g_=g: e.tensor_copy(
                out=self.VNARE[:, g_ * 2048: (g_ + 1) * 2048].rearrange(
                    "p (b c) -> p b c", c=128)[:, :, 64:128],
                in_=self.psT[:, :].rearrange("p (b c) -> p b c", c=64)),
                waits=[mtr], inc=True)
            self._psT_free = mv2
            vn_ms.append(mv2)
        P.barrier()
        for h in range(16):
            self._l1_attn_head(h, q_ms, k_ms, vn_ms)
        P.barrier()
        xprev = [None, None]
        for dt in range(NT):
            slab, w_ms, fh = self.wring.load(
                self.dram["w_1o"].rearrange("(k p) e -> p k e", p=128)
                [:, :, dt * 128:(dt + 1) * 128], 1024)
            buf = self.XS0 if dt % 2 == 0 else self.Q2
            ml = P.dma("y", buf[:, :], self.xs(dt), waits=[xprev[dt % 2]])
            bout = self.C["p_bout"][:, dt:dt + 1]
            mevs = []
            for c in range(NCH):
                islot, ps, fz = self.slA.next()
                mm = self.mm_proj_chunk(ps, slab, c, False, [w_ms, fz],
                                        src=self.OUT)
                mev = P.op("v", lambda e, p_=ps, c_=c, b_=bout, bf_=buf:
                           e.scalar_tensor_tensor(
                               out=bf_[:, c_ * 512:(c_ + 1) * 512],
                               in0=p_, scalar=b_,
                               in1=bf_[:, c_ * 512:(c_ + 1) * 512],
                               op0=ALU.add, op1=ALU.add),
                           waits=[mm, ml], inc=True)
                self.slA.consumed(islot, mev)
                mevs.append(mev)
            mw = P.dma("y", self.xs(dt), buf[:, :], waits=mevs)
            xprev[dt % 2] = mw
            WRing.free(fh, P.now("t"))
        P.barrier()

    def _l1_attn_head(self, h, q_ms, k_ms, vn_ms):
        P = self.P
        qt = h // 2
        hr = 64 * (h % 2)
        g = h // 4
        for ib in range(16):
            L = (ib + 1) * 128
            mm = None
            for c0 in range(0, L, 512):
                w_ = min(512, L - c0)
                mm = P.op("t", lambda e, c0_=c0, w2_=w_, ib_=ib, qt_=qt,
                          hr_=hr, g_=g: e.matmul(
                    self.psA[:, c0_:c0_ + w2_],
                    self.QARE[hr_:hr_ + 64,
                              qt_ * S + ib_ * 128: qt_ * S + ib_ * 128 + 128],
                    self.KARE[hr_:hr_ + 64, g_ * S + c0_: g_ * S + c0_ + w2_],
                    start=True, stop=True),
                    waits=[q_ms[qt], k_ms[g], self._psA_free], inc=True)
            msc = P.op("v", lambda e, L_=L: e.tensor_scalar(
                out=self.SMF[:, 0:L_], in0=self.psA[:, 0:L_],
                scalar1=0.125, scalar2=None, op0=ALU.mult),
                waits=[mm, self._sm_free[0]], inc=True)
            mdg = P.op("v", lambda e, L_=L: e.tensor_add(
                out=self.SMF[:, L_ - 128:L_], in0=self.SMF[:, L_ - 128:L_],
                in1=self.C["c_diag"][:, :]), waits=[msc], inc=True)
            self._psA_free = mdg
            mnm = P.op("v", lambda e, L_=L: e.reduce_max(
                out=self.SMALL[:, 0:1], in_=self.SMF[:, 0:L_],
                axis=AX.X, negate=True),
                waits=[mdg, self._stat_free[0]], inc=True)
            mz0 = P.op("v", lambda e: e.memset(
                self.SMALL[:, 2:3], 0.0), waits=[self._stat_free[0]],
                inc=True)
            mex = P.op("s", lambda e, L_=L: e.activation(
                out=self.BD[:, 0:L_], in_=self.SMF[:, 0:L_],
                func=AF.Exp, bias=self.SMALL[:, 0:1],
                accum_out=self.SMALL[:, 2:3]),
                waits=[mnm, mz0, self._p0_free[0]], inc=True)
            self._sm_free[0] = mex
            miz = P.op("v", lambda e: e.reciprocal(
                out=self.SMALL[:, 4:5], in_=self.SMALL[:, 2:3]),
                waits=[mex], inc=True)
            mpp = P.op("v", lambda e, L_=L: e.tensor_scalar(
                out=self.BD[:, 0:L_], in0=self.BD[:, 0:L_],
                scalar1=self.SMALL[:, 4:5], scalar2=None, op0=ALU.mult),
                waits=[Prog.sync(miz)], inc=True)
            self._stat_free[0] = mpp
            mpt_all = []
            for base in range(0, ib + 1, 8):
                hi = min(base + 8, ib + 1)
                mtr = None
                for jb in range(base, hi):
                    mtr = P.op("t", lambda e, j_=jb, b0_=base: e.transpose(
                        self.psT[:, (j_ - b0_) * 128:(j_ - b0_) * 128 + 128],
                        self.BD[:, j_ * 128:(j_ + 1) * 128],
                        self.C["c_identb"][:, :]),
                        waits=[mpp, self._psT_free], inc=(jb == hi - 1))
                mpt = P.op("v", lambda e, b0_=base, hi_=hi: e.tensor_copy(
                    out=self.PT1[:, b0_ * 128: hi_ * 128],
                    in_=self.psT[:, 0:(hi_ - b0_) * 128]),
                    waits=[mtr, self._pt_read], inc=True)
                self._psT_free = mpt
                mpt_all.append(mpt)
            self._p0_free[0] = mpt_all[-1]
            mav = None
            for jb in range(ib + 1):
                mav = P.op("t", lambda e, j_=jb, g_=g,
                           st_=(jb == 0), sp_=(jb == ib): e.matmul(
                    self.psF[:, 0:128],
                    self.VNARE[:, g_ * 2048 + j_ * 128:
                               g_ * 2048 + (j_ + 1) * 128],
                    self.PT1[:, j_ * 128:(j_ + 1) * 128],
                    start=st_, stop=sp_),
                    waits=mpt_all + [vn_ms[g], self._psF_free],
                    inc=(jb == ib))
            self._pt_read = mav
            mo = P.op("s", lambda e, h_=h, hr_=hr, ib_=ib: e.copy(
                out=self.OUT[hr_:hr_ + 64,
                             (h_ // 2) * S + ib_ * 128:
                             (h_ // 2) * S + ib_ * 128 + 128],
                in_=self.psF[hr_:hr_ + 64, 0:128]), waits=[mav], inc=True)
            self._psF_free = mo

    def phase_D(self):
        P = self.P
        P.barrier()
        self.emit_ln(src_sbuf=None, dst=self.XN)
        P.barrier()
        self._reset_attn_tracking()
        wg, wg_ms, wg_fh = self.wring.load(
            self.dram["w_g"].rearrange("(k p) e -> p k e", p=128), 64)
        bg = self.C["p_bg"][0:8, 0:1]
        lg_ms = []
        for c in range(NCH):
            mm = None
            for kt in range(NT):
                mm = P.op("t", lambda e, k_=kt, c_=c, w_=wg: e.matmul(
                    self.psF[0:8, 0:512], w_[:, k_ * 8:(k_ + 1) * 8],
                    self.XN[:, k_ * S + c_ * 512: k_ * S + (c_ + 1) * 512],
                    start=(k_ == 0), stop=(k_ == NT - 1)),
                    waits=[wg_ms, self._psF_free] if kt == 0 else (),
                    inc=(kt == NT - 1))
            mev = P.op("v", lambda e, c_=c, b_=bg: e.tensor_scalar(
                out=self.MU[0:8, c_ * 512:(c_ + 1) * 512],
                in0=self.psF[0:8, 0:512], scalar1=b_, scalar2=None,
                op0=ALU.add), waits=[mm], inc=True)
            self._psF_free = mev
            lg_ms.append(mev)
        WRing.free(wg_fh, P.now("t"))
        lt_ms = []
        for sbk in range(16):
            mtr = P.op("t", lambda e, s_=sbk: e.transpose(
                self.psF[:, 8:16], self.MU[0:8, s_ * 128:(s_ + 1) * 128],
                self.C["c_identf"][0:8, 0:8]),
                waits=lg_ms + [self._psF_free], inc=True)
            mev = P.op("v", lambda e, s_=sbk: e.tensor_copy(
                out=self.RS[:, s_ * 8:(s_ + 1) * 8],
                in_=self.psF[:, 8:16]), waits=[mtr], inc=True)
            self._psF_free = mev
            lt_ms.append(mev)
        w_ms = []
        for sbk in range(16):
            k0 = sbk * 8
            LT = self.RS[:, k0:k0 + 8]
            m1 = P.op("v", lambda e, L_=LT: e.reduce_max(
                out=self.SMALL[:, 0:1], in_=L_, axis=AX.X, negate=True),
                waits=[lt_ms[sbk], self._stat_free[0]], inc=True)
            m2 = P.op("v", lambda e, L_=LT, k_=k0: e.tensor_scalar(
                out=self.Q1[:, k_:k_ + 8], in0=L_,
                scalar1=self.SMALL[:, 0:1], scalar2=None, op0=ALU.add),
                waits=[Prog.sync(m1)], inc=True)
            m3 = P.op("v", lambda e, k_=k0: e.tensor_single_scalar(
                out=self.Q2[:, k_:k_ + 8], in_=self.Q1[:, k_:k_ + 8],
                scalar=0.0, op=ALU.is_ge), waits=[Prog.sync(m2)], inc=True)
            m4 = P.op("v", lambda e, L_=LT, k_=k0: e.scalar_tensor_tensor(
                out=self.Q2[:, k_:k_ + 8], in0=self.Q2[:, k_:k_ + 8],
                scalar=-1e30, in1=L_, op0=ALU.mult, op1=ALU.add),
                waits=[Prog.sync(m3)], inc=True)
            m5 = P.op("v", lambda e, k_=k0: e.reduce_max(
                out=self.SMALL[:, 1:2], in_=self.Q2[:, k_:k_ + 8],
                axis=AX.X, negate=True), waits=[Prog.sync(m4)], inc=True)
            m6 = P.op("s", lambda e, k_=k0: e.activation(
                out=self.Q1[:, k_:k_ + 8], in_=self.Q1[:, k_:k_ + 8],
                func=AF.Exp), waits=[m2, m3], inc=True)
            m7 = P.op("v", lambda e, L_=LT, k_=k0: e.tensor_scalar(
                out=self.Q2[:, k_:k_ + 8], in0=L_,
                scalar1=self.SMALL[:, 1:2], scalar2=None, op0=ALU.add),
                waits=[Prog.sync(m5), m6], inc=True)
            m8 = P.op("v", lambda e, k_=k0: e.tensor_single_scalar(
                out=self.Q2[:, k_:k_ + 8], in_=self.Q2[:, k_:k_ + 8],
                scalar=0.0, op=ALU.is_ge), waits=[Prog.sync(m7)], inc=True)
            m9 = P.op("v", lambda e: e.tensor_sub(
                out=self.SMALL[:, 2:3], in0=self.SMALL[:, 0:1],
                in1=self.SMALL[:, 1:2]), waits=[Prog.sync(m5)], inc=True)
            m10 = P.op("s", lambda e: e.activation(
                out=self.SMALL[:, 3:4], in_=self.SMALL[:, 2:3],
                func=AF.Exp), waits=[m9], inc=True)
            m11 = P.op("v", lambda e: e.tensor_scalar(
                out=self.SMALL[:, 4:5], in0=self.SMALL[:, 3:4],
                scalar1=1.0, scalar2=None, op0=ALU.add),
                waits=[m10], inc=True)
            m12 = P.op("v", lambda e: e.reciprocal(
                out=self.SMALL[:, 5:6], in_=self.SMALL[:, 4:5]),
                waits=[Prog.sync(m11)], inc=True)
            m13 = P.op("v", lambda e, k_=k0: e.tensor_mul(
                out=self.Q1[:, k_:k_ + 8], in0=self.Q1[:, k_:k_ + 8],
                in1=self.Q2[:, k_:k_ + 8]), waits=[m6, Prog.sync(m8)],
                inc=True)
            m14 = P.op("v", lambda e, k_=k0: e.tensor_scalar(
                out=self.Q1[:, k_:k_ + 8], in0=self.Q1[:, k_:k_ + 8],
                scalar1=self.SMALL[:, 5:6], scalar2=None, op0=ALU.mult),
                waits=[Prog.sync(m12), Prog.sync(m13)], inc=True)
            self._stat_free[0] = m14
            w_ms.append(m14)
        wt_ms = []
        for sbk in range(16):
            mtr = P.op("t", lambda e, s_=sbk: e.transpose(
                self.psF[0:8, 16:144], self.Q1[:, s_ * 8:(s_ + 1) * 8],
                self.C["c_identf"][:, :]),
                waits=[w_ms[sbk], self._psF_free], inc=True)
            mev = P.op("v", lambda e, s_=sbk: e.tensor_copy(
                out=self.MU[0:8, s_ * 128:(s_ + 1) * 128],
                in_=self.psF[0:8, 16:144]), waits=[mtr], inc=True)
            self._psF_free = mev
            wt_ms.append(mev)
        mwb = P.op("s", lambda e: e.copy(
            out=self.BC[0:8, :], in_=self.MU[0:8, :]), waits=wt_ms,
            inc=True)
        mb2c = P.dma("g", self.BA[0:8, 0:1024],
                     self.dram["p_moeb2t"][:, :])
        self._region_ms = {}
        smf_slots = [self.SMF[:, i * 512:(i + 1) * 512] for i in range(4)]
        smf_free = [None] * 4
        for dt in range(NT):
            for c in range(NCH):
                islot, ps, fz = self.slB.next()
                mm = P.op("t", lambda e, d_=dt, c_=c, p_=ps: e.matmul(
                    p_, self.BA[0:8, d_ * 128:(d_ + 1) * 128],
                    self.BC[0:8, c_ * 512:(c_ + 1) * 512],
                    start=True, stop=True),
                    waits=[mb2c, mwb, fz], inc=True)
                sl = self._sidx % 4
                mev = P.op("v", lambda e, p_=ps, s_=smf_slots[sl]:
                           e.tensor_copy(out=s_, in_=p_),
                           waits=[mm, smf_free[sl]], inc=True)
                self.slB.consumed(islot, mev)
                md = P.dma("g", self.xs(dt, c * 512, (c + 1) * 512),
                           smf_slots[sl], waits=[mev], accum=True)
                smf_free[sl] = md
                self._region_ms[(dt, c)] = md
                self._sidx += 1
        for sc in (0, 1):
            for ex in range(8):
                self._moe_expert(ex, sc, smf_slots, smf_free)
        P.barrier()

    def _moe_expert(self, ex, sc, smf_slots, smf_free):
        P = self.P
        wb_ms = []
        for c in range(NCH):
            islot, ps, fz = self.slB.next()
            mm = P.op("t", lambda e, c_=c, p_=ps, x_=ex: e.matmul(
                p_, self.C["c_sel8"][:, x_ * 128:(x_ + 1) * 128],
                self.BC[0:8, c_ * 512:(c_ + 1) * 512],
                start=True, stop=True), waits=[fz, self._wbc_read],
                inc=True)
            mev = P.op("v", lambda e, c_=c, p_=ps: e.tensor_copy(
                out=self.BB[:, c_ * 512:(c_ + 1) * 512], in_=p_),
                waits=[mm], inc=True)
            self.slB.consumed(islot, mev)
            wb_ms.append(mev)
        wbc_ms = wb_ms[-1]
        for ft in range(32):
            slab, w_ms, fh = self.wring.load(
                self.dram["w_moe1"][ex].rearrange("(k p) f -> p k f", p=128)
                [:, :, ft * 128:(ft + 1) * 128], 1024)
            b1 = self.C["p_moeb1"][:, ex * 32 + ft: ex * 32 + ft + 1]
            for c in range(2):
                cg = sc * 2 + c
                islot, ps, fz = self.slA.next()
                mm = self.mm_proj_chunk(ps, slab, cg, False,
                                        [w_ms, fz, self._hh_free])
                mg = P.op("s", lambda e, p_=ps, f_=ft, c_=c, b_=b1:
                          e.activation(
                              out=self.HHARE[:, f_ * 1024 + c_ * 512:
                                             f_ * 1024 + (c_ + 1) * 512],
                              in_=p_, func=AF.Gelu, bias=b_),
                          waits=[mm], inc=True)
                self.slA.consumed(islot, mg)
            WRing.free(fh, P.now("t"))
        hh_ready = P.now("s")
        for dt in range(NT):
            slabs = []
            for half in range(2):
                slab, w_ms, fh = self.wring.load(
                    self.dram["w_moe2"][ex].rearrange(
                        "(k p) d -> p k d", p=128)
                    [:, half * 16:(half + 1) * 16, dt * 128:(dt + 1) * 128],
                    2048)
                slabs.append((slab, w_ms, fh))
            for c in range(2):
                cg = sc * 2 + c
                islot, ps, fz = self.slB.next()
                mm = None
                for kt in range(32):
                    slab, w_ms, fh = slabs[kt // 16]
                    mm = P.op("t", lambda e, k_=kt, p_=ps, c_=c, w_=slab:
                              e.matmul(
                                  p_, w_[:, (k_ % 16) * 128:
                                         (k_ % 16) * 128 + 128],
                                  self.HHARE[:, k_ * 1024 + c_ * 512:
                                             k_ * 1024 + (c_ + 1) * 512],
                                  start=(k_ == 0), stop=(k_ == 31)),
                              waits=([slabs[0][1], slabs[1][1], fz,
                                      hh_ready] if kt == 0 else ()),
                              inc=(kt == 31))
                sl = self._sidx % 4
                mev = P.op("v", lambda e, p_=ps, s_=smf_slots[sl], c_=cg:
                           e.tensor_mul(out=s_, in0=p_,
                                        in1=self.BB[:, c_ * 512:
                                                    (c_ + 1) * 512]),
                           waits=[mm, smf_free[sl], wbc_ms], inc=True)
                self.slB.consumed(islot, mev)
                md = P.dma("g", self.xs(dt, cg * 512, (cg + 1) * 512),
                           smf_slots[sl],
                           waits=[mev, self._region_ms.get((dt, cg))],
                           accum=True)
                smf_free[sl] = md
                self._region_ms[(dt, cg)] = md
                self._sidx += 1
            for slab, w_ms, fh in slabs:
                WRing.free(fh, P.now("t"))
        self._hh_free = P.now("t")
        self._wbc_read = P.now("v")



_NCORES = 8

def _prep_inputs(inputs):
    f = np.ascontiguousarray
    d = {}
    d["w_zh"] = f(np.stack([inputs["L0_attn_Wz"], inputs["L0_attn_Wh"]]))
    d["p_bz"] = _tile_vec(inputs["L0_attn_bz"], 8)
    d["p_bh"] = _tile_vec(inputs["L0_attn_bh"], 8)
    d["w_l0qkv"] = f(inputs["L0_ffn_Wqkv"])
    d["w_l0o"] = f(inputs["L0_ffn_Wo"])
    w1 = inputs["L0_ffn_W1"]
    w1p = np.zeros((6, D, 5632), dtype=np.float32)
    w1p[:, :, 0:2730] = w1[:, :, 0:2730]
    w1p[:, :, 2816:2816 + 2730] = w1[:, :, 2730:5460]
    d["w_l0w1p"] = w1p
    w2 = inputs["L0_ffn_W2"]
    w2p = np.zeros((6, 2816, D), dtype=np.float32)
    w2p[:, 0:2730, :] = w2
    d["w_l0w2p"] = w2p
    qs, ks = inputs["L0_ffn_q_scale"], inputs["L0_ffn_k_scale"]
    d["p_l0qs"] = f(np.concatenate([qs.T, qs.T], 0).astype(np.float32))
    d["p_l0ks"] = f(np.concatenate([ks.T, ks.T], 0).astype(np.float32))
    d["p_l0g"] = np.concatenate(
        [_tile_vec(inputs["L0_ffn_ln_g"][i], 8) for i in range(6)], 1)
    d["p_l0b"] = np.concatenate(
        [_tile_vec(inputs["L0_ffn_ln_b"][i], 8) for i in range(6)], 1)
    d["p_l0b1"] = np.concatenate(
        [np.concatenate([_tile_vec(inputs["L0_ffn_b1"][i, :2730], 22),
                         _tile_vec(inputs["L0_ffn_b1"][i, 2730:], 22)], 1)
         for i in range(6)], 1)
    d["p_l0b2"] = np.concatenate(
        [_tile_vec(inputs["L0_ffn_b2"][i], 8) for i in range(6)], 1)
    d["w_1qa"] = f(inputs["L1_attn_Wqa"])
    d["w_1qb"] = f(inputs["L1_attn_Wqb"][:, _perm_wqb()])
    d["w_1kva"] = f(inputs["L1_attn_Wkva"])
    kvb = inputs["L1_attn_Wkvb"]
    k2 = np.zeros((32, 512), dtype=np.float32)
    vv = np.zeros((32, 256), dtype=np.float32)
    for g in range(4):
        kcols = [g * 128 + 4 * r if r < 32 else g * 128 + 4 * (r - 32) + 2
                 for r in range(64)]
        for r in range(64):
            k2[:, g * 128 + r] = kvb[:, kcols[r]]
            k2[:, g * 128 + 64 + r] = kvb[:, kcols[r]]
            vv[:, g * 64 + r] = kvb[:, g * 128 + 2 * r + 1]
    d["w_1kvb_k2"] = k2
    d["w_1kvb_v"] = vv
    d["w_1o"] = f(inputs["L1_attn_Wout"])
    d["p_bout"] = _tile_vec(inputs["L1_attn_bout"], 8)
    d["w_moe1"] = f(inputs["L1_ffn_W1"])
    d["w_moe2"] = f(inputs["L1_ffn_W2"])
    d["p_moeb1"] = np.concatenate(
        [_tile_vec(inputs["L1_ffn_b1"][e], 32) for e in range(8)], 1)
    d["p_moeb2t"] = f(inputs["L1_ffn_b2"].astype(np.float32))
    d["w_g"] = f(inputs["L1_ffn_Wg"])
    bg = np.zeros((128, 1), dtype=np.float32)
    bg[0:8, 0] = inputs["L1_ffn_bg"]
    d["p_bg"] = bg
    cst, tbl = _consts()
    for k, v in {**cst, **tbl}.items():
        d[k] = f(v)
    return d

_PHASES = ("A", "B", "C", "D")

def kernel(**inputs):
    mk = MK(phases=_PHASES)
    nc = mk.build()
    shared = _prep_inputs(inputs)
    x = np.ascontiguousarray(inputs["L0_attn_x"].astype(np.float32))
    in_maps = []
    for c in range(_NCORES):
        m = dict(shared)
        m["x_in"] = x[c]
        in_maps.append(m)
    res = run_bass_kernel_spmd(nc, in_maps, core_ids=list(range(_NCORES)))
    return np.stack([res.results[c]["x_out"] for c in range(_NCORES)])


# version 08
# speedup vs baseline: 1.0615x; 1.0615x over previous
import numpy as np
import ml_dtypes
import concourse.bass as bass
import concourse.mybir as mybir
from concourse.bass_utils import run_bass_kernel_spmd

F32 = mybir.dt.float32
BF16 = mybir.dt.bfloat16
U8 = mybir.dt.uint8
AF = mybir.ActivationFunctionType
ALU = mybir.AluOpType
AX = mybir.AxisListType

D = 1024
S = 2048
NT = 8
NCH = 4
NEG = -1e30


def _rope_tables_l0():
    half = np.arange(0, 64, 2, dtype=np.float64)
    inv_freq = 1.0 / 10000.0 ** (half / 64.0)
    t = np.arange(S, dtype=np.float64)
    freqs = t[:, None] * inv_freq[None, :]
    base_scale = (half + 0.4 * 64) / (1.4 * 64)
    power = (t - S // 2) / (256 // 2)
    scale = base_scale[None, :] ** power[:, None]
    cos, sin = np.cos(freqs), np.sin(freqs)
    def tile128(m):
        r = np.concatenate([m.T, m.T], 0)
        return np.concatenate([r, r], 0)
    return (tile128(cos * scale), tile128(sin * scale),
            tile128(1.0 / (scale * scale)))

def _rope_tables_l1():
    half = np.arange(0, 64, 2, dtype=np.float64)
    inv_freq = 1.0 / 10000.0 ** (half / 64.0)
    t = np.arange(S, dtype=np.float64)
    freqs = t[:, None] * inv_freq[None, :]
    def tile128(m):
        r = np.concatenate([m.T, m.T], 0)
        return np.concatenate([r, r], 0)
    return tile128(np.cos(freqs)), tile128(np.sin(freqs))

def _local_masks():
    r = np.arange(128)[:, None]
    c = np.arange(512)[None, :]
    mk = []
    for off in (0, 128):
        ok = (c >= r + off) & (c <= r + off + 256)
        mk.append(np.where(ok, 0.0, NEG))
    return mk

def _l1_diag_mask():
    r = np.arange(128)[:, None]
    c = np.arange(128)[None, :]
    return np.where(c <= r, 0.0, NEG)

def _rt128():
    R = np.zeros((64, 64))
    R[0:32, 32:64] = -np.eye(32)
    R[32:64, 0:32] = np.eye(32)
    M = np.zeros((128, 128))
    M[0:64, 0:64] = R.T
    M[64:128, 64:128] = R.T
    return M

def _bd64():
    M = np.zeros((128, 128))
    M[0:64, 0:64] = 1.0
    M[64:128, 64:128] = 1.0
    return M

def _perm_wqb():
    perm = np.zeros(1024, dtype=np.int64)
    for h in range(16):
        for r in range(64):
            d = 2 * r if r < 32 else 2 * (r - 32) + 1
            perm[h * 64 + r] = h * 64 + d
    return perm

def _perm_wkvb():
    perm = np.zeros(512, dtype=np.int64)
    for g in range(4):
        for r in range(128):
            if r < 32:
                c = 4 * r
            elif r < 64:
                c = 4 * (r - 32) + 2
            else:
                c = 2 * (r - 64) + 1
            perm[g * 128 + r] = g * 128 + c
    return perm

def _consts():
    mke, mko = _local_masks()
    sel8 = np.zeros((8, 8 * 128))
    for e in range(8):
        sel8[e, e * 128:(e + 1) * 128] = 1.0
    scal = np.zeros((128, 8))
    scal[:, 0] = 1e-5; scal[:, 1] = 1.0; scal[:, 2] = -1.0
    scal[:, 3] = -0.5; scal[:, 4] = 8.0; scal[:, 5] = 1.0 / 1024.0
    d = {
        "c_scal": scal,
        "c_identf": np.eye(128), "c_identb": np.eye(128),
        "c_ones": np.ones((128, 128)), "c_bd64": _bd64(), "c_rt128": _rt128(),
        "c_mke": mke, "c_mko": mko, "c_diag": _l1_diag_mask(), "c_sel8": sel8,
    }
    cs, sn, i2 = _rope_tables_l0()
    c1, s1 = _rope_tables_l1()
    tbl = {"t_cs": cs, "t_sn": sn, "t_i2": i2, "t_r1c": c1, "t_r1s": s1}
    return ({k: v.astype(np.float32) for k, v in d.items()},
            {k: v.astype(np.float32) for k, v in tbl.items()})

_CONST_F32 = {"c_identf", "c_scal"}

def _tile_vec(v, ntile):
    out = np.zeros((128, ntile), dtype=np.float32)
    for t in range(ntile):
        seg = v[t * 128:(t + 1) * 128]
        out[:seg.shape[0], t] = seg
    return out


ENGS = ("t", "v", "s", "g", "y")

NDSEM = 12

class Prog:
    def __init__(self):
        self.nc = bass.Bass("TRN2", target_bir_lowering=False)
        self.ops = {e: [] for e in ENGS}
        self.cnt = {e: 0 for e in ENGS}
        self.dcnt = {"g": 0, "y": 0}

    def op(self, eng, fn, waits=(), inc=False):
        waits = tuple(w for w in waits if w is not None)
        self.ops[eng].append(("op", fn, waits, inc))
        if inc:
            self.cnt[eng] += 1
            return (eng, self.cnt[eng])
        return None

    def dma(self, queue, out, in_, waits=(), accum=False):
        waits = tuple(w for w in waits if w is not None)
        idx = self.dcnt[queue]
        slot = idx % NDSEM
        rnd = idx // NDSEM
        if rnd > 0:
            waits = waits + ((queue + "d%d" % slot, rnd * 16),)
        self.ops[queue].append(("dma", (out, in_, accum, slot), waits, True))
        self.dcnt[queue] += 1
        return (queue + "d%d" % slot, (rnd + 1) * 16)

    def barrier(self):
        snap = [(e, self.cnt[e]) for e in ENGS if self.cnt[e] > 0]
        for q in ("g", "y"):
            n = self.dcnt[q]
            for slot in range(min(n, NDSEM)):
                cnt = n // NDSEM + (1 if slot < n % NDSEM else 0)
                snap.append((q + "d%d" % slot, cnt * 16))
        for e in ENGS:
            ws = tuple(snap)
            if ws:
                self.ops[e].append(("wait", None, ws, False))

    def now(self, eng):
        return (eng, self.cnt[eng])

    @staticmethod
    def sync(ms):
        return (ms[0], ms[1], "f") if ms is not None else None

    def emit(self):
        nc = self.nc
        self.sems = {}
        for e in ENGS:
            self.sems[e] = nc.semaphore("sem_" + e).__enter__()
        for q in ("g", "y"):
            for slot in range(NDSEM):
                key = q + "d%d" % slot
                self.sems[key] = nc.semaphore("sem_" + key).__enter__()
        lastw = {e: {} for e in ENGS}

        def run(engobj, ename):
            lw = lastw[ename]
            for kind, fn, waits, inc in self.ops[ename]:
                for ms in waits:
                    sk, val = ms[0], ms[1]
                    if lw.get(sk, -1) >= val:
                        continue
                    engobj.wait_ge(self.sems[sk], val)
                    lw[sk] = val
                if kind == "op":
                    inst = fn(engobj)
                    if inc:
                        inst.then_inc(self.sems[ename], 1)
                elif kind == "dma":
                    out, in_, accum, slot = fn
                    key = ename + "d%d" % slot
                    if accum:
                        engobj.dma_start(out=out, in_=in_,
                                         accum_op=ALU.add).then_inc(
                            self.sems[key], 16)
                    else:
                        engobj.dma_start(out=out, in_=in_).then_inc(
                            self.sems[key], 16)

        with nc.Block() as block:
            block.tensor(lambda e: run(e, "t"))
            block.vector(lambda e: run(e, "v"))
            block.scalar(lambda e: run(e, "s"))
            block.gpsimd(lambda e: run(e, "g"))
            block.sync(lambda e: run(e, "y"))


class PSlots:
    def __init__(self, aps):
        self.aps = aps
        self.free_ms = [None] * len(aps)
        self.i = 0

    def next(self):
        i = self.i
        self.i = (self.i + 1) % len(self.aps)
        return i, self.aps[i], self.free_ms[i]

    def consumed(self, i, ms):
        self.free_ms[i] = ms


class WRing:
    def __init__(self, P, ware, cols):
        self.P, self.ware, self.cols = P, ware, cols
        self.head = 0
        self.live = []

    def load(self, dram_ap, ncols, queue="g"):
        if self.head + ncols > self.cols:
            self.head = 0
        start, end = self.head, self.head + ncols
        waits, keep = [], []
        for ent in self.live:
            if ent[0] < end and ent[1] > start:
                waits.append(ent[2][0])
            else:
                keep.append(ent)
        self.live = keep
        ap = self.ware[:, start:end]
        ms = self.P.dma(queue, ap, dram_ap, waits=waits)
        fh = [None]
        self.live.append((start, end, fh))
        self.head = end
        return ap, ms, fh

    @staticmethod
    def free(fh, ms):
        fh[0] = ms



class MK:
    def __init__(self, phases=("A", "B", "C", "D"), l0_layers=6, dbg=None):
        self.P = Prog()
        self.phases = phases
        self.l0_layers = l0_layers
        self.dbg = dbg
        nc = self.P.nc
        self.dram = {}

        def din(name, shape):
            self.dram[name] = nc.dram_tensor(name, list(shape), F32,
                                             kind="ExternalInput")
        din("x_in", (S, D))
        din("w_zh", (2, D, D))
        din("w_l0qkv", (6, D, 3072)); din("w_l0o", (6, D, D))
        din("w_l0w1p", (6, D, 5632)); din("w_l0w2p", (6, 2816, D))
        din("w_1qa", (D, 64)); din("w_1qb", (64, D))
        din("w_1kva", (D, 32))
        din("w_1kvb_k2", (32, 512)); din("w_1kvb_v", (32, 256))
        din("w_1o", (D, D))
        din("w_moe1", (8, D, 4096)); din("w_moe2", (8, 4096, D))
        din("w_g", (D, 8))
        self.pshapes = {
            "p_bz": (128, 8), "p_bh": (128, 8), "p_l0qs": (128, 6),
            "p_l0ks": (128, 6), "p_l0g": (128, 48), "p_l0b": (128, 48),
            "p_l0b1": (128, 44 * 6), "p_l0b2": (128, 48),
            "p_bout": (128, 8), "p_moeb1": (128, 256),
            "p_bg": (128, 1)}
        for pn, shp in self.pshapes.items():
            din(pn, shp)
        din("p_moeb2t", (8, 1024))
        cst, tbl = _consts()
        self._cst_np, self._tbl_np = cst, tbl
        for cn, cv in cst.items():
            din(cn, cv.shape)
        for tn, tv in tbl.items():
            din(tn, tv.shape)
        self.dram["x_out"] = nc.dram_tensor("x_out", [S, D], F32,
                                            kind="ExternalOutput")
        self.dram["xstate"] = nc.dram_tensor("xstate", [128, NT * S], F32,
                                             kind="Internal")
        self._alloc_sbuf()

    def _alloc_sbuf(self):
        nc = self.P.nc
        def sb(name, shape, dt):
            return nc.alloc_sbuf_tensor(name, list(shape), dt)
        self.XN = sb("XN", [128, NT * S], BF16)
        self.OUT = sb("OUT", [128, NT * S], BF16)
        self.Y = sb("Y", [128, NT * S], F32)
        self.WARE = sb("WARE", [128, 6 * 1024], BF16)
        self.XTC = sb("XTC", [128, 12 * 1024], U8)
        self.PLC = sb("PLC", [128, 46 * 1024], U8)
        xt = nc.lookup_mloc(self.XTC).addr
        pl = nc.lookup_mloc(self.PLC).addr
        yb = nc.lookup_mloc(self.Y).addr

        def at(name, shape, dt, base, off):
            return nc.alloc_sbuf_tensor_at(name, list(shape), dt,
                                           offset=base + off)
        self.TA = [at("TA0", [128, 1024], F32, xt, 0),
                   at("TA1", [128, 1024], F32, xt, 4096),
                   at("TA2", [128, 1024], F32, xt, 8192)]
        self.XS0 = at("XS0", [128, S], F32, xt, 0)
        self.TCS = at("TCS", [128, S], BF16, xt, 0)
        self.TSN = at("TSN", [128, S], BF16, xt, 4096)
        self.TI2 = at("TI2", [128, S], BF16, xt, 8192)
        self.T1C = at("T1C", [128, S], BF16, xt, 0)
        self.T1S = at("T1S", [128, S], BF16, xt, 4096)
        self.MU = at("MU", [128, S], F32, pl, 0)
        self.RS = at("RS", [128, S], F32, pl, 8192)
        self.Q1 = at("Q1", [128, S], F32, pl, 16384)
        self.Q2 = at("Q2", [128, S], F32, pl, 24576)
        self.BA = at("BA", [128, S], BF16, pl, 32768)
        self.BB = at("BB", [128, S], BF16, pl, 36864)
        self.BC = at("BC", [128, S], BF16, pl, 40960)
        self.SMALL = at("SMALL", [128, 512], F32, pl, 45056)
        self.BD = at("BD", [128, S], BF16, pl, 16384)
        self.BE = at("BE", [128, S], BF16, pl, 20480)
        self.SMF = at("SMF", [128, S], F32, pl, 24576)
        self.QARE = at("QARE", [128, 16 * 1024], BF16, yb, 0)
        self.KARE = at("KARE", [128, 8 * 1024], BF16, yb, 32768)
        self.VNARE = at("VNARE", [128, 8 * 1024], BF16, yb, 49152)
        self.HHARE = at("HHARE", [128, 32 * 1024], BF16, yb, 0)
        cst, tbl = self._cst_np, self._tbl_np
        self.C = {}
        for cn, cv in cst.items():
            dt = F32 if cn in _CONST_F32 else BF16
            self.C[cn] = sb("sb_" + cn, list(cv.shape), dt)
        for pn, shp in self.pshapes.items():
            self.C[pn] = sb("sb_" + pn, list(shp), F32)
        self.C["p_nbz"] = sb("sb_p_nbz", [128, 8], F32)
        self.psA = nc.alloc_psum_tensor("psA", [128, 4 * 512], F32)
        self.psB = nc.alloc_psum_tensor("psB", [128, 1 * 512], F32)
        self.psT = nc.alloc_psum_tensor("psT", [128, 2048], BF16)
        self.psF = nc.alloc_psum_tensor("psF", [128, 512], F32)
        self.slA = PSlots([self.psA[:, i * 512:(i + 1) * 512]
                           for i in range(4)])
        self.slB = PSlots([self.psB[:, i * 512:(i + 1) * 512]
                           for i in range(1)])
        self.wring = WRing(self.P, self.WARE, 6 * 1024)

    def load_consts(self):
        P = self.P
        for cn in self._cst_np:
            q = "y" if cn in _CONST_F32 else "g"
            P.dma(q, self.C[cn][:, :], self.dram[cn][:, :])
        for pn in self.pshapes:
            P.dma("y", self.C[pn][:, :], self.dram[pn][:, :])
        P.barrier()
        P.op("v", lambda e: e.tensor_scalar(
            out=self.C["p_nbz"][:, :], in0=self.C["p_bz"][:, :],
            scalar1=-1.0, scalar2=None, op0=ALU.mult), inc=True)
        P.barrier()

    def load_l0_tables(self):
        P = self.P
        for tn, h in (("t_cs", self.TCS), ("t_sn", self.TSN),
                      ("t_i2", self.TI2)):
            P.dma("g", h[:, :], self.dram[tn][:, :])

    def load_l1_tables(self):
        P = self.P
        for tn, h in (("t_r1c", self.T1C), ("t_r1s", self.T1S)):
            P.dma("g", h[:, :], self.dram[tn][:, :])

    def xs(self, t, c0=0, c1=S):
        return self.dram["xstate"][:, t * S + c0: t * S + c1]

    def xs3d(self, c0, n):
        return self.dram["xstate"][:, :].rearrange(
            "p (t s) -> p t s", t=NT)[:, :, c0:c0 + n]

    def phase_A(self):
        P = self.P
        cI = self.C["c_identf"]
        X = self.Y
        prev = [None, None]
        for sb_ in range(16):
            half = sb_ % 2
            stg = self.TA[half]
            mload = P.dma("y", stg[:, :],
                          self.dram["x_in"][sb_ * 128:(sb_ + 1) * 128, :],
                          waits=[prev[half]])
            mt = None
            for t in range(NT):
                src = stg[:, t * 128:(t + 1) * 128]
                pb = self.psA[:, half * 1024 + t * 128:
                              half * 1024 + t * 128 + 128]
                mt = P.op("t", lambda e, s_=src, p_=pb: e.transpose(
                    p_, s_, cI[:, :]),
                    waits=[mload, prev[half]], inc=(t == NT - 1))
            mv = P.op("v", lambda e, h_=half, s_=sb_: e.tensor_copy(
                out=X[:, :].rearrange("p (t s) -> p t s", t=NT)
                    [:, :, s_ * 128:s_ * 128 + 128],
                in_=self.psA[:, h_ * 1024:(h_ + 1) * 1024]
                    .rearrange("p (t s) -> p t s", s=128)),
                waits=[mt], inc=True)
            prev[half] = mv
        P.barrier()
        self.emit_ln(src_sbuf=X, dst=self.XN)
        P.barrier()
        self.emit_mingru(X)
        for t in range(NT):
            P.dma("y", self.xs(t), X[:, t * S:(t + 1) * S])
        P.barrier()

    def phase_E(self):
        P = self.P
        cI = self.C["c_identf"]
        prev = [None, None]
        prev_out = None
        for sb_ in range(16):
            half = sb_ % 2
            stg = self.TA[half]
            mload = P.dma("y", stg[:, :].rearrange("p (t s) -> p t s", s=128),
                          self.xs3d(sb_ * 128, 128), waits=[prev[half]])
            mt = None
            for t in range(NT):
                src = stg[:, t * 128:(t + 1) * 128]
                pb = self.psA[:, half * 1024 + t * 128:
                              half * 1024 + t * 128 + 128]
                mt = P.op("t", lambda e, s_=src, p_=pb: e.transpose(
                    p_, s_, cI[:, :]),
                    waits=[mload, prev[half]], inc=(t == NT - 1))
            mv = P.op("v", lambda e, h_=half: e.tensor_copy(
                out=self.TA[2][:, :],
                in_=self.psA[:, h_ * 1024:(h_ + 1) * 1024]),
                waits=[mt, prev_out], inc=True)
            prev[half] = mv
            prev_out = P.dma("y",
                             self.dram["x_out"][sb_ * 128:(sb_ + 1) * 128, :],
                             self.TA[2][:, :], waits=[mv])
        P.barrier()

    def emit_ln(self, src_sbuf, dst, shift=False, gain=False, gcol=0):
        P = self.P
        ones = self.C["c_ones"]
        RAW = self.OUT
        raw_ms = []
        for t in range(NT):
            if src_sbuf is not None:
                if shift and t >= 4:
                    m1 = P.op("s", lambda e, t_=t: e.copy(
                        out=RAW[:, t_ * S + 1:(t_ + 1) * S],
                        in_=src_sbuf[:, t_ * S:(t_ + 1) * S - 1]), inc=True)
                    m2 = P.op("v", lambda e, t_=t: e.memset(
                        RAW[:, t_ * S: t_ * S + 1], 0.0), inc=True)
                    raw_ms.append((m1, m2))
                else:
                    m1 = P.op("s", lambda e, t_=t: e.copy(
                        out=RAW[:, t_ * S:(t_ + 1) * S],
                        in_=src_sbuf[:, t_ * S:(t_ + 1) * S]), inc=True)
                    raw_ms.append((m1,))
            else:
                m1 = P.dma("g", RAW[:, t * S:(t + 1) * S], self.xs(t))
                raw_ms.append((m1,))
        for t in range(NT):
            for c in range(NCH):
                P.op("t", lambda e, t_=t, c_=c: e.matmul(
                    self.psA[:, c_ * 512:(c_ + 1) * 512], ones[:, :],
                    RAW[:, t_ * S + c_ * 512: t_ * S + (c_ + 1) * 512],
                    start=(t_ == 0), stop=(t_ == NT - 1)),
                    waits=list(raw_ms[t]),
                    inc=(t == NT - 1 and c == NCH - 1))
        msum = P.now("t")
        mmu = None
        for c in range(NCH):
            mmu = P.op("v", lambda e, c_=c: e.tensor_scalar(
                out=self.MU[:, c_ * 512:(c_ + 1) * 512],
                in0=self.psA[:, c_ * 512:(c_ + 1) * 512],
                scalar1=1.0 / 1024.0, scalar2=None, op0=ALU.mult),
                waits=[msum], inc=(c == NCH - 1))
        sqfree = [None, None]
        for t in range(NT):
            buf = self.BA if t % 2 == 0 else self.BB
            m = P.op("s", lambda e, t_=t, b_=buf: e.activation(
                out=b_[:, :], in_=RAW[:, t_ * S:(t_ + 1) * S],
                func=AF.Square),
                waits=list(raw_ms[t]) + [sqfree[t % 2]], inc=True)
            last = None
            for c in range(NCH):
                last = P.op("t", lambda e, t_=t, c_=c, b_=buf: e.matmul(
                    self.psA[:, c_ * 512:(c_ + 1) * 512], ones[:, :],
                    b_[:, c_ * 512:(c_ + 1) * 512],
                    start=(t_ == 0), stop=(t_ == NT - 1)),
                    waits=[m] + ([mmu] if t == 0 else []),
                    inc=(c == NCH - 1))
            sqfree[t % 2] = last
        msq = P.now("t")
        mq = P.op("s", lambda e: e.activation(
            out=self.Q1[:, :], in_=self.MU[:, :], func=AF.Square),
            waits=[mmu], inc=True)
        mv = None
        for c in range(NCH):
            mv = P.op("v", lambda e, c_=c: e.scalar_tensor_tensor(
                out=self.RS[:, c_ * 512:(c_ + 1) * 512],
                in0=self.psA[:, c_ * 512:(c_ + 1) * 512],
                scalar=1.0 / 1024.0,
                in1=self.Q1[:, c_ * 512:(c_ + 1) * 512],
                op0=ALU.mult, op1=ALU.subtract),
                waits=[msq, mq], inc=(c == NCH - 1))
        m1 = P.op("s", lambda e: e.activation(
            out=self.RS[:, :], in_=self.RS[:, :], func=AF.Ln,
            bias=self.C["c_scal"][:, 0:1]),
            waits=[mv], inc=True)
        m2 = P.op("s", lambda e: e.activation(
            out=self.RS[:, :], in_=self.RS[:, :], func=AF.Exp, scale=-0.5),
            waits=[m1], inc=True)
        ap_ms = []
        xt_free = [None, None]
        prev_mo = None
        for t in range(NT):
            if src_sbuf is not None:
                sap, sw = src_sbuf[:, t * S:(t + 1) * S], None
            else:
                buf = self.XS0 if t % 2 == 0 else self.Q2
                sw = P.dma("g", buf[:, :], self.xs(t), waits=[xt_free[t % 2]])
                sap = buf[:, :]
            if shift and t >= 4:
                ma = P.op("v", lambda e, s_=sap: e.tensor_sub(
                    out=self.Q1[:, 1:S], in0=s_[:, 0:S - 1],
                    in1=self.MU[:, 1:S]), waits=[mmu, sw, mq, mv, prev_mo], inc=True)
                mb = P.op("v", lambda e: e.tensor_scalar(
                    out=self.Q1[:, 0:1], in0=self.MU[:, 0:1], scalar1=-1.0,
                    scalar2=None, op0=ALU.mult), waits=[mmu], inc=True)
                tw = [ma, Prog.sync(mb)]
            else:
                ma = P.op("v", lambda e, s_=sap: e.tensor_sub(
                    out=self.Q1[:, :], in0=s_, in1=self.MU[:, :]),
                    waits=[mmu, sw, mq, mv, prev_mo], inc=True)
                tw = [ma]
            if not gain:
                mo = P.op("v", lambda e, t_=t: e.tensor_mul(
                    out=dst[:, t_ * S:(t_ + 1) * S], in0=self.Q1[:, :],
                    in1=self.RS[:, :]), waits=tw + [m2], inc=True)
            else:
                mt2 = P.op("v", lambda e: e.tensor_mul(
                    out=self.Q1[:, :], in0=self.Q1[:, :], in1=self.RS[:, :]),
                    waits=tw + [m2], inc=True)
                g_ap = self.C["p_l0g"][:, gcol * 8 + t: gcol * 8 + t + 1]
                b_ap = self.C["p_l0b"][:, gcol * 8 + t: gcol * 8 + t + 1]
                mo = P.op("v", lambda e, t_=t, g_=g_ap, b_=b_ap: e.tensor_scalar(
                    out=dst[:, t_ * S:(t_ + 1) * S], in0=self.Q1[:, :],
                    scalar1=g_, scalar2=b_, op0=ALU.mult, op1=ALU.add),
                    waits=[mt2], inc=True)
            ap_ms.append(mo)
            prev_mo = mo
            if src_sbuf is None:
                xt_free[t % 2] = mo
        self._ln_done = ap_ms
        return ap_ms

    def emit_mingru(self, X):
        P = self.P
        q1_free = [None] * NCH
        mu_free = None
        for et in range(NT):
            wz_ap, wz_ms, wz_fh = self.wring.load(
                self.dram["w_zh"][0].rearrange("(k p) e -> p k e", p=128)
                [:, :, et * 128:(et + 1) * 128], 1024)
            wh_ap, wh_ms, wh_fh = self.wring.load(
                self.dram["w_zh"][1].rearrange("(k p) e -> p k e", p=128)
                [:, :, et * 128:(et + 1) * 128], 1024)
            bz = self.C["p_bz"][:, et:et + 1]
            bh = self.C["p_bh"][:, et:et + 1]
            nbz = self.C["p_nbz"][:, et:et + 1]
            zms = []
            for c in range(NCH):
                iz, pz, fz = self.slA.next()
                mmz = None
                for kt in range(NT):
                    mmz = P.op("t", lambda e, k_=kt, c_=c, p_=pz, w_=wz_ap:
                               e.matmul(p_, w_[:, k_ * 128:(k_ + 1) * 128],
                                        self.XN[:, k_ * S + c_ * 512:
                                                k_ * S + (c_ + 1) * 512],
                                        start=(k_ == 0), stop=(k_ == NT - 1)),
                               waits=([wz_ms, fz] + list(self._ln_done)
                                      if kt == 0 else []),
                               inc=(kt == NT - 1))
                ih, ph, fh2 = self.slB.next()
                mmh = None
                for kt in range(NT):
                    mmh = P.op("t", lambda e, k_=kt, c_=c, p_=ph, w_=wh_ap:
                               e.matmul(p_, w_[:, k_ * 128:(k_ + 1) * 128],
                                        self.XN[:, k_ * S + c_ * 512:
                                                k_ * S + (c_ + 1) * 512],
                                        start=(k_ == 0), stop=(k_ == NT - 1)),
                               waits=[wh_ms, fh2] if kt == 0 else [],
                               inc=(kt == NT - 1))
                mz = P.op("s", lambda e, c_=c, p_=pz, b_=bz: e.activation(
                    out=self.Q1[:, c_ * 512:(c_ + 1) * 512], in_=p_,
                    func=AF.Sigmoid, bias=b_),
                    waits=[mmz, q1_free[c]], inc=True)
                mb = P.op("v", lambda e, c_=c, p_=ph, b_=bh:
                          e.scalar_tensor_tensor(
                              out=self.Q2[:, c_ * 512:(c_ + 1) * 512],
                              in0=p_, scalar=b_,
                              in1=self.Q1[:, c_ * 512:(c_ + 1) * 512],
                              op0=ALU.add, op1=ALU.mult),
                          waits=[mmh, mz], inc=True)
                self.slB.consumed(ih, mb)
                q1_free[c] = mb
                ma = P.op("s", lambda e, c_=c, p_=pz, nb_=nbz: e.activation(
                    out=self.MU[:, c_ * 512:(c_ + 1) * 512],
                    in_=p_, func=AF.Sigmoid, scale=-1.0, bias=nb_),
                    waits=[mmz, mu_free], inc=True)
                self.slA.consumed(iz, ma)
                zms += [mb, ma]
            WRing.free(wz_fh, P.now("t"))
            WRing.free(wh_fh, P.now("t"))
            msc = P.op("v", lambda e: e.tensor_tensor_scan(
                out=self.RS[:, :], data0=self.MU[:, :], data1=self.Q2[:, :],
                initial=0.0, op0=ALU.mult, op1=ALU.add), waits=zms, inc=True)
            mu_free = msc
            P.op("v", lambda e, t_=et: e.tensor_add(
                out=X[:, t_ * S:(t_ + 1) * S],
                in0=X[:, t_ * S:(t_ + 1) * S], in1=self.RS[:, :]),
                waits=[msc], inc=True)
        P.barrier()

    def build(self):
        P = self.P
        self._ln_done = []
        self.load_consts()
        if "A" in self.phases:
            self.phase_A()
        if "B" in self.phases:
            self.phase_B()
        if "C" in self.phases:
            self.phase_C()
        if "D" in self.phases:
            self.phase_D()
        self.phase_E()
        P.emit()
        return P.nc

    def mm_proj_chunk(self, ps, wap, c, shift, w0, src=None, nk=NT):
        P = self.P
        src = self.XN if src is None else src
        last = None
        for kt in range(nk):
            sh = shift and kt >= nk // 2
            if not sh:
                rhs = src[:, kt * S + c * 512: kt * S + (c + 1) * 512]
                out = ps
            elif c > 0:
                rhs = src[:, kt * S + c * 512 - 1: kt * S + (c + 1) * 512 - 1]
                out = ps
            else:
                rhs = src[:, kt * S: kt * S + 511]
                out = ps[:, 1:512]
            last = P.op("t", lambda e, k_=kt, r_=rhs, o_=out, w_=wap:
                        e.matmul(o_, w_[:, k_ * 128:(k_ + 1) * 128], r_,
                                 start=(k_ == 0), stop=(k_ == nk - 1)),
                        waits=(w0 if kt == 0 else ()), inc=(kt == nk - 1))
        return last

    def _qk_pipeline(self, lay, wap, w_ms, w_fh, dst, scale_ap, use_i2,
                     cast_now):
        P = self.P
        mev_all = []
        mln_all = []
        for c in range(NCH):
            islot, ps, fz = self.slA.next()
            mm = self.mm_proj_chunk(ps, wap, c, True, [w_ms, fz, cast_now])
            mev = P.op("v", lambda e, c_=c, p_=ps: e.tensor_copy(
                out=dst[:, c_ * 512:(c_ + 1) * 512], in_=p_),
                waits=[mm, self._dst_free.get(id(dst))], inc=True)
            self.slA.consumed(islot, mev)
            mev_all.append(mev)
            msq = P.op("s", lambda e, c_=c: e.activation(
                out=self.BA[:, c_ * 512:(c_ + 1) * 512],
                in_=dst[:, c_ * 512:(c_ + 1) * 512], func=AF.Square),
                waits=[mev, self._ba_free, self._ba_read], inc=True)
            mss = P.op("t", lambda e, c_=c: e.matmul(
                self.psF[:, :], self.C["c_bd64"][:, :],
                self.BA[:, c_ * 512:(c_ + 1) * 512], start=True, stop=True),
                waits=[msq, self._psF_free], inc=True)
            self._ba_free = mss
            mln = P.op("s", lambda e, c_=c: e.activation(
                out=self.MU[:, c_ * 512:(c_ + 1) * 512], in_=self.psF[:, :],
                func=AF.Ln, bias=self.C["c_scal"][:, 6:7]),
                waits=[mss, self._mu_free], inc=True)
            self._psF_free = mln
            mln_all.append(mln)
        mex = P.op("s", lambda e: e.activation(
            out=self.MU[:, :], in_=self.MU[:, :], func=AF.Exp, scale=-0.5),
            waits=mln_all, inc=True)
        msc = P.op("s", lambda e, s_=scale_ap: e.mul(
            out=self.MU[:, :], in_=self.MU[:, :], mul=s_),
            waits=[mex], inc=True)
        mqn = P.op("v", lambda e: e.tensor_mul(
            out=dst[:, :], in0=dst[:, :], in1=self.MU[:, :]),
            waits=[msc] + mev_all, inc=True)
        self._mu_free = mqn
        msm_all = []
        mro = None
        for c in range(NCH):
            islot, ps, fz = self.slB.next()
            mro = P.op("t", lambda e, c_=c, p_=ps: e.matmul(
                p_, self.C["c_rt128"][:, :],
                dst[:, c_ * 512:(c_ + 1) * 512], start=True, stop=True),
                waits=[mqn, fz], inc=True)
            msm = P.op("v", lambda e, c_=c, p_=ps: e.tensor_mul(
                out=self.SMF[:, c_ * 512:(c_ + 1) * 512], in0=p_,
                in1=self.TSN[:, c_ * 512:(c_ + 1) * 512]),
                waits=[mro, self._sm_free[0], self._sm_free[1]], inc=True)
            self.slB.consumed(islot, msm)
            msm_all.append(msm)
        mtc = P.op("v", lambda e: e.tensor_mul(
            out=dst[:, :], in0=dst[:, :], in1=self.TCS[:, :]),
            waits=[mro], inc=True)
        mad = P.op("v", lambda e: e.tensor_add(
            out=dst[:, :], in0=dst[:, :], in1=self.SMF[:, :]),
            waits=[mtc] + msm_all, inc=True)
        self._smf_read = mad
        if use_i2:
            mad = P.op("v", lambda e: e.tensor_mul(
                out=dst[:, :], in0=dst[:, :], in1=self.TI2[:, :]),
                waits=[mad], inc=True)
        WRing.free(w_fh, P.now("t"))
        return mad

    def l0_attn_hg(self, lay, hg, cast_now):
        P = self.P
        def colpan(sel, c0):
            return self.dram["w_l0qkv"][lay].rearrange(
                "(k p) e -> p k e", p=128)[:, :, c0 + hg * 128:
                                           c0 + hg * 128 + 128]
        wq, wq_ms, wq_fh = self.wring.load(colpan(0, 0), 1024)
        wk, wk_ms, wk_fh = self.wring.load(colpan(1, 1024), 1024)
        wv, wv_ms, wv_fh = self.wring.load(colpan(2, 2048), 1024)
        vt_ms = []
        for c in range(NCH):
            islot, ps, fz = self.slA.next()
            mm = self.mm_proj_chunk(ps, wv, c, True, [wv_ms, fz, cast_now])
            mev = P.op("s", lambda e, c_=c, p_=ps: e.copy(
                out=self.BC[:, c_ * 512:(c_ + 1) * 512], in_=p_),
                waits=[mm, self._bc_free], inc=True)
            self.slA.consumed(islot, mev)
            vt_ms.append(mev)
        WRing.free(wv_fh, P.now("t"))
        qs_ap = self.C["p_l0qs"][:, lay:lay + 1]
        ks_ap = self.C["p_l0ks"][:, lay:lay + 1]
        mq = self._qk_pipeline(lay, wq, wq_ms, wq_fh, self.BD, qs_ap,
                               False, cast_now)
        mk = self._qk_pipeline(lay, wk, wk_ms, wk_fh, self.BE, ks_ap,
                               True, cast_now)
        vn_ms = []
        for hh in (0, 1):
            mtr = None
            for jb in range(16):
                mtr = P.op("t", lambda e, h_=hh, j_=jb: e.transpose(
                    self.psT[:, j_ * 64:(j_ + 1) * 64],
                    self.BC[h_ * 64:h_ * 64 + 64, j_ * 128:(j_ + 1) * 128],
                    self.C["c_identb"][h_ * 64:h_ * 64 + 64,
                                       h_ * 64:h_ * 64 + 64]),
                    waits=vt_ms + [self._psT_free[0], self._pt_read[0]],
                    inc=(jb == 15))
            mvn = P.op("v", lambda e, h_=hh: e.tensor_copy(
                out=self.BA[:, :].rearrange("p (b c) -> p b c", b=16)
                    [:, :, h_ * 64:h_ * 64 + 64],
                in_=self.psT[:, 0:1024].rearrange("p (b c) -> p b c", c=64)),
                waits=[mtr, self._ba_free], inc=True)
            self._psT_free[0] = mvn
            vn_ms.append(mvn)
        if self.dbg == "qk0" and hg == 0:
            P.barrier()
            for src_, ti in ((self.BD, 0), (self.BE, 1), (self.BC, 2),
                             (self.BA, 3)):
                P.op("v", lambda e, s_=src_, t_=ti: e.tensor_copy(
                    out=self.Y[:, t_ * S:(t_ + 1) * S], in_=s_[:, :]),
                    inc=True)
            self._dump_y()
            self._dbg_stop = True
            return []
        out_ms = []
        units = [(hh, w) for hh in (0, 1) for w in range(8)]
        pend = []
        for ui, (hh, w) in enumerate(units):
            pend.append(self._l0_attn_s(hg, hh, w, ui, [mq, mk]))
            if len(pend) > 2:
                out_ms.append(self._l0_attn_av(vn_ms, pend.pop(0)))
        while pend:
            out_ms.append(self._l0_attn_av(vn_ms, pend.pop(0)))
        tnow = P.now("t")
        self._ba_read = tnow
        self._bc_free = tnow
        self._dst_free[id(self.BD)] = tnow
        self._dst_free[id(self.BE)] = tnow
        return out_ms

    def _l0_attn_s(self, hg, hh, w, ui, qk_ms):
        P = self.P
        par = ui % 2
        hr = 64 * hh
        width = 512 if w > 0 else 256
        j0 = (w * 256 - 256) if w > 0 else 0
        mpps = []
        p0s = []
        for ib in (0, 1):
            sl = par * 2 + ib
            i0 = w * 256 + ib * 128
            mask = self.C["c_mke"] if ib == 0 else self.C["c_mko"]
            mask_ap = mask[:, 0:width] if w > 0 else mask[:, 256:512]
            islot, ps, fz = self.slA.next()
            msmm = P.op("t", lambda e, p_=ps, i0_=i0, hr_=hr, j0_=j0,
                        wd_=width: e.matmul(
                p_[:, 0:wd_],
                self.BD[hr_:hr_ + 64, i0_:i0_ + 128],
                self.BE[hr_:hr_ + 64, j0_:j0_ + wd_],
                start=True, stop=True),
                waits=qk_ms + [fz], inc=True)
            msm = P.op("v", lambda e, p_=ps, m_=mask_ap, s_=sl,
                       wd_=width: e.scalar_tensor_tensor(
                           out=self.SMF[:, s_ * 512: s_ * 512 + wd_],
                           in0=p_[:, 0:wd_], scalar=8.0, in1=m_,
                           op0=ALU.mult, op1=ALU.add),
                       waits=[msmm, self._sm_free[sl]], inc=True)
            self.slA.consumed(islot, msm)
            mnm = P.op("v", lambda e, s_=sl, wd_=width: e.reduce_max(
                out=self.SMALL[:, s_:s_ + 1],
                in_=self.SMF[:, s_ * 512: s_ * 512 + wd_],
                axis=AX.X, negate=True),
                waits=[msm, self._stat_free[sl]], inc=True)
            p0 = self.BB[:, sl * 512: sl * 512 + width]
            mz0 = P.op("v", lambda e, s_=sl: e.memset(
                self.SMALL[:, 4 + s_:5 + s_], 0.0),
                waits=[self._stat_free[sl]], inc=True)
            mex = P.op("s", lambda e, p0_=p0, s_=sl, wd_=width:
                       e.activation(
                out=p0_, in_=self.SMF[:, s_ * 512: s_ * 512 + wd_],
                func=AF.Exp, bias=self.SMALL[:, s_:s_ + 1],
                accum_out=self.SMALL[:, 4 + s_:5 + s_]),
                waits=[mnm, self._p0_free[sl], mz0], inc=True)
            self._sm_free[sl] = mex
            miz = P.op("v", lambda e, s_=sl: e.reciprocal(
                out=self.SMALL[:, 8 + s_:9 + s_],
                in_=self.SMALL[:, 4 + s_:5 + s_]), waits=[mex], inc=True)
            mpp = P.op("v", lambda e, p0_=p0, s_=sl: e.tensor_scalar(
                out=p0_, in0=p0_, scalar1=self.SMALL[:, 8 + s_:9 + s_],
                scalar2=None, op0=ALU.mult), waits=[Prog.sync(miz)],
                inc=True)
            self._stat_free[sl] = mpp
            mpps.append(mpp)
            p0s.append(p0)
        return (hg, hh, w, ui, mpps, p0s)

    def _l0_attn_av(self, vn_ms, ctx):
        P = self.P
        hg, hh, w, ui, mpps, p0s = ctx
        par = ui % 2
        hr = 64 * hh
        njb = 4 if w > 0 else 2
        tbase = par * 1024
        tr_last = None
        for ib in (0, 1):
            p0 = p0s[ib]
            for jb in range(njb):
                jbg = jb + (2 if w == 0 else 0)
                tr_last = P.op("t", lambda e, p0_=p0, j_=jb, jg_=jbg,
                               ib_=ib, tb_=tbase: e.transpose(
                    self.psT[:, tb_ + jg_ * 256 + ib_ * 128:
                             tb_ + jg_ * 256 + ib_ * 128 + 128],
                    p0_[:, j_ * 128:(j_ + 1) * 128],
                    self.C["c_identb"][:, :]),
                    waits=[mpps[ib], self._psT_free[par]],
                    inc=(ib == 1 and jb == njb - 1))
        mpt = P.op("v", lambda e, tb_=tbase, pr_=par: e.tensor_copy(
            out=self.BC[:, pr_ * 1024: pr_ * 1024 + 1024],
            in_=self.psT[:, tb_: tb_ + 1024]),
            waits=[tr_last, self._pt_read[par]], inc=True)
        self._psT_free[par] = mpt
        for ib in (0, 1):
            self._p0_free[par * 2 + ib] = mpt
        jlist = list(range(2, 4)) if w == 0 else list(range(4))
        mav = None
        for n_, jbg in enumerate(jlist):
            blk = 2 * w - 2 + jbg
            mav = P.op("t", lambda e, j_=jbg, b_=blk, pr_=par,
                       st_=(n_ == 0), sp_=(n_ == len(jlist) - 1): e.matmul(
                self.psF[:, 0:256],
                self.BA[:, b_ * 128: b_ * 128 + 128],
                self.BC[:, pr_ * 1024 + j_ * 256:
                        pr_ * 1024 + j_ * 256 + 256],
                start=st_, stop=sp_),
                waits=[mpt] + vn_ms + [self._psF_free],
                inc=(n_ == len(jlist) - 1))
        self._pt_read[par] = mav
        mo = P.op("s", lambda e, hr_=hr, hg_=hg, w_=w: e.copy(
            out=self.OUT[hr_:hr_ + 64,
                         hg_ * S + w_ * 256: hg_ * S + w_ * 256 + 256],
            in_=self.psF[hr_:hr_ + 64, 0:256]), waits=[mav], inc=True)
        self._psF_free = mo
        return mo

    def l0_wo(self, lay):
        P = self.P
        for dt in range(NT):
            slab, w_ms, fh = self.wring.load(
                self.dram["w_l0o"][lay].rearrange("(k p) e -> p k e", p=128)
                [:, :, dt * 128:(dt + 1) * 128], 1024)
            for c in range(NCH):
                islot, ps, fz = self.slA.next()
                mm = self.mm_proj_chunk(ps, slab, c, False, [w_ms, fz],
                                        src=self.OUT)
                mev = P.op("v", lambda e, p_=ps, d_=dt, c_=c: e.tensor_add(
                    out=self.Y[:, d_ * S + c_ * 512: d_ * S + (c_ + 1) * 512],
                    in0=p_,
                    in1=self.Y[:, d_ * S + c_ * 512: d_ * S + (c_ + 1) * 512]),
                    waits=[mm], inc=True)
                self.slA.consumed(islot, mev)
            WRing.free(fh, P.now("t"))

    def l0_ffn(self, lay):
        P = self.P
        qsplit = [(0, 6), (6, 12), (12, 18), (18, 22)]
        g_free = [None] * NCH
        hh_free = None
        for qr, (f0, f1) in enumerate(qsplit):
            for fa in range(f0, f1):
                wa, wa_ms, wa_fh = self.wring.load(
                    self.dram["w_l0w1p"][lay].rearrange(
                        "(k p) e -> p k e", p=128)
                    [:, :, fa * 128:(fa + 1) * 128], 1024)
                wg, wg_ms, wg_fh = self.wring.load(
                    self.dram["w_l0w1p"][lay].rearrange(
                        "(k p) e -> p k e", p=128)
                    [:, :, 2816 + fa * 128: 2816 + (fa + 1) * 128], 1024)
                b1a = self.C["p_l0b1"][:, lay * 44 + fa: lay * 44 + fa + 1]
                b1g = self.C["p_l0b1"][:, lay * 44 + 22 + fa:
                                       lay * 44 + 22 + fa + 1]
                hhofs = (fa - f0) * S
                for c in range(NCH):
                    ia, pa, fza = self.slA.next()
                    mma = self.mm_proj_chunk(pa, wa, c, False, [wa_ms, fza])
                    ig, pg, fzg = self.slB.next()
                    mmg = self.mm_proj_chunk(pg, wg, c, False, [wg_ms, fzg])
                    mg = P.op("s", lambda e, p_=pg, c_=c, b_=b1g:
                              e.activation(
                                  out=self.BA[:, c_ * 512:(c_ + 1) * 512],
                                  in_=p_, func=AF.Gelu, bias=b_),
                              waits=[mmg, g_free[c]], inc=True)
                    self.slB.consumed(ig, mg)
                    mh = P.op("v", lambda e, p_=pa, c_=c, b_=b1a, o_=hhofs:
                              e.scalar_tensor_tensor(
                                  out=self.OUT[:, o_ + c_ * 512:
                                               o_ + (c_ + 1) * 512],
                                  in0=p_, scalar=b_,
                                  in1=self.BA[:, c_ * 512:(c_ + 1) * 512],
                                  op0=ALU.add, op1=ALU.mult),
                              waits=[mma, mg, hh_free], inc=True)
                    self.slA.consumed(ia, mh)
                    g_free[c] = mh
                WRing.free(wa_fh, P.now("t"))
                WRing.free(wg_fh, P.now("t"))
            hh_ready = P.now("v")
            nk = f1 - f0
            for dt in range(NT):
                slab, w_ms, fh = self.wring.load(
                    self.dram["w_l0w2p"][lay].rearrange(
                        "(k p) d -> p k d", p=128)
                    [:, f0:f1, dt * 128:(dt + 1) * 128], nk * 128)
                b2 = self.C["p_l0b2"][:, lay * 8 + dt: lay * 8 + dt + 1]
                for c in range(NCH):
                    islot, ps, fz = self.slA.next()
                    mm = self.mm_proj_chunk(ps, slab, c, False,
                                            [w_ms, fz, hh_ready],
                                            src=self.OUT, nk=nk)
                    if qr == 0:
                        mev = P.op("v", lambda e, p_=ps, d_=dt, c_=c, b_=b2:
                                   e.scalar_tensor_tensor(
                                       out=self.Y[:, d_ * S + c_ * 512:
                                                  d_ * S + (c_ + 1) * 512],
                                       in0=p_, scalar=b_,
                                       in1=self.Y[:, d_ * S + c_ * 512:
                                                  d_ * S + (c_ + 1) * 512],
                                       op0=ALU.add, op1=ALU.add),
                                   waits=[mm], inc=True)
                    else:
                        mev = P.op("v", lambda e, p_=ps, d_=dt, c_=c:
                                   e.tensor_add(
                                       out=self.Y[:, d_ * S + c_ * 512:
                                                  d_ * S + (c_ + 1) * 512],
                                       in0=p_,
                                       in1=self.Y[:, d_ * S + c_ * 512:
                                                  d_ * S + (c_ + 1) * 512]),
                                   waits=[mm], inc=True)
                    self.slA.consumed(islot, mev)
                WRing.free(fh, P.now("t"))
            hh_free = P.now("t")

    def _dump_y(self):
        P = self.P
        P.barrier()
        for t in range(NT):
            P.dma("y", self.xs(t), self.Y[:, t * S:(t + 1) * S])
        P.barrier()

    def phase_B(self):
        P = self.P
        P.barrier()
        self.emit_ln(src_sbuf=None, dst=self.Y)
        P.barrier()
        if self.dbg == "ln":
            self._dump_y()
            return
        self.load_l0_tables()
        P.barrier()
        for lay in range(self.l0_layers):
            for t in range(NT):
                P.op("s", lambda e, t_=t: e.copy(
                    out=self.XN[:, t_ * S:(t_ + 1) * S],
                    in_=self.Y[:, t_ * S:(t_ + 1) * S]),
                    inc=(t == NT - 1))
            cast_now = P.now("s")
            self._reset_attn_tracking()
            self._dbg_stop = False
            for hg in range(NT):
                self.l0_attn_hg(lay, hg, cast_now)
                if self._dbg_stop:
                    return
            P.barrier()
            if self.dbg == "out0" and lay == 0:
                for t in range(NT):
                    P.op("v", lambda e, t_=t: e.tensor_copy(
                        out=self.Y[:, t_ * S:(t_ + 1) * S],
                        in_=self.OUT[:, t_ * S:(t_ + 1) * S]), inc=True)
                self._dump_y()
                return
            self.l0_wo(lay)
            P.barrier()
            if self.dbg == "attn" and lay == 0:
                self._dump_y()
                return
            self.emit_ln(src_sbuf=self.Y, dst=self.XN, shift=True,
                         gain=True, gcol=lay)
            P.barrier()
            if self.dbg == "ffnln" and lay == 0:
                for t in range(NT):
                    P.op("v", lambda e, t_=t: e.tensor_copy(
                        out=self.Y[:, t_ * S:(t_ + 1) * S],
                        in_=self.XN[:, t_ * S:(t_ + 1) * S]), inc=True)
                self._dump_y()
                return
            self.l0_ffn(lay)
            P.barrier()
        prev = [None, None]
        for t in range(NT):
            buf = self.XS0 if t % 2 == 0 else self.Q2
            ml = P.dma("y", buf[:, :], self.xs(t), waits=[prev[t % 2]])
            ma = P.op("v", lambda e, t_=t, b_=buf: e.tensor_add(
                out=self.Y[:, t_ * S:(t_ + 1) * S],
                in0=self.Y[:, t_ * S:(t_ + 1) * S], in1=b_[:, :]),
                waits=[ml], inc=True)
            prev[t % 2] = ma
            P.dma("y", self.xs(t), self.Y[:, t * S:(t + 1) * S], waits=[ma])
        P.barrier()

    def _reset_attn_tracking(self):
        self._psA_free = None
        self._vtt_free = None
        self._hh_free = None
        self._wbc_read = None
        self._wbc_ms = {}
        self._sidx = 0
        self._dst_free = {}
        self._ba_free = None
        self._ba_read = None
        self._bc_free = None
        self._mu_free = None
        self._smf_free = None
        self._smf_read = None
        self._psF_free = None
        self._psT_free = [None, None]
        self._pt_read = [None, None]
        self._sm_free = [None] * 4
        self._p0_free = [None] * 4
        self._stat_free = [None] * 4

    def _l1_rope(self, tile_ap, rot_rows, ready):
        P = self.P
        msm_all = []
        mro = None
        for c in range(NCH):
            islot, ps, fz = self.slB.next()
            mro = P.op("t", lambda e, c_=c, p_=ps, rr_=rot_rows: e.matmul(
                p_[0:rr_, :], self.C["c_rt128"][0:rr_, 0:rr_],
                tile_ap[0:rr_, c_ * 512:(c_ + 1) * 512],
                start=True, stop=True),
                waits=[ready, fz], inc=True)
            msm = P.op("v", lambda e, c_=c, p_=ps, rr_=rot_rows:
                       e.tensor_mul(
                           out=self.SMF[0:rr_, c_ * 512:(c_ + 1) * 512],
                           in0=p_[0:rr_, :],
                           in1=self.T1S[0:rr_, c_ * 512:(c_ + 1) * 512]),
                       waits=[mro, self._smf_read], inc=True)
            self.slB.consumed(islot, msm)
            msm_all.append(msm)
        mtc = P.op("v", lambda e, rr_=rot_rows: e.tensor_mul(
            out=tile_ap[0:rr_, :], in0=tile_ap[0:rr_, :],
            in1=self.T1C[0:rr_, :]), waits=[mro], inc=True)
        mad = P.op("v", lambda e, rr_=rot_rows: e.tensor_add(
            out=tile_ap[0:rr_, :], in0=tile_ap[0:rr_, :],
            in1=self.SMF[0:rr_, :]), waits=[mtc] + msm_all, inc=True)
        self._smf_read = mad
        return mad

    def phase_C(self):
        P = self.P
        self.PT1 = self.BE
        self.VTT = self.BC
        P.barrier()
        self.emit_ln(src_sbuf=None, dst=self.XN)
        P.barrier()
        self.load_l1_tables()
        P.barrier()
        self._reset_attn_tracking()
        self._smf_read = None
        wqa, wqa_ms, wqa_fh = self.wring.load(
            self.dram["w_1qa"].rearrange("(k p) e -> p k e", p=128), 512)
        t1_ms = []
        for c in range(NCH):
            islot, ps, fz = self.slB.next()
            mm = None
            for kt in range(NT):
                mm = P.op("t", lambda e, k_=kt, c_=c, p_=ps, w_=wqa:
                          e.matmul(p_[0:64, :], w_[:, k_ * 64:(k_ + 1) * 64],
                                   self.XN[:, k_ * S + c_ * 512:
                                           k_ * S + (c_ + 1) * 512],
                                   start=(k_ == 0), stop=(k_ == NT - 1)),
                          waits=[wqa_ms, fz] if kt == 0 else (),
                          inc=(kt == NT - 1))
            mev = P.op("s", lambda e, c_=c, p_=ps: e.copy(
                out=self.BA[0:64, c_ * 512:(c_ + 1) * 512], in_=p_[0:64, :]),
                waits=[mm], inc=True)
            self.slB.consumed(islot, mev)
            t1_ms.append(mev)
        WRing.free(wqa_fh, P.now("t"))
        wqb = self.BB[0:64, 0:1024]
        wqb_ms = P.dma("g", wqb, self.dram["w_1qb"][:, :])
        q_ms = []
        for et in range(NT):
            evs = []
            for c in range(NCH):
                islot, ps, fz = self.slA.next()
                mm = P.op("t", lambda e, c_=c, p_=ps, e_=et, w_=wqb:
                          e.matmul(p_, w_[:, e_ * 128:(e_ + 1) * 128],
                                   self.BA[0:64, c_ * 512:(c_ + 1) * 512],
                                   start=True, stop=True),
                          waits=[wqb_ms] + t1_ms + [fz], inc=True)
                mev = P.op("v", lambda e, c_=c, p_=ps, e_=et: e.tensor_copy(
                    out=self.QARE[:, e_ * S + c_ * 512:
                                  e_ * S + (c_ + 1) * 512], in_=p_),
                    waits=[mm], inc=True)
                self.slA.consumed(islot, mev)
                evs.append(mev)
            mq = self._l1_rope(self.QARE[:, et * S:(et + 1) * S], 128,
                               evs[-1])
            q_ms.append(mq)
        wkva, wkva_ms, wkva_fh = self.wring.load(
            self.dram["w_1kva"].rearrange("(k p) e -> p k e", p=128), 256)
        t2_ms = []
        for c in range(NCH):
            islot, ps, fz = self.slB.next()
            mm = None
            for kt in range(NT):
                mm = P.op("t", lambda e, k_=kt, c_=c, p_=ps, w_=wkva:
                          e.matmul(p_[0:32, :], w_[:, k_ * 32:(k_ + 1) * 32],
                                   self.XN[:, k_ * S + c_ * 512:
                                           k_ * S + (c_ + 1) * 512],
                                   start=(k_ == 0), stop=(k_ == NT - 1)),
                          waits=[wkva_ms, fz] + t1_ms if kt == 0 else (),
                          inc=(kt == NT - 1))
            mev = P.op("s", lambda e, c_=c, p_=ps: e.copy(
                out=self.BA[0:32, c_ * 512:(c_ + 1) * 512], in_=p_[0:32, :]),
                waits=[mm] + q_ms, inc=True)
            self.slB.consumed(islot, mev)
            t2_ms.append(mev)
        WRing.free(wkva_fh, P.now("t"))
        wk2 = self.BB[0:32, 1024:1536]
        wk2_ms = P.dma("g", wk2, self.dram["w_1kvb_k2"][:, :])
        k_ms = []
        for g in range(4):
            evs = []
            for c in range(NCH):
                islot, ps, fz = self.slA.next()
                mm = P.op("t", lambda e, c_=c, p_=ps, g_=g, w_=wk2:
                          e.matmul(p_, w_[:, g_ * 128:(g_ + 1) * 128],
                                   self.BA[0:32, c_ * 512:(c_ + 1) * 512],
                                   start=True, stop=True),
                          waits=[wk2_ms] + t2_ms + [fz], inc=True)
                mev = P.op("v", lambda e, c_=c, p_=ps, g_=g: e.tensor_copy(
                    out=self.KARE[:, g_ * S + c_ * 512:
                                  g_ * S + (c_ + 1) * 512], in_=p_),
                    waits=[mm], inc=True)
                self.slA.consumed(islot, mev)
                evs.append(mev)
            mk = self._l1_rope(self.KARE[:, g * S:(g + 1) * S], 128,
                               evs[-1])
            k_ms.append(mk)
        wv = self.BB[0:32, 1536:1792]
        wv_ms = P.dma("g", wv, self.dram["w_1kvb_v"][:, :])
        vn_ms = []
        for g in range(4):
            evs = []
            for c in range(NCH):
                islot, ps, fz = self.slB.next()
                mm = P.op("t", lambda e, c_=c, p_=ps, g_=g, w_=wv:
                          e.matmul(p_[0:64, :], w_[:, g_ * 64:(g_ + 1) * 64],
                                   self.BA[0:32, c_ * 512:(c_ + 1) * 512],
                                   start=True, stop=True),
                          waits=[wv_ms] + t2_ms + [fz, self._vtt_free],
                          inc=True)
                mev = P.op("s", lambda e, c_=c, p_=ps: e.copy(
                    out=self.VTT[0:64, c_ * 512:(c_ + 1) * 512],
                    in_=p_[0:64, :]), waits=[mm], inc=True)
                self.slB.consumed(islot, mev)
                evs.append(mev)
            mtr = None
            for jb in range(16):
                mtr = P.op("t", lambda e, j_=jb: e.transpose(
                    self.psT[:, j_ * 64:(j_ + 1) * 64],
                    self.VTT[0:64, j_ * 128:(j_ + 1) * 128],
                    self.C["c_identb"][0:64, 0:64]),
                    waits=evs + [self._psT_free[0], self._pt_read[0]],
                    inc=(jb == 15))
            self._vtt_free = mtr
            mv1 = P.op("v", lambda e, g_=g: e.tensor_copy(
                out=self.VNARE[:, g_ * 2048: (g_ + 1) * 2048].rearrange(
                    "p (b c) -> p b c", c=128)[:, :, 0:64],
                in_=self.psT[:, 0:1024].rearrange("p (b c) -> p b c", c=64)),
                waits=[mtr], inc=True)
            mv2 = P.op("v", lambda e, g_=g: e.tensor_copy(
                out=self.VNARE[:, g_ * 2048: (g_ + 1) * 2048].rearrange(
                    "p (b c) -> p b c", c=128)[:, :, 64:128],
                in_=self.psT[:, 0:1024].rearrange("p (b c) -> p b c", c=64)),
                waits=[mtr], inc=True)
            self._psT_free[0] = mv2
            vn_ms.append(mv2)
        P.barrier()
        for h in range(16):
            self._l1_attn_head(h, q_ms, k_ms, vn_ms)
        P.barrier()
        xprev = [None, None]
        for dt in range(NT):
            slab, w_ms, fh = self.wring.load(
                self.dram["w_1o"].rearrange("(k p) e -> p k e", p=128)
                [:, :, dt * 128:(dt + 1) * 128], 1024)
            buf = self.XS0 if dt % 2 == 0 else self.Q2
            ml = P.dma("y", buf[:, :], self.xs(dt), waits=[xprev[dt % 2]])
            bout = self.C["p_bout"][:, dt:dt + 1]
            mevs = []
            for c in range(NCH):
                islot, ps, fz = self.slA.next()
                mm = self.mm_proj_chunk(ps, slab, c, False, [w_ms, fz],
                                        src=self.OUT)
                mev = P.op("v", lambda e, p_=ps, c_=c, b_=bout, bf_=buf:
                           e.scalar_tensor_tensor(
                               out=bf_[:, c_ * 512:(c_ + 1) * 512],
                               in0=p_, scalar=b_,
                               in1=bf_[:, c_ * 512:(c_ + 1) * 512],
                               op0=ALU.add, op1=ALU.add),
                           waits=[mm, ml], inc=True)
                self.slA.consumed(islot, mev)
                mevs.append(mev)
            mw = P.dma("y", self.xs(dt), buf[:, :], waits=mevs)
            xprev[dt % 2] = mw
            WRing.free(fh, P.now("t"))
        P.barrier()

    def _l1_attn_head(self, h, q_ms, k_ms, vn_ms):
        P = self.P
        qt = h // 2
        hr = 64 * (h % 2)
        g = h // 4
        for ib in range(16):
            L = (ib + 1) * 128
            mm = None
            for c0 in range(0, L, 512):
                w_ = min(512, L - c0)
                mm = P.op("t", lambda e, c0_=c0, w2_=w_, ib_=ib, qt_=qt,
                          hr_=hr, g_=g: e.matmul(
                    self.psA[:, c0_:c0_ + w2_],
                    self.QARE[hr_:hr_ + 64,
                              qt_ * S + ib_ * 128: qt_ * S + ib_ * 128 + 128],
                    self.KARE[hr_:hr_ + 64, g_ * S + c0_: g_ * S + c0_ + w2_],
                    start=True, stop=True),
                    waits=[q_ms[qt], k_ms[g], self._psA_free], inc=True)
            msc = P.op("v", lambda e, L_=L: e.tensor_scalar(
                out=self.SMF[:, 0:L_], in0=self.psA[:, 0:L_],
                scalar1=0.125, scalar2=None, op0=ALU.mult),
                waits=[mm, self._sm_free[0]], inc=True)
            mdg = P.op("v", lambda e, L_=L: e.tensor_add(
                out=self.SMF[:, L_ - 128:L_], in0=self.SMF[:, L_ - 128:L_],
                in1=self.C["c_diag"][:, :]), waits=[msc], inc=True)
            self._psA_free = mdg
            mnm = P.op("v", lambda e, L_=L: e.reduce_max(
                out=self.SMALL[:, 0:1], in_=self.SMF[:, 0:L_],
                axis=AX.X, negate=True),
                waits=[mdg, self._stat_free[0]], inc=True)
            mz0 = P.op("v", lambda e: e.memset(
                self.SMALL[:, 2:3], 0.0), waits=[self._stat_free[0]],
                inc=True)
            mex = P.op("s", lambda e, L_=L: e.activation(
                out=self.BD[:, 0:L_], in_=self.SMF[:, 0:L_],
                func=AF.Exp, bias=self.SMALL[:, 0:1],
                accum_out=self.SMALL[:, 2:3]),
                waits=[mnm, mz0, self._p0_free[0]], inc=True)
            self._sm_free[0] = mex
            miz = P.op("v", lambda e: e.reciprocal(
                out=self.SMALL[:, 4:5], in_=self.SMALL[:, 2:3]),
                waits=[mex], inc=True)
            mpp = P.op("v", lambda e, L_=L: e.tensor_scalar(
                out=self.BD[:, 0:L_], in0=self.BD[:, 0:L_],
                scalar1=self.SMALL[:, 4:5], scalar2=None, op0=ALU.mult),
                waits=[Prog.sync(miz)], inc=True)
            self._stat_free[0] = mpp
            mpt_all = []
            for bi, base in enumerate(range(0, ib + 1, 8)):
                hi = min(base + 8, ib + 1)
                par = bi % 2
                mtr = None
                for jb in range(base, hi):
                    mtr = P.op("t", lambda e, j_=jb, b0_=base, pr_=par:
                               e.transpose(
                        self.psT[:, pr_ * 1024 + (j_ - b0_) * 128:
                                 pr_ * 1024 + (j_ - b0_) * 128 + 128],
                        self.BD[:, j_ * 128:(j_ + 1) * 128],
                        self.C["c_identb"][:, :]),
                        waits=[mpp, self._psT_free[par]], inc=(jb == hi - 1))
                mpt = P.op("v", lambda e, b0_=base, hi_=hi, pr_=par:
                           e.tensor_copy(
                    out=self.PT1[:, b0_ * 128: hi_ * 128],
                    in_=self.psT[:, pr_ * 1024:
                                 pr_ * 1024 + (hi_ - b0_) * 128]),
                    waits=[mtr, self._pt_read[par]], inc=True)
                self._psT_free[par] = mpt
                mpt_all.append(mpt)
            self._p0_free[0] = mpt_all[-1]
            mav = None
            for jb in range(ib + 1):
                mav = P.op("t", lambda e, j_=jb, g_=g,
                           st_=(jb == 0), sp_=(jb == ib): e.matmul(
                    self.psF[:, 0:128],
                    self.VNARE[:, g_ * 2048 + j_ * 128:
                               g_ * 2048 + (j_ + 1) * 128],
                    self.PT1[:, j_ * 128:(j_ + 1) * 128],
                    start=st_, stop=sp_),
                    waits=mpt_all + [vn_ms[g], self._psF_free],
                    inc=(jb == ib))
            self._pt_read[0] = mav
            self._pt_read[1] = mav
            mo = P.op("s", lambda e, h_=h, hr_=hr, ib_=ib: e.copy(
                out=self.OUT[hr_:hr_ + 64,
                             (h_ // 2) * S + ib_ * 128:
                             (h_ // 2) * S + ib_ * 128 + 128],
                in_=self.psF[hr_:hr_ + 64, 0:128]), waits=[mav], inc=True)
            self._psF_free = mo

    def phase_D(self):
        P = self.P
        P.barrier()
        self.emit_ln(src_sbuf=None, dst=self.XN)
        P.barrier()
        self._reset_attn_tracking()
        wg, wg_ms, wg_fh = self.wring.load(
            self.dram["w_g"].rearrange("(k p) e -> p k e", p=128), 64)
        bg = self.C["p_bg"][0:8, 0:1]
        lg_ms = []
        for c in range(NCH):
            mm = None
            for kt in range(NT):
                mm = P.op("t", lambda e, k_=kt, c_=c, w_=wg: e.matmul(
                    self.psF[0:8, 0:512], w_[:, k_ * 8:(k_ + 1) * 8],
                    self.XN[:, k_ * S + c_ * 512: k_ * S + (c_ + 1) * 512],
                    start=(k_ == 0), stop=(k_ == NT - 1)),
                    waits=[wg_ms, self._psF_free] if kt == 0 else (),
                    inc=(kt == NT - 1))
            mev = P.op("v", lambda e, c_=c, b_=bg: e.tensor_scalar(
                out=self.MU[0:8, c_ * 512:(c_ + 1) * 512],
                in0=self.psF[0:8, 0:512], scalar1=b_, scalar2=None,
                op0=ALU.add), waits=[mm], inc=True)
            self._psF_free = mev
            lg_ms.append(mev)
        WRing.free(wg_fh, P.now("t"))
        lt_ms = []
        for sbk in range(16):
            mtr = P.op("t", lambda e, s_=sbk: e.transpose(
                self.psF[:, 8:16], self.MU[0:8, s_ * 128:(s_ + 1) * 128],
                self.C["c_identf"][0:8, 0:8]),
                waits=lg_ms + [self._psF_free], inc=True)
            mev = P.op("v", lambda e, s_=sbk: e.tensor_copy(
                out=self.RS[:, s_ * 8:(s_ + 1) * 8],
                in_=self.psF[:, 8:16]), waits=[mtr], inc=True)
            self._psF_free = mev
            lt_ms.append(mev)
        w_ms = []
        for sbk in range(16):
            k0 = sbk * 8
            LT = self.RS[:, k0:k0 + 8]
            m1 = P.op("v", lambda e, L_=LT: e.reduce_max(
                out=self.SMALL[:, 0:1], in_=L_, axis=AX.X, negate=True),
                waits=[lt_ms[sbk], self._stat_free[0]], inc=True)
            m2 = P.op("v", lambda e, L_=LT, k_=k0: e.tensor_scalar(
                out=self.Q1[:, k_:k_ + 8], in0=L_,
                scalar1=self.SMALL[:, 0:1], scalar2=None, op0=ALU.add),
                waits=[Prog.sync(m1)], inc=True)
            m3 = P.op("v", lambda e, k_=k0: e.tensor_single_scalar(
                out=self.Q2[:, k_:k_ + 8], in_=self.Q1[:, k_:k_ + 8],
                scalar=0.0, op=ALU.is_ge), waits=[Prog.sync(m2)], inc=True)
            m4 = P.op("v", lambda e, L_=LT, k_=k0: e.scalar_tensor_tensor(
                out=self.Q2[:, k_:k_ + 8], in0=self.Q2[:, k_:k_ + 8],
                scalar=-1e30, in1=L_, op0=ALU.mult, op1=ALU.add),
                waits=[Prog.sync(m3)], inc=True)
            m5 = P.op("v", lambda e, k_=k0: e.reduce_max(
                out=self.SMALL[:, 1:2], in_=self.Q2[:, k_:k_ + 8],
                axis=AX.X, negate=True), waits=[Prog.sync(m4)], inc=True)
            m6 = P.op("s", lambda e, k_=k0: e.activation(
                out=self.Q1[:, k_:k_ + 8], in_=self.Q1[:, k_:k_ + 8],
                func=AF.Exp), waits=[m2, m3], inc=True)
            m7 = P.op("v", lambda e, L_=LT, k_=k0: e.tensor_scalar(
                out=self.Q2[:, k_:k_ + 8], in0=L_,
                scalar1=self.SMALL[:, 1:2], scalar2=None, op0=ALU.add),
                waits=[Prog.sync(m5), m6], inc=True)
            m8 = P.op("v", lambda e, k_=k0: e.tensor_single_scalar(
                out=self.Q2[:, k_:k_ + 8], in_=self.Q2[:, k_:k_ + 8],
                scalar=0.0, op=ALU.is_ge), waits=[Prog.sync(m7)], inc=True)
            m9 = P.op("v", lambda e: e.tensor_sub(
                out=self.SMALL[:, 2:3], in0=self.SMALL[:, 0:1],
                in1=self.SMALL[:, 1:2]), waits=[Prog.sync(m5)], inc=True)
            m10 = P.op("s", lambda e: e.activation(
                out=self.SMALL[:, 3:4], in_=self.SMALL[:, 2:3],
                func=AF.Exp), waits=[m9], inc=True)
            m11 = P.op("v", lambda e: e.tensor_scalar(
                out=self.SMALL[:, 4:5], in0=self.SMALL[:, 3:4],
                scalar1=1.0, scalar2=None, op0=ALU.add),
                waits=[m10], inc=True)
            m12 = P.op("v", lambda e: e.reciprocal(
                out=self.SMALL[:, 5:6], in_=self.SMALL[:, 4:5]),
                waits=[Prog.sync(m11)], inc=True)
            m13 = P.op("v", lambda e, k_=k0: e.tensor_mul(
                out=self.Q1[:, k_:k_ + 8], in0=self.Q1[:, k_:k_ + 8],
                in1=self.Q2[:, k_:k_ + 8]), waits=[m6, Prog.sync(m8)],
                inc=True)
            m14 = P.op("v", lambda e, k_=k0: e.tensor_scalar(
                out=self.Q1[:, k_:k_ + 8], in0=self.Q1[:, k_:k_ + 8],
                scalar1=self.SMALL[:, 5:6], scalar2=None, op0=ALU.mult),
                waits=[Prog.sync(m12), Prog.sync(m13)], inc=True)
            self._stat_free[0] = m14
            w_ms.append(m14)
        wt_ms = []
        for sbk in range(16):
            mtr = P.op("t", lambda e, s_=sbk: e.transpose(
                self.psF[0:8, 16:144], self.Q1[:, s_ * 8:(s_ + 1) * 8],
                self.C["c_identf"][:, :]),
                waits=[w_ms[sbk], self._psF_free], inc=True)
            mev = P.op("v", lambda e, s_=sbk: e.tensor_copy(
                out=self.MU[0:8, s_ * 128:(s_ + 1) * 128],
                in_=self.psF[0:8, 16:144]), waits=[mtr], inc=True)
            self._psF_free = mev
            wt_ms.append(mev)
        mwb = P.op("s", lambda e: e.copy(
            out=self.BC[0:8, :], in_=self.MU[0:8, :]), waits=wt_ms,
            inc=True)
        mb2c = P.dma("g", self.BA[0:8, 0:1024],
                     self.dram["p_moeb2t"][:, :])
        self._region_ms = {}
        smf_slots = [self.SMF[:, i * 512:(i + 1) * 512] for i in range(4)]
        smf_free = [None] * 4
        for dt in range(NT):
            for c in range(NCH):
                islot, ps, fz = self.slB.next()
                mm = P.op("t", lambda e, d_=dt, c_=c, p_=ps: e.matmul(
                    p_, self.BA[0:8, d_ * 128:(d_ + 1) * 128],
                    self.BC[0:8, c_ * 512:(c_ + 1) * 512],
                    start=True, stop=True),
                    waits=[mb2c, mwb, fz], inc=True)
                sl = self._sidx % 4
                mev = P.op("v", lambda e, p_=ps, s_=smf_slots[sl]:
                           e.tensor_copy(out=s_, in_=p_),
                           waits=[mm, smf_free[sl]], inc=True)
                self.slB.consumed(islot, mev)
                md = P.dma("g", self.xs(dt, c * 512, (c + 1) * 512),
                           smf_slots[sl], waits=[mev], accum=True)
                smf_free[sl] = md
                self._region_ms[(dt, c)] = md
                self._sidx += 1
        for sc in (0, 1):
            for ex in range(8):
                self._moe_expert(ex, sc, smf_slots, smf_free)
        P.barrier()

    def _moe_expert(self, ex, sc, smf_slots, smf_free):
        P = self.P
        wb_ms = []
        for c in range(NCH):
            islot, ps, fz = self.slB.next()
            mm = P.op("t", lambda e, c_=c, p_=ps, x_=ex: e.matmul(
                p_, self.C["c_sel8"][:, x_ * 128:(x_ + 1) * 128],
                self.BC[0:8, c_ * 512:(c_ + 1) * 512],
                start=True, stop=True), waits=[fz, self._wbc_read],
                inc=True)
            mev = P.op("v", lambda e, c_=c, p_=ps: e.tensor_copy(
                out=self.BB[:, c_ * 512:(c_ + 1) * 512], in_=p_),
                waits=[mm], inc=True)
            self.slB.consumed(islot, mev)
            wb_ms.append(mev)
        wbc_ms = wb_ms[-1]
        for ft in range(32):
            slab, w_ms, fh = self.wring.load(
                self.dram["w_moe1"][ex].rearrange("(k p) f -> p k f", p=128)
                [:, :, ft * 128:(ft + 1) * 128], 1024)
            b1 = self.C["p_moeb1"][:, ex * 32 + ft: ex * 32 + ft + 1]
            for c in range(2):
                cg = sc * 2 + c
                islot, ps, fz = self.slA.next()
                mm = self.mm_proj_chunk(ps, slab, cg, False,
                                        [w_ms, fz, self._hh_free])
                mg = P.op("s", lambda e, p_=ps, f_=ft, c_=c, b_=b1:
                          e.activation(
                              out=self.HHARE[:, f_ * 1024 + c_ * 512:
                                             f_ * 1024 + (c_ + 1) * 512],
                              in_=p_, func=AF.Gelu, bias=b_),
                          waits=[mm], inc=True)
                self.slA.consumed(islot, mg)
            WRing.free(fh, P.now("t"))
        hh_ready = P.now("s")
        for dt in range(NT):
            slabs = []
            for half in range(2):
                slab, w_ms, fh = self.wring.load(
                    self.dram["w_moe2"][ex].rearrange(
                        "(k p) d -> p k d", p=128)
                    [:, half * 16:(half + 1) * 16, dt * 128:(dt + 1) * 128],
                    2048)
                slabs.append((slab, w_ms, fh))
            for c in range(2):
                cg = sc * 2 + c
                islot, ps, fz = self.slB.next()
                mm = None
                for kt in range(32):
                    slab, w_ms, fh = slabs[kt // 16]
                    mm = P.op("t", lambda e, k_=kt, p_=ps, c_=c, w_=slab:
                              e.matmul(
                                  p_, w_[:, (k_ % 16) * 128:
                                         (k_ % 16) * 128 + 128],
                                  self.HHARE[:, k_ * 1024 + c_ * 512:
                                             k_ * 1024 + (c_ + 1) * 512],
                                  start=(k_ == 0), stop=(k_ == 31)),
                              waits=([slabs[0][1], slabs[1][1], fz,
                                      hh_ready] if kt == 0 else ()),
                              inc=(kt == 31))
                sl = self._sidx % 4
                mev = P.op("v", lambda e, p_=ps, s_=smf_slots[sl], c_=cg:
                           e.tensor_mul(out=s_, in0=p_,
                                        in1=self.BB[:, c_ * 512:
                                                    (c_ + 1) * 512]),
                           waits=[mm, smf_free[sl], wbc_ms], inc=True)
                self.slB.consumed(islot, mev)
                md = P.dma("g", self.xs(dt, cg * 512, (cg + 1) * 512),
                           smf_slots[sl],
                           waits=[mev, self._region_ms.get((dt, cg))],
                           accum=True)
                smf_free[sl] = md
                self._region_ms[(dt, cg)] = md
                self._sidx += 1
            for slab, w_ms, fh in slabs:
                WRing.free(fh, P.now("t"))
        self._hh_free = P.now("t")
        self._wbc_read = P.now("v")



_NCORES = 8

def _prep_inputs(inputs):
    f = np.ascontiguousarray
    d = {}
    d["w_zh"] = f(np.stack([inputs["L0_attn_Wz"], inputs["L0_attn_Wh"]]))
    d["p_bz"] = _tile_vec(inputs["L0_attn_bz"], 8)
    d["p_bh"] = _tile_vec(inputs["L0_attn_bh"], 8)
    d["w_l0qkv"] = f(inputs["L0_ffn_Wqkv"])
    d["w_l0o"] = f(inputs["L0_ffn_Wo"])
    w1 = inputs["L0_ffn_W1"]
    w1p = np.zeros((6, D, 5632), dtype=np.float32)
    w1p[:, :, 0:2730] = w1[:, :, 0:2730]
    w1p[:, :, 2816:2816 + 2730] = w1[:, :, 2730:5460]
    d["w_l0w1p"] = w1p
    w2 = inputs["L0_ffn_W2"]
    w2p = np.zeros((6, 2816, D), dtype=np.float32)
    w2p[:, 0:2730, :] = w2
    d["w_l0w2p"] = w2p
    qs, ks = inputs["L0_ffn_q_scale"], inputs["L0_ffn_k_scale"]
    d["p_l0qs"] = f(np.concatenate([qs.T, qs.T], 0).astype(np.float32))
    d["p_l0ks"] = f(np.concatenate([ks.T, ks.T], 0).astype(np.float32))
    d["p_l0g"] = np.concatenate(
        [_tile_vec(inputs["L0_ffn_ln_g"][i], 8) for i in range(6)], 1)
    d["p_l0b"] = np.concatenate(
        [_tile_vec(inputs["L0_ffn_ln_b"][i], 8) for i in range(6)], 1)
    d["p_l0b1"] = np.concatenate(
        [np.concatenate([_tile_vec(inputs["L0_ffn_b1"][i, :2730], 22),
                         _tile_vec(inputs["L0_ffn_b1"][i, 2730:], 22)], 1)
         for i in range(6)], 1)
    d["p_l0b2"] = np.concatenate(
        [_tile_vec(inputs["L0_ffn_b2"][i], 8) for i in range(6)], 1)
    d["w_1qa"] = f(inputs["L1_attn_Wqa"])
    d["w_1qb"] = f(inputs["L1_attn_Wqb"][:, _perm_wqb()])
    d["w_1kva"] = f(inputs["L1_attn_Wkva"])
    kvb = inputs["L1_attn_Wkvb"]
    k2 = np.zeros((32, 512), dtype=np.float32)
    vv = np.zeros((32, 256), dtype=np.float32)
    for g in range(4):
        kcols = [g * 128 + 4 * r if r < 32 else g * 128 + 4 * (r - 32) + 2
                 for r in range(64)]
        for r in range(64):
            k2[:, g * 128 + r] = kvb[:, kcols[r]]
            k2[:, g * 128 + 64 + r] = kvb[:, kcols[r]]
            vv[:, g * 64 + r] = kvb[:, g * 128 + 2 * r + 1]
    d["w_1kvb_k2"] = k2
    d["w_1kvb_v"] = vv
    d["w_1o"] = f(inputs["L1_attn_Wout"])
    d["p_bout"] = _tile_vec(inputs["L1_attn_bout"], 8)
    d["w_moe1"] = f(inputs["L1_ffn_W1"])
    d["w_moe2"] = f(inputs["L1_ffn_W2"])
    d["p_moeb1"] = np.concatenate(
        [_tile_vec(inputs["L1_ffn_b1"][e], 32) for e in range(8)], 1)
    d["p_moeb2t"] = f(inputs["L1_ffn_b2"].astype(np.float32))
    d["w_g"] = f(inputs["L1_ffn_Wg"])
    bg = np.zeros((128, 1), dtype=np.float32)
    bg[0:8, 0] = inputs["L1_ffn_bg"]
    d["p_bg"] = bg
    cst, tbl = _consts()
    for k, v in {**cst, **tbl}.items():
        d[k] = f(v)
    return d

_PHASES = ("A", "B", "C", "D")

def kernel(**inputs):
    mk = MK(phases=_PHASES)
    nc = mk.build()
    shared = _prep_inputs(inputs)
    x = np.ascontiguousarray(inputs["L0_attn_x"].astype(np.float32))
    in_maps = []
    for c in range(_NCORES):
        m = dict(shared)
        m["x_in"] = x[c]
        in_maps.append(m)
    res = run_bass_kernel_spmd(nc, in_maps, core_ids=list(range(_NCORES)))
    return np.stack([res.results[c]["x_out"] for c in range(_NCORES)])


# version 11
# speedup vs baseline: 1.0784x; 1.0159x over previous
import numpy as np
import ml_dtypes
import concourse.bass as bass
import concourse.mybir as mybir
from concourse.bass_utils import run_bass_kernel_spmd

F32 = mybir.dt.float32
BF16 = mybir.dt.bfloat16
U8 = mybir.dt.uint8
AF = mybir.ActivationFunctionType
ALU = mybir.AluOpType
AX = mybir.AxisListType

D = 1024
S = 2048
NT = 8
NCH = 4
NEG = -1e30


def _rope_tables_l0():
    half = np.arange(0, 64, 2, dtype=np.float64)
    inv_freq = 1.0 / 10000.0 ** (half / 64.0)
    t = np.arange(S, dtype=np.float64)
    freqs = t[:, None] * inv_freq[None, :]
    base_scale = (half + 0.4 * 64) / (1.4 * 64)
    power = (t - S // 2) / (256 // 2)
    scale = base_scale[None, :] ** power[:, None]
    cos, sin = np.cos(freqs), np.sin(freqs)
    def tile128(m):
        r = np.concatenate([m.T, m.T], 0)
        return np.concatenate([r, r], 0)
    return (tile128(cos * scale), tile128(sin * scale),
            tile128(1.0 / (scale * scale)))

def _rope_tables_l1():
    half = np.arange(0, 64, 2, dtype=np.float64)
    inv_freq = 1.0 / 10000.0 ** (half / 64.0)
    t = np.arange(S, dtype=np.float64)
    freqs = t[:, None] * inv_freq[None, :]
    def tile128(m):
        r = np.concatenate([m.T, m.T], 0)
        return np.concatenate([r, r], 0)
    return tile128(np.cos(freqs)), tile128(np.sin(freqs))

def _local_masks():
    r = np.arange(128)[:, None]
    c = np.arange(512)[None, :]
    mk = []
    for off in (0, 128):
        ok = (c >= r + off) & (c <= r + off + 256)
        mk.append(np.where(ok, 0.0, NEG))
    return mk

def _l1_diag_mask():
    r = np.arange(128)[:, None]
    c = np.arange(128)[None, :]
    return np.where(c <= r, 0.0, NEG)

def _rt128():
    R = np.zeros((64, 64))
    R[0:32, 32:64] = -np.eye(32)
    R[32:64, 0:32] = np.eye(32)
    M = np.zeros((128, 128))
    M[0:64, 0:64] = R.T
    M[64:128, 64:128] = R.T
    return M

def _bd64():
    M = np.zeros((128, 128))
    M[0:64, 0:64] = 1.0
    M[64:128, 64:128] = 1.0
    return M

def _perm_wqb():
    perm = np.zeros(1024, dtype=np.int64)
    for h in range(16):
        for r in range(64):
            d = 2 * r if r < 32 else 2 * (r - 32) + 1
            perm[h * 64 + r] = h * 64 + d
    return perm

def _perm_wkvb():
    perm = np.zeros(512, dtype=np.int64)
    for g in range(4):
        for r in range(128):
            if r < 32:
                c = 4 * r
            elif r < 64:
                c = 4 * (r - 32) + 2
            else:
                c = 2 * (r - 64) + 1
            perm[g * 128 + r] = g * 128 + c
    return perm

def _consts():
    mke, mko = _local_masks()
    sel8 = np.zeros((8, 8 * 128))
    for e in range(8):
        sel8[e, e * 128:(e + 1) * 128] = 1.0
    scal = np.zeros((128, 8))
    scal[:, 0] = 1e-5; scal[:, 1] = 1.0; scal[:, 2] = -1.0
    scal[:, 3] = -0.5; scal[:, 4] = 8.0; scal[:, 5] = 1.0 / 1024.0
    d = {
        "c_scal": scal,
        "c_identf": np.eye(128), "c_identb": np.eye(128),
        "c_ones": np.ones((128, 128)), "c_bd64": _bd64(), "c_rt128": _rt128(),
        "c_mke": mke, "c_mko": mko, "c_diag": _l1_diag_mask(), "c_sel8": sel8,
    }
    cs, sn, i2 = _rope_tables_l0()
    c1, s1 = _rope_tables_l1()
    tbl = {"t_cs": cs, "t_sn": sn, "t_i2": i2, "t_r1c": c1, "t_r1s": s1}
    return ({k: v.astype(np.float32) for k, v in d.items()},
            {k: v.astype(np.float32) for k, v in tbl.items()})

_CONST_F32 = {"c_identf", "c_scal"}

def _tile_vec(v, ntile):
    out = np.zeros((128, ntile), dtype=np.float32)
    for t in range(ntile):
        seg = v[t * 128:(t + 1) * 128]
        out[:seg.shape[0], t] = seg
    return out


ENGS = ("t", "v", "s", "g", "y")

NDSEM = 12

class Prog:
    def __init__(self):
        self.nc = bass.Bass("TRN2", target_bir_lowering=False)
        self.ops = {e: [] for e in ENGS}
        self.cnt = {e: 0 for e in ENGS}
        self.dcnt = {"g": 0, "y": 0}

    def op(self, eng, fn, waits=(), inc=False):
        waits = tuple(w for w in waits if w is not None)
        self.ops[eng].append(("op", fn, waits, inc))
        if inc:
            self.cnt[eng] += 1
            return (eng, self.cnt[eng])
        return None

    def dma(self, queue, out, in_, waits=(), accum=False):
        waits = tuple(w for w in waits if w is not None)
        idx = self.dcnt[queue]
        slot = idx % NDSEM
        rnd = idx // NDSEM
        if rnd > 0:
            waits = waits + ((queue + "d%d" % slot, rnd * 16),)
        self.ops[queue].append(("dma", (out, in_, accum, slot), waits, True))
        self.dcnt[queue] += 1
        return (queue + "d%d" % slot, (rnd + 1) * 16)

    def barrier(self):
        snap = [(e, self.cnt[e]) for e in ENGS if self.cnt[e] > 0]
        for q in ("g", "y"):
            n = self.dcnt[q]
            for slot in range(min(n, NDSEM)):
                cnt = n // NDSEM + (1 if slot < n % NDSEM else 0)
                snap.append((q + "d%d" % slot, cnt * 16))
        for e in ENGS:
            ws = tuple(snap)
            if ws:
                self.ops[e].append(("wait", None, ws, False))

    def now(self, eng):
        return (eng, self.cnt[eng])

    @staticmethod
    def sync(ms):
        return (ms[0], ms[1], "f") if ms is not None else None

    def emit(self):
        nc = self.nc
        self.sems = {}
        for e in ENGS:
            self.sems[e] = nc.semaphore("sem_" + e).__enter__()
        for q in ("g", "y"):
            for slot in range(NDSEM):
                key = q + "d%d" % slot
                self.sems[key] = nc.semaphore("sem_" + key).__enter__()
        lastw = {e: {} for e in ENGS}

        def run(engobj, ename):
            lw = lastw[ename]
            for kind, fn, waits, inc in self.ops[ename]:
                for ms in waits:
                    sk, val = ms[0], ms[1]
                    if lw.get(sk, -1) >= val:
                        continue
                    engobj.wait_ge(self.sems[sk], val)
                    lw[sk] = val
                if kind == "op":
                    inst = fn(engobj)
                    if inc:
                        inst.then_inc(self.sems[ename], 1)
                elif kind == "dma":
                    out, in_, accum, slot = fn
                    key = ename + "d%d" % slot
                    if accum:
                        engobj.dma_start(out=out, in_=in_,
                                         accum_op=ALU.add).then_inc(
                            self.sems[key], 16)
                    else:
                        engobj.dma_start(out=out, in_=in_).then_inc(
                            self.sems[key], 16)

        with nc.Block() as block:
            block.tensor(lambda e: run(e, "t"))
            block.vector(lambda e: run(e, "v"))
            block.scalar(lambda e: run(e, "s"))
            block.gpsimd(lambda e: run(e, "g"))
            block.sync(lambda e: run(e, "y"))


class PSlots:
    def __init__(self, aps):
        self.aps = aps
        self.free_ms = [None] * len(aps)
        self.i = 0

    def next(self):
        i = self.i
        self.i = (self.i + 1) % len(self.aps)
        return i, self.aps[i], self.free_ms[i]

    def consumed(self, i, ms):
        self.free_ms[i] = ms


class WRing:
    def __init__(self, P, ware, cols):
        self.P, self.ware, self.cols = P, ware, cols
        self.head = 0
        self.live = []

    def load(self, dram_ap, ncols, queue="g"):
        if self.head + ncols > self.cols:
            self.head = 0
        start, end = self.head, self.head + ncols
        waits, keep = [], []
        for ent in self.live:
            if ent[0] < end and ent[1] > start:
                waits.append(ent[2][0])
            else:
                keep.append(ent)
        self.live = keep
        ap = self.ware[:, start:end]
        ms = self.P.dma(queue, ap, dram_ap, waits=waits)
        fh = [None]
        self.live.append((start, end, fh))
        self.head = end
        return ap, ms, fh

    @staticmethod
    def free(fh, ms):
        fh[0] = ms



class MK:
    def __init__(self, phases=("A", "B", "C", "D"), l0_layers=6, dbg=None):
        self.P = Prog()
        self.phases = phases
        self.l0_layers = l0_layers
        self.dbg = dbg
        nc = self.P.nc
        self.dram = {}

        def din(name, shape):
            self.dram[name] = nc.dram_tensor(name, list(shape), F32,
                                             kind="ExternalInput")
        din("x_in", (S, D))
        din("w_zh", (2, D, D))
        din("w_l0qkv", (6, D, 3072)); din("w_l0o", (6, D, D))
        din("w_l0w1p", (6, D, 5632)); din("w_l0w2p", (6, 2816, D))
        din("w_1qa", (D, 64)); din("w_1qb", (64, D))
        din("w_1kva", (D, 32))
        din("w_1kvb_k2", (32, 512)); din("w_1kvb_v", (32, 256))
        din("w_1o", (D, D))
        din("w_moe1", (8, D, 4096)); din("w_moe2", (8, 4096, D))
        din("w_g", (D, 8))
        self.pshapes = {
            "p_bz": (128, 8), "p_bh": (128, 8), "p_l0qs": (128, 6),
            "p_l0ks": (128, 6), "p_l0g": (128, 48), "p_l0b": (128, 48),
            "p_l0b1": (128, 44 * 6), "p_l0b2": (128, 48),
            "p_bout": (128, 8), "p_moeb1": (128, 256),
            "p_bg": (128, 1)}
        for pn, shp in self.pshapes.items():
            din(pn, shp)
        din("p_moeb2t", (8, 1024))
        cst, tbl = _consts()
        self._cst_np, self._tbl_np = cst, tbl
        for cn, cv in cst.items():
            din(cn, cv.shape)
        for tn, tv in tbl.items():
            din(tn, tv.shape)
        self.dram["x_out"] = nc.dram_tensor("x_out", [S, D], F32,
                                            kind="ExternalOutput")
        self.dram["xstate"] = nc.dram_tensor("xstate", [128, NT * S], F32,
                                             kind="Internal")
        self._alloc_sbuf()

    def _alloc_sbuf(self):
        nc = self.P.nc
        def sb(name, shape, dt):
            return nc.alloc_sbuf_tensor(name, list(shape), dt)
        self.XN = sb("XN", [128, NT * S], BF16)
        self.OUT = sb("OUT", [128, NT * S], BF16)
        self.Y = sb("Y", [128, NT * S], F32)
        self.WARE = sb("WARE", [128, 6 * 1024], BF16)
        self.XTC = sb("XTC", [128, 12 * 1024], U8)
        self.PLC = sb("PLC", [128, 46 * 1024], U8)
        xt = nc.lookup_mloc(self.XTC).addr
        pl = nc.lookup_mloc(self.PLC).addr
        yb = nc.lookup_mloc(self.Y).addr

        def at(name, shape, dt, base, off):
            return nc.alloc_sbuf_tensor_at(name, list(shape), dt,
                                           offset=base + off)
        self.TA = [at("TA0", [128, 1024], F32, xt, 0),
                   at("TA1", [128, 1024], F32, xt, 4096),
                   at("TA2", [128, 1024], F32, xt, 8192)]
        self.XS0 = at("XS0", [128, S], F32, xt, 0)
        self.TCS = at("TCS", [128, S], BF16, xt, 0)
        self.TSN = at("TSN", [128, S], BF16, xt, 4096)
        self.TI2 = at("TI2", [128, S], BF16, xt, 8192)
        self.T1C = at("T1C", [128, S], BF16, xt, 0)
        self.T1S = at("T1S", [128, S], BF16, xt, 4096)
        self.MU = at("MU", [128, S], F32, pl, 0)
        self.RS = at("RS", [128, S], F32, pl, 8192)
        self.Q1 = at("Q1", [128, S], F32, pl, 16384)
        self.Q2 = at("Q2", [128, S], F32, pl, 24576)
        self.BA = at("BA", [128, S], BF16, pl, 32768)
        self.BB = at("BB", [128, S], BF16, pl, 36864)
        self.BC = at("BC", [128, S], BF16, pl, 40960)
        self.SMALL = at("SMALL", [128, 512], F32, pl, 45056)
        self.BD = at("BD", [128, S], BF16, pl, 16384)
        self.BE = at("BE", [128, S], BF16, pl, 20480)
        self.SMF = at("SMF", [128, S], F32, pl, 24576)
        self.QARE = at("QARE", [128, 16 * 1024], BF16, yb, 0)
        self.KARE = at("KARE", [128, 8 * 1024], BF16, yb, 32768)
        self.VNARE = at("VNARE", [128, 8 * 1024], BF16, yb, 49152)
        self.HHARE = at("HHARE", [128, 32 * 1024], BF16, yb, 0)
        cst, tbl = self._cst_np, self._tbl_np
        self.C = {}
        for cn, cv in cst.items():
            dt = F32 if cn in _CONST_F32 else BF16
            self.C[cn] = sb("sb_" + cn, list(cv.shape), dt)
        for pn, shp in self.pshapes.items():
            self.C[pn] = sb("sb_" + pn, list(shp), F32)
        self.C["p_nbz"] = sb("sb_p_nbz", [128, 8], F32)
        self.psA = nc.alloc_psum_tensor("psA", [128, 4 * 512], F32)
        self.psB = nc.alloc_psum_tensor("psB", [128, 1 * 512], F32)
        self.psT = nc.alloc_psum_tensor("psT", [128, 2048], BF16)
        self.psF = nc.alloc_psum_tensor("psF", [128, 512], F32)
        self.slA = PSlots([self.psA[:, i * 512:(i + 1) * 512]
                           for i in range(4)])
        self.slB = PSlots([self.psB[:, i * 512:(i + 1) * 512]
                           for i in range(1)])
        self.wring = WRing(self.P, self.WARE, 6 * 1024)

    def load_consts(self):
        P = self.P
        for cn in self._cst_np:
            q = "y" if cn in _CONST_F32 else "g"
            P.dma(q, self.C[cn][:, :], self.dram[cn][:, :])
        for pn in self.pshapes:
            P.dma("y", self.C[pn][:, :], self.dram[pn][:, :])
        P.barrier()
        P.op("v", lambda e: e.tensor_scalar(
            out=self.C["p_nbz"][:, :], in0=self.C["p_bz"][:, :],
            scalar1=-1.0, scalar2=None, op0=ALU.mult), inc=True)
        P.barrier()

    def load_l0_tables(self):
        P = self.P
        for tn, h in (("t_cs", self.TCS), ("t_sn", self.TSN),
                      ("t_i2", self.TI2)):
            P.dma("g", h[:, :], self.dram[tn][:, :])

    def load_l1_tables(self):
        P = self.P
        for tn, h in (("t_r1c", self.T1C), ("t_r1s", self.T1S)):
            P.dma("g", h[:, :], self.dram[tn][:, :])

    def xs(self, t, c0=0, c1=S):
        return self.dram["xstate"][:, t * S + c0: t * S + c1]

    def xs3d(self, c0, n):
        return self.dram["xstate"][:, :].rearrange(
            "p (t s) -> p t s", t=NT)[:, :, c0:c0 + n]

    def phase_A(self):
        P = self.P
        cI = self.C["c_identf"]
        X = self.Y
        prev = [None, None]
        for sb_ in range(16):
            half = sb_ % 2
            stg = self.TA[half]
            mload = P.dma("y", stg[:, :],
                          self.dram["x_in"][sb_ * 128:(sb_ + 1) * 128, :],
                          waits=[prev[half]])
            mt = None
            for t in range(NT):
                src = stg[:, t * 128:(t + 1) * 128]
                pb = self.psA[:, half * 1024 + t * 128:
                              half * 1024 + t * 128 + 128]
                mt = P.op("t", lambda e, s_=src, p_=pb: e.transpose(
                    p_, s_, cI[:, :]),
                    waits=[mload, prev[half]], inc=(t == NT - 1))
            mv = P.op("v", lambda e, h_=half, s_=sb_: e.tensor_copy(
                out=X[:, :].rearrange("p (t s) -> p t s", t=NT)
                    [:, :, s_ * 128:s_ * 128 + 128],
                in_=self.psA[:, h_ * 1024:(h_ + 1) * 1024]
                    .rearrange("p (t s) -> p t s", s=128)),
                waits=[mt], inc=True)
            prev[half] = mv
        P.barrier()
        self.emit_ln(src_sbuf=X, dst=self.XN)
        P.barrier()
        self.emit_mingru(X)
        for t in range(NT):
            P.dma("y", self.xs(t), X[:, t * S:(t + 1) * S])
        P.barrier()

    def phase_E(self):
        P = self.P
        cI = self.C["c_identf"]
        prev = [None, None]
        prev_out = None
        for sb_ in range(16):
            half = sb_ % 2
            stg = self.TA[half]
            mload = P.dma("y", stg[:, :].rearrange("p (t s) -> p t s", s=128),
                          self.xs3d(sb_ * 128, 128), waits=[prev[half]])
            mt = None
            for t in range(NT):
                src = stg[:, t * 128:(t + 1) * 128]
                pb = self.psA[:, half * 1024 + t * 128:
                              half * 1024 + t * 128 + 128]
                mt = P.op("t", lambda e, s_=src, p_=pb: e.transpose(
                    p_, s_, cI[:, :]),
                    waits=[mload, prev[half]], inc=(t == NT - 1))
            mv = P.op("v", lambda e, h_=half: e.tensor_copy(
                out=self.TA[2][:, :],
                in_=self.psA[:, h_ * 1024:(h_ + 1) * 1024]),
                waits=[mt, prev_out], inc=True)
            prev[half] = mv
            prev_out = P.dma("y",
                             self.dram["x_out"][sb_ * 128:(sb_ + 1) * 128, :],
                             self.TA[2][:, :], waits=[mv])
        P.barrier()

    def emit_ln(self, src_sbuf, dst, shift=False, gain=False, gcol=0):
        P = self.P
        ones = self.C["c_ones"]
        RAW = self.OUT
        raw_ms = []
        for t in range(NT):
            if src_sbuf is not None:
                if shift and t >= 4:
                    m1 = P.op("s", lambda e, t_=t: e.copy(
                        out=RAW[:, t_ * S + 1:(t_ + 1) * S],
                        in_=src_sbuf[:, t_ * S:(t_ + 1) * S - 1]), inc=True)
                    m2 = P.op("v", lambda e, t_=t: e.memset(
                        RAW[:, t_ * S: t_ * S + 1], 0.0), inc=True)
                    raw_ms.append((m1, m2))
                else:
                    m1 = P.op("s", lambda e, t_=t: e.copy(
                        out=RAW[:, t_ * S:(t_ + 1) * S],
                        in_=src_sbuf[:, t_ * S:(t_ + 1) * S]), inc=True)
                    raw_ms.append((m1,))
            else:
                m1 = P.dma("g", RAW[:, t * S:(t + 1) * S], self.xs(t))
                raw_ms.append((m1,))
        for t in range(NT):
            for c in range(NCH):
                P.op("t", lambda e, t_=t, c_=c: e.matmul(
                    self.psA[:, c_ * 512:(c_ + 1) * 512], ones[:, :],
                    RAW[:, t_ * S + c_ * 512: t_ * S + (c_ + 1) * 512],
                    start=(t_ == 0), stop=(t_ == NT - 1)),
                    waits=list(raw_ms[t]),
                    inc=(t == NT - 1 and c == NCH - 1))
        msum = P.now("t")
        mmu = None
        for c in range(NCH):
            mmu = P.op("v", lambda e, c_=c: e.tensor_scalar(
                out=self.MU[:, c_ * 512:(c_ + 1) * 512],
                in0=self.psA[:, c_ * 512:(c_ + 1) * 512],
                scalar1=1.0 / 1024.0, scalar2=None, op0=ALU.mult),
                waits=[msum], inc=(c == NCH - 1))
        sqfree = [None, None]
        for t in range(NT):
            buf = self.BA if t % 2 == 0 else self.BB
            m = P.op("s", lambda e, t_=t, b_=buf: e.activation(
                out=b_[:, :], in_=RAW[:, t_ * S:(t_ + 1) * S],
                func=AF.Square),
                waits=list(raw_ms[t]) + [sqfree[t % 2]], inc=True)
            last = None
            for c in range(NCH):
                last = P.op("t", lambda e, t_=t, c_=c, b_=buf: e.matmul(
                    self.psA[:, c_ * 512:(c_ + 1) * 512], ones[:, :],
                    b_[:, c_ * 512:(c_ + 1) * 512],
                    start=(t_ == 0), stop=(t_ == NT - 1)),
                    waits=[m] + ([mmu] if t == 0 else []),
                    inc=(c == NCH - 1))
            sqfree[t % 2] = last
        msq = P.now("t")
        mq = P.op("s", lambda e: e.activation(
            out=self.Q1[:, :], in_=self.MU[:, :], func=AF.Square),
            waits=[mmu], inc=True)
        mv = None
        for c in range(NCH):
            mv = P.op("v", lambda e, c_=c: e.scalar_tensor_tensor(
                out=self.RS[:, c_ * 512:(c_ + 1) * 512],
                in0=self.psA[:, c_ * 512:(c_ + 1) * 512],
                scalar=1.0 / 1024.0,
                in1=self.Q1[:, c_ * 512:(c_ + 1) * 512],
                op0=ALU.mult, op1=ALU.subtract),
                waits=[msq, mq], inc=(c == NCH - 1))
        m1 = P.op("s", lambda e: e.activation(
            out=self.RS[:, :], in_=self.RS[:, :], func=AF.Ln,
            bias=self.C["c_scal"][:, 0:1]),
            waits=[mv], inc=True)
        m2 = P.op("s", lambda e: e.activation(
            out=self.RS[:, :], in_=self.RS[:, :], func=AF.Exp, scale=-0.5),
            waits=[m1], inc=True)
        ap_ms = []
        xt_free = [None, None]
        prev_mo = None
        for t in range(NT):
            if src_sbuf is not None:
                sap, sw = src_sbuf[:, t * S:(t + 1) * S], None
            else:
                buf = self.XS0 if t % 2 == 0 else self.Q2
                sw = P.dma("g", buf[:, :], self.xs(t), waits=[xt_free[t % 2]])
                sap = buf[:, :]
            if shift and t >= 4:
                ma = P.op("v", lambda e, s_=sap: e.tensor_sub(
                    out=self.Q1[:, 1:S], in0=s_[:, 0:S - 1],
                    in1=self.MU[:, 1:S]), waits=[mmu, sw, mq, mv, prev_mo], inc=True)
                mb = P.op("v", lambda e: e.tensor_scalar(
                    out=self.Q1[:, 0:1], in0=self.MU[:, 0:1], scalar1=-1.0,
                    scalar2=None, op0=ALU.mult), waits=[mmu], inc=True)
                tw = [ma, Prog.sync(mb)]
            else:
                ma = P.op("v", lambda e, s_=sap: e.tensor_sub(
                    out=self.Q1[:, :], in0=s_, in1=self.MU[:, :]),
                    waits=[mmu, sw, mq, mv, prev_mo], inc=True)
                tw = [ma]
            if not gain:
                mo = P.op("v", lambda e, t_=t: e.tensor_mul(
                    out=dst[:, t_ * S:(t_ + 1) * S], in0=self.Q1[:, :],
                    in1=self.RS[:, :]), waits=tw + [m2], inc=True)
            else:
                mt2 = P.op("v", lambda e: e.tensor_mul(
                    out=self.Q1[:, :], in0=self.Q1[:, :], in1=self.RS[:, :]),
                    waits=tw + [m2], inc=True)
                g_ap = self.C["p_l0g"][:, gcol * 8 + t: gcol * 8 + t + 1]
                b_ap = self.C["p_l0b"][:, gcol * 8 + t: gcol * 8 + t + 1]
                mo = P.op("v", lambda e, t_=t, g_=g_ap, b_=b_ap: e.tensor_scalar(
                    out=dst[:, t_ * S:(t_ + 1) * S], in0=self.Q1[:, :],
                    scalar1=g_, scalar2=b_, op0=ALU.mult, op1=ALU.add),
                    waits=[mt2], inc=True)
            ap_ms.append(mo)
            prev_mo = mo
            if src_sbuf is None:
                xt_free[t % 2] = mo
        self._ln_done = ap_ms
        return ap_ms

    def emit_mingru(self, X):
        P = self.P
        q1_free = [None] * NCH
        mu_free = None
        for et in range(NT):
            wz_ap, wz_ms, wz_fh = self.wring.load(
                self.dram["w_zh"][0].rearrange("(k p) e -> p k e", p=128)
                [:, :, et * 128:(et + 1) * 128], 1024)
            wh_ap, wh_ms, wh_fh = self.wring.load(
                self.dram["w_zh"][1].rearrange("(k p) e -> p k e", p=128)
                [:, :, et * 128:(et + 1) * 128], 1024)
            bz = self.C["p_bz"][:, et:et + 1]
            bh = self.C["p_bh"][:, et:et + 1]
            nbz = self.C["p_nbz"][:, et:et + 1]
            zms = []
            for c in range(NCH):
                iz, pz, fz = self.slA.next()
                mmz = None
                for kt in range(NT):
                    mmz = P.op("t", lambda e, k_=kt, c_=c, p_=pz, w_=wz_ap:
                               e.matmul(p_, w_[:, k_ * 128:(k_ + 1) * 128],
                                        self.XN[:, k_ * S + c_ * 512:
                                                k_ * S + (c_ + 1) * 512],
                                        start=(k_ == 0), stop=(k_ == NT - 1)),
                               waits=([wz_ms, fz] + list(self._ln_done)
                                      if kt == 0 else []),
                               inc=(kt == NT - 1))
                ih, ph, fh2 = self.slB.next()
                mmh = None
                for kt in range(NT):
                    mmh = P.op("t", lambda e, k_=kt, c_=c, p_=ph, w_=wh_ap:
                               e.matmul(p_, w_[:, k_ * 128:(k_ + 1) * 128],
                                        self.XN[:, k_ * S + c_ * 512:
                                                k_ * S + (c_ + 1) * 512],
                                        start=(k_ == 0), stop=(k_ == NT - 1)),
                               waits=[wh_ms, fh2] if kt == 0 else [],
                               inc=(kt == NT - 1))
                mz = P.op("s", lambda e, c_=c, p_=pz, b_=bz: e.activation(
                    out=self.Q1[:, c_ * 512:(c_ + 1) * 512], in_=p_,
                    func=AF.Sigmoid, bias=b_),
                    waits=[mmz, q1_free[c]], inc=True)
                mb = P.op("v", lambda e, c_=c, p_=ph, b_=bh:
                          e.scalar_tensor_tensor(
                              out=self.Q2[:, c_ * 512:(c_ + 1) * 512],
                              in0=p_, scalar=b_,
                              in1=self.Q1[:, c_ * 512:(c_ + 1) * 512],
                              op0=ALU.add, op1=ALU.mult),
                          waits=[mmh, mz], inc=True)
                self.slB.consumed(ih, mb)
                q1_free[c] = mb
                ma = P.op("s", lambda e, c_=c, p_=pz, nb_=nbz: e.activation(
                    out=self.MU[:, c_ * 512:(c_ + 1) * 512],
                    in_=p_, func=AF.Sigmoid, scale=-1.0, bias=nb_),
                    waits=[mmz, mu_free], inc=True)
                self.slA.consumed(iz, ma)
                zms += [mb, ma]
            WRing.free(wz_fh, P.now("t"))
            WRing.free(wh_fh, P.now("t"))
            msc = P.op("v", lambda e: e.tensor_tensor_scan(
                out=self.RS[:, :], data0=self.MU[:, :], data1=self.Q2[:, :],
                initial=0.0, op0=ALU.mult, op1=ALU.add), waits=zms, inc=True)
            mu_free = msc
            P.op("v", lambda e, t_=et: e.tensor_add(
                out=X[:, t_ * S:(t_ + 1) * S],
                in0=X[:, t_ * S:(t_ + 1) * S], in1=self.RS[:, :]),
                waits=[msc], inc=True)
        P.barrier()

    def build(self):
        P = self.P
        self._ln_done = []
        self.load_consts()
        if "A" in self.phases:
            self.phase_A()
        if "B" in self.phases:
            self.phase_B()
        if "C" in self.phases:
            self.phase_C()
        if "D" in self.phases:
            self.phase_D()
        self.phase_E()
        P.emit()
        return P.nc

    def mm_proj_chunk(self, ps, wap, c, shift, w0, src=None, nk=NT):
        P = self.P
        src = self.XN if src is None else src
        last = None
        for kt in range(nk):
            sh = shift and kt >= nk // 2
            if not sh:
                rhs = src[:, kt * S + c * 512: kt * S + (c + 1) * 512]
                out = ps
            elif c > 0:
                rhs = src[:, kt * S + c * 512 - 1: kt * S + (c + 1) * 512 - 1]
                out = ps
            else:
                rhs = src[:, kt * S: kt * S + 511]
                out = ps[:, 1:512]
            last = P.op("t", lambda e, k_=kt, r_=rhs, o_=out, w_=wap:
                        e.matmul(o_, w_[:, k_ * 128:(k_ + 1) * 128], r_,
                                 start=(k_ == 0), stop=(k_ == nk - 1)),
                        waits=(w0 if kt == 0 else ()), inc=(kt == nk - 1))
        return last

    def _qk_pipeline(self, lay, wap, w_ms, w_fh, dst, scale_ap, use_i2,
                     cast_now):
        P = self.P
        mev_all = []
        mln_all = []
        for c in range(NCH):
            islot, ps, fz = self.slA.next()
            mm = self.mm_proj_chunk(ps, wap, c, True, [w_ms, fz, cast_now])
            mev = P.op("v", lambda e, c_=c, p_=ps: e.tensor_copy(
                out=dst[:, c_ * 512:(c_ + 1) * 512], in_=p_),
                waits=[mm, self._dst_free.get(id(dst))], inc=True)
            self.slA.consumed(islot, mev)
            mev_all.append(mev)
            msq = P.op("s", lambda e, c_=c: e.activation(
                out=self.BA[:, c_ * 512:(c_ + 1) * 512],
                in_=dst[:, c_ * 512:(c_ + 1) * 512], func=AF.Square),
                waits=[mev, self._ba_free, self._ba_read], inc=True)
            mss = P.op("t", lambda e, c_=c: e.matmul(
                self.psF[:, :], self.C["c_bd64"][:, :],
                self.BA[:, c_ * 512:(c_ + 1) * 512], start=True, stop=True),
                waits=[msq, self._psF_free], inc=True)
            self._ba_free = mss
            mln = P.op("s", lambda e, c_=c: e.activation(
                out=self.MU[:, c_ * 512:(c_ + 1) * 512], in_=self.psF[:, :],
                func=AF.Ln, bias=self.C["c_scal"][:, 6:7]),
                waits=[mss, self._mu_free], inc=True)
            self._psF_free = mln
            mln_all.append(mln)
        mex = P.op("s", lambda e: e.activation(
            out=self.MU[:, :], in_=self.MU[:, :], func=AF.Exp, scale=-0.5),
            waits=mln_all, inc=True)
        msc = P.op("s", lambda e, s_=scale_ap: e.mul(
            out=self.MU[:, :], in_=self.MU[:, :], mul=s_),
            waits=[mex], inc=True)
        mqn = P.op("v", lambda e: e.tensor_mul(
            out=dst[:, :], in0=dst[:, :], in1=self.MU[:, :]),
            waits=[msc] + mev_all, inc=True)
        self._mu_free = mqn
        msm_all = []
        mro = None
        for c in range(NCH):
            islot, ps, fz = self.slB.next()
            mro = P.op("t", lambda e, c_=c, p_=ps: e.matmul(
                p_, self.C["c_rt128"][:, :],
                dst[:, c_ * 512:(c_ + 1) * 512], start=True, stop=True),
                waits=[mqn, fz], inc=True)
            msm = P.op("v", lambda e, c_=c, p_=ps: e.tensor_mul(
                out=self.SMF[:, c_ * 512:(c_ + 1) * 512], in0=p_,
                in1=self.TSN[:, c_ * 512:(c_ + 1) * 512]),
                waits=[mro] + self._sm_free[0:4], inc=True)
            self.slB.consumed(islot, msm)
            msm_all.append(msm)
        mtc = P.op("v", lambda e: e.tensor_mul(
            out=dst[:, :], in0=dst[:, :], in1=self.TCS[:, :]),
            waits=[mro], inc=True)
        mad = P.op("v", lambda e: e.tensor_add(
            out=dst[:, :], in0=dst[:, :], in1=self.SMF[:, :]),
            waits=[mtc] + msm_all, inc=True)
        self._smf_read = mad
        if use_i2:
            mad = P.op("v", lambda e: e.tensor_mul(
                out=dst[:, :], in0=dst[:, :], in1=self.TI2[:, :]),
                waits=[mad], inc=True)
        WRing.free(w_fh, P.now("t"))
        return mad

    def l0_attn_hg(self, lay, hg, cast_now):
        P = self.P
        def colpan(sel, c0):
            return self.dram["w_l0qkv"][lay].rearrange(
                "(k p) e -> p k e", p=128)[:, :, c0 + hg * 128:
                                           c0 + hg * 128 + 128]
        wq, wq_ms, wq_fh = self.wring.load(colpan(0, 0), 1024)
        wk, wk_ms, wk_fh = self.wring.load(colpan(1, 1024), 1024)
        wv, wv_ms, wv_fh = self.wring.load(colpan(2, 2048), 1024)
        vt_ms = []
        for c in range(NCH):
            islot, ps, fz = self.slA.next()
            mm = self.mm_proj_chunk(ps, wv, c, True, [wv_ms, fz, cast_now])
            mev = P.op("s", lambda e, c_=c, p_=ps: e.copy(
                out=self.BC[:, c_ * 512:(c_ + 1) * 512], in_=p_),
                waits=[mm, self._bc_free], inc=True)
            self.slA.consumed(islot, mev)
            vt_ms.append(mev)
        WRing.free(wv_fh, P.now("t"))
        qs_ap = self.C["p_l0qs"][:, lay:lay + 1]
        ks_ap = self.C["p_l0ks"][:, lay:lay + 1]
        mq = self._qk_pipeline(lay, wq, wq_ms, wq_fh, self.BD, qs_ap,
                               False, cast_now)
        mk = self._qk_pipeline(lay, wk, wk_ms, wk_fh, self.BE, ks_ap,
                               True, cast_now)
        vn_ms = []
        for hh in (0, 1):
            mtr = None
            for jb in range(16):
                mtr = P.op("t", lambda e, h_=hh, j_=jb: e.transpose(
                    self.psT[:, j_ * 64:(j_ + 1) * 64],
                    self.BC[h_ * 64:h_ * 64 + 64, j_ * 128:(j_ + 1) * 128],
                    self.C["c_identb"][h_ * 64:h_ * 64 + 64,
                                       h_ * 64:h_ * 64 + 64]),
                    waits=vt_ms + [self._psT_free[0], self._pt_read[0]],
                    inc=(jb == 15))
            mvn = P.op("v", lambda e, h_=hh: e.tensor_copy(
                out=self.BA[:, :].rearrange("p (b c) -> p b c", b=16)
                    [:, :, h_ * 64:h_ * 64 + 64],
                in_=self.psT[:, 0:1024].rearrange("p (b c) -> p b c", c=64)),
                waits=[mtr, self._ba_free], inc=True)
            self._psT_free[0] = mvn
            vn_ms.append(mvn)
        if self.dbg == "qk0" and hg == 0:
            P.barrier()
            for src_, ti in ((self.BD, 0), (self.BE, 1), (self.BC, 2),
                             (self.BA, 3)):
                P.op("v", lambda e, s_=src_, t_=ti: e.tensor_copy(
                    out=self.Y[:, t_ * S:(t_ + 1) * S], in_=s_[:, :]),
                    inc=True)
            self._dump_y()
            self._dbg_stop = True
            return []
        out_ms = []
        units = [(hh, w) for hh in (0, 1) for w in range(8)]
        pend = []
        for ui, (hh, w) in enumerate(units):
            pend.append(self._l0_attn_s(hg, hh, w, ui, [mq, mk]))
            if len(pend) > 2:
                out_ms.append(self._l0_attn_av(vn_ms, pend.pop(0)))
        while pend:
            out_ms.append(self._l0_attn_av(vn_ms, pend.pop(0)))
        tnow = P.now("t")
        self._ba_read = tnow
        self._bc_free = tnow
        self._dst_free[id(self.BD)] = tnow
        self._dst_free[id(self.BE)] = tnow
        return out_ms

    def _l0_attn_s(self, hg, hh, w, ui, qk_ms):
        P = self.P
        par = ui % 2
        hr = 64 * hh
        width = 512 if w > 0 else 256
        j0 = (w * 256 - 256) if w > 0 else 0
        mpps = []
        p0s = []
        for ib in (0, 1):
            sl = par * 2 + ib
            i0 = w * 256 + ib * 128
            mask = self.C["c_mke"] if ib == 0 else self.C["c_mko"]
            mask_ap = mask[:, 0:width] if w > 0 else mask[:, 256:512]
            islot, ps, fz = self.slA.next()
            msmm = P.op("t", lambda e, p_=ps, i0_=i0, hr_=hr, j0_=j0,
                        wd_=width: e.matmul(
                p_[:, 0:wd_],
                self.BD[hr_:hr_ + 64, i0_:i0_ + 128],
                self.BE[hr_:hr_ + 64, j0_:j0_ + wd_],
                start=True, stop=True),
                waits=qk_ms + [fz], inc=True)
            msm = P.op("v", lambda e, p_=ps, m_=mask_ap, s_=sl,
                       wd_=width: e.scalar_tensor_tensor(
                           out=self.SMF[:, s_ * 512: s_ * 512 + wd_],
                           in0=p_[:, 0:wd_], scalar=8.0, in1=m_,
                           op0=ALU.mult, op1=ALU.add),
                       waits=[msmm, self._sm_free[sl]], inc=True)
            self.slA.consumed(islot, msm)
            mnm = P.op("v", lambda e, s_=sl, wd_=width: e.reduce_max(
                out=self.SMALL[:, s_:s_ + 1],
                in_=self.SMF[:, s_ * 512: s_ * 512 + wd_],
                axis=AX.X, negate=True),
                waits=[msm, self._stat_free[sl]], inc=True)
            p0 = self.BB[:, sl * 512: sl * 512 + width]
            mz0 = P.op("v", lambda e, s_=sl: e.memset(
                self.SMALL[:, 4 + s_:5 + s_], 0.0),
                waits=[self._stat_free[sl]], inc=True)
            mex = P.op("s", lambda e, p0_=p0, s_=sl, wd_=width:
                       e.activation(
                out=p0_, in_=self.SMF[:, s_ * 512: s_ * 512 + wd_],
                func=AF.Exp, bias=self.SMALL[:, s_:s_ + 1],
                accum_out=self.SMALL[:, 4 + s_:5 + s_]),
                waits=[mnm, self._p0_free[sl], mz0], inc=True)
            self._sm_free[sl] = mex
            miz = P.op("v", lambda e, s_=sl: e.reciprocal(
                out=self.SMALL[:, 8 + s_:9 + s_],
                in_=self.SMALL[:, 4 + s_:5 + s_]), waits=[mex], inc=True)
            mpp = P.op("v", lambda e, p0_=p0, s_=sl: e.tensor_scalar(
                out=p0_, in0=p0_, scalar1=self.SMALL[:, 8 + s_:9 + s_],
                scalar2=None, op0=ALU.mult), waits=[Prog.sync(miz)],
                inc=True)
            self._stat_free[sl] = mpp
            mpps.append(mpp)
            p0s.append(p0)
        return (hg, hh, w, ui, mpps, p0s)

    def _l0_attn_av(self, vn_ms, ctx):
        P = self.P
        hg, hh, w, ui, mpps, p0s = ctx
        par = ui % 2
        hr = 64 * hh
        njb = 4 if w > 0 else 2
        tbase = par * 1024
        tr_last = None
        for ib in (0, 1):
            p0 = p0s[ib]
            for jb in range(njb):
                jbg = jb + (2 if w == 0 else 0)
                tr_last = P.op("t", lambda e, p0_=p0, j_=jb, jg_=jbg,
                               ib_=ib, tb_=tbase: e.transpose(
                    self.psT[:, tb_ + jg_ * 256 + ib_ * 128:
                             tb_ + jg_ * 256 + ib_ * 128 + 128],
                    p0_[:, j_ * 128:(j_ + 1) * 128],
                    self.C["c_identb"][:, :]),
                    waits=[mpps[ib], self._psT_free[par]],
                    inc=(ib == 1 and jb == njb - 1))
        mpt = P.op("v", lambda e, tb_=tbase, pr_=par: e.tensor_copy(
            out=self.BC[:, pr_ * 1024: pr_ * 1024 + 1024],
            in_=self.psT[:, tb_: tb_ + 1024]),
            waits=[tr_last, self._pt_read[par]], inc=True)
        self._psT_free[par] = mpt
        for ib in (0, 1):
            self._p0_free[par * 2 + ib] = mpt
        jlist = list(range(2, 4)) if w == 0 else list(range(4))
        mav = None
        for n_, jbg in enumerate(jlist):
            blk = 2 * w - 2 + jbg
            mav = P.op("t", lambda e, j_=jbg, b_=blk, pr_=par,
                       st_=(n_ == 0), sp_=(n_ == len(jlist) - 1): e.matmul(
                self.psF[:, 0:256],
                self.BA[:, b_ * 128: b_ * 128 + 128],
                self.BC[:, pr_ * 1024 + j_ * 256:
                        pr_ * 1024 + j_ * 256 + 256],
                start=st_, stop=sp_),
                waits=[mpt] + vn_ms + [self._psF_free],
                inc=(n_ == len(jlist) - 1))
        self._pt_read[par] = mav
        mo = P.op("s", lambda e, hr_=hr, hg_=hg, w_=w: e.copy(
            out=self.OUT[hr_:hr_ + 64,
                         hg_ * S + w_ * 256: hg_ * S + w_ * 256 + 256],
            in_=self.psF[hr_:hr_ + 64, 0:256]), waits=[mav], inc=True)
        self._psF_free = mo
        return mo

    def l0_wo(self, lay):
        P = self.P
        for dt in range(NT):
            slab, w_ms, fh = self.wring.load(
                self.dram["w_l0o"][lay].rearrange("(k p) e -> p k e", p=128)
                [:, :, dt * 128:(dt + 1) * 128], 1024)
            for c in range(NCH):
                islot, ps, fz = self.slA.next()
                mm = self.mm_proj_chunk(ps, slab, c, False, [w_ms, fz],
                                        src=self.OUT)
                mev = P.op("v", lambda e, p_=ps, d_=dt, c_=c: e.tensor_add(
                    out=self.Y[:, d_ * S + c_ * 512: d_ * S + (c_ + 1) * 512],
                    in0=p_,
                    in1=self.Y[:, d_ * S + c_ * 512: d_ * S + (c_ + 1) * 512]),
                    waits=[mm], inc=True)
                self.slA.consumed(islot, mev)
            WRing.free(fh, P.now("t"))

    def l0_ffn(self, lay):
        P = self.P
        qsplit = [(0, 6), (6, 12), (12, 18), (18, 22)]
        g_free = [None] * NCH
        hh_free = None
        for qr, (f0, f1) in enumerate(qsplit):
            for fa in range(f0, f1):
                wa, wa_ms, wa_fh = self.wring.load(
                    self.dram["w_l0w1p"][lay].rearrange(
                        "(k p) e -> p k e", p=128)
                    [:, :, fa * 128:(fa + 1) * 128], 1024)
                wg, wg_ms, wg_fh = self.wring.load(
                    self.dram["w_l0w1p"][lay].rearrange(
                        "(k p) e -> p k e", p=128)
                    [:, :, 2816 + fa * 128: 2816 + (fa + 1) * 128], 1024)
                b1a = self.C["p_l0b1"][:, lay * 44 + fa: lay * 44 + fa + 1]
                b1g = self.C["p_l0b1"][:, lay * 44 + 22 + fa:
                                       lay * 44 + 22 + fa + 1]
                hhofs = (fa - f0) * S
                for c in range(NCH):
                    ia, pa, fza = self.slA.next()
                    mma = self.mm_proj_chunk(pa, wa, c, False, [wa_ms, fza])
                    ig, pg, fzg = self.slB.next()
                    mmg = self.mm_proj_chunk(pg, wg, c, False, [wg_ms, fzg])
                    mg = P.op("s", lambda e, p_=pg, c_=c, b_=b1g:
                              e.activation(
                                  out=self.BA[:, c_ * 512:(c_ + 1) * 512],
                                  in_=p_, func=AF.Gelu, bias=b_),
                              waits=[mmg, g_free[c]], inc=True)
                    self.slB.consumed(ig, mg)
                    mh = P.op("v", lambda e, p_=pa, c_=c, b_=b1a, o_=hhofs:
                              e.scalar_tensor_tensor(
                                  out=self.OUT[:, o_ + c_ * 512:
                                               o_ + (c_ + 1) * 512],
                                  in0=p_, scalar=b_,
                                  in1=self.BA[:, c_ * 512:(c_ + 1) * 512],
                                  op0=ALU.add, op1=ALU.mult),
                              waits=[mma, mg, hh_free], inc=True)
                    self.slA.consumed(ia, mh)
                    g_free[c] = mh
                WRing.free(wa_fh, P.now("t"))
                WRing.free(wg_fh, P.now("t"))
            hh_ready = P.now("v")
            nk = f1 - f0
            for dt in range(NT):
                slab, w_ms, fh = self.wring.load(
                    self.dram["w_l0w2p"][lay].rearrange(
                        "(k p) d -> p k d", p=128)
                    [:, f0:f1, dt * 128:(dt + 1) * 128], nk * 128)
                b2 = self.C["p_l0b2"][:, lay * 8 + dt: lay * 8 + dt + 1]
                for c in range(NCH):
                    islot, ps, fz = self.slA.next()
                    mm = self.mm_proj_chunk(ps, slab, c, False,
                                            [w_ms, fz, hh_ready],
                                            src=self.OUT, nk=nk)
                    if qr == 0:
                        mev = P.op("v", lambda e, p_=ps, d_=dt, c_=c, b_=b2:
                                   e.scalar_tensor_tensor(
                                       out=self.Y[:, d_ * S + c_ * 512:
                                                  d_ * S + (c_ + 1) * 512],
                                       in0=p_, scalar=b_,
                                       in1=self.Y[:, d_ * S + c_ * 512:
                                                  d_ * S + (c_ + 1) * 512],
                                       op0=ALU.add, op1=ALU.add),
                                   waits=[mm], inc=True)
                    else:
                        mev = P.op("v", lambda e, p_=ps, d_=dt, c_=c:
                                   e.tensor_add(
                                       out=self.Y[:, d_ * S + c_ * 512:
                                                  d_ * S + (c_ + 1) * 512],
                                       in0=p_,
                                       in1=self.Y[:, d_ * S + c_ * 512:
                                                  d_ * S + (c_ + 1) * 512]),
                                   waits=[mm], inc=True)
                    self.slA.consumed(islot, mev)
                WRing.free(fh, P.now("t"))
            hh_free = P.now("t")

    def _dump_y(self):
        P = self.P
        P.barrier()
        for t in range(NT):
            P.dma("y", self.xs(t), self.Y[:, t * S:(t + 1) * S])
        P.barrier()

    def phase_B(self):
        P = self.P
        P.barrier()
        self.emit_ln(src_sbuf=None, dst=self.Y)
        P.barrier()
        if self.dbg == "ln":
            self._dump_y()
            return
        self.load_l0_tables()
        P.barrier()
        for lay in range(self.l0_layers):
            for t in range(NT):
                P.op("s", lambda e, t_=t: e.copy(
                    out=self.XN[:, t_ * S:(t_ + 1) * S],
                    in_=self.Y[:, t_ * S:(t_ + 1) * S]),
                    inc=(t == NT - 1))
            cast_now = P.now("s")
            self._reset_attn_tracking()
            self._dbg_stop = False
            for hg in range(NT):
                self.l0_attn_hg(lay, hg, cast_now)
                if self._dbg_stop:
                    return
            P.barrier()
            if self.dbg == "out0" and lay == 0:
                for t in range(NT):
                    P.op("v", lambda e, t_=t: e.tensor_copy(
                        out=self.Y[:, t_ * S:(t_ + 1) * S],
                        in_=self.OUT[:, t_ * S:(t_ + 1) * S]), inc=True)
                self._dump_y()
                return
            self.l0_wo(lay)
            P.barrier()
            if self.dbg == "attn" and lay == 0:
                self._dump_y()
                return
            self.emit_ln(src_sbuf=self.Y, dst=self.XN, shift=True,
                         gain=True, gcol=lay)
            P.barrier()
            if self.dbg == "ffnln" and lay == 0:
                for t in range(NT):
                    P.op("v", lambda e, t_=t: e.tensor_copy(
                        out=self.Y[:, t_ * S:(t_ + 1) * S],
                        in_=self.XN[:, t_ * S:(t_ + 1) * S]), inc=True)
                self._dump_y()
                return
            self.l0_ffn(lay)
            P.barrier()
        prev = [None, None]
        for t in range(NT):
            buf = self.XS0 if t % 2 == 0 else self.Q2
            ml = P.dma("y", buf[:, :], self.xs(t), waits=[prev[t % 2]])
            ma = P.op("v", lambda e, t_=t, b_=buf: e.tensor_add(
                out=self.Y[:, t_ * S:(t_ + 1) * S],
                in0=self.Y[:, t_ * S:(t_ + 1) * S], in1=b_[:, :]),
                waits=[ml], inc=True)
            prev[t % 2] = ma
            P.dma("y", self.xs(t), self.Y[:, t * S:(t + 1) * S], waits=[ma])
        P.barrier()

    def _reset_attn_tracking(self):
        self._psA_free = None
        self._vtt_free = None
        self._hh_free = None
        self._wbc_read = None
        self._wbc_ms = None
        self._sidx = 0
        self._dst_free = {}
        self._ba_free = None
        self._ba_read = None
        self._bc_free = None
        self._mu_free = None
        self._smf_free = None
        self._smf_read = None
        self._psF_free = None
        self._psT_free = [None, None]
        self._pt_read = [None, None]
        self._rot_free = [None, None]
        self._sm_free = [None] * 4
        self._p0_free = [None] * 4
        self._stat_free = [None] * 4

    def _l1_rope(self, tile_ap, rot_rows, ready):
        P = self.P
        msm_all = []
        mro = None
        for c in range(NCH):
            islot, ps, fz = self.slB.next()
            mro = P.op("t", lambda e, c_=c, p_=ps, rr_=rot_rows: e.matmul(
                p_[0:rr_, :], self.C["c_rt128"][0:rr_, 0:rr_],
                tile_ap[0:rr_, c_ * 512:(c_ + 1) * 512],
                start=True, stop=True),
                waits=[ready, fz], inc=True)
            msm = P.op("v", lambda e, c_=c, p_=ps, rr_=rot_rows:
                       e.tensor_mul(
                           out=self.SMF[0:rr_, c_ * 512:(c_ + 1) * 512],
                           in0=p_[0:rr_, :],
                           in1=self.T1S[0:rr_, c_ * 512:(c_ + 1) * 512]),
                       waits=[mro, self._smf_read], inc=True)
            self.slB.consumed(islot, msm)
            msm_all.append(msm)
        mtc = P.op("v", lambda e, rr_=rot_rows: e.tensor_mul(
            out=tile_ap[0:rr_, :], in0=tile_ap[0:rr_, :],
            in1=self.T1C[0:rr_, :]), waits=[mro], inc=True)
        mad = P.op("v", lambda e, rr_=rot_rows: e.tensor_add(
            out=tile_ap[0:rr_, :], in0=tile_ap[0:rr_, :],
            in1=self.SMF[0:rr_, :]), waits=[mtc] + msm_all, inc=True)
        self._smf_read = mad
        return mad

    def phase_C(self):
        P = self.P
        self.PT1 = self.BE
        self.VTT = self.BC
        P.barrier()
        self.emit_ln(src_sbuf=None, dst=self.XN)
        P.barrier()
        self.load_l1_tables()
        P.barrier()
        self._reset_attn_tracking()
        self._smf_read = None
        wqa, wqa_ms, wqa_fh = self.wring.load(
            self.dram["w_1qa"].rearrange("(k p) e -> p k e", p=128), 512)
        t1_ms = []
        for c in range(NCH):
            islot, ps, fz = self.slB.next()
            mm = None
            for kt in range(NT):
                mm = P.op("t", lambda e, k_=kt, c_=c, p_=ps, w_=wqa:
                          e.matmul(p_[0:64, :], w_[:, k_ * 64:(k_ + 1) * 64],
                                   self.XN[:, k_ * S + c_ * 512:
                                           k_ * S + (c_ + 1) * 512],
                                   start=(k_ == 0), stop=(k_ == NT - 1)),
                          waits=[wqa_ms, fz] if kt == 0 else (),
                          inc=(kt == NT - 1))
            mev = P.op("s", lambda e, c_=c, p_=ps: e.copy(
                out=self.BA[0:64, c_ * 512:(c_ + 1) * 512], in_=p_[0:64, :]),
                waits=[mm], inc=True)
            self.slB.consumed(islot, mev)
            t1_ms.append(mev)
        WRing.free(wqa_fh, P.now("t"))
        wqb = self.BB[0:64, 0:1024]
        wqb_ms = P.dma("g", wqb, self.dram["w_1qb"][:, :])
        q_ms = []
        for et in range(NT):
            evs = []
            for c in range(NCH):
                islot, ps, fz = self.slA.next()
                mm = P.op("t", lambda e, c_=c, p_=ps, e_=et, w_=wqb:
                          e.matmul(p_, w_[:, e_ * 128:(e_ + 1) * 128],
                                   self.BA[0:64, c_ * 512:(c_ + 1) * 512],
                                   start=True, stop=True),
                          waits=[wqb_ms] + t1_ms + [fz], inc=True)
                mev = P.op("v", lambda e, c_=c, p_=ps, e_=et: e.tensor_copy(
                    out=self.QARE[:, e_ * S + c_ * 512:
                                  e_ * S + (c_ + 1) * 512], in_=p_),
                    waits=[mm], inc=True)
                self.slA.consumed(islot, mev)
                evs.append(mev)
            mq = self._l1_rope(self.QARE[:, et * S:(et + 1) * S], 128,
                               evs[-1])
            q_ms.append(mq)
        wkva, wkva_ms, wkva_fh = self.wring.load(
            self.dram["w_1kva"].rearrange("(k p) e -> p k e", p=128), 256)
        t2_ms = []
        for c in range(NCH):
            islot, ps, fz = self.slB.next()
            mm = None
            for kt in range(NT):
                mm = P.op("t", lambda e, k_=kt, c_=c, p_=ps, w_=wkva:
                          e.matmul(p_[0:32, :], w_[:, k_ * 32:(k_ + 1) * 32],
                                   self.XN[:, k_ * S + c_ * 512:
                                           k_ * S + (c_ + 1) * 512],
                                   start=(k_ == 0), stop=(k_ == NT - 1)),
                          waits=[wkva_ms, fz] + t1_ms if kt == 0 else (),
                          inc=(kt == NT - 1))
            mev = P.op("s", lambda e, c_=c, p_=ps: e.copy(
                out=self.BA[0:32, c_ * 512:(c_ + 1) * 512], in_=p_[0:32, :]),
                waits=[mm] + q_ms, inc=True)
            self.slB.consumed(islot, mev)
            t2_ms.append(mev)
        WRing.free(wkva_fh, P.now("t"))
        wk2 = self.BB[0:32, 1024:1536]
        wk2_ms = P.dma("g", wk2, self.dram["w_1kvb_k2"][:, :])
        k_ms = []
        for g in range(4):
            evs = []
            for c in range(NCH):
                islot, ps, fz = self.slA.next()
                mm = P.op("t", lambda e, c_=c, p_=ps, g_=g, w_=wk2:
                          e.matmul(p_, w_[:, g_ * 128:(g_ + 1) * 128],
                                   self.BA[0:32, c_ * 512:(c_ + 1) * 512],
                                   start=True, stop=True),
                          waits=[wk2_ms] + t2_ms + [fz], inc=True)
                mev = P.op("v", lambda e, c_=c, p_=ps, g_=g: e.tensor_copy(
                    out=self.KARE[:, g_ * S + c_ * 512:
                                  g_ * S + (c_ + 1) * 512], in_=p_),
                    waits=[mm], inc=True)
                self.slA.consumed(islot, mev)
                evs.append(mev)
            mk = self._l1_rope(self.KARE[:, g * S:(g + 1) * S], 128,
                               evs[-1])
            k_ms.append(mk)
        wv = self.BB[0:32, 1536:1792]
        wv_ms = P.dma("g", wv, self.dram["w_1kvb_v"][:, :])
        vn_ms = []
        for g in range(4):
            evs = []
            for c in range(NCH):
                islot, ps, fz = self.slB.next()
                mm = P.op("t", lambda e, c_=c, p_=ps, g_=g, w_=wv:
                          e.matmul(p_[0:64, :], w_[:, g_ * 64:(g_ + 1) * 64],
                                   self.BA[0:32, c_ * 512:(c_ + 1) * 512],
                                   start=True, stop=True),
                          waits=[wv_ms] + t2_ms + [fz, self._vtt_free],
                          inc=True)
                mev = P.op("s", lambda e, c_=c, p_=ps: e.copy(
                    out=self.VTT[0:64, c_ * 512:(c_ + 1) * 512],
                    in_=p_[0:64, :]), waits=[mm], inc=True)
                self.slB.consumed(islot, mev)
                evs.append(mev)
            mtr = None
            for jb in range(16):
                mtr = P.op("t", lambda e, j_=jb: e.transpose(
                    self.psT[:, j_ * 64:(j_ + 1) * 64],
                    self.VTT[0:64, j_ * 128:(j_ + 1) * 128],
                    self.C["c_identb"][0:64, 0:64]),
                    waits=evs + [self._psT_free[0], self._pt_read[0]],
                    inc=(jb == 15))
            self._vtt_free = mtr
            mv1 = P.op("v", lambda e, g_=g: e.tensor_copy(
                out=self.VNARE[:, g_ * 2048: (g_ + 1) * 2048].rearrange(
                    "p (b c) -> p b c", c=128)[:, :, 0:64],
                in_=self.psT[:, 0:1024].rearrange("p (b c) -> p b c", c=64)),
                waits=[mtr], inc=True)
            mv2 = P.op("v", lambda e, g_=g: e.tensor_copy(
                out=self.VNARE[:, g_ * 2048: (g_ + 1) * 2048].rearrange(
                    "p (b c) -> p b c", c=128)[:, :, 64:128],
                in_=self.psT[:, 0:1024].rearrange("p (b c) -> p b c", c=64)),
                waits=[mtr], inc=True)
            self._psT_free[0] = mv2
            vn_ms.append(mv2)
        P.barrier()
        for h in range(16):
            self._l1_attn_head(h, q_ms, k_ms, vn_ms)
        P.barrier()
        xprev = [None, None]
        for dt in range(NT):
            slab, w_ms, fh = self.wring.load(
                self.dram["w_1o"].rearrange("(k p) e -> p k e", p=128)
                [:, :, dt * 128:(dt + 1) * 128], 1024)
            buf = self.XS0 if dt % 2 == 0 else self.Q2
            ml = P.dma("y", buf[:, :], self.xs(dt), waits=[xprev[dt % 2]])
            bout = self.C["p_bout"][:, dt:dt + 1]
            mevs = []
            for c in range(NCH):
                islot, ps, fz = self.slA.next()
                mm = self.mm_proj_chunk(ps, slab, c, False, [w_ms, fz],
                                        src=self.OUT)
                mev = P.op("v", lambda e, p_=ps, c_=c, b_=bout, bf_=buf:
                           e.scalar_tensor_tensor(
                               out=bf_[:, c_ * 512:(c_ + 1) * 512],
                               in0=p_, scalar=b_,
                               in1=bf_[:, c_ * 512:(c_ + 1) * 512],
                               op0=ALU.add, op1=ALU.add),
                           waits=[mm, ml], inc=True)
                self.slA.consumed(islot, mev)
                mevs.append(mev)
            mw = P.dma("y", self.xs(dt), buf[:, :], waits=mevs)
            xprev[dt % 2] = mw
            WRing.free(fh, P.now("t"))
        P.barrier()

    def _l1_attn_head(self, h, q_ms, k_ms, vn_ms):
        P = self.P
        qt = h // 2
        hr = 64 * (h % 2)
        g = h // 4
        for ib in range(16):
            L = (ib + 1) * 128
            mm = None
            for c0 in range(0, L, 512):
                w_ = min(512, L - c0)
                mm = P.op("t", lambda e, c0_=c0, w2_=w_, ib_=ib, qt_=qt,
                          hr_=hr, g_=g: e.matmul(
                    self.psA[:, c0_:c0_ + w2_],
                    self.QARE[hr_:hr_ + 64,
                              qt_ * S + ib_ * 128: qt_ * S + ib_ * 128 + 128],
                    self.KARE[hr_:hr_ + 64, g_ * S + c0_: g_ * S + c0_ + w2_],
                    start=True, stop=True),
                    waits=[q_ms[qt], k_ms[g], self._psA_free], inc=True)
            msc = P.op("v", lambda e, L_=L: e.tensor_scalar(
                out=self.SMF[:, 0:L_], in0=self.psA[:, 0:L_],
                scalar1=0.125, scalar2=None, op0=ALU.mult),
                waits=[mm, self._sm_free[0]], inc=True)
            mdg = P.op("v", lambda e, L_=L: e.tensor_add(
                out=self.SMF[:, L_ - 128:L_], in0=self.SMF[:, L_ - 128:L_],
                in1=self.C["c_diag"][:, :]), waits=[msc], inc=True)
            self._psA_free = mdg
            mnm = P.op("v", lambda e, L_=L: e.reduce_max(
                out=self.SMALL[:, 0:1], in_=self.SMF[:, 0:L_],
                axis=AX.X, negate=True),
                waits=[mdg, self._stat_free[0]], inc=True)
            mz0 = P.op("v", lambda e: e.memset(
                self.SMALL[:, 2:3], 0.0), waits=[self._stat_free[0]],
                inc=True)
            mex = P.op("s", lambda e, L_=L: e.activation(
                out=self.BD[:, 0:L_], in_=self.SMF[:, 0:L_],
                func=AF.Exp, bias=self.SMALL[:, 0:1],
                accum_out=self.SMALL[:, 2:3]),
                waits=[mnm, mz0, self._p0_free[0]], inc=True)
            self._sm_free[0] = mex
            miz = P.op("v", lambda e: e.reciprocal(
                out=self.SMALL[:, 4:5], in_=self.SMALL[:, 2:3]),
                waits=[mex], inc=True)
            mpp = P.op("v", lambda e, L_=L: e.tensor_scalar(
                out=self.BD[:, 0:L_], in0=self.BD[:, 0:L_],
                scalar1=self.SMALL[:, 4:5], scalar2=None, op0=ALU.mult),
                waits=[Prog.sync(miz)], inc=True)
            self._stat_free[0] = mpp
            mpt_all = []
            for bi, base in enumerate(range(0, ib + 1, 8)):
                hi = min(base + 8, ib + 1)
                par = bi % 2
                mtr = None
                for jb in range(base, hi):
                    mtr = P.op("t", lambda e, j_=jb, b0_=base, pr_=par:
                               e.transpose(
                        self.psT[:, pr_ * 1024 + (j_ - b0_) * 128:
                                 pr_ * 1024 + (j_ - b0_) * 128 + 128],
                        self.BD[:, j_ * 128:(j_ + 1) * 128],
                        self.C["c_identb"][:, :]),
                        waits=[mpp, self._psT_free[par]], inc=(jb == hi - 1))
                mpt = P.op("v", lambda e, b0_=base, hi_=hi, pr_=par:
                           e.tensor_copy(
                    out=self.PT1[:, b0_ * 128: hi_ * 128],
                    in_=self.psT[:, pr_ * 1024:
                                 pr_ * 1024 + (hi_ - b0_) * 128]),
                    waits=[mtr, self._pt_read[par]], inc=True)
                self._psT_free[par] = mpt
                mpt_all.append(mpt)
            self._p0_free[0] = mpt_all[-1]
            mav = None
            for jb in range(ib + 1):
                mav = P.op("t", lambda e, j_=jb, g_=g,
                           st_=(jb == 0), sp_=(jb == ib): e.matmul(
                    self.psF[:, 0:128],
                    self.VNARE[:, g_ * 2048 + j_ * 128:
                               g_ * 2048 + (j_ + 1) * 128],
                    self.PT1[:, j_ * 128:(j_ + 1) * 128],
                    start=st_, stop=sp_),
                    waits=mpt_all + [vn_ms[g], self._psF_free],
                    inc=(jb == ib))
            self._pt_read[0] = mav
            self._pt_read[1] = mav
            mo = P.op("s", lambda e, h_=h, hr_=hr, ib_=ib: e.copy(
                out=self.OUT[hr_:hr_ + 64,
                             (h_ // 2) * S + ib_ * 128:
                             (h_ // 2) * S + ib_ * 128 + 128],
                in_=self.psF[hr_:hr_ + 64, 0:128]), waits=[mav], inc=True)
            self._psF_free = mo

    def phase_D(self):
        P = self.P
        P.barrier()
        self.emit_ln(src_sbuf=None, dst=self.XN)
        P.barrier()
        self._reset_attn_tracking()
        wg, wg_ms, wg_fh = self.wring.load(
            self.dram["w_g"].rearrange("(k p) e -> p k e", p=128), 64)
        bg = self.C["p_bg"][0:8, 0:1]
        lg_ms = []
        for c in range(NCH):
            mm = None
            for kt in range(NT):
                mm = P.op("t", lambda e, k_=kt, c_=c, w_=wg: e.matmul(
                    self.psF[0:8, 0:512], w_[:, k_ * 8:(k_ + 1) * 8],
                    self.XN[:, k_ * S + c_ * 512: k_ * S + (c_ + 1) * 512],
                    start=(k_ == 0), stop=(k_ == NT - 1)),
                    waits=[wg_ms, self._psF_free] if kt == 0 else (),
                    inc=(kt == NT - 1))
            mev = P.op("v", lambda e, c_=c, b_=bg: e.tensor_scalar(
                out=self.MU[0:8, c_ * 512:(c_ + 1) * 512],
                in0=self.psF[0:8, 0:512], scalar1=b_, scalar2=None,
                op0=ALU.add), waits=[mm], inc=True)
            self._psF_free = mev
            lg_ms.append(mev)
        WRing.free(wg_fh, P.now("t"))
        lt_ms = []
        for sbk in range(16):
            mtr = P.op("t", lambda e, s_=sbk: e.transpose(
                self.psF[:, 8:16], self.MU[0:8, s_ * 128:(s_ + 1) * 128],
                self.C["c_identf"][0:8, 0:8]),
                waits=lg_ms + [self._psF_free], inc=True)
            mev = P.op("v", lambda e, s_=sbk: e.tensor_copy(
                out=self.RS[:, s_ * 8:(s_ + 1) * 8],
                in_=self.psF[:, 8:16]), waits=[mtr], inc=True)
            self._psF_free = mev
            lt_ms.append(mev)
        w_ms = []
        for sbk in range(16):
            k0 = sbk * 8
            LT = self.RS[:, k0:k0 + 8]
            m1 = P.op("v", lambda e, L_=LT: e.reduce_max(
                out=self.SMALL[:, 0:1], in_=L_, axis=AX.X, negate=True),
                waits=[lt_ms[sbk], self._stat_free[0]], inc=True)
            m2 = P.op("v", lambda e, L_=LT, k_=k0: e.tensor_scalar(
                out=self.Q1[:, k_:k_ + 8], in0=L_,
                scalar1=self.SMALL[:, 0:1], scalar2=None, op0=ALU.add),
                waits=[Prog.sync(m1)], inc=True)
            m3 = P.op("v", lambda e, k_=k0: e.tensor_single_scalar(
                out=self.Q2[:, k_:k_ + 8], in_=self.Q1[:, k_:k_ + 8],
                scalar=0.0, op=ALU.is_ge), waits=[Prog.sync(m2)], inc=True)
            m4 = P.op("v", lambda e, L_=LT, k_=k0: e.scalar_tensor_tensor(
                out=self.Q2[:, k_:k_ + 8], in0=self.Q2[:, k_:k_ + 8],
                scalar=-1e30, in1=L_, op0=ALU.mult, op1=ALU.add),
                waits=[Prog.sync(m3)], inc=True)
            m5 = P.op("v", lambda e, k_=k0: e.reduce_max(
                out=self.SMALL[:, 1:2], in_=self.Q2[:, k_:k_ + 8],
                axis=AX.X, negate=True), waits=[Prog.sync(m4)], inc=True)
            m6 = P.op("s", lambda e, k_=k0: e.activation(
                out=self.Q1[:, k_:k_ + 8], in_=self.Q1[:, k_:k_ + 8],
                func=AF.Exp), waits=[m2, m3], inc=True)
            m7 = P.op("v", lambda e, L_=LT, k_=k0: e.tensor_scalar(
                out=self.Q2[:, k_:k_ + 8], in0=L_,
                scalar1=self.SMALL[:, 1:2], scalar2=None, op0=ALU.add),
                waits=[Prog.sync(m5), m6], inc=True)
            m8 = P.op("v", lambda e, k_=k0: e.tensor_single_scalar(
                out=self.Q2[:, k_:k_ + 8], in_=self.Q2[:, k_:k_ + 8],
                scalar=0.0, op=ALU.is_ge), waits=[Prog.sync(m7)], inc=True)
            m9 = P.op("v", lambda e: e.tensor_sub(
                out=self.SMALL[:, 2:3], in0=self.SMALL[:, 0:1],
                in1=self.SMALL[:, 1:2]), waits=[Prog.sync(m5)], inc=True)
            m10 = P.op("s", lambda e: e.activation(
                out=self.SMALL[:, 3:4], in_=self.SMALL[:, 2:3],
                func=AF.Exp), waits=[m9], inc=True)
            m11 = P.op("v", lambda e: e.tensor_scalar(
                out=self.SMALL[:, 4:5], in0=self.SMALL[:, 3:4],
                scalar1=1.0, scalar2=None, op0=ALU.add),
                waits=[m10], inc=True)
            m12 = P.op("v", lambda e: e.reciprocal(
                out=self.SMALL[:, 5:6], in_=self.SMALL[:, 4:5]),
                waits=[Prog.sync(m11)], inc=True)
            m13 = P.op("v", lambda e, k_=k0: e.tensor_mul(
                out=self.Q1[:, k_:k_ + 8], in0=self.Q1[:, k_:k_ + 8],
                in1=self.Q2[:, k_:k_ + 8]), waits=[m6, Prog.sync(m8)],
                inc=True)
            m14 = P.op("v", lambda e, k_=k0: e.tensor_scalar(
                out=self.Q1[:, k_:k_ + 8], in0=self.Q1[:, k_:k_ + 8],
                scalar1=self.SMALL[:, 5:6], scalar2=None, op0=ALU.mult),
                waits=[Prog.sync(m12), Prog.sync(m13)], inc=True)
            self._stat_free[0] = m14
            w_ms.append(m14)
        wt_ms = []
        for sbk in range(16):
            mtr = P.op("t", lambda e, s_=sbk: e.transpose(
                self.psF[0:8, 16:144], self.Q1[:, s_ * 8:(s_ + 1) * 8],
                self.C["c_identf"][:, :]),
                waits=[w_ms[sbk], self._psF_free], inc=True)
            mev = P.op("v", lambda e, s_=sbk: e.tensor_copy(
                out=self.MU[0:8, s_ * 128:(s_ + 1) * 128],
                in_=self.psF[0:8, 16:144]), waits=[mtr], inc=True)
            self._psF_free = mev
            wt_ms.append(mev)
        mwb = P.op("s", lambda e: e.copy(
            out=self.BC[0:8, :], in_=self.MU[0:8, :]), waits=wt_ms,
            inc=True)
        mb2c = P.dma("g", self.BA[0:8, 0:1024],
                     self.dram["p_moeb2t"][:, :])
        self._region_ms = {}
        smf_slots = [self.SMF[:, i * 512:(i + 1) * 512] for i in range(4)]
        smf_free = [None] * 4
        for dt in range(NT):
            for c in range(NCH):
                islot, ps, fz = self.slB.next()
                mm = P.op("t", lambda e, d_=dt, c_=c, p_=ps: e.matmul(
                    p_, self.BA[0:8, d_ * 128:(d_ + 1) * 128],
                    self.BC[0:8, c_ * 512:(c_ + 1) * 512],
                    start=True, stop=True),
                    waits=[mb2c, mwb, fz], inc=True)
                sl = self._sidx % 4
                mev = P.op("v", lambda e, p_=ps, s_=smf_slots[sl]:
                           e.tensor_copy(out=s_, in_=p_),
                           waits=[mm, smf_free[sl]], inc=True)
                self.slB.consumed(islot, mev)
                md = P.dma("g", self.xs(dt, c * 512, (c + 1) * 512),
                           smf_slots[sl], waits=[mev], accum=True)
                smf_free[sl] = md
                self._region_ms[(dt, c)] = md
                self._sidx += 1
        for ex in range(8):
            for fh in (0, 1):
                self._moe_expert(ex, fh, smf_slots, smf_free)
        P.barrier()

    def _moe_expert(self, ex, fh2, smf_slots, smf_free):
        P = self.P
        wb_ms = []
        if fh2 == 0:
            for c in range(NCH):
                islot, ps, fz = self.slB.next()
                mm = P.op("t", lambda e, c_=c, p_=ps, x_=ex: e.matmul(
                    p_, self.C["c_sel8"][:, x_ * 128:(x_ + 1) * 128],
                    self.BC[0:8, c_ * 512:(c_ + 1) * 512],
                    start=True, stop=True), waits=[fz, self._wbc_read],
                    inc=True)
                mev = P.op("v", lambda e, c_=c, p_=ps: e.tensor_copy(
                    out=self.BB[:, c_ * 512:(c_ + 1) * 512], in_=p_),
                    waits=[mm], inc=True)
                self.slB.consumed(islot, mev)
                wb_ms.append(mev)
            self._wbc_ms = wb_ms[-1]
        wbc_ms = self._wbc_ms
        for ft in range(16):
            fg = fh2 * 16 + ft
            slab, w_ms, wfh = self.wring.load(
                self.dram["w_moe1"][ex].rearrange("(k p) f -> p k f", p=128)
                [:, :, fg * 128:(fg + 1) * 128], 1024)
            b1 = self.C["p_moeb1"][:, ex * 32 + fg: ex * 32 + fg + 1]
            for cg in range(NCH):
                islot, ps, fz = self.slA.next()
                mm = self.mm_proj_chunk(ps, slab, cg, False,
                                        [w_ms, fz, self._hh_free])
                mg = P.op("s", lambda e, p_=ps, f_=ft, c_=cg, b_=b1:
                          e.activation(
                              out=self.HHARE[:, f_ * 2048 + c_ * 512:
                                             f_ * 2048 + (c_ + 1) * 512],
                              in_=p_, func=AF.Gelu, bias=b_),
                          waits=[mm], inc=True)
                self.slA.consumed(islot, mg)
            WRing.free(wfh, P.now("t"))
        hh_ready = P.now("s")
        for dt in range(NT):
            slab, w_ms, wfh = self.wring.load(
                self.dram["w_moe2"][ex].rearrange(
                    "(k p) d -> p k d", p=128)
                [:, fh2 * 16:(fh2 + 1) * 16, dt * 128:(dt + 1) * 128],
                2048)
            for cg in range(NCH):
                islot, ps, fz = self.slA.next()
                mm = None
                for kt in range(16):
                    mm = P.op("t", lambda e, k_=kt, p_=ps, c_=cg, w_=slab:
                              e.matmul(
                                  p_, w_[:, k_ * 128: k_ * 128 + 128],
                                  self.HHARE[:, k_ * 2048 + c_ * 512:
                                             k_ * 2048 + (c_ + 1) * 512],
                                  start=(k_ == 0), stop=(k_ == 15)),
                              waits=([w_ms, fz, hh_ready]
                                     if kt == 0 else ()),
                              inc=(kt == 15))
                sl = self._sidx % 4
                mev = P.op("v", lambda e, p_=ps, s_=smf_slots[sl], c_=cg:
                           e.tensor_mul(out=s_, in0=p_,
                                        in1=self.BB[:, c_ * 512:
                                                    (c_ + 1) * 512]),
                           waits=[mm, smf_free[sl], wbc_ms], inc=True)
                self.slA.consumed(islot, mev)
                md = P.dma("g", self.xs(dt, cg * 512, (cg + 1) * 512),
                           smf_slots[sl],
                           waits=[mev, self._region_ms.get((dt, cg))],
                           accum=True)
                smf_free[sl] = md
                self._region_ms[(dt, cg)] = md
                self._sidx += 1
            WRing.free(wfh, P.now("t"))
        self._hh_free = P.now("t")
        self._wbc_read = P.now("v")



_NCORES = 8

def _prep_inputs(inputs):
    f = np.ascontiguousarray
    d = {}
    d["w_zh"] = f(np.stack([inputs["L0_attn_Wz"], inputs["L0_attn_Wh"]]))
    d["p_bz"] = _tile_vec(inputs["L0_attn_bz"], 8)
    d["p_bh"] = _tile_vec(inputs["L0_attn_bh"], 8)
    d["w_l0qkv"] = f(inputs["L0_ffn_Wqkv"])
    d["w_l0o"] = f(inputs["L0_ffn_Wo"])
    w1 = inputs["L0_ffn_W1"]
    w1p = np.zeros((6, D, 5632), dtype=np.float32)
    w1p[:, :, 0:2730] = w1[:, :, 0:2730]
    w1p[:, :, 2816:2816 + 2730] = w1[:, :, 2730:5460]
    d["w_l0w1p"] = w1p
    w2 = inputs["L0_ffn_W2"]
    w2p = np.zeros((6, 2816, D), dtype=np.float32)
    w2p[:, 0:2730, :] = w2
    d["w_l0w2p"] = w2p
    qs, ks = inputs["L0_ffn_q_scale"], inputs["L0_ffn_k_scale"]
    d["p_l0qs"] = f(np.concatenate([qs.T, qs.T], 0).astype(np.float32))
    d["p_l0ks"] = f(np.concatenate([ks.T, ks.T], 0).astype(np.float32))
    d["p_l0g"] = np.concatenate(
        [_tile_vec(inputs["L0_ffn_ln_g"][i], 8) for i in range(6)], 1)
    d["p_l0b"] = np.concatenate(
        [_tile_vec(inputs["L0_ffn_ln_b"][i], 8) for i in range(6)], 1)
    d["p_l0b1"] = np.concatenate(
        [np.concatenate([_tile_vec(inputs["L0_ffn_b1"][i, :2730], 22),
                         _tile_vec(inputs["L0_ffn_b1"][i, 2730:], 22)], 1)
         for i in range(6)], 1)
    d["p_l0b2"] = np.concatenate(
        [_tile_vec(inputs["L0_ffn_b2"][i], 8) for i in range(6)], 1)
    d["w_1qa"] = f(inputs["L1_attn_Wqa"])
    d["w_1qb"] = f(inputs["L1_attn_Wqb"][:, _perm_wqb()])
    d["w_1kva"] = f(inputs["L1_attn_Wkva"])
    kvb = inputs["L1_attn_Wkvb"]
    k2 = np.zeros((32, 512), dtype=np.float32)
    vv = np.zeros((32, 256), dtype=np.float32)
    for g in range(4):
        kcols = [g * 128 + 4 * r if r < 32 else g * 128 + 4 * (r - 32) + 2
                 for r in range(64)]
        for r in range(64):
            k2[:, g * 128 + r] = kvb[:, kcols[r]]
            k2[:, g * 128 + 64 + r] = kvb[:, kcols[r]]
            vv[:, g * 64 + r] = kvb[:, g * 128 + 2 * r + 1]
    d["w_1kvb_k2"] = k2
    d["w_1kvb_v"] = vv
    d["w_1o"] = f(inputs["L1_attn_Wout"])
    d["p_bout"] = _tile_vec(inputs["L1_attn_bout"], 8)
    d["w_moe1"] = f(inputs["L1_ffn_W1"])
    d["w_moe2"] = f(inputs["L1_ffn_W2"])
    d["p_moeb1"] = np.concatenate(
        [_tile_vec(inputs["L1_ffn_b1"][e], 32) for e in range(8)], 1)
    d["p_moeb2t"] = f(inputs["L1_ffn_b2"].astype(np.float32))
    d["w_g"] = f(inputs["L1_ffn_Wg"])
    bg = np.zeros((128, 1), dtype=np.float32)
    bg[0:8, 0] = inputs["L1_ffn_bg"]
    d["p_bg"] = bg
    cst, tbl = _consts()
    for k, v in {**cst, **tbl}.items():
        d[k] = f(v)
    return d

_PHASES = ("A", "B", "C", "D")

def kernel(**inputs):
    mk = MK(phases=_PHASES)
    nc = mk.build()
    shared = _prep_inputs(inputs)
    x = np.ascontiguousarray(inputs["L0_attn_x"].astype(np.float32))
    in_maps = []
    for c in range(_NCORES):
        m = dict(shared)
        m["x_in"] = x[c]
        in_maps.append(m)
    res = run_bass_kernel_spmd(nc, in_maps, core_ids=list(range(_NCORES)))
    return np.stack([res.results[c]["x_out"] for c in range(_NCORES)])


# version 12 (final)
# speedup vs baseline: 1.0917x; 1.0124x over previous
import numpy as np
import ml_dtypes
import concourse.bass as bass
import concourse.mybir as mybir
from concourse.bass_utils import run_bass_kernel_spmd

F32 = mybir.dt.float32
BF16 = mybir.dt.bfloat16
U8 = mybir.dt.uint8
AF = mybir.ActivationFunctionType
ALU = mybir.AluOpType
AX = mybir.AxisListType

D = 1024
S = 2048
NT = 8
NCH = 4
NEG = -1e30


def _rope_tables_l0():
    half = np.arange(0, 64, 2, dtype=np.float64)
    inv_freq = 1.0 / 10000.0 ** (half / 64.0)
    t = np.arange(S, dtype=np.float64)
    freqs = t[:, None] * inv_freq[None, :]
    base_scale = (half + 0.4 * 64) / (1.4 * 64)
    power = (t - S // 2) / (256 // 2)
    scale = base_scale[None, :] ** power[:, None]
    cos, sin = np.cos(freqs), np.sin(freqs)
    def tile128(m):
        r = np.concatenate([m.T, m.T], 0)
        return np.concatenate([r, r], 0)
    return (tile128(cos * scale), tile128(sin * scale),
            tile128(1.0 / (scale * scale)))

def _rope_tables_l1():
    half = np.arange(0, 64, 2, dtype=np.float64)
    inv_freq = 1.0 / 10000.0 ** (half / 64.0)
    t = np.arange(S, dtype=np.float64)
    freqs = t[:, None] * inv_freq[None, :]
    def tile128(m):
        r = np.concatenate([m.T, m.T], 0)
        return np.concatenate([r, r], 0)
    return tile128(np.cos(freqs)), tile128(np.sin(freqs))

def _local_masks():
    r = np.arange(128)[:, None]
    c = np.arange(512)[None, :]
    mk = []
    for off in (0, 128):
        ok = (c >= r + off) & (c <= r + off + 256)
        mk.append(np.where(ok, 0.0, NEG))
    return mk

def _l1_diag_mask():
    r = np.arange(128)[:, None]
    c = np.arange(128)[None, :]
    return np.where(c <= r, 0.0, NEG)

def _rt128():
    R = np.zeros((64, 64))
    R[0:32, 32:64] = -np.eye(32)
    R[32:64, 0:32] = np.eye(32)
    M = np.zeros((128, 128))
    M[0:64, 0:64] = R.T
    M[64:128, 64:128] = R.T
    return M

def _bd64():
    M = np.zeros((128, 128))
    M[0:64, 0:64] = 1.0
    M[64:128, 64:128] = 1.0
    return M

def _perm_wqb():
    perm = np.zeros(1024, dtype=np.int64)
    for h in range(16):
        for r in range(64):
            d = 2 * r if r < 32 else 2 * (r - 32) + 1
            perm[h * 64 + r] = h * 64 + d
    return perm

def _perm_wkvb():
    perm = np.zeros(512, dtype=np.int64)
    for g in range(4):
        for r in range(128):
            if r < 32:
                c = 4 * r
            elif r < 64:
                c = 4 * (r - 32) + 2
            else:
                c = 2 * (r - 64) + 1
            perm[g * 128 + r] = g * 128 + c
    return perm

def _consts():
    mke, mko = _local_masks()
    sel8 = np.zeros((8, 8 * 128))
    for e in range(8):
        sel8[e, e * 128:(e + 1) * 128] = 1.0
    scal = np.zeros((128, 8))
    scal[:, 0] = 1e-5; scal[:, 1] = 1.0; scal[:, 2] = -1.0
    scal[:, 3] = -0.5; scal[:, 4] = 8.0; scal[:, 5] = 1.0 / 1024.0
    d = {
        "c_scal": scal,
        "c_identf": np.eye(128), "c_identb": np.eye(128),
        "c_ones": np.ones((128, 128)), "c_bd64": _bd64(), "c_rt128": _rt128(),
        "c_mke": mke, "c_mko": mko, "c_diag": _l1_diag_mask(), "c_sel8": sel8,
    }
    cs, sn, i2 = _rope_tables_l0()
    c1, s1 = _rope_tables_l1()
    tbl = {"t_cs": cs, "t_sn": sn, "t_i2": i2, "t_r1c": c1, "t_r1s": s1}
    return ({k: v.astype(np.float32) for k, v in d.items()},
            {k: v.astype(np.float32) for k, v in tbl.items()})

_CONST_F32 = {"c_identf", "c_scal"}

def _tile_vec(v, ntile):
    out = np.zeros((128, ntile), dtype=np.float32)
    for t in range(ntile):
        seg = v[t * 128:(t + 1) * 128]
        out[:seg.shape[0], t] = seg
    return out


ENGS = ("t", "v", "s", "g", "y")

NDSEM = 12

class Prog:
    def __init__(self):
        self.nc = bass.Bass("TRN2", target_bir_lowering=False)
        self.ops = {e: [] for e in ENGS}
        self.cnt = {e: 0 for e in ENGS}
        self.dcnt = {"g": 0, "y": 0}

    def op(self, eng, fn, waits=(), inc=False):
        waits = tuple(w for w in waits if w is not None)
        self.ops[eng].append(("op", fn, waits, inc))
        if inc:
            self.cnt[eng] += 1
            return (eng, self.cnt[eng])
        return None

    def dma(self, queue, out, in_, waits=(), accum=False):
        waits = tuple(w for w in waits if w is not None)
        idx = self.dcnt[queue]
        slot = idx % NDSEM
        rnd = idx // NDSEM
        if rnd > 0:
            waits = waits + ((queue + "d%d" % slot, rnd * 16),)
        self.ops[queue].append(("dma", (out, in_, accum, slot), waits, True))
        self.dcnt[queue] += 1
        return (queue + "d%d" % slot, (rnd + 1) * 16)

    def barrier(self):
        snap = [(e, self.cnt[e]) for e in ENGS if self.cnt[e] > 0]
        for q in ("g", "y"):
            n = self.dcnt[q]
            for slot in range(min(n, NDSEM)):
                cnt = n // NDSEM + (1 if slot < n % NDSEM else 0)
                snap.append((q + "d%d" % slot, cnt * 16))
        for e in ENGS:
            ws = tuple(snap)
            if ws:
                self.ops[e].append(("wait", None, ws, False))

    def now(self, eng):
        return (eng, self.cnt[eng])

    @staticmethod
    def sync(ms):
        return (ms[0], ms[1], "f") if ms is not None else None

    def emit(self):
        nc = self.nc
        self.sems = {}
        for e in ENGS:
            self.sems[e] = nc.semaphore("sem_" + e).__enter__()
        for q in ("g", "y"):
            for slot in range(NDSEM):
                key = q + "d%d" % slot
                self.sems[key] = nc.semaphore("sem_" + key).__enter__()
        lastw = {e: {} for e in ENGS}

        def run(engobj, ename):
            lw = lastw[ename]
            for kind, fn, waits, inc in self.ops[ename]:
                for ms in waits:
                    sk, val = ms[0], ms[1]
                    if lw.get(sk, -1) >= val:
                        continue
                    engobj.wait_ge(self.sems[sk], val)
                    lw[sk] = val
                if kind == "op":
                    inst = fn(engobj)
                    if inc:
                        inst.then_inc(self.sems[ename], 1)
                elif kind == "dma":
                    out, in_, accum, slot = fn
                    key = ename + "d%d" % slot
                    if accum:
                        engobj.dma_start(out=out, in_=in_,
                                         accum_op=ALU.add).then_inc(
                            self.sems[key], 16)
                    else:
                        engobj.dma_start(out=out, in_=in_).then_inc(
                            self.sems[key], 16)

        with nc.Block() as block:
            block.tensor(lambda e: run(e, "t"))
            block.vector(lambda e: run(e, "v"))
            block.scalar(lambda e: run(e, "s"))
            block.gpsimd(lambda e: run(e, "g"))
            block.sync(lambda e: run(e, "y"))


class PSlots:
    def __init__(self, aps):
        self.aps = aps
        self.free_ms = [None] * len(aps)
        self.i = 0

    def next(self):
        i = self.i
        self.i = (self.i + 1) % len(self.aps)
        return i, self.aps[i], self.free_ms[i]

    def consumed(self, i, ms):
        self.free_ms[i] = ms


class WRing:
    def __init__(self, P, ware, cols):
        self.P, self.ware, self.cols = P, ware, cols
        self.head = 0
        self.live = []

    def load(self, dram_ap, ncols, queue="g"):
        if self.head + ncols > self.cols:
            self.head = 0
        start, end = self.head, self.head + ncols
        waits, keep = [], []
        for ent in self.live:
            if ent[0] < end and ent[1] > start:
                waits.append(ent[2][0])
            else:
                keep.append(ent)
        self.live = keep
        ap = self.ware[:, start:end]
        ms = self.P.dma(queue, ap, dram_ap, waits=waits)
        fh = [None]
        self.live.append((start, end, fh))
        self.head = end
        return ap, ms, fh

    @staticmethod
    def free(fh, ms):
        fh[0] = ms



class MK:
    def __init__(self, phases=("A", "B", "C", "D"), l0_layers=6, dbg=None):
        self.P = Prog()
        self.phases = phases
        self.l0_layers = l0_layers
        self.dbg = dbg
        nc = self.P.nc
        self.dram = {}

        def din(name, shape):
            self.dram[name] = nc.dram_tensor(name, list(shape), F32,
                                             kind="ExternalInput")
        din("x_in", (S, D))
        din("w_zh", (2, D, D))
        din("w_l0qkv", (6, D, 3072)); din("w_l0o", (6, D, D))
        din("w_l0w1p", (6, D, 5632)); din("w_l0w2p", (6, 2816, D))
        din("w_1qa", (D, 64)); din("w_1qb", (64, D))
        din("w_1kva", (D, 32))
        din("w_1kvb_k2", (32, 512)); din("w_1kvb_v", (32, 256))
        din("w_1o", (D, D))
        din("w_moe1", (8, D, 4096)); din("w_moe2", (8, 4096, D))
        din("w_g", (D, 8))
        self.pshapes = {
            "p_bz": (128, 8), "p_bh": (128, 8), "p_l0qs": (128, 6),
            "p_l0ks": (128, 6), "p_l0g": (128, 48), "p_l0b": (128, 48),
            "p_l0b1": (128, 44 * 6), "p_l0b2": (128, 48),
            "p_bout": (128, 8), "p_moeb1": (128, 256),
            "p_bg": (128, 1)}
        for pn, shp in self.pshapes.items():
            din(pn, shp)
        din("p_moeb2t", (8, 1024))
        cst, tbl = _consts()
        self._cst_np, self._tbl_np = cst, tbl
        for cn, cv in cst.items():
            din(cn, cv.shape)
        for tn, tv in tbl.items():
            din(tn, tv.shape)
        self.dram["x_out"] = nc.dram_tensor("x_out", [S, D], F32,
                                            kind="ExternalOutput")
        self.dram["xstate"] = nc.dram_tensor("xstate", [128, NT * S], F32,
                                             kind="Internal")
        self._alloc_sbuf()

    def _alloc_sbuf(self):
        nc = self.P.nc
        def sb(name, shape, dt):
            return nc.alloc_sbuf_tensor(name, list(shape), dt)
        self.XN = sb("XN", [128, NT * S], BF16)
        self.OUT = sb("OUT", [128, NT * S], BF16)
        self.Y = sb("Y", [128, NT * S], F32)
        self.WARE = sb("WARE", [128, 6 * 1024], BF16)
        self.XTC = sb("XTC", [128, 12 * 1024], U8)
        self.PLC = sb("PLC", [128, 46 * 1024], U8)
        xt = nc.lookup_mloc(self.XTC).addr
        pl = nc.lookup_mloc(self.PLC).addr
        yb = nc.lookup_mloc(self.Y).addr

        def at(name, shape, dt, base, off):
            return nc.alloc_sbuf_tensor_at(name, list(shape), dt,
                                           offset=base + off)
        self.TA = [at("TA0", [128, 1024], F32, xt, 0),
                   at("TA1", [128, 1024], F32, xt, 4096),
                   at("TA2", [128, 1024], F32, xt, 8192)]
        self.XS0 = at("XS0", [128, S], F32, xt, 0)
        self.TCS = at("TCS", [128, S], BF16, xt, 0)
        self.TSN = at("TSN", [128, S], BF16, xt, 4096)
        self.TI2 = at("TI2", [128, S], BF16, xt, 8192)
        self.T1C = at("T1C", [128, S], BF16, xt, 0)
        self.T1S = at("T1S", [128, S], BF16, xt, 4096)
        self.MU = at("MU", [128, S], F32, pl, 0)
        self.RS = at("RS", [128, S], F32, pl, 8192)
        self.Q1 = at("Q1", [128, S], F32, pl, 16384)
        self.Q2 = at("Q2", [128, S], F32, pl, 24576)
        self.BA = at("BA", [128, S], BF16, pl, 32768)
        self.BB = at("BB", [128, S], BF16, pl, 36864)
        self.BC = at("BC", [128, S], BF16, pl, 40960)
        self.SMALL = at("SMALL", [128, 512], F32, pl, 45056)
        self.BD = at("BD", [128, S], BF16, pl, 16384)
        self.BE = at("BE", [128, S], BF16, pl, 20480)
        self.SMF = at("SMF", [128, S], F32, pl, 24576)
        self.QARE = at("QARE", [128, 16 * 1024], BF16, yb, 0)
        self.KARE = at("KARE", [128, 8 * 1024], BF16, yb, 32768)
        self.VNARE = at("VNARE", [128, 8 * 1024], BF16, yb, 49152)
        self.HHARE = at("HHARE", [128, 32 * 1024], BF16, yb, 0)
        cst, tbl = self._cst_np, self._tbl_np
        self.C = {}
        for cn, cv in cst.items():
            dt = F32 if cn in _CONST_F32 else BF16
            self.C[cn] = sb("sb_" + cn, list(cv.shape), dt)
        for pn, shp in self.pshapes.items():
            self.C[pn] = sb("sb_" + pn, list(shp), F32)
        self.C["p_nbz"] = sb("sb_p_nbz", [128, 8], F32)
        self.psA = nc.alloc_psum_tensor("psA", [128, 4 * 512], F32)
        self.psB = nc.alloc_psum_tensor("psB", [128, 1 * 512], F32)
        self.psT = nc.alloc_psum_tensor("psT", [128, 2048], BF16)
        self.psF = nc.alloc_psum_tensor("psF", [128, 512], F32)
        self.slA = PSlots([self.psA[:, i * 512:(i + 1) * 512]
                           for i in range(4)])
        self.slB = PSlots([self.psB[:, i * 512:(i + 1) * 512]
                           for i in range(1)])
        self.wring = WRing(self.P, self.WARE, 6 * 1024)

    def load_consts(self):
        P = self.P
        for cn in self._cst_np:
            q = "y" if cn in _CONST_F32 else "g"
            P.dma(q, self.C[cn][:, :], self.dram[cn][:, :])
        for pn in self.pshapes:
            P.dma("y", self.C[pn][:, :], self.dram[pn][:, :])
        P.barrier()
        P.op("v", lambda e: e.tensor_scalar(
            out=self.C["p_nbz"][:, :], in0=self.C["p_bz"][:, :],
            scalar1=-1.0, scalar2=None, op0=ALU.mult), inc=True)
        P.barrier()

    def load_l0_tables(self):
        P = self.P
        for tn, h in (("t_cs", self.TCS), ("t_sn", self.TSN),
                      ("t_i2", self.TI2)):
            P.dma("g", h[:, :], self.dram[tn][:, :])

    def load_l1_tables(self):
        P = self.P
        for tn, h in (("t_r1c", self.T1C), ("t_r1s", self.T1S)):
            P.dma("g", h[:, :], self.dram[tn][:, :])

    def xs(self, t, c0=0, c1=S):
        return self.dram["xstate"][:, t * S + c0: t * S + c1]

    def xs3d(self, c0, n):
        return self.dram["xstate"][:, :].rearrange(
            "p (t s) -> p t s", t=NT)[:, :, c0:c0 + n]

    def phase_A(self):
        P = self.P
        cI = self.C["c_identf"]
        X = self.Y
        prev = [None, None]
        for sb_ in range(16):
            half = sb_ % 2
            stg = self.TA[half]
            mload = P.dma("y", stg[:, :],
                          self.dram["x_in"][sb_ * 128:(sb_ + 1) * 128, :],
                          waits=[prev[half]])
            mt = None
            for t in range(NT):
                src = stg[:, t * 128:(t + 1) * 128]
                pb = self.psA[:, half * 1024 + t * 128:
                              half * 1024 + t * 128 + 128]
                mt = P.op("t", lambda e, s_=src, p_=pb: e.transpose(
                    p_, s_, cI[:, :]),
                    waits=[mload, prev[half]], inc=(t == NT - 1))
            mv = P.op("v", lambda e, h_=half, s_=sb_: e.tensor_copy(
                out=X[:, :].rearrange("p (t s) -> p t s", t=NT)
                    [:, :, s_ * 128:s_ * 128 + 128],
                in_=self.psA[:, h_ * 1024:(h_ + 1) * 1024]
                    .rearrange("p (t s) -> p t s", s=128)),
                waits=[mt], inc=True)
            prev[half] = mv
        P.barrier()
        self.emit_ln(src_sbuf=X, dst=self.XN)
        P.barrier()
        self.emit_mingru(X)
        for t in range(NT):
            P.dma("y", self.xs(t), X[:, t * S:(t + 1) * S])
        P.barrier()

    def phase_E(self):
        P = self.P
        cI = self.C["c_identf"]
        prev = [None, None]
        prev_out = None
        for sb_ in range(16):
            half = sb_ % 2
            stg = self.TA[half]
            mload = P.dma("y", stg[:, :].rearrange("p (t s) -> p t s", s=128),
                          self.xs3d(sb_ * 128, 128), waits=[prev[half]])
            mt = None
            for t in range(NT):
                src = stg[:, t * 128:(t + 1) * 128]
                pb = self.psA[:, half * 1024 + t * 128:
                              half * 1024 + t * 128 + 128]
                mt = P.op("t", lambda e, s_=src, p_=pb: e.transpose(
                    p_, s_, cI[:, :]),
                    waits=[mload, prev[half]], inc=(t == NT - 1))
            mv = P.op("v", lambda e, h_=half: e.tensor_copy(
                out=self.TA[2][:, :],
                in_=self.psA[:, h_ * 1024:(h_ + 1) * 1024]),
                waits=[mt, prev_out], inc=True)
            prev[half] = mv
            prev_out = P.dma("y",
                             self.dram["x_out"][sb_ * 128:(sb_ + 1) * 128, :],
                             self.TA[2][:, :], waits=[mv])
        P.barrier()

    def emit_ln(self, src_sbuf, dst, shift=False, gain=False, gcol=0):
        P = self.P
        ones = self.C["c_ones"]
        RAW = self.OUT
        raw_ms = []
        for t in range(NT):
            if src_sbuf is not None:
                if shift and t >= 4:
                    m1 = P.op("s", lambda e, t_=t: e.copy(
                        out=RAW[:, t_ * S + 1:(t_ + 1) * S],
                        in_=src_sbuf[:, t_ * S:(t_ + 1) * S - 1]), inc=True)
                    m2 = P.op("v", lambda e, t_=t: e.memset(
                        RAW[:, t_ * S: t_ * S + 1], 0.0), inc=True)
                    raw_ms.append((m1, m2))
                else:
                    m1 = P.op("s", lambda e, t_=t: e.copy(
                        out=RAW[:, t_ * S:(t_ + 1) * S],
                        in_=src_sbuf[:, t_ * S:(t_ + 1) * S]), inc=True)
                    raw_ms.append((m1,))
            else:
                m1 = P.dma("g", RAW[:, t * S:(t + 1) * S], self.xs(t))
                raw_ms.append((m1,))
        for t in range(NT):
            for c in range(NCH):
                P.op("t", lambda e, t_=t, c_=c: e.matmul(
                    self.psA[:, c_ * 512:(c_ + 1) * 512], ones[:, :],
                    RAW[:, t_ * S + c_ * 512: t_ * S + (c_ + 1) * 512],
                    start=(t_ == 0), stop=(t_ == NT - 1)),
                    waits=list(raw_ms[t]),
                    inc=(t == NT - 1 and c == NCH - 1))
        msum = P.now("t")
        mmu = None
        for c in range(NCH):
            mmu = P.op("v", lambda e, c_=c: e.tensor_scalar(
                out=self.MU[:, c_ * 512:(c_ + 1) * 512],
                in0=self.psA[:, c_ * 512:(c_ + 1) * 512],
                scalar1=1.0 / 1024.0, scalar2=None, op0=ALU.mult),
                waits=[msum], inc=(c == NCH - 1))
        sqfree = [None, None]
        for t in range(NT):
            buf = self.BA if t % 2 == 0 else self.BB
            m = P.op("s", lambda e, t_=t, b_=buf: e.activation(
                out=b_[:, :], in_=RAW[:, t_ * S:(t_ + 1) * S],
                func=AF.Square),
                waits=list(raw_ms[t]) + [sqfree[t % 2]], inc=True)
            last = None
            for c in range(NCH):
                last = P.op("t", lambda e, t_=t, c_=c, b_=buf: e.matmul(
                    self.psA[:, c_ * 512:(c_ + 1) * 512], ones[:, :],
                    b_[:, c_ * 512:(c_ + 1) * 512],
                    start=(t_ == 0), stop=(t_ == NT - 1)),
                    waits=[m] + ([mmu] if t == 0 else []),
                    inc=(c == NCH - 1))
            sqfree[t % 2] = last
        msq = P.now("t")
        mq = P.op("s", lambda e: e.activation(
            out=self.Q1[:, :], in_=self.MU[:, :], func=AF.Square),
            waits=[mmu], inc=True)
        mv = None
        for c in range(NCH):
            mv = P.op("v", lambda e, c_=c: e.scalar_tensor_tensor(
                out=self.RS[:, c_ * 512:(c_ + 1) * 512],
                in0=self.psA[:, c_ * 512:(c_ + 1) * 512],
                scalar=1.0 / 1024.0,
                in1=self.Q1[:, c_ * 512:(c_ + 1) * 512],
                op0=ALU.mult, op1=ALU.subtract),
                waits=[msq, mq], inc=(c == NCH - 1))
        m1 = P.op("s", lambda e: e.activation(
            out=self.RS[:, :], in_=self.RS[:, :], func=AF.Ln,
            bias=self.C["c_scal"][:, 0:1]),
            waits=[mv], inc=True)
        m2 = P.op("s", lambda e: e.activation(
            out=self.RS[:, :], in_=self.RS[:, :], func=AF.Exp, scale=-0.5),
            waits=[m1], inc=True)
        ap_ms = []
        xt_free = [None, None]
        prev_mo = None
        for t in range(NT):
            if src_sbuf is not None:
                sap, sw = src_sbuf[:, t * S:(t + 1) * S], None
            else:
                buf = self.XS0 if t % 2 == 0 else self.Q2
                sw = P.dma("g", buf[:, :], self.xs(t), waits=[xt_free[t % 2]])
                sap = buf[:, :]
            if shift and t >= 4:
                ma = P.op("v", lambda e, s_=sap: e.tensor_sub(
                    out=self.Q1[:, 1:S], in0=s_[:, 0:S - 1],
                    in1=self.MU[:, 1:S]), waits=[mmu, sw, mq, mv, prev_mo], inc=True)
                mb = P.op("v", lambda e: e.tensor_scalar(
                    out=self.Q1[:, 0:1], in0=self.MU[:, 0:1], scalar1=-1.0,
                    scalar2=None, op0=ALU.mult), waits=[mmu], inc=True)
                tw = [ma, Prog.sync(mb)]
            else:
                ma = P.op("v", lambda e, s_=sap: e.tensor_sub(
                    out=self.Q1[:, :], in0=s_, in1=self.MU[:, :]),
                    waits=[mmu, sw, mq, mv, prev_mo], inc=True)
                tw = [ma]
            if not gain:
                mo = P.op("v", lambda e, t_=t: e.tensor_mul(
                    out=dst[:, t_ * S:(t_ + 1) * S], in0=self.Q1[:, :],
                    in1=self.RS[:, :]), waits=tw + [m2], inc=True)
            else:
                mt2 = P.op("v", lambda e: e.tensor_mul(
                    out=self.Q1[:, :], in0=self.Q1[:, :], in1=self.RS[:, :]),
                    waits=tw + [m2], inc=True)
                g_ap = self.C["p_l0g"][:, gcol * 8 + t: gcol * 8 + t + 1]
                b_ap = self.C["p_l0b"][:, gcol * 8 + t: gcol * 8 + t + 1]
                mo = P.op("v", lambda e, t_=t, g_=g_ap, b_=b_ap: e.tensor_scalar(
                    out=dst[:, t_ * S:(t_ + 1) * S], in0=self.Q1[:, :],
                    scalar1=g_, scalar2=b_, op0=ALU.mult, op1=ALU.add),
                    waits=[mt2], inc=True)
            ap_ms.append(mo)
            prev_mo = mo
            if src_sbuf is None:
                xt_free[t % 2] = mo
        self._ln_done = ap_ms
        return ap_ms

    def emit_mingru(self, X):
        P = self.P
        q1_free = [None] * NCH
        mu_free = None
        for et in range(NT):
            wz_ap, wz_ms, wz_fh = self.wring.load(
                self.dram["w_zh"][0].rearrange("(k p) e -> p k e", p=128)
                [:, :, et * 128:(et + 1) * 128], 1024)
            wh_ap, wh_ms, wh_fh = self.wring.load(
                self.dram["w_zh"][1].rearrange("(k p) e -> p k e", p=128)
                [:, :, et * 128:(et + 1) * 128], 1024)
            bz = self.C["p_bz"][:, et:et + 1]
            bh = self.C["p_bh"][:, et:et + 1]
            nbz = self.C["p_nbz"][:, et:et + 1]
            zms = []
            for c in range(NCH):
                iz, pz, fz = self.slA.next()
                mmz = None
                for kt in range(NT):
                    mmz = P.op("t", lambda e, k_=kt, c_=c, p_=pz, w_=wz_ap:
                               e.matmul(p_, w_[:, k_ * 128:(k_ + 1) * 128],
                                        self.XN[:, k_ * S + c_ * 512:
                                                k_ * S + (c_ + 1) * 512],
                                        start=(k_ == 0), stop=(k_ == NT - 1)),
                               waits=([wz_ms, fz] + list(self._ln_done)
                                      if kt == 0 else []),
                               inc=(kt == NT - 1))
                ih, ph, fh2 = self.slB.next()
                mmh = None
                for kt in range(NT):
                    mmh = P.op("t", lambda e, k_=kt, c_=c, p_=ph, w_=wh_ap:
                               e.matmul(p_, w_[:, k_ * 128:(k_ + 1) * 128],
                                        self.XN[:, k_ * S + c_ * 512:
                                                k_ * S + (c_ + 1) * 512],
                                        start=(k_ == 0), stop=(k_ == NT - 1)),
                               waits=[wh_ms, fh2] if kt == 0 else [],
                               inc=(kt == NT - 1))
                mz = P.op("s", lambda e, c_=c, p_=pz, b_=bz: e.activation(
                    out=self.Q1[:, c_ * 512:(c_ + 1) * 512], in_=p_,
                    func=AF.Sigmoid, bias=b_),
                    waits=[mmz, q1_free[c]], inc=True)
                mb = P.op("v", lambda e, c_=c, p_=ph, b_=bh:
                          e.scalar_tensor_tensor(
                              out=self.Q2[:, c_ * 512:(c_ + 1) * 512],
                              in0=p_, scalar=b_,
                              in1=self.Q1[:, c_ * 512:(c_ + 1) * 512],
                              op0=ALU.add, op1=ALU.mult),
                          waits=[mmh, mz], inc=True)
                self.slB.consumed(ih, mb)
                q1_free[c] = mb
                ma = P.op("s", lambda e, c_=c, p_=pz, nb_=nbz: e.activation(
                    out=self.MU[:, c_ * 512:(c_ + 1) * 512],
                    in_=p_, func=AF.Sigmoid, scale=-1.0, bias=nb_),
                    waits=[mmz, mu_free], inc=True)
                self.slA.consumed(iz, ma)
                zms += [mb, ma]
            WRing.free(wz_fh, P.now("t"))
            WRing.free(wh_fh, P.now("t"))
            msc = P.op("v", lambda e: e.tensor_tensor_scan(
                out=self.RS[:, :], data0=self.MU[:, :], data1=self.Q2[:, :],
                initial=0.0, op0=ALU.mult, op1=ALU.add), waits=zms, inc=True)
            mu_free = msc
            P.op("v", lambda e, t_=et: e.tensor_add(
                out=X[:, t_ * S:(t_ + 1) * S],
                in0=X[:, t_ * S:(t_ + 1) * S], in1=self.RS[:, :]),
                waits=[msc], inc=True)
        P.barrier()

    def build(self):
        P = self.P
        self._ln_done = []
        self.load_consts()
        if "A" in self.phases:
            self.phase_A()
        if "B" in self.phases:
            self.phase_B()
        if "C" in self.phases:
            self.phase_C()
        if "D" in self.phases:
            self.phase_D()
        self.phase_E()
        P.emit()
        return P.nc

    def mm_proj_chunk(self, ps, wap, c, shift, w0, src=None, nk=NT):
        P = self.P
        src = self.XN if src is None else src
        last = None
        for kt in range(nk):
            sh = shift and kt >= nk // 2
            if not sh:
                rhs = src[:, kt * S + c * 512: kt * S + (c + 1) * 512]
                out = ps
            elif c > 0:
                rhs = src[:, kt * S + c * 512 - 1: kt * S + (c + 1) * 512 - 1]
                out = ps
            else:
                rhs = src[:, kt * S: kt * S + 511]
                out = ps[:, 1:512]
            last = P.op("t", lambda e, k_=kt, r_=rhs, o_=out, w_=wap:
                        e.matmul(o_, w_[:, k_ * 128:(k_ + 1) * 128], r_,
                                 start=(k_ == 0), stop=(k_ == nk - 1)),
                        waits=(w0 if kt == 0 else ()), inc=(kt == nk - 1))
        return last

    def _qk_pipeline(self, lay, wap, w_ms, w_fh, dst, scale_ap, use_i2,
                     cast_now):
        P = self.P
        mev_all = []
        mln_all = []
        for c in range(NCH):
            islot, ps, fz = self.slA.next()
            mm = self.mm_proj_chunk(ps, wap, c, True, [w_ms, fz, cast_now])
            mev = P.op("s", lambda e, c_=c, p_=ps: e.copy(
                out=dst[:, c_ * 512:(c_ + 1) * 512], in_=p_),
                waits=[mm, self._dst_free.get(id(dst))], inc=True)
            self.slA.consumed(islot, mev)
            mev_all.append(mev)
            msq = P.op("s", lambda e, c_=c: e.activation(
                out=self.BA[:, c_ * 512:(c_ + 1) * 512],
                in_=dst[:, c_ * 512:(c_ + 1) * 512], func=AF.Square),
                waits=[mev, self._ba_free, self._ba_read], inc=True)
            mss = P.op("t", lambda e, c_=c: e.matmul(
                self.psF[:, :], self.C["c_bd64"][:, :],
                self.BA[:, c_ * 512:(c_ + 1) * 512], start=True, stop=True),
                waits=[msq, self._psF_free], inc=True)
            self._ba_free = mss
            mln = P.op("s", lambda e, c_=c: e.activation(
                out=self.MU[:, c_ * 512:(c_ + 1) * 512], in_=self.psF[:, :],
                func=AF.Ln, bias=self.C["c_scal"][:, 6:7]),
                waits=[mss, self._mu_free], inc=True)
            self._psF_free = mln
            mln_all.append(mln)
        mex = P.op("s", lambda e: e.activation(
            out=self.MU[:, :], in_=self.MU[:, :], func=AF.Exp, scale=-0.5),
            waits=mln_all, inc=True)
        msc = P.op("s", lambda e, s_=scale_ap: e.mul(
            out=self.MU[:, :], in_=self.MU[:, :], mul=s_),
            waits=[mex], inc=True)
        mqn = P.op("v", lambda e: e.tensor_mul(
            out=dst[:, :], in0=dst[:, :], in1=self.MU[:, :]),
            waits=[msc] + mev_all, inc=True)
        self._mu_free = mqn
        msm_all = []
        mro = None
        for c in range(NCH):
            islot, ps, fz = self.slB.next()
            mro = P.op("t", lambda e, c_=c, p_=ps: e.matmul(
                p_, self.C["c_rt128"][:, :],
                dst[:, c_ * 512:(c_ + 1) * 512], start=True, stop=True),
                waits=[mqn, fz], inc=True)
            msm = P.op("v", lambda e, c_=c, p_=ps: e.tensor_mul(
                out=self.SMF[:, c_ * 512:(c_ + 1) * 512], in0=p_,
                in1=self.TSN[:, c_ * 512:(c_ + 1) * 512]),
                waits=[mro] + self._sm_free[0:4], inc=True)
            self.slB.consumed(islot, msm)
            msm_all.append(msm)
        mtc = P.op("v", lambda e: e.tensor_mul(
            out=dst[:, :], in0=dst[:, :], in1=self.TCS[:, :]),
            waits=[mro], inc=True)
        mad = P.op("v", lambda e: e.tensor_add(
            out=dst[:, :], in0=dst[:, :], in1=self.SMF[:, :]),
            waits=[mtc] + msm_all, inc=True)
        self._smf_read = mad
        if use_i2:
            mad = P.op("v", lambda e: e.tensor_mul(
                out=dst[:, :], in0=dst[:, :], in1=self.TI2[:, :]),
                waits=[mad], inc=True)
        WRing.free(w_fh, P.now("t"))
        return mad

    def l0_attn_hg(self, lay, hg, cast_now):
        P = self.P
        def colpan(sel, c0):
            return self.dram["w_l0qkv"][lay].rearrange(
                "(k p) e -> p k e", p=128)[:, :, c0 + hg * 128:
                                           c0 + hg * 128 + 128]
        wq, wq_ms, wq_fh = self.wring.load(colpan(0, 0), 1024)
        wk, wk_ms, wk_fh = self.wring.load(colpan(1, 1024), 1024)
        wv, wv_ms, wv_fh = self.wring.load(colpan(2, 2048), 1024)
        vt_ms = []
        for c in range(NCH):
            islot, ps, fz = self.slA.next()
            mm = self.mm_proj_chunk(ps, wv, c, True, [wv_ms, fz, cast_now])
            mev = P.op("s", lambda e, c_=c, p_=ps: e.copy(
                out=self.BC[:, c_ * 512:(c_ + 1) * 512], in_=p_),
                waits=[mm, self._bc_free], inc=True)
            self.slA.consumed(islot, mev)
            vt_ms.append(mev)
        WRing.free(wv_fh, P.now("t"))
        qs_ap = self.C["p_l0qs"][:, lay:lay + 1]
        ks_ap = self.C["p_l0ks"][:, lay:lay + 1]
        mq = self._qk_pipeline(lay, wq, wq_ms, wq_fh, self.BD, qs_ap,
                               False, cast_now)
        mk = self._qk_pipeline(lay, wk, wk_ms, wk_fh, self.BE, ks_ap,
                               True, cast_now)
        vn_ms = []
        for hh in (0, 1):
            mtr = None
            for jb in range(16):
                mtr = P.op("t", lambda e, h_=hh, j_=jb: e.transpose(
                    self.psT[:, j_ * 64:(j_ + 1) * 64],
                    self.BC[h_ * 64:h_ * 64 + 64, j_ * 128:(j_ + 1) * 128],
                    self.C["c_identb"][h_ * 64:h_ * 64 + 64,
                                       h_ * 64:h_ * 64 + 64]),
                    waits=vt_ms + [self._psT_free[0], self._pt_read[0]],
                    inc=(jb == 15))
            mvn = P.op("v", lambda e, h_=hh: e.tensor_copy(
                out=self.BA[:, :].rearrange("p (b c) -> p b c", b=16)
                    [:, :, h_ * 64:h_ * 64 + 64],
                in_=self.psT[:, 0:1024].rearrange("p (b c) -> p b c", c=64)),
                waits=[mtr, self._ba_free], inc=True)
            self._psT_free[0] = mvn
            vn_ms.append(mvn)
        if self.dbg == "qk0" and hg == 0:
            P.barrier()
            for src_, ti in ((self.BD, 0), (self.BE, 1), (self.BC, 2),
                             (self.BA, 3)):
                P.op("v", lambda e, s_=src_, t_=ti: e.tensor_copy(
                    out=self.Y[:, t_ * S:(t_ + 1) * S], in_=s_[:, :]),
                    inc=True)
            self._dump_y()
            self._dbg_stop = True
            return []
        out_ms = []
        units = [(hh, w) for hh in (0, 1) for w in range(8)]
        pend = []
        for ui, (hh, w) in enumerate(units):
            pend.append(self._l0_attn_s(hg, hh, w, ui, [mq, mk]))
            if len(pend) > 2:
                out_ms.append(self._l0_attn_av(vn_ms, pend.pop(0)))
        while pend:
            out_ms.append(self._l0_attn_av(vn_ms, pend.pop(0)))
        tnow = P.now("t")
        self._ba_read = tnow
        self._bc_free = tnow
        self._dst_free[id(self.BD)] = tnow
        self._dst_free[id(self.BE)] = tnow
        return out_ms

    def _l0_attn_s(self, hg, hh, w, ui, qk_ms):
        P = self.P
        par = ui % 2
        hr = 64 * hh
        width = 512 if w > 0 else 256
        j0 = (w * 256 - 256) if w > 0 else 0
        mpps = []
        p0s = []
        for ib in (0, 1):
            sl = par * 2 + ib
            i0 = w * 256 + ib * 128
            mask = self.C["c_mke"] if ib == 0 else self.C["c_mko"]
            mask_ap = mask[:, 0:width] if w > 0 else mask[:, 256:512]
            islot, ps, fz = self.slA.next()
            msmm = P.op("t", lambda e, p_=ps, i0_=i0, hr_=hr, j0_=j0,
                        wd_=width: e.matmul(
                p_[:, 0:wd_],
                self.BD[hr_:hr_ + 64, i0_:i0_ + 128],
                self.BE[hr_:hr_ + 64, j0_:j0_ + wd_],
                start=True, stop=True),
                waits=qk_ms + [fz], inc=True)
            msm = P.op("v", lambda e, p_=ps, m_=mask_ap, s_=sl,
                       wd_=width: e.scalar_tensor_tensor(
                           out=self.SMF[:, s_ * 512: s_ * 512 + wd_],
                           in0=p_[:, 0:wd_], scalar=8.0, in1=m_,
                           op0=ALU.mult, op1=ALU.add),
                       waits=[msmm, self._sm_free[sl]], inc=True)
            self.slA.consumed(islot, msm)
            mnm = P.op("v", lambda e, s_=sl, wd_=width: e.reduce_max(
                out=self.SMALL[:, s_:s_ + 1],
                in_=self.SMF[:, s_ * 512: s_ * 512 + wd_],
                axis=AX.X, negate=True),
                waits=[msm, self._stat_free[sl]], inc=True)
            p0 = self.BB[:, sl * 512: sl * 512 + width]
            mz0 = P.op("v", lambda e, s_=sl: e.memset(
                self.SMALL[:, 4 + s_:5 + s_], 0.0),
                waits=[self._stat_free[sl]], inc=True)
            mex = P.op("s", lambda e, p0_=p0, s_=sl, wd_=width:
                       e.activation(
                out=p0_, in_=self.SMF[:, s_ * 512: s_ * 512 + wd_],
                func=AF.Exp, bias=self.SMALL[:, s_:s_ + 1],
                accum_out=self.SMALL[:, 4 + s_:5 + s_]),
                waits=[mnm, self._p0_free[sl], mz0], inc=True)
            self._sm_free[sl] = mex
            miz = P.op("v", lambda e, s_=sl: e.reciprocal(
                out=self.SMALL[:, 8 + s_:9 + s_],
                in_=self.SMALL[:, 4 + s_:5 + s_]), waits=[mex], inc=True)
            mpp = P.op("v", lambda e, p0_=p0, s_=sl: e.tensor_scalar(
                out=p0_, in0=p0_, scalar1=self.SMALL[:, 8 + s_:9 + s_],
                scalar2=None, op0=ALU.mult), waits=[Prog.sync(miz)],
                inc=True)
            self._stat_free[sl] = mpp
            mpps.append(mpp)
            p0s.append(p0)
        return (hg, hh, w, ui, mpps, p0s)

    def _l0_attn_av(self, vn_ms, ctx):
        P = self.P
        hg, hh, w, ui, mpps, p0s = ctx
        par = ui % 2
        hr = 64 * hh
        njb = 4 if w > 0 else 2
        tbase = par * 1024
        tr_last = None
        for ib in (0, 1):
            p0 = p0s[ib]
            for jb in range(njb):
                jbg = jb + (2 if w == 0 else 0)
                tr_last = P.op("t", lambda e, p0_=p0, j_=jb, jg_=jbg,
                               ib_=ib, tb_=tbase: e.transpose(
                    self.psT[:, tb_ + jg_ * 256 + ib_ * 128:
                             tb_ + jg_ * 256 + ib_ * 128 + 128],
                    p0_[:, j_ * 128:(j_ + 1) * 128],
                    self.C["c_identb"][:, :]),
                    waits=[mpps[ib], self._psT_free[par]],
                    inc=(ib == 1 and jb == njb - 1))
        mpt = P.op("s", lambda e, tb_=tbase, pr_=par: e.copy(
            out=self.BC[:, pr_ * 1024: pr_ * 1024 + 1024],
            in_=self.psT[:, tb_: tb_ + 1024]),
            waits=[tr_last, self._pt_read[par]], inc=True)
        self._psT_free[par] = mpt
        for ib in (0, 1):
            self._p0_free[par * 2 + ib] = mpt
        jlist = list(range(2, 4)) if w == 0 else list(range(4))
        mav = None
        for n_, jbg in enumerate(jlist):
            blk = 2 * w - 2 + jbg
            mav = P.op("t", lambda e, j_=jbg, b_=blk, pr_=par,
                       st_=(n_ == 0), sp_=(n_ == len(jlist) - 1): e.matmul(
                self.psF[:, 0:256],
                self.BA[:, b_ * 128: b_ * 128 + 128],
                self.BC[:, pr_ * 1024 + j_ * 256:
                        pr_ * 1024 + j_ * 256 + 256],
                start=st_, stop=sp_),
                waits=[mpt] + vn_ms + [self._psF_free],
                inc=(n_ == len(jlist) - 1))
        self._pt_read[par] = mav
        mo = P.op("s", lambda e, hr_=hr, hg_=hg, w_=w: e.copy(
            out=self.OUT[hr_:hr_ + 64,
                         hg_ * S + w_ * 256: hg_ * S + w_ * 256 + 256],
            in_=self.psF[hr_:hr_ + 64, 0:256]), waits=[mav], inc=True)
        self._psF_free = mo
        return mo

    def l0_wo(self, lay):
        P = self.P
        for dt in range(NT):
            slab, w_ms, fh = self.wring.load(
                self.dram["w_l0o"][lay].rearrange("(k p) e -> p k e", p=128)
                [:, :, dt * 128:(dt + 1) * 128], 1024)
            for c in range(NCH):
                islot, ps, fz = self.slA.next()
                mm = self.mm_proj_chunk(ps, slab, c, False, [w_ms, fz],
                                        src=self.OUT)
                mev = P.op("v", lambda e, p_=ps, d_=dt, c_=c: e.tensor_add(
                    out=self.Y[:, d_ * S + c_ * 512: d_ * S + (c_ + 1) * 512],
                    in0=p_,
                    in1=self.Y[:, d_ * S + c_ * 512: d_ * S + (c_ + 1) * 512]),
                    waits=[mm], inc=True)
                self.slA.consumed(islot, mev)
            WRing.free(fh, P.now("t"))

    def l0_ffn(self, lay):
        P = self.P
        qsplit = [(0, 6), (6, 12), (12, 18), (18, 22)]
        g_free = [None] * NCH
        hh_free = None
        for qr, (f0, f1) in enumerate(qsplit):
            for fa in range(f0, f1):
                wa, wa_ms, wa_fh = self.wring.load(
                    self.dram["w_l0w1p"][lay].rearrange(
                        "(k p) e -> p k e", p=128)
                    [:, :, fa * 128:(fa + 1) * 128], 1024)
                wg, wg_ms, wg_fh = self.wring.load(
                    self.dram["w_l0w1p"][lay].rearrange(
                        "(k p) e -> p k e", p=128)
                    [:, :, 2816 + fa * 128: 2816 + (fa + 1) * 128], 1024)
                b1a = self.C["p_l0b1"][:, lay * 44 + fa: lay * 44 + fa + 1]
                b1g = self.C["p_l0b1"][:, lay * 44 + 22 + fa:
                                       lay * 44 + 22 + fa + 1]
                hhofs = (fa - f0) * S
                for c in range(NCH):
                    ia, pa, fza = self.slA.next()
                    mma = self.mm_proj_chunk(pa, wa, c, False, [wa_ms, fza])
                    ig, pg, fzg = self.slB.next()
                    mmg = self.mm_proj_chunk(pg, wg, c, False, [wg_ms, fzg])
                    mg = P.op("s", lambda e, p_=pg, c_=c, b_=b1g:
                              e.activation(
                                  out=self.BA[:, c_ * 512:(c_ + 1) * 512],
                                  in_=p_, func=AF.Gelu, bias=b_),
                              waits=[mmg, g_free[c]], inc=True)
                    self.slB.consumed(ig, mg)
                    mh = P.op("v", lambda e, p_=pa, c_=c, b_=b1a, o_=hhofs:
                              e.scalar_tensor_tensor(
                                  out=self.OUT[:, o_ + c_ * 512:
                                               o_ + (c_ + 1) * 512],
                                  in0=p_, scalar=b_,
                                  in1=self.BA[:, c_ * 512:(c_ + 1) * 512],
                                  op0=ALU.add, op1=ALU.mult),
                              waits=[mma, mg, hh_free], inc=True)
                    self.slA.consumed(ia, mh)
                    g_free[c] = mh
                WRing.free(wa_fh, P.now("t"))
                WRing.free(wg_fh, P.now("t"))
            hh_ready = P.now("v")
            nk = f1 - f0
            for dt in range(NT):
                slab, w_ms, fh = self.wring.load(
                    self.dram["w_l0w2p"][lay].rearrange(
                        "(k p) d -> p k d", p=128)
                    [:, f0:f1, dt * 128:(dt + 1) * 128], nk * 128)
                b2 = self.C["p_l0b2"][:, lay * 8 + dt: lay * 8 + dt + 1]
                for c in range(NCH):
                    islot, ps, fz = self.slA.next()
                    mm = self.mm_proj_chunk(ps, slab, c, False,
                                            [w_ms, fz, hh_ready],
                                            src=self.OUT, nk=nk)
                    if qr == 0:
                        mev = P.op("v", lambda e, p_=ps, d_=dt, c_=c, b_=b2:
                                   e.scalar_tensor_tensor(
                                       out=self.Y[:, d_ * S + c_ * 512:
                                                  d_ * S + (c_ + 1) * 512],
                                       in0=p_, scalar=b_,
                                       in1=self.Y[:, d_ * S + c_ * 512:
                                                  d_ * S + (c_ + 1) * 512],
                                       op0=ALU.add, op1=ALU.add),
                                   waits=[mm], inc=True)
                    else:
                        mev = P.op("v", lambda e, p_=ps, d_=dt, c_=c:
                                   e.tensor_add(
                                       out=self.Y[:, d_ * S + c_ * 512:
                                                  d_ * S + (c_ + 1) * 512],
                                       in0=p_,
                                       in1=self.Y[:, d_ * S + c_ * 512:
                                                  d_ * S + (c_ + 1) * 512]),
                                   waits=[mm], inc=True)
                    self.slA.consumed(islot, mev)
                WRing.free(fh, P.now("t"))
            hh_free = P.now("t")

    def _dump_y(self):
        P = self.P
        P.barrier()
        for t in range(NT):
            P.dma("y", self.xs(t), self.Y[:, t * S:(t + 1) * S])
        P.barrier()

    def phase_B(self):
        P = self.P
        P.barrier()
        self.emit_ln(src_sbuf=None, dst=self.Y)
        P.barrier()
        if self.dbg == "ln":
            self._dump_y()
            return
        self.load_l0_tables()
        P.barrier()
        for lay in range(self.l0_layers):
            for t in range(NT):
                P.op("s", lambda e, t_=t: e.copy(
                    out=self.XN[:, t_ * S:(t_ + 1) * S],
                    in_=self.Y[:, t_ * S:(t_ + 1) * S]),
                    inc=(t == NT - 1))
            cast_now = P.now("s")
            self._reset_attn_tracking()
            self._dbg_stop = False
            for hg in range(NT):
                self.l0_attn_hg(lay, hg, cast_now)
                if self._dbg_stop:
                    return
            P.barrier()
            if self.dbg == "out0" and lay == 0:
                for t in range(NT):
                    P.op("v", lambda e, t_=t: e.tensor_copy(
                        out=self.Y[:, t_ * S:(t_ + 1) * S],
                        in_=self.OUT[:, t_ * S:(t_ + 1) * S]), inc=True)
                self._dump_y()
                return
            self.l0_wo(lay)
            P.barrier()
            if self.dbg == "attn" and lay == 0:
                self._dump_y()
                return
            self.emit_ln(src_sbuf=self.Y, dst=self.XN, shift=True,
                         gain=True, gcol=lay)
            P.barrier()
            if self.dbg == "ffnln" and lay == 0:
                for t in range(NT):
                    P.op("v", lambda e, t_=t: e.tensor_copy(
                        out=self.Y[:, t_ * S:(t_ + 1) * S],
                        in_=self.XN[:, t_ * S:(t_ + 1) * S]), inc=True)
                self._dump_y()
                return
            self.l0_ffn(lay)
            P.barrier()
        prev = [None, None]
        for t in range(NT):
            buf = self.XS0 if t % 2 == 0 else self.Q2
            ml = P.dma("y", buf[:, :], self.xs(t), waits=[prev[t % 2]])
            ma = P.op("v", lambda e, t_=t, b_=buf: e.tensor_add(
                out=self.Y[:, t_ * S:(t_ + 1) * S],
                in0=self.Y[:, t_ * S:(t_ + 1) * S], in1=b_[:, :]),
                waits=[ml], inc=True)
            prev[t % 2] = ma
            P.dma("y", self.xs(t), self.Y[:, t * S:(t + 1) * S], waits=[ma])
        P.barrier()

    def _reset_attn_tracking(self):
        self._psA_free = None
        self._vtt_free = None
        self._hh_free = None
        self._wbc_read = None
        self._wbc_ms = None
        self._sidx = 0
        self._dst_free = {}
        self._ba_free = None
        self._ba_read = None
        self._bc_free = None
        self._mu_free = None
        self._smf_free = None
        self._smf_read = None
        self._psF_free = None
        self._psT_free = [None, None]
        self._pt_read = [None, None]
        self._rot_free = [None, None]
        self._sm_free = [None] * 4
        self._p0_free = [None] * 4
        self._stat_free = [None] * 4

    def _l1_rope(self, tile_ap, rot_rows, ready):
        P = self.P
        msm_all = []
        mro = None
        for c in range(NCH):
            islot, ps, fz = self.slB.next()
            mro = P.op("t", lambda e, c_=c, p_=ps, rr_=rot_rows: e.matmul(
                p_[0:rr_, :], self.C["c_rt128"][0:rr_, 0:rr_],
                tile_ap[0:rr_, c_ * 512:(c_ + 1) * 512],
                start=True, stop=True),
                waits=[ready, fz], inc=True)
            msm = P.op("v", lambda e, c_=c, p_=ps, rr_=rot_rows:
                       e.tensor_mul(
                           out=self.SMF[0:rr_, c_ * 512:(c_ + 1) * 512],
                           in0=p_[0:rr_, :],
                           in1=self.T1S[0:rr_, c_ * 512:(c_ + 1) * 512]),
                       waits=[mro, self._smf_read], inc=True)
            self.slB.consumed(islot, msm)
            msm_all.append(msm)
        mtc = P.op("v", lambda e, rr_=rot_rows: e.tensor_mul(
            out=tile_ap[0:rr_, :], in0=tile_ap[0:rr_, :],
            in1=self.T1C[0:rr_, :]), waits=[mro], inc=True)
        mad = P.op("v", lambda e, rr_=rot_rows: e.tensor_add(
            out=tile_ap[0:rr_, :], in0=tile_ap[0:rr_, :],
            in1=self.SMF[0:rr_, :]), waits=[mtc] + msm_all, inc=True)
        self._smf_read = mad
        return mad

    def phase_C(self):
        P = self.P
        self.PT1 = self.BE
        self.VTT = self.BC
        P.barrier()
        self.emit_ln(src_sbuf=None, dst=self.XN)
        P.barrier()
        self.load_l1_tables()
        P.barrier()
        self._reset_attn_tracking()
        self._smf_read = None
        wqa, wqa_ms, wqa_fh = self.wring.load(
            self.dram["w_1qa"].rearrange("(k p) e -> p k e", p=128), 512)
        t1_ms = []
        for c in range(NCH):
            islot, ps, fz = self.slB.next()
            mm = None
            for kt in range(NT):
                mm = P.op("t", lambda e, k_=kt, c_=c, p_=ps, w_=wqa:
                          e.matmul(p_[0:64, :], w_[:, k_ * 64:(k_ + 1) * 64],
                                   self.XN[:, k_ * S + c_ * 512:
                                           k_ * S + (c_ + 1) * 512],
                                   start=(k_ == 0), stop=(k_ == NT - 1)),
                          waits=[wqa_ms, fz] if kt == 0 else (),
                          inc=(kt == NT - 1))
            mev = P.op("s", lambda e, c_=c, p_=ps: e.copy(
                out=self.BA[0:64, c_ * 512:(c_ + 1) * 512], in_=p_[0:64, :]),
                waits=[mm], inc=True)
            self.slB.consumed(islot, mev)
            t1_ms.append(mev)
        WRing.free(wqa_fh, P.now("t"))
        wqb = self.BB[0:64, 0:1024]
        wqb_ms = P.dma("g", wqb, self.dram["w_1qb"][:, :])
        q_ms = []
        for et in range(NT):
            evs = []
            for c in range(NCH):
                islot, ps, fz = self.slA.next()
                mm = P.op("t", lambda e, c_=c, p_=ps, e_=et, w_=wqb:
                          e.matmul(p_, w_[:, e_ * 128:(e_ + 1) * 128],
                                   self.BA[0:64, c_ * 512:(c_ + 1) * 512],
                                   start=True, stop=True),
                          waits=[wqb_ms] + t1_ms + [fz], inc=True)
                mev = P.op("v", lambda e, c_=c, p_=ps, e_=et: e.tensor_copy(
                    out=self.QARE[:, e_ * S + c_ * 512:
                                  e_ * S + (c_ + 1) * 512], in_=p_),
                    waits=[mm], inc=True)
                self.slA.consumed(islot, mev)
                evs.append(mev)
            mq = self._l1_rope(self.QARE[:, et * S:(et + 1) * S], 128,
                               evs[-1])
            q_ms.append(mq)
        wkva, wkva_ms, wkva_fh = self.wring.load(
            self.dram["w_1kva"].rearrange("(k p) e -> p k e", p=128), 256)
        t2_ms = []
        for c in range(NCH):
            islot, ps, fz = self.slB.next()
            mm = None
            for kt in range(NT):
                mm = P.op("t", lambda e, k_=kt, c_=c, p_=ps, w_=wkva:
                          e.matmul(p_[0:32, :], w_[:, k_ * 32:(k_ + 1) * 32],
                                   self.XN[:, k_ * S + c_ * 512:
                                           k_ * S + (c_ + 1) * 512],
                                   start=(k_ == 0), stop=(k_ == NT - 1)),
                          waits=[wkva_ms, fz] + t1_ms if kt == 0 else (),
                          inc=(kt == NT - 1))
            mev = P.op("s", lambda e, c_=c, p_=ps: e.copy(
                out=self.BA[0:32, c_ * 512:(c_ + 1) * 512], in_=p_[0:32, :]),
                waits=[mm] + q_ms, inc=True)
            self.slB.consumed(islot, mev)
            t2_ms.append(mev)
        WRing.free(wkva_fh, P.now("t"))
        wk2 = self.BB[0:32, 1024:1536]
        wk2_ms = P.dma("g", wk2, self.dram["w_1kvb_k2"][:, :])
        k_ms = []
        for g in range(4):
            evs = []
            for c in range(NCH):
                islot, ps, fz = self.slA.next()
                mm = P.op("t", lambda e, c_=c, p_=ps, g_=g, w_=wk2:
                          e.matmul(p_, w_[:, g_ * 128:(g_ + 1) * 128],
                                   self.BA[0:32, c_ * 512:(c_ + 1) * 512],
                                   start=True, stop=True),
                          waits=[wk2_ms] + t2_ms + [fz], inc=True)
                mev = P.op("v", lambda e, c_=c, p_=ps, g_=g: e.tensor_copy(
                    out=self.KARE[:, g_ * S + c_ * 512:
                                  g_ * S + (c_ + 1) * 512], in_=p_),
                    waits=[mm], inc=True)
                self.slA.consumed(islot, mev)
                evs.append(mev)
            mk = self._l1_rope(self.KARE[:, g * S:(g + 1) * S], 128,
                               evs[-1])
            k_ms.append(mk)
        wv = self.BB[0:32, 1536:1792]
        wv_ms = P.dma("g", wv, self.dram["w_1kvb_v"][:, :])
        vn_ms = []
        for g in range(4):
            evs = []
            for c in range(NCH):
                islot, ps, fz = self.slB.next()
                mm = P.op("t", lambda e, c_=c, p_=ps, g_=g, w_=wv:
                          e.matmul(p_[0:64, :], w_[:, g_ * 64:(g_ + 1) * 64],
                                   self.BA[0:32, c_ * 512:(c_ + 1) * 512],
                                   start=True, stop=True),
                          waits=[wv_ms] + t2_ms + [fz, self._vtt_free],
                          inc=True)
                mev = P.op("s", lambda e, c_=c, p_=ps: e.copy(
                    out=self.VTT[0:64, c_ * 512:(c_ + 1) * 512],
                    in_=p_[0:64, :]), waits=[mm], inc=True)
                self.slB.consumed(islot, mev)
                evs.append(mev)
            mtr = None
            for jb in range(16):
                mtr = P.op("t", lambda e, j_=jb: e.transpose(
                    self.psT[:, j_ * 64:(j_ + 1) * 64],
                    self.VTT[0:64, j_ * 128:(j_ + 1) * 128],
                    self.C["c_identb"][0:64, 0:64]),
                    waits=evs + [self._psT_free[0], self._pt_read[0]],
                    inc=(jb == 15))
            self._vtt_free = mtr
            mv1 = P.op("v", lambda e, g_=g: e.tensor_copy(
                out=self.VNARE[:, g_ * 2048: (g_ + 1) * 2048].rearrange(
                    "p (b c) -> p b c", c=128)[:, :, 0:64],
                in_=self.psT[:, 0:1024].rearrange("p (b c) -> p b c", c=64)),
                waits=[mtr], inc=True)
            mv2 = P.op("v", lambda e, g_=g: e.tensor_copy(
                out=self.VNARE[:, g_ * 2048: (g_ + 1) * 2048].rearrange(
                    "p (b c) -> p b c", c=128)[:, :, 64:128],
                in_=self.psT[:, 0:1024].rearrange("p (b c) -> p b c", c=64)),
                waits=[mtr], inc=True)
            self._psT_free[0] = mv2
            vn_ms.append(mv2)
        P.barrier()
        for h in range(16):
            self._l1_attn_head(h, q_ms, k_ms, vn_ms)
        P.barrier()
        xprev = [None, None]
        for dt in range(NT):
            slab, w_ms, fh = self.wring.load(
                self.dram["w_1o"].rearrange("(k p) e -> p k e", p=128)
                [:, :, dt * 128:(dt + 1) * 128], 1024)
            buf = self.XS0 if dt % 2 == 0 else self.Q2
            ml = P.dma("y", buf[:, :], self.xs(dt), waits=[xprev[dt % 2]])
            bout = self.C["p_bout"][:, dt:dt + 1]
            mevs = []
            for c in range(NCH):
                islot, ps, fz = self.slA.next()
                mm = self.mm_proj_chunk(ps, slab, c, False, [w_ms, fz],
                                        src=self.OUT)
                mev = P.op("v", lambda e, p_=ps, c_=c, b_=bout, bf_=buf:
                           e.scalar_tensor_tensor(
                               out=bf_[:, c_ * 512:(c_ + 1) * 512],
                               in0=p_, scalar=b_,
                               in1=bf_[:, c_ * 512:(c_ + 1) * 512],
                               op0=ALU.add, op1=ALU.add),
                           waits=[mm, ml], inc=True)
                self.slA.consumed(islot, mev)
                mevs.append(mev)
            mw = P.dma("y", self.xs(dt), buf[:, :], waits=mevs)
            xprev[dt % 2] = mw
            WRing.free(fh, P.now("t"))
        P.barrier()

    def _l1_attn_head(self, h, q_ms, k_ms, vn_ms):
        P = self.P
        qt = h // 2
        hr = 64 * (h % 2)
        g = h // 4
        for ib in range(16):
            L = (ib + 1) * 128
            mm = None
            for c0 in range(0, L, 512):
                w_ = min(512, L - c0)
                mm = P.op("t", lambda e, c0_=c0, w2_=w_, ib_=ib, qt_=qt,
                          hr_=hr, g_=g: e.matmul(
                    self.psA[:, c0_:c0_ + w2_],
                    self.QARE[hr_:hr_ + 64,
                              qt_ * S + ib_ * 128: qt_ * S + ib_ * 128 + 128],
                    self.KARE[hr_:hr_ + 64, g_ * S + c0_: g_ * S + c0_ + w2_],
                    start=True, stop=True),
                    waits=[q_ms[qt], k_ms[g], self._psA_free], inc=True)
            msc = P.op("v", lambda e, L_=L: e.tensor_scalar(
                out=self.SMF[:, 0:L_], in0=self.psA[:, 0:L_],
                scalar1=0.125, scalar2=None, op0=ALU.mult),
                waits=[mm, self._sm_free[0]], inc=True)
            mdg = P.op("v", lambda e, L_=L: e.tensor_add(
                out=self.SMF[:, L_ - 128:L_], in0=self.SMF[:, L_ - 128:L_],
                in1=self.C["c_diag"][:, :]), waits=[msc], inc=True)
            self._psA_free = mdg
            mnm = P.op("v", lambda e, L_=L: e.reduce_max(
                out=self.SMALL[:, 0:1], in_=self.SMF[:, 0:L_],
                axis=AX.X, negate=True),
                waits=[mdg, self._stat_free[0]], inc=True)
            mz0 = P.op("v", lambda e: e.memset(
                self.SMALL[:, 2:3], 0.0), waits=[self._stat_free[0]],
                inc=True)
            mex = P.op("s", lambda e, L_=L: e.activation(
                out=self.BD[:, 0:L_], in_=self.SMF[:, 0:L_],
                func=AF.Exp, bias=self.SMALL[:, 0:1],
                accum_out=self.SMALL[:, 2:3]),
                waits=[mnm, mz0, self._p0_free[0]], inc=True)
            self._sm_free[0] = mex
            miz = P.op("v", lambda e: e.reciprocal(
                out=self.SMALL[:, 4:5], in_=self.SMALL[:, 2:3]),
                waits=[mex], inc=True)
            mpp = P.op("v", lambda e, L_=L: e.tensor_scalar(
                out=self.BD[:, 0:L_], in0=self.BD[:, 0:L_],
                scalar1=self.SMALL[:, 4:5], scalar2=None, op0=ALU.mult),
                waits=[Prog.sync(miz)], inc=True)
            self._stat_free[0] = mpp
            mpt_all = []
            for bi, base in enumerate(range(0, ib + 1, 8)):
                hi = min(base + 8, ib + 1)
                par = bi % 2
                mtr = None
                for jb in range(base, hi):
                    mtr = P.op("t", lambda e, j_=jb, b0_=base, pr_=par:
                               e.transpose(
                        self.psT[:, pr_ * 1024 + (j_ - b0_) * 128:
                                 pr_ * 1024 + (j_ - b0_) * 128 + 128],
                        self.BD[:, j_ * 128:(j_ + 1) * 128],
                        self.C["c_identb"][:, :]),
                        waits=[mpp, self._psT_free[par]], inc=(jb == hi - 1))
                mpt = P.op("s", lambda e, b0_=base, hi_=hi, pr_=par:
                           e.copy(
                    out=self.PT1[:, b0_ * 128: hi_ * 128],
                    in_=self.psT[:, pr_ * 1024:
                                 pr_ * 1024 + (hi_ - b0_) * 128]),
                    waits=[mtr, self._pt_read[par]], inc=True)
                self._psT_free[par] = mpt
                mpt_all.append(mpt)
            self._p0_free[0] = mpt_all[-1]
            mav = None
            for jb in range(ib + 1):
                mav = P.op("t", lambda e, j_=jb, g_=g,
                           st_=(jb == 0), sp_=(jb == ib): e.matmul(
                    self.psF[:, 0:128],
                    self.VNARE[:, g_ * 2048 + j_ * 128:
                               g_ * 2048 + (j_ + 1) * 128],
                    self.PT1[:, j_ * 128:(j_ + 1) * 128],
                    start=st_, stop=sp_),
                    waits=mpt_all + [vn_ms[g], self._psF_free],
                    inc=(jb == ib))
            self._pt_read[0] = mav
            self._pt_read[1] = mav
            mo = P.op("s", lambda e, h_=h, hr_=hr, ib_=ib: e.copy(
                out=self.OUT[hr_:hr_ + 64,
                             (h_ // 2) * S + ib_ * 128:
                             (h_ // 2) * S + ib_ * 128 + 128],
                in_=self.psF[hr_:hr_ + 64, 0:128]), waits=[mav], inc=True)
            self._psF_free = mo

    def phase_D(self):
        P = self.P
        P.barrier()
        self.emit_ln(src_sbuf=None, dst=self.XN)
        P.barrier()
        self._reset_attn_tracking()
        wg, wg_ms, wg_fh = self.wring.load(
            self.dram["w_g"].rearrange("(k p) e -> p k e", p=128), 64)
        bg = self.C["p_bg"][0:8, 0:1]
        lg_ms = []
        for c in range(NCH):
            mm = None
            for kt in range(NT):
                mm = P.op("t", lambda e, k_=kt, c_=c, w_=wg: e.matmul(
                    self.psF[0:8, 0:512], w_[:, k_ * 8:(k_ + 1) * 8],
                    self.XN[:, k_ * S + c_ * 512: k_ * S + (c_ + 1) * 512],
                    start=(k_ == 0), stop=(k_ == NT - 1)),
                    waits=[wg_ms, self._psF_free] if kt == 0 else (),
                    inc=(kt == NT - 1))
            mev = P.op("v", lambda e, c_=c, b_=bg: e.tensor_scalar(
                out=self.MU[0:8, c_ * 512:(c_ + 1) * 512],
                in0=self.psF[0:8, 0:512], scalar1=b_, scalar2=None,
                op0=ALU.add), waits=[mm], inc=True)
            self._psF_free = mev
            lg_ms.append(mev)
        WRing.free(wg_fh, P.now("t"))
        lt_ms = []
        for sbk in range(16):
            mtr = P.op("t", lambda e, s_=sbk: e.transpose(
                self.psF[:, 8:16], self.MU[0:8, s_ * 128:(s_ + 1) * 128],
                self.C["c_identf"][0:8, 0:8]),
                waits=lg_ms + [self._psF_free], inc=True)
            mev = P.op("v", lambda e, s_=sbk: e.tensor_copy(
                out=self.RS[:, s_ * 8:(s_ + 1) * 8],
                in_=self.psF[:, 8:16]), waits=[mtr], inc=True)
            self._psF_free = mev
            lt_ms.append(mev)
        w_ms = []
        for sbk in range(16):
            k0 = sbk * 8
            LT = self.RS[:, k0:k0 + 8]
            m1 = P.op("v", lambda e, L_=LT: e.reduce_max(
                out=self.SMALL[:, 0:1], in_=L_, axis=AX.X, negate=True),
                waits=[lt_ms[sbk], self._stat_free[0]], inc=True)
            m2 = P.op("v", lambda e, L_=LT, k_=k0: e.tensor_scalar(
                out=self.Q1[:, k_:k_ + 8], in0=L_,
                scalar1=self.SMALL[:, 0:1], scalar2=None, op0=ALU.add),
                waits=[Prog.sync(m1)], inc=True)
            m3 = P.op("v", lambda e, k_=k0: e.tensor_single_scalar(
                out=self.Q2[:, k_:k_ + 8], in_=self.Q1[:, k_:k_ + 8],
                scalar=0.0, op=ALU.is_ge), waits=[Prog.sync(m2)], inc=True)
            m4 = P.op("v", lambda e, L_=LT, k_=k0: e.scalar_tensor_tensor(
                out=self.Q2[:, k_:k_ + 8], in0=self.Q2[:, k_:k_ + 8],
                scalar=-1e30, in1=L_, op0=ALU.mult, op1=ALU.add),
                waits=[Prog.sync(m3)], inc=True)
            m5 = P.op("v", lambda e, k_=k0: e.reduce_max(
                out=self.SMALL[:, 1:2], in_=self.Q2[:, k_:k_ + 8],
                axis=AX.X, negate=True), waits=[Prog.sync(m4)], inc=True)
            m6 = P.op("s", lambda e, k_=k0: e.activation(
                out=self.Q1[:, k_:k_ + 8], in_=self.Q1[:, k_:k_ + 8],
                func=AF.Exp), waits=[m2, m3], inc=True)
            m7 = P.op("v", lambda e, L_=LT, k_=k0: e.tensor_scalar(
                out=self.Q2[:, k_:k_ + 8], in0=L_,
                scalar1=self.SMALL[:, 1:2], scalar2=None, op0=ALU.add),
                waits=[Prog.sync(m5), m6], inc=True)
            m8 = P.op("v", lambda e, k_=k0: e.tensor_single_scalar(
                out=self.Q2[:, k_:k_ + 8], in_=self.Q2[:, k_:k_ + 8],
                scalar=0.0, op=ALU.is_ge), waits=[Prog.sync(m7)], inc=True)
            m9 = P.op("v", lambda e: e.tensor_sub(
                out=self.SMALL[:, 2:3], in0=self.SMALL[:, 0:1],
                in1=self.SMALL[:, 1:2]), waits=[Prog.sync(m5)], inc=True)
            m10 = P.op("s", lambda e: e.activation(
                out=self.SMALL[:, 3:4], in_=self.SMALL[:, 2:3],
                func=AF.Exp), waits=[m9], inc=True)
            m11 = P.op("v", lambda e: e.tensor_scalar(
                out=self.SMALL[:, 4:5], in0=self.SMALL[:, 3:4],
                scalar1=1.0, scalar2=None, op0=ALU.add),
                waits=[m10], inc=True)
            m12 = P.op("v", lambda e: e.reciprocal(
                out=self.SMALL[:, 5:6], in_=self.SMALL[:, 4:5]),
                waits=[Prog.sync(m11)], inc=True)
            m13 = P.op("v", lambda e, k_=k0: e.tensor_mul(
                out=self.Q1[:, k_:k_ + 8], in0=self.Q1[:, k_:k_ + 8],
                in1=self.Q2[:, k_:k_ + 8]), waits=[m6, Prog.sync(m8)],
                inc=True)
            m14 = P.op("v", lambda e, k_=k0: e.tensor_scalar(
                out=self.Q1[:, k_:k_ + 8], in0=self.Q1[:, k_:k_ + 8],
                scalar1=self.SMALL[:, 5:6], scalar2=None, op0=ALU.mult),
                waits=[Prog.sync(m12), Prog.sync(m13)], inc=True)
            self._stat_free[0] = m14
            w_ms.append(m14)
        wt_ms = []
        for sbk in range(16):
            mtr = P.op("t", lambda e, s_=sbk: e.transpose(
                self.psF[0:8, 16:144], self.Q1[:, s_ * 8:(s_ + 1) * 8],
                self.C["c_identf"][:, :]),
                waits=[w_ms[sbk], self._psF_free], inc=True)
            mev = P.op("v", lambda e, s_=sbk: e.tensor_copy(
                out=self.MU[0:8, s_ * 128:(s_ + 1) * 128],
                in_=self.psF[0:8, 16:144]), waits=[mtr], inc=True)
            self._psF_free = mev
            wt_ms.append(mev)
        mwb = P.op("s", lambda e: e.copy(
            out=self.BC[0:8, :], in_=self.MU[0:8, :]), waits=wt_ms,
            inc=True)
        mb2c = P.dma("g", self.BA[0:8, 0:1024],
                     self.dram["p_moeb2t"][:, :])
        self._region_ms = {}
        smf_slots = [self.SMF[:, i * 512:(i + 1) * 512] for i in range(4)]
        smf_free = [None] * 4
        for dt in range(NT):
            for c in range(NCH):
                islot, ps, fz = self.slB.next()
                mm = P.op("t", lambda e, d_=dt, c_=c, p_=ps: e.matmul(
                    p_, self.BA[0:8, d_ * 128:(d_ + 1) * 128],
                    self.BC[0:8, c_ * 512:(c_ + 1) * 512],
                    start=True, stop=True),
                    waits=[mb2c, mwb, fz], inc=True)
                sl = self._sidx % 4
                mev = P.op("v", lambda e, p_=ps, s_=smf_slots[sl]:
                           e.tensor_copy(out=s_, in_=p_),
                           waits=[mm, smf_free[sl]], inc=True)
                self.slB.consumed(islot, mev)
                md = P.dma("g", self.xs(dt, c * 512, (c + 1) * 512),
                           smf_slots[sl], waits=[mev], accum=True)
                smf_free[sl] = md
                self._region_ms[(dt, c)] = md
                self._sidx += 1
        for ex in range(8):
            for fh in (0, 1):
                self._moe_expert(ex, fh, smf_slots, smf_free)
        P.barrier()

    def _moe_expert(self, ex, fh2, smf_slots, smf_free):
        P = self.P
        wb_ms = []
        if fh2 == 0:
            for c in range(NCH):
                islot, ps, fz = self.slB.next()
                mm = P.op("t", lambda e, c_=c, p_=ps, x_=ex: e.matmul(
                    p_, self.C["c_sel8"][:, x_ * 128:(x_ + 1) * 128],
                    self.BC[0:8, c_ * 512:(c_ + 1) * 512],
                    start=True, stop=True), waits=[fz, self._wbc_read],
                    inc=True)
                mev = P.op("v", lambda e, c_=c, p_=ps: e.tensor_copy(
                    out=self.BB[:, c_ * 512:(c_ + 1) * 512], in_=p_),
                    waits=[mm], inc=True)
                self.slB.consumed(islot, mev)
                wb_ms.append(mev)
            self._wbc_ms = wb_ms[-1]
        wbc_ms = self._wbc_ms
        for ft in range(16):
            fg = fh2 * 16 + ft
            slab, w_ms, wfh = self.wring.load(
                self.dram["w_moe1"][ex].rearrange("(k p) f -> p k f", p=128)
                [:, :, fg * 128:(fg + 1) * 128], 1024)
            b1 = self.C["p_moeb1"][:, ex * 32 + fg: ex * 32 + fg + 1]
            for cg in range(NCH):
                islot, ps, fz = self.slA.next()
                mm = self.mm_proj_chunk(ps, slab, cg, False,
                                        [w_ms, fz, self._hh_free])
                mg = P.op("s", lambda e, p_=ps, f_=ft, c_=cg, b_=b1:
                          e.activation(
                              out=self.HHARE[:, f_ * 2048 + c_ * 512:
                                             f_ * 2048 + (c_ + 1) * 512],
                              in_=p_, func=AF.Gelu, bias=b_),
                          waits=[mm], inc=True)
                self.slA.consumed(islot, mg)
            WRing.free(wfh, P.now("t"))
        hh_ready = P.now("s")
        for dt in range(NT):
            slab, w_ms, wfh = self.wring.load(
                self.dram["w_moe2"][ex].rearrange(
                    "(k p) d -> p k d", p=128)
                [:, fh2 * 16:(fh2 + 1) * 16, dt * 128:(dt + 1) * 128],
                2048)
            for cg in range(NCH):
                islot, ps, fz = self.slA.next()
                mm = None
                for kt in range(16):
                    mm = P.op("t", lambda e, k_=kt, p_=ps, c_=cg, w_=slab:
                              e.matmul(
                                  p_, w_[:, k_ * 128: k_ * 128 + 128],
                                  self.HHARE[:, k_ * 2048 + c_ * 512:
                                             k_ * 2048 + (c_ + 1) * 512],
                                  start=(k_ == 0), stop=(k_ == 15)),
                              waits=([w_ms, fz, hh_ready]
                                     if kt == 0 else ()),
                              inc=(kt == 15))
                sl = self._sidx % 4
                mev = P.op("v", lambda e, p_=ps, s_=smf_slots[sl], c_=cg:
                           e.tensor_mul(out=s_, in0=p_,
                                        in1=self.BB[:, c_ * 512:
                                                    (c_ + 1) * 512]),
                           waits=[mm, smf_free[sl], wbc_ms], inc=True)
                self.slA.consumed(islot, mev)
                md = P.dma("g", self.xs(dt, cg * 512, (cg + 1) * 512),
                           smf_slots[sl],
                           waits=[mev, self._region_ms.get((dt, cg))],
                           accum=True)
                smf_free[sl] = md
                self._region_ms[(dt, cg)] = md
                self._sidx += 1
            WRing.free(wfh, P.now("t"))
        self._hh_free = P.now("t")
        self._wbc_read = P.now("v")



_NCORES = 8

def _prep_inputs(inputs):
    f = np.ascontiguousarray
    d = {}
    d["w_zh"] = f(np.stack([inputs["L0_attn_Wz"], inputs["L0_attn_Wh"]]))
    d["p_bz"] = _tile_vec(inputs["L0_attn_bz"], 8)
    d["p_bh"] = _tile_vec(inputs["L0_attn_bh"], 8)
    d["w_l0qkv"] = f(inputs["L0_ffn_Wqkv"])
    d["w_l0o"] = f(inputs["L0_ffn_Wo"])
    w1 = inputs["L0_ffn_W1"]
    w1p = np.zeros((6, D, 5632), dtype=np.float32)
    w1p[:, :, 0:2730] = w1[:, :, 0:2730]
    w1p[:, :, 2816:2816 + 2730] = w1[:, :, 2730:5460]
    d["w_l0w1p"] = w1p
    w2 = inputs["L0_ffn_W2"]
    w2p = np.zeros((6, 2816, D), dtype=np.float32)
    w2p[:, 0:2730, :] = w2
    d["w_l0w2p"] = w2p
    qs, ks = inputs["L0_ffn_q_scale"], inputs["L0_ffn_k_scale"]
    d["p_l0qs"] = f(np.concatenate([qs.T, qs.T], 0).astype(np.float32))
    d["p_l0ks"] = f(np.concatenate([ks.T, ks.T], 0).astype(np.float32))
    d["p_l0g"] = np.concatenate(
        [_tile_vec(inputs["L0_ffn_ln_g"][i], 8) for i in range(6)], 1)
    d["p_l0b"] = np.concatenate(
        [_tile_vec(inputs["L0_ffn_ln_b"][i], 8) for i in range(6)], 1)
    d["p_l0b1"] = np.concatenate(
        [np.concatenate([_tile_vec(inputs["L0_ffn_b1"][i, :2730], 22),
                         _tile_vec(inputs["L0_ffn_b1"][i, 2730:], 22)], 1)
         for i in range(6)], 1)
    d["p_l0b2"] = np.concatenate(
        [_tile_vec(inputs["L0_ffn_b2"][i], 8) for i in range(6)], 1)
    d["w_1qa"] = f(inputs["L1_attn_Wqa"])
    d["w_1qb"] = f(inputs["L1_attn_Wqb"][:, _perm_wqb()])
    d["w_1kva"] = f(inputs["L1_attn_Wkva"])
    kvb = inputs["L1_attn_Wkvb"]
    k2 = np.zeros((32, 512), dtype=np.float32)
    vv = np.zeros((32, 256), dtype=np.float32)
    for g in range(4):
        kcols = [g * 128 + 4 * r if r < 32 else g * 128 + 4 * (r - 32) + 2
                 for r in range(64)]
        for r in range(64):
            k2[:, g * 128 + r] = kvb[:, kcols[r]]
            k2[:, g * 128 + 64 + r] = kvb[:, kcols[r]]
            vv[:, g * 64 + r] = kvb[:, g * 128 + 2 * r + 1]
    d["w_1kvb_k2"] = k2
    d["w_1kvb_v"] = vv
    d["w_1o"] = f(inputs["L1_attn_Wout"])
    d["p_bout"] = _tile_vec(inputs["L1_attn_bout"], 8)
    d["w_moe1"] = f(inputs["L1_ffn_W1"])
    d["w_moe2"] = f(inputs["L1_ffn_W2"])
    d["p_moeb1"] = np.concatenate(
        [_tile_vec(inputs["L1_ffn_b1"][e], 32) for e in range(8)], 1)
    d["p_moeb2t"] = f(inputs["L1_ffn_b2"].astype(np.float32))
    d["w_g"] = f(inputs["L1_ffn_Wg"])
    bg = np.zeros((128, 1), dtype=np.float32)
    bg[0:8, 0] = inputs["L1_ffn_bg"]
    d["p_bg"] = bg
    cst, tbl = _consts()
    for k, v in {**cst, **tbl}.items():
        d[k] = f(v)
    return d

_PHASES = ("A", "B", "C", "D")

def kernel(**inputs):
    mk = MK(phases=_PHASES)
    nc = mk.build()
    shared = _prep_inputs(inputs)
    x = np.ascontiguousarray(inputs["L0_attn_x"].astype(np.float32))
    in_maps = []
    for c in range(_NCORES):
        m = dict(shared)
        m["x_in"] = x[c]
        in_maps.append(m)
    res = run_bass_kernel_spmd(nc, in_maps, core_ids=list(range(_NCORES)))
    return np.stack([res.results[c]["x_out"] for c in range(_NCORES)])


# version 13
# speedup vs baseline: 1.0934x; 1.0015x over previous
import numpy as np
import ml_dtypes
import concourse.bass as bass
import concourse.mybir as mybir
from concourse.bass_utils import run_bass_kernel_spmd

F32 = mybir.dt.float32
BF16 = mybir.dt.bfloat16
U8 = mybir.dt.uint8
AF = mybir.ActivationFunctionType
ALU = mybir.AluOpType
AX = mybir.AxisListType

D = 1024
S = 2048
NT = 8
NCH = 4
NEG = -1e30


def _rope_tables_l0():
    half = np.arange(0, 64, 2, dtype=np.float64)
    inv_freq = 1.0 / 10000.0 ** (half / 64.0)
    t = np.arange(S, dtype=np.float64)
    freqs = t[:, None] * inv_freq[None, :]
    base_scale = (half + 0.4 * 64) / (1.4 * 64)
    power = (t - S // 2) / (256 // 2)
    scale = base_scale[None, :] ** power[:, None]
    cos, sin = np.cos(freqs), np.sin(freqs)
    def tile128(m):
        r = np.concatenate([m.T, m.T], 0)
        return np.concatenate([r, r], 0)
    return (tile128(cos * scale), tile128(sin * scale),
            tile128(1.0 / (scale * scale)))

def _rope_tables_l1():
    half = np.arange(0, 64, 2, dtype=np.float64)
    inv_freq = 1.0 / 10000.0 ** (half / 64.0)
    t = np.arange(S, dtype=np.float64)
    freqs = t[:, None] * inv_freq[None, :]
    def tile128(m):
        r = np.concatenate([m.T, m.T], 0)
        return np.concatenate([r, r], 0)
    return tile128(np.cos(freqs)), tile128(np.sin(freqs))

def _local_masks():
    r = np.arange(128)[:, None]
    c = np.arange(512)[None, :]
    mk = []
    for off in (0, 128):
        ok = (c >= r + off) & (c <= r + off + 256)
        mk.append(np.where(ok, 0.0, NEG))
    return mk

def _l1_diag_mask():
    r = np.arange(128)[:, None]
    c = np.arange(128)[None, :]
    return np.where(c <= r, 0.0, NEG)

def _rt128():
    R = np.zeros((64, 64))
    R[0:32, 32:64] = -np.eye(32)
    R[32:64, 0:32] = np.eye(32)
    M = np.zeros((128, 128))
    M[0:64, 0:64] = R.T
    M[64:128, 64:128] = R.T
    return M

def _bd64():
    M = np.zeros((128, 128))
    M[0:64, 0:64] = 1.0
    M[64:128, 64:128] = 1.0
    return M

def _perm_wqb():
    perm = np.zeros(1024, dtype=np.int64)
    for h in range(16):
        for r in range(64):
            d = 2 * r if r < 32 else 2 * (r - 32) + 1
            perm[h * 64 + r] = h * 64 + d
    return perm

def _perm_wkvb():
    perm = np.zeros(512, dtype=np.int64)
    for g in range(4):
        for r in range(128):
            if r < 32:
                c = 4 * r
            elif r < 64:
                c = 4 * (r - 32) + 2
            else:
                c = 2 * (r - 64) + 1
            perm[g * 128 + r] = g * 128 + c
    return perm

def _consts():
    mke, mko = _local_masks()
    sel8 = np.zeros((8, 8 * 128))
    for e in range(8):
        sel8[e, e * 128:(e + 1) * 128] = 1.0
    scal = np.zeros((128, 8))
    scal[:, 0] = 1e-5; scal[:, 1] = 1.0; scal[:, 2] = -1.0
    scal[:, 3] = -0.5; scal[:, 4] = 8.0; scal[:, 5] = 1.0 / 1024.0
    d = {
        "c_scal": scal,
        "c_identf": np.eye(128), "c_identb": np.eye(128),
        "c_ones": np.ones((128, 128)), "c_bd64": _bd64(), "c_rt128": _rt128(),
        "c_mke": mke, "c_mko": mko, "c_diag": _l1_diag_mask(), "c_sel8": sel8,
    }
    cs, sn, i2 = _rope_tables_l0()
    c1, s1 = _rope_tables_l1()
    tbl = {"t_cs": cs, "t_sn": sn, "t_i2": i2, "t_r1c": c1, "t_r1s": s1}
    return ({k: v.astype(np.float32) for k, v in d.items()},
            {k: v.astype(np.float32) for k, v in tbl.items()})

_CONST_F32 = {"c_identf", "c_scal"}

def _tile_vec(v, ntile):
    out = np.zeros((128, ntile), dtype=np.float32)
    for t in range(ntile):
        seg = v[t * 128:(t + 1) * 128]
        out[:seg.shape[0], t] = seg
    return out


ENGS = ("t", "v", "s", "g", "y")

NDSEM = 12

class Prog:
    def __init__(self):
        self.nc = bass.Bass("TRN2", target_bir_lowering=False)
        self.ops = {e: [] for e in ENGS}
        self.cnt = {e: 0 for e in ENGS}
        self.dcnt = {"g": 0, "y": 0}

    def op(self, eng, fn, waits=(), inc=False):
        waits = tuple(w for w in waits if w is not None)
        self.ops[eng].append(("op", fn, waits, inc))
        if inc:
            self.cnt[eng] += 1
            return (eng, self.cnt[eng])
        return None

    def dma(self, queue, out, in_, waits=(), accum=False):
        waits = tuple(w for w in waits if w is not None)
        idx = self.dcnt[queue]
        slot = idx % NDSEM
        rnd = idx // NDSEM
        if rnd > 0:
            waits = waits + ((queue + "d%d" % slot, rnd * 16),)
        self.ops[queue].append(("dma", (out, in_, accum, slot), waits, True))
        self.dcnt[queue] += 1
        return (queue + "d%d" % slot, (rnd + 1) * 16)

    def barrier(self):
        snap = [(e, self.cnt[e]) for e in ENGS if self.cnt[e] > 0]
        for q in ("g", "y"):
            n = self.dcnt[q]
            for slot in range(min(n, NDSEM)):
                cnt = n // NDSEM + (1 if slot < n % NDSEM else 0)
                snap.append((q + "d%d" % slot, cnt * 16))
        for e in ENGS:
            ws = tuple(snap)
            if ws:
                self.ops[e].append(("wait", None, ws, False))

    def now(self, eng):
        return (eng, self.cnt[eng])

    @staticmethod
    def sync(ms):
        return (ms[0], ms[1], "f") if ms is not None else None

    def emit(self):
        nc = self.nc
        self.sems = {}
        for e in ENGS:
            self.sems[e] = nc.semaphore("sem_" + e).__enter__()
        for q in ("g", "y"):
            for slot in range(NDSEM):
                key = q + "d%d" % slot
                self.sems[key] = nc.semaphore("sem_" + key).__enter__()
        lastw = {e: {} for e in ENGS}

        def run(engobj, ename):
            lw = lastw[ename]
            for kind, fn, waits, inc in self.ops[ename]:
                for ms in waits:
                    sk, val = ms[0], ms[1]
                    if lw.get(sk, -1) >= val:
                        continue
                    engobj.wait_ge(self.sems[sk], val)
                    lw[sk] = val
                if kind == "op":
                    inst = fn(engobj)
                    if inc:
                        inst.then_inc(self.sems[ename], 1)
                elif kind == "dma":
                    out, in_, accum, slot = fn
                    key = ename + "d%d" % slot
                    if accum:
                        engobj.dma_start(out=out, in_=in_,
                                         accum_op=ALU.add).then_inc(
                            self.sems[key], 16)
                    else:
                        engobj.dma_start(out=out, in_=in_).then_inc(
                            self.sems[key], 16)

        with nc.Block() as block:
            block.tensor(lambda e: run(e, "t"))
            block.vector(lambda e: run(e, "v"))
            block.scalar(lambda e: run(e, "s"))
            block.gpsimd(lambda e: run(e, "g"))
            block.sync(lambda e: run(e, "y"))


class PSlots:
    def __init__(self, aps):
        self.aps = aps
        self.free_ms = [None] * len(aps)
        self.i = 0

    def next(self):
        i = self.i
        self.i = (self.i + 1) % len(self.aps)
        return i, self.aps[i], self.free_ms[i]

    def consumed(self, i, ms):
        self.free_ms[i] = ms


class WRing:
    def __init__(self, P, ware, cols):
        self.P, self.ware, self.cols = P, ware, cols
        self.head = 0
        self.live = []

    def load(self, dram_ap, ncols, queue="g"):
        if self.head + ncols > self.cols:
            self.head = 0
        start, end = self.head, self.head + ncols
        waits, keep = [], []
        for ent in self.live:
            if ent[0] < end and ent[1] > start:
                waits.append(ent[2][0])
            else:
                keep.append(ent)
        self.live = keep
        ap = self.ware[:, start:end]
        ms = self.P.dma(queue, ap, dram_ap, waits=waits)
        fh = [None]
        self.live.append((start, end, fh))
        self.head = end
        return ap, ms, fh

    @staticmethod
    def free(fh, ms):
        fh[0] = ms



class MK:
    def __init__(self, phases=("A", "B", "C", "D"), l0_layers=6, dbg=None):
        self.P = Prog()
        self.phases = phases
        self.l0_layers = l0_layers
        self.dbg = dbg
        nc = self.P.nc
        self.dram = {}

        def din(name, shape):
            self.dram[name] = nc.dram_tensor(name, list(shape), F32,
                                             kind="ExternalInput")
        din("x_in", (S, D))
        din("w_zh", (2, D, D))
        din("w_l0qkv", (6, D, 3072)); din("w_l0o", (6, D, D))
        din("w_l0w1p", (6, D, 5632)); din("w_l0w2p", (6, 2816, D))
        din("w_1qa", (D, 64)); din("w_1qb", (64, D))
        din("w_1kva", (D, 32))
        din("w_1kvb_k2", (32, 512)); din("w_1kvb_v", (32, 256))
        din("w_1o", (D, D))
        din("w_moe1", (8, D, 4096)); din("w_moe2", (8, 4096, D))
        din("w_g", (D, 8))
        self.pshapes = {
            "p_bz": (128, 8), "p_bh": (128, 8), "p_l0qs": (128, 6),
            "p_l0ks": (128, 6), "p_l0g": (128, 48), "p_l0b": (128, 48),
            "p_l0b1": (128, 44 * 6), "p_l0b2": (128, 48),
            "p_bout": (128, 8), "p_moeb1": (128, 256),
            "p_bg": (128, 1)}
        for pn, shp in self.pshapes.items():
            din(pn, shp)
        din("p_moeb2t", (8, 1024))
        cst, tbl = _consts()
        self._cst_np, self._tbl_np = cst, tbl
        for cn, cv in cst.items():
            din(cn, cv.shape)
        for tn, tv in tbl.items():
            din(tn, tv.shape)
        self.dram["x_out"] = nc.dram_tensor("x_out", [S, D], F32,
                                            kind="ExternalOutput")
        self.dram["xstate"] = nc.dram_tensor("xstate", [128, NT * S], F32,
                                             kind="Internal")
        self._alloc_sbuf()

    def _alloc_sbuf(self):
        nc = self.P.nc
        def sb(name, shape, dt):
            return nc.alloc_sbuf_tensor(name, list(shape), dt)
        self.XN = sb("XN", [128, NT * S], BF16)
        self.OUT = sb("OUT", [128, NT * S], BF16)
        self.Y = sb("Y", [128, NT * S], F32)
        self.WARE = sb("WARE", [128, 6 * 1024], BF16)
        self.XTC = sb("XTC", [128, 12 * 1024], U8)
        self.PLC = sb("PLC", [128, 46 * 1024], U8)
        xt = nc.lookup_mloc(self.XTC).addr
        pl = nc.lookup_mloc(self.PLC).addr
        yb = nc.lookup_mloc(self.Y).addr

        def at(name, shape, dt, base, off):
            return nc.alloc_sbuf_tensor_at(name, list(shape), dt,
                                           offset=base + off)
        self.TA = [at("TA0", [128, 1024], F32, xt, 0),
                   at("TA1", [128, 1024], F32, xt, 4096),
                   at("TA2", [128, 1024], F32, xt, 8192)]
        self.XS0 = at("XS0", [128, S], F32, xt, 0)
        self.TCS = at("TCS", [128, S], BF16, xt, 0)
        self.TSN = at("TSN", [128, S], BF16, xt, 4096)
        self.TI2 = at("TI2", [128, S], BF16, xt, 8192)
        self.T1C = at("T1C", [128, S], BF16, xt, 0)
        self.T1S = at("T1S", [128, S], BF16, xt, 4096)
        self.MU = at("MU", [128, S], F32, pl, 0)
        self.RS = at("RS", [128, S], F32, pl, 8192)
        self.Q1 = at("Q1", [128, S], F32, pl, 16384)
        self.Q2 = at("Q2", [128, S], F32, pl, 24576)
        self.BA = at("BA", [128, S], BF16, pl, 32768)
        self.BB = at("BB", [128, S], BF16, pl, 36864)
        self.BC = at("BC", [128, S], BF16, pl, 40960)
        self.SMALL = at("SMALL", [128, 512], F32, pl, 45056)
        self.BD = at("BD", [128, S], BF16, pl, 16384)
        self.BE = at("BE", [128, S], BF16, pl, 20480)
        self.SMF = at("SMF", [128, S], F32, pl, 24576)
        self.QARE = at("QARE", [128, 16 * 1024], BF16, yb, 0)
        self.KARE = at("KARE", [128, 8 * 1024], BF16, yb, 32768)
        self.VNARE = at("VNARE", [128, 8 * 1024], BF16, yb, 49152)
        self.HHARE = at("HHARE", [128, 32 * 1024], BF16, yb, 0)
        cst, tbl = self._cst_np, self._tbl_np
        self.C = {}
        for cn, cv in cst.items():
            dt = F32 if cn in _CONST_F32 else BF16
            self.C[cn] = sb("sb_" + cn, list(cv.shape), dt)
        for pn, shp in self.pshapes.items():
            self.C[pn] = sb("sb_" + pn, list(shp), F32)
        self.C["p_nbz"] = sb("sb_p_nbz", [128, 8], F32)
        self.psA = nc.alloc_psum_tensor("psA", [128, 4 * 512], F32)
        self.psB = nc.alloc_psum_tensor("psB", [128, 1 * 512], F32)
        self.psT = nc.alloc_psum_tensor("psT", [128, 2048], BF16)
        self.psF = nc.alloc_psum_tensor("psF", [128, 512], F32)
        self.slA = PSlots([self.psA[:, i * 512:(i + 1) * 512]
                           for i in range(4)])
        self.slB = PSlots([self.psB[:, i * 512:(i + 1) * 512]
                           for i in range(1)])
        self.wring = WRing(self.P, self.WARE, 6 * 1024)

    def load_consts(self):
        P = self.P
        for cn in self._cst_np:
            q = "y" if cn in _CONST_F32 else "g"
            P.dma(q, self.C[cn][:, :], self.dram[cn][:, :])
        for pn in self.pshapes:
            P.dma("y", self.C[pn][:, :], self.dram[pn][:, :])
        P.barrier()
        P.op("v", lambda e: e.tensor_scalar(
            out=self.C["p_nbz"][:, :], in0=self.C["p_bz"][:, :],
            scalar1=-1.0, scalar2=None, op0=ALU.mult), inc=True)
        P.barrier()

    def load_l0_tables(self):
        P = self.P
        for tn, h in (("t_cs", self.TCS), ("t_sn", self.TSN),
                      ("t_i2", self.TI2)):
            P.dma("g", h[:, :], self.dram[tn][:, :])

    def load_l1_tables(self):
        P = self.P
        for tn, h in (("t_r1c", self.T1C), ("t_r1s", self.T1S)):
            P.dma("g", h[:, :], self.dram[tn][:, :])

    def xs(self, t, c0=0, c1=S):
        return self.dram["xstate"][:, t * S + c0: t * S + c1]

    def xs3d(self, c0, n):
        return self.dram["xstate"][:, :].rearrange(
            "p (t s) -> p t s", t=NT)[:, :, c0:c0 + n]

    def phase_A(self):
        P = self.P
        cI = self.C["c_identf"]
        X = self.Y
        prev = [None, None]
        for sb_ in range(16):
            half = sb_ % 2
            stg = self.TA[half]
            mload = P.dma("y", stg[:, :],
                          self.dram["x_in"][sb_ * 128:(sb_ + 1) * 128, :],
                          waits=[prev[half]])
            mt = None
            for t in range(NT):
                src = stg[:, t * 128:(t + 1) * 128]
                pb = self.psA[:, half * 1024 + t * 128:
                              half * 1024 + t * 128 + 128]
                mt = P.op("t", lambda e, s_=src, p_=pb: e.transpose(
                    p_, s_, cI[:, :]),
                    waits=[mload, prev[half]], inc=(t == NT - 1))
            mv = P.op("v", lambda e, h_=half, s_=sb_: e.tensor_copy(
                out=X[:, :].rearrange("p (t s) -> p t s", t=NT)
                    [:, :, s_ * 128:s_ * 128 + 128],
                in_=self.psA[:, h_ * 1024:(h_ + 1) * 1024]
                    .rearrange("p (t s) -> p t s", s=128)),
                waits=[mt], inc=True)
            prev[half] = mv
        P.barrier()
        self.emit_ln(src_sbuf=X, dst=self.XN)
        P.barrier()
        self.emit_mingru(X)
        for t in range(NT):
            P.dma("y", self.xs(t), X[:, t * S:(t + 1) * S])
        P.barrier()

    def phase_E(self):
        P = self.P
        cI = self.C["c_identf"]
        prev = [None, None]
        prev_out = None
        for sb_ in range(16):
            half = sb_ % 2
            stg = self.TA[half]
            mload = P.dma("y", stg[:, :].rearrange("p (t s) -> p t s", s=128),
                          self.xs3d(sb_ * 128, 128), waits=[prev[half]])
            mt = None
            for t in range(NT):
                src = stg[:, t * 128:(t + 1) * 128]
                pb = self.psA[:, half * 1024 + t * 128:
                              half * 1024 + t * 128 + 128]
                mt = P.op("t", lambda e, s_=src, p_=pb: e.transpose(
                    p_, s_, cI[:, :]),
                    waits=[mload, prev[half]], inc=(t == NT - 1))
            mv = P.op("v", lambda e, h_=half: e.tensor_copy(
                out=self.TA[2][:, :],
                in_=self.psA[:, h_ * 1024:(h_ + 1) * 1024]),
                waits=[mt, prev_out], inc=True)
            prev[half] = mv
            prev_out = P.dma("y",
                             self.dram["x_out"][sb_ * 128:(sb_ + 1) * 128, :],
                             self.TA[2][:, :], waits=[mv])
        P.barrier()

    def emit_ln(self, src_sbuf, dst, shift=False, gain=False, gcol=0):
        P = self.P
        ones = self.C["c_ones"]
        RAW = self.OUT
        raw_ms = []
        for t in range(NT):
            if src_sbuf is not None:
                if shift and t >= 4:
                    m1 = P.op("s", lambda e, t_=t: e.copy(
                        out=RAW[:, t_ * S + 1:(t_ + 1) * S],
                        in_=src_sbuf[:, t_ * S:(t_ + 1) * S - 1]), inc=True)
                    m2 = P.op("v", lambda e, t_=t: e.memset(
                        RAW[:, t_ * S: t_ * S + 1], 0.0), inc=True)
                    raw_ms.append((m1, m2))
                else:
                    m1 = P.op("s", lambda e, t_=t: e.copy(
                        out=RAW[:, t_ * S:(t_ + 1) * S],
                        in_=src_sbuf[:, t_ * S:(t_ + 1) * S]), inc=True)
                    raw_ms.append((m1,))
            else:
                m1 = P.dma("g", RAW[:, t * S:(t + 1) * S], self.xs(t))
                raw_ms.append((m1,))
        for t in range(NT):
            for c in range(NCH):
                P.op("t", lambda e, t_=t, c_=c: e.matmul(
                    self.psA[:, c_ * 512:(c_ + 1) * 512], ones[:, :],
                    RAW[:, t_ * S + c_ * 512: t_ * S + (c_ + 1) * 512],
                    start=(t_ == 0), stop=(t_ == NT - 1)),
                    waits=list(raw_ms[t]),
                    inc=(t == NT - 1 and c == NCH - 1))
        msum = P.now("t")
        mmu = None
        for c in range(NCH):
            mmu = P.op("v", lambda e, c_=c: e.tensor_scalar(
                out=self.MU[:, c_ * 512:(c_ + 1) * 512],
                in0=self.psA[:, c_ * 512:(c_ + 1) * 512],
                scalar1=1.0 / 1024.0, scalar2=None, op0=ALU.mult),
                waits=[msum], inc=(c == NCH - 1))
        sqfree = [None, None]
        for t in range(NT):
            buf = self.BA if t % 2 == 0 else self.BB
            m = P.op("s", lambda e, t_=t, b_=buf: e.activation(
                out=b_[:, :], in_=RAW[:, t_ * S:(t_ + 1) * S],
                func=AF.Square),
                waits=list(raw_ms[t]) + [sqfree[t % 2]], inc=True)
            last = None
            for c in range(NCH):
                last = P.op("t", lambda e, t_=t, c_=c, b_=buf: e.matmul(
                    self.psA[:, c_ * 512:(c_ + 1) * 512], ones[:, :],
                    b_[:, c_ * 512:(c_ + 1) * 512],
                    start=(t_ == 0), stop=(t_ == NT - 1)),
                    waits=[m] + ([mmu] if t == 0 else []),
                    inc=(c == NCH - 1))
            sqfree[t % 2] = last
        msq = P.now("t")
        mq = P.op("s", lambda e: e.activation(
            out=self.Q1[:, :], in_=self.MU[:, :], func=AF.Square),
            waits=[mmu], inc=True)
        mv = None
        for c in range(NCH):
            mv = P.op("v", lambda e, c_=c: e.scalar_tensor_tensor(
                out=self.RS[:, c_ * 512:(c_ + 1) * 512],
                in0=self.psA[:, c_ * 512:(c_ + 1) * 512],
                scalar=1.0 / 1024.0,
                in1=self.Q1[:, c_ * 512:(c_ + 1) * 512],
                op0=ALU.mult, op1=ALU.subtract),
                waits=[msq, mq], inc=(c == NCH - 1))
        m1 = P.op("s", lambda e: e.activation(
            out=self.RS[:, :], in_=self.RS[:, :], func=AF.Ln,
            bias=self.C["c_scal"][:, 0:1]),
            waits=[mv], inc=True)
        m2 = P.op("s", lambda e: e.activation(
            out=self.RS[:, :], in_=self.RS[:, :], func=AF.Exp, scale=-0.5),
            waits=[m1], inc=True)
        ap_ms = []
        xt_free = [None, None]
        prev_mo = None
        for t in range(NT):
            if src_sbuf is not None:
                sap, sw = src_sbuf[:, t * S:(t + 1) * S], None
            else:
                buf = self.XS0 if t % 2 == 0 else self.Q2
                sw = P.dma("g", buf[:, :], self.xs(t), waits=[xt_free[t % 2]])
                sap = buf[:, :]
            if shift and t >= 4:
                ma = P.op("v", lambda e, s_=sap: e.tensor_sub(
                    out=self.Q1[:, 1:S], in0=s_[:, 0:S - 1],
                    in1=self.MU[:, 1:S]), waits=[mmu, sw, mq, mv, prev_mo], inc=True)
                mb = P.op("v", lambda e: e.tensor_scalar(
                    out=self.Q1[:, 0:1], in0=self.MU[:, 0:1], scalar1=-1.0,
                    scalar2=None, op0=ALU.mult), waits=[mmu], inc=True)
                tw = [ma, Prog.sync(mb)]
            else:
                ma = P.op("v", lambda e, s_=sap: e.tensor_sub(
                    out=self.Q1[:, :], in0=s_, in1=self.MU[:, :]),
                    waits=[mmu, sw, mq, mv, prev_mo], inc=True)
                tw = [ma]
            if not gain:
                mo = P.op("v", lambda e, t_=t: e.tensor_mul(
                    out=dst[:, t_ * S:(t_ + 1) * S], in0=self.Q1[:, :],
                    in1=self.RS[:, :]), waits=tw + [m2], inc=True)
            else:
                mt2 = P.op("v", lambda e: e.tensor_mul(
                    out=self.Q1[:, :], in0=self.Q1[:, :], in1=self.RS[:, :]),
                    waits=tw + [m2], inc=True)
                g_ap = self.C["p_l0g"][:, gcol * 8 + t: gcol * 8 + t + 1]
                b_ap = self.C["p_l0b"][:, gcol * 8 + t: gcol * 8 + t + 1]
                mo = P.op("v", lambda e, t_=t, g_=g_ap, b_=b_ap: e.tensor_scalar(
                    out=dst[:, t_ * S:(t_ + 1) * S], in0=self.Q1[:, :],
                    scalar1=g_, scalar2=b_, op0=ALU.mult, op1=ALU.add),
                    waits=[mt2], inc=True)
            ap_ms.append(mo)
            prev_mo = mo
            if src_sbuf is None:
                xt_free[t % 2] = mo
        self._ln_done = ap_ms
        return ap_ms

    def emit_mingru(self, X):
        P = self.P
        q1_free = [None] * NCH
        mu_free = None
        for et in range(NT):
            wz_ap, wz_ms, wz_fh = self.wring.load(
                self.dram["w_zh"][0].rearrange("(k p) e -> p k e", p=128)
                [:, :, et * 128:(et + 1) * 128], 1024)
            wh_ap, wh_ms, wh_fh = self.wring.load(
                self.dram["w_zh"][1].rearrange("(k p) e -> p k e", p=128)
                [:, :, et * 128:(et + 1) * 128], 1024)
            bz = self.C["p_bz"][:, et:et + 1]
            bh = self.C["p_bh"][:, et:et + 1]
            nbz = self.C["p_nbz"][:, et:et + 1]
            zms = []
            for c in range(NCH):
                iz, pz, fz = self.slA.next()
                mmz = None
                for kt in range(NT):
                    mmz = P.op("t", lambda e, k_=kt, c_=c, p_=pz, w_=wz_ap:
                               e.matmul(p_, w_[:, k_ * 128:(k_ + 1) * 128],
                                        self.XN[:, k_ * S + c_ * 512:
                                                k_ * S + (c_ + 1) * 512],
                                        start=(k_ == 0), stop=(k_ == NT - 1)),
                               waits=([wz_ms, fz] + list(self._ln_done)
                                      if kt == 0 else []),
                               inc=(kt == NT - 1))
                ih, ph, fh2 = self.slB.next()
                mmh = None
                for kt in range(NT):
                    mmh = P.op("t", lambda e, k_=kt, c_=c, p_=ph, w_=wh_ap:
                               e.matmul(p_, w_[:, k_ * 128:(k_ + 1) * 128],
                                        self.XN[:, k_ * S + c_ * 512:
                                                k_ * S + (c_ + 1) * 512],
                                        start=(k_ == 0), stop=(k_ == NT - 1)),
                               waits=[wh_ms, fh2] if kt == 0 else [],
                               inc=(kt == NT - 1))
                mz = P.op("s", lambda e, c_=c, p_=pz, b_=bz: e.activation(
                    out=self.Q1[:, c_ * 512:(c_ + 1) * 512], in_=p_,
                    func=AF.Sigmoid, bias=b_),
                    waits=[mmz, q1_free[c]], inc=True)
                mb = P.op("v", lambda e, c_=c, p_=ph, b_=bh:
                          e.scalar_tensor_tensor(
                              out=self.Q2[:, c_ * 512:(c_ + 1) * 512],
                              in0=p_, scalar=b_,
                              in1=self.Q1[:, c_ * 512:(c_ + 1) * 512],
                              op0=ALU.add, op1=ALU.mult),
                          waits=[mmh, mz], inc=True)
                self.slB.consumed(ih, mb)
                q1_free[c] = mb
                ma = P.op("s", lambda e, c_=c, p_=pz, nb_=nbz: e.activation(
                    out=self.MU[:, c_ * 512:(c_ + 1) * 512],
                    in_=p_, func=AF.Sigmoid, scale=-1.0, bias=nb_),
                    waits=[mmz, mu_free], inc=True)
                self.slA.consumed(iz, ma)
                zms += [mb, ma]
            WRing.free(wz_fh, P.now("t"))
            WRing.free(wh_fh, P.now("t"))
            msc = P.op("v", lambda e: e.tensor_tensor_scan(
                out=self.RS[:, :], data0=self.MU[:, :], data1=self.Q2[:, :],
                initial=0.0, op0=ALU.mult, op1=ALU.add), waits=zms, inc=True)
            mu_free = msc
            P.op("v", lambda e, t_=et: e.tensor_add(
                out=X[:, t_ * S:(t_ + 1) * S],
                in0=X[:, t_ * S:(t_ + 1) * S], in1=self.RS[:, :]),
                waits=[msc], inc=True)
        P.barrier()

    def build(self):
        P = self.P
        self._ln_done = []
        self.load_consts()
        if "A" in self.phases:
            self.phase_A()
        if "B" in self.phases:
            self.phase_B()
        if "C" in self.phases:
            self.phase_C()
        if "D" in self.phases:
            self.phase_D()
        self.phase_E()
        P.emit()
        return P.nc

    def mm_proj_chunk(self, ps, wap, c, shift, w0, src=None, nk=NT):
        P = self.P
        src = self.XN if src is None else src
        last = None
        for kt in range(nk):
            sh = shift and kt >= nk // 2
            if not sh:
                rhs = src[:, kt * S + c * 512: kt * S + (c + 1) * 512]
                out = ps
            elif c > 0:
                rhs = src[:, kt * S + c * 512 - 1: kt * S + (c + 1) * 512 - 1]
                out = ps
            else:
                rhs = src[:, kt * S: kt * S + 511]
                out = ps[:, 1:512]
            last = P.op("t", lambda e, k_=kt, r_=rhs, o_=out, w_=wap:
                        e.matmul(o_, w_[:, k_ * 128:(k_ + 1) * 128], r_,
                                 start=(k_ == 0), stop=(k_ == nk - 1)),
                        waits=(w0 if kt == 0 else ()), inc=(kt == nk - 1))
        return last

    def _qk_pipeline(self, lay, wap, w_ms, w_fh, dst, scale_ap, use_i2,
                     cast_now):
        P = self.P
        mev_all = []
        mln_all = []
        for c in range(NCH):
            islot, ps, fz = self.slA.next()
            mm = self.mm_proj_chunk(ps, wap, c, True, [w_ms, fz, cast_now])
            mev = P.op("s", lambda e, c_=c, p_=ps: e.copy(
                out=dst[:, c_ * 512:(c_ + 1) * 512], in_=p_),
                waits=[mm, self._dst_free.get(id(dst))], inc=True)
            self.slA.consumed(islot, mev)
            mev_all.append(mev)
            msq = P.op("s", lambda e, c_=c: e.activation(
                out=self.BA[:, c_ * 512:(c_ + 1) * 512],
                in_=dst[:, c_ * 512:(c_ + 1) * 512], func=AF.Square),
                waits=[mev, self._ba_free, self._ba_read], inc=True)
            mss = P.op("t", lambda e, c_=c: e.matmul(
                self.psF[:, :], self.C["c_bd64"][:, :],
                self.BA[:, c_ * 512:(c_ + 1) * 512], start=True, stop=True),
                waits=[msq, self._psF_free], inc=True)
            self._ba_free = mss
            mln = P.op("s", lambda e, c_=c: e.activation(
                out=self.MU[:, c_ * 512:(c_ + 1) * 512], in_=self.psF[:, :],
                func=AF.Ln, bias=self.C["c_scal"][:, 6:7]),
                waits=[mss, self._mu_free], inc=True)
            self._psF_free = mln
            mln_all.append(mln)
        mex = P.op("s", lambda e: e.activation(
            out=self.MU[:, :], in_=self.MU[:, :], func=AF.Exp, scale=-0.5),
            waits=mln_all, inc=True)
        msc = P.op("s", lambda e, s_=scale_ap: e.mul(
            out=self.MU[:, :], in_=self.MU[:, :], mul=s_),
            waits=[mex], inc=True)
        mqn = P.op("v", lambda e: e.tensor_mul(
            out=dst[:, :], in0=dst[:, :], in1=self.MU[:, :]),
            waits=[msc] + mev_all, inc=True)
        self._mu_free = mqn
        msm_all = []
        mro = None
        for c in range(NCH):
            islot, ps, fz = self.slB.next()
            mro = P.op("t", lambda e, c_=c, p_=ps: e.matmul(
                p_, self.C["c_rt128"][:, :],
                dst[:, c_ * 512:(c_ + 1) * 512], start=True, stop=True),
                waits=[mqn, fz], inc=True)
            msm = P.op("v", lambda e, c_=c, p_=ps: e.tensor_mul(
                out=self.SMF[:, c_ * 512:(c_ + 1) * 512], in0=p_,
                in1=self.TSN[:, c_ * 512:(c_ + 1) * 512]),
                waits=[mro] + self._sm_free[0:4], inc=True)
            self.slB.consumed(islot, msm)
            msm_all.append(msm)
        mtc = P.op("v", lambda e: e.tensor_mul(
            out=dst[:, :], in0=dst[:, :], in1=self.TCS[:, :]),
            waits=[mro], inc=True)
        mad = P.op("v", lambda e: e.tensor_add(
            out=dst[:, :], in0=dst[:, :], in1=self.SMF[:, :]),
            waits=[mtc] + msm_all, inc=True)
        self._smf_read = mad
        if use_i2:
            mad = P.op("v", lambda e: e.tensor_mul(
                out=dst[:, :], in0=dst[:, :], in1=self.TI2[:, :]),
                waits=[mad], inc=True)
        WRing.free(w_fh, P.now("t"))
        return mad

    def l0_attn_hg(self, lay, hg, cast_now):
        P = self.P
        def colpan(sel, c0):
            return self.dram["w_l0qkv"][lay].rearrange(
                "(k p) e -> p k e", p=128)[:, :, c0 + hg * 128:
                                           c0 + hg * 128 + 128]
        wq, wq_ms, wq_fh = self.wring.load(colpan(0, 0), 1024)
        wk, wk_ms, wk_fh = self.wring.load(colpan(1, 1024), 1024)
        wv, wv_ms, wv_fh = self.wring.load(colpan(2, 2048), 1024)
        vt_ms = []
        for c in range(NCH):
            islot, ps, fz = self.slA.next()
            mm = self.mm_proj_chunk(ps, wv, c, True, [wv_ms, fz, cast_now])
            mev = P.op("s", lambda e, c_=c, p_=ps: e.copy(
                out=self.BC[:, c_ * 512:(c_ + 1) * 512], in_=p_),
                waits=[mm, self._bc_free], inc=True)
            self.slA.consumed(islot, mev)
            vt_ms.append(mev)
        WRing.free(wv_fh, P.now("t"))
        qs_ap = self.C["p_l0qs"][:, lay:lay + 1]
        ks_ap = self.C["p_l0ks"][:, lay:lay + 1]
        mq = self._qk_pipeline(lay, wq, wq_ms, wq_fh, self.BD, qs_ap,
                               False, cast_now)
        mk = self._qk_pipeline(lay, wk, wk_ms, wk_fh, self.BE, ks_ap,
                               True, cast_now)
        vn_ms = []
        for hh in (0, 1):
            mtr = None
            for jb in range(16):
                mtr = P.op("t", lambda e, h_=hh, j_=jb: e.transpose(
                    self.psT[:, j_ * 64:(j_ + 1) * 64],
                    self.BC[h_ * 64:h_ * 64 + 64, j_ * 128:(j_ + 1) * 128],
                    self.C["c_identb"][h_ * 64:h_ * 64 + 64,
                                       h_ * 64:h_ * 64 + 64]),
                    waits=vt_ms + [self._psT_free[0], self._pt_read[0]],
                    inc=(jb == 15))
            mvn = P.op("v", lambda e, h_=hh: e.tensor_copy(
                out=self.BA[:, :].rearrange("p (b c) -> p b c", b=16)
                    [:, :, h_ * 64:h_ * 64 + 64],
                in_=self.psT[:, 0:1024].rearrange("p (b c) -> p b c", c=64)),
                waits=[mtr, self._ba_free], inc=True)
            self._psT_free[0] = mvn
            vn_ms.append(mvn)
        if self.dbg == "qk0" and hg == 0:
            P.barrier()
            for src_, ti in ((self.BD, 0), (self.BE, 1), (self.BC, 2),
                             (self.BA, 3)):
                P.op("v", lambda e, s_=src_, t_=ti: e.tensor_copy(
                    out=self.Y[:, t_ * S:(t_ + 1) * S], in_=s_[:, :]),
                    inc=True)
            self._dump_y()
            self._dbg_stop = True
            return []
        out_ms = []
        units = [(hh, w) for hh in (0, 1) for w in range(8)]
        pend = []
        for ui, (hh, w) in enumerate(units):
            pend.append(self._l0_attn_s(hg, hh, w, ui, [mq, mk]))
            if len(pend) > 2:
                out_ms.append(self._l0_attn_av(vn_ms, pend.pop(0)))
        while pend:
            out_ms.append(self._l0_attn_av(vn_ms, pend.pop(0)))
        tnow = P.now("t")
        self._ba_read = tnow
        self._bc_free = tnow
        self._dst_free[id(self.BD)] = tnow
        self._dst_free[id(self.BE)] = tnow
        return out_ms

    def _l0_attn_s(self, hg, hh, w, ui, qk_ms):
        P = self.P
        par = ui % 2
        hr = 64 * hh
        width = 512 if w > 0 else 256
        j0 = (w * 256 - 256) if w > 0 else 0
        mpps = []
        p0s = []
        for ib in (0, 1):
            sl = par * 2 + ib
            i0 = w * 256 + ib * 128
            mask = self.C["c_mke"] if ib == 0 else self.C["c_mko"]
            mask_ap = mask[:, 0:width] if w > 0 else mask[:, 256:512]
            islot, ps, fz = self.slA.next()
            msmm = P.op("t", lambda e, p_=ps, i0_=i0, hr_=hr, j0_=j0,
                        wd_=width: e.matmul(
                p_[:, 0:wd_],
                self.BD[hr_:hr_ + 64, i0_:i0_ + 128],
                self.BE[hr_:hr_ + 64, j0_:j0_ + wd_],
                start=True, stop=True),
                waits=qk_ms + [fz], inc=True)
            msm = P.op("v", lambda e, p_=ps, m_=mask_ap, s_=sl,
                       wd_=width: e.scalar_tensor_tensor(
                           out=self.SMF[:, s_ * 512: s_ * 512 + wd_],
                           in0=p_[:, 0:wd_], scalar=8.0, in1=m_,
                           op0=ALU.mult, op1=ALU.add),
                       waits=[msmm, self._sm_free[sl]], inc=True)
            self.slA.consumed(islot, msm)
            mnm = P.op("v", lambda e, s_=sl, wd_=width: e.reduce_max(
                out=self.SMALL[:, s_:s_ + 1],
                in_=self.SMF[:, s_ * 512: s_ * 512 + wd_],
                axis=AX.X, negate=True),
                waits=[msm, self._stat_free[sl]], inc=True)
            p0 = self.BB[:, sl * 512: sl * 512 + width]
            mz0 = P.op("v", lambda e, s_=sl: e.memset(
                self.SMALL[:, 4 + s_:5 + s_], 0.0),
                waits=[self._stat_free[sl]], inc=True)
            mex = P.op("s", lambda e, p0_=p0, s_=sl, wd_=width:
                       e.activation(
                out=p0_, in_=self.SMF[:, s_ * 512: s_ * 512 + wd_],
                func=AF.Exp, bias=self.SMALL[:, s_:s_ + 1],
                accum_out=self.SMALL[:, 4 + s_:5 + s_]),
                waits=[mnm, self._p0_free[sl], mz0], inc=True)
            self._sm_free[sl] = mex
            miz = P.op("v", lambda e, s_=sl: e.reciprocal(
                out=self.SMALL[:, 8 + s_:9 + s_],
                in_=self.SMALL[:, 4 + s_:5 + s_]), waits=[mex], inc=True)
            mpp = P.op("v", lambda e, p0_=p0, s_=sl: e.tensor_scalar(
                out=p0_, in0=p0_, scalar1=self.SMALL[:, 8 + s_:9 + s_],
                scalar2=None, op0=ALU.mult), waits=[Prog.sync(miz)],
                inc=True)
            self._stat_free[sl] = mpp
            mpps.append(mpp)
            p0s.append(p0)
        return (hg, hh, w, ui, mpps, p0s)

    def _l0_attn_av(self, vn_ms, ctx):
        P = self.P
        hg, hh, w, ui, mpps, p0s = ctx
        par = ui % 2
        hr = 64 * hh
        njb = 4 if w > 0 else 2
        tbase = par * 1024
        tr_last = None
        for ib in (0, 1):
            p0 = p0s[ib]
            for jb in range(njb):
                jbg = jb + (2 if w == 0 else 0)
                tr_last = P.op("t", lambda e, p0_=p0, j_=jb, jg_=jbg,
                               ib_=ib, tb_=tbase: e.transpose(
                    self.psT[:, tb_ + jg_ * 256 + ib_ * 128:
                             tb_ + jg_ * 256 + ib_ * 128 + 128],
                    p0_[:, j_ * 128:(j_ + 1) * 128],
                    self.C["c_identb"][:, :]),
                    waits=[mpps[ib], self._psT_free[par]],
                    inc=(ib == 1 and jb == njb - 1))
        mpt = P.op("s", lambda e, tb_=tbase, pr_=par: e.copy(
            out=self.BC[:, pr_ * 1024: pr_ * 1024 + 1024],
            in_=self.psT[:, tb_: tb_ + 1024]),
            waits=[tr_last, self._pt_read[par]], inc=True)
        self._psT_free[par] = mpt
        for ib in (0, 1):
            self._p0_free[par * 2 + ib] = mpt
        jlist = list(range(2, 4)) if w == 0 else list(range(4))
        mav = None
        for n_, jbg in enumerate(jlist):
            blk = 2 * w - 2 + jbg
            mav = P.op("t", lambda e, j_=jbg, b_=blk, pr_=par,
                       st_=(n_ == 0), sp_=(n_ == len(jlist) - 1): e.matmul(
                self.psF[:, 0:256],
                self.BA[:, b_ * 128: b_ * 128 + 128],
                self.BC[:, pr_ * 1024 + j_ * 256:
                        pr_ * 1024 + j_ * 256 + 256],
                start=st_, stop=sp_),
                waits=[mpt] + vn_ms + [self._psF_free],
                inc=(n_ == len(jlist) - 1))
        self._pt_read[par] = mav
        mo = P.op("s", lambda e, hr_=hr, hg_=hg, w_=w: e.copy(
            out=self.OUT[hr_:hr_ + 64,
                         hg_ * S + w_ * 256: hg_ * S + w_ * 256 + 256],
            in_=self.psF[hr_:hr_ + 64, 0:256]), waits=[mav], inc=True)
        self._psF_free = mo
        return mo

    def l0_wo(self, lay):
        P = self.P
        for dt in range(NT):
            slab, w_ms, fh = self.wring.load(
                self.dram["w_l0o"][lay].rearrange("(k p) e -> p k e", p=128)
                [:, :, dt * 128:(dt + 1) * 128], 1024)
            for c in range(NCH):
                islot, ps, fz = self.slA.next()
                mm = self.mm_proj_chunk(ps, slab, c, False, [w_ms, fz],
                                        src=self.OUT)
                mev = P.op("v", lambda e, p_=ps, d_=dt, c_=c: e.tensor_add(
                    out=self.Y[:, d_ * S + c_ * 512: d_ * S + (c_ + 1) * 512],
                    in0=p_,
                    in1=self.Y[:, d_ * S + c_ * 512: d_ * S + (c_ + 1) * 512]),
                    waits=[mm], inc=True)
                self.slA.consumed(islot, mev)
            WRing.free(fh, P.now("t"))

    def l0_ffn(self, lay):
        P = self.P
        qsplit = [(0, 6), (6, 12), (12, 18), (18, 22)]
        g_free = [None] * NCH
        hh_free = None
        for qr, (f0, f1) in enumerate(qsplit):
            for fa in range(f0, f1):
                wa, wa_ms, wa_fh = self.wring.load(
                    self.dram["w_l0w1p"][lay].rearrange(
                        "(k p) e -> p k e", p=128)
                    [:, :, fa * 128:(fa + 1) * 128], 1024)
                wg, wg_ms, wg_fh = self.wring.load(
                    self.dram["w_l0w1p"][lay].rearrange(
                        "(k p) e -> p k e", p=128)
                    [:, :, 2816 + fa * 128: 2816 + (fa + 1) * 128], 1024)
                b1a = self.C["p_l0b1"][:, lay * 44 + fa: lay * 44 + fa + 1]
                b1g = self.C["p_l0b1"][:, lay * 44 + 22 + fa:
                                       lay * 44 + 22 + fa + 1]
                hhofs = (fa - f0) * S
                for c in range(NCH):
                    ia, pa, fza = self.slA.next()
                    mma = self.mm_proj_chunk(pa, wa, c, False, [wa_ms, fza])
                    ig, pg, fzg = self.slB.next()
                    mmg = self.mm_proj_chunk(pg, wg, c, False, [wg_ms, fzg])
                    mg = P.op("s", lambda e, p_=pg, c_=c, b_=b1g:
                              e.activation(
                                  out=self.BA[:, c_ * 512:(c_ + 1) * 512],
                                  in_=p_, func=AF.Gelu, bias=b_),
                              waits=[mmg, g_free[c]], inc=True)
                    self.slB.consumed(ig, mg)
                    mh = P.op("v", lambda e, p_=pa, c_=c, b_=b1a, o_=hhofs:
                              e.scalar_tensor_tensor(
                                  out=self.OUT[:, o_ + c_ * 512:
                                               o_ + (c_ + 1) * 512],
                                  in0=p_, scalar=b_,
                                  in1=self.BA[:, c_ * 512:(c_ + 1) * 512],
                                  op0=ALU.add, op1=ALU.mult),
                              waits=[mma, mg, hh_free], inc=True)
                    self.slA.consumed(ia, mh)
                    g_free[c] = mh
                WRing.free(wa_fh, P.now("t"))
                WRing.free(wg_fh, P.now("t"))
            hh_ready = P.now("v")
            nk = f1 - f0
            for dt in range(NT):
                slab, w_ms, fh = self.wring.load(
                    self.dram["w_l0w2p"][lay].rearrange(
                        "(k p) d -> p k d", p=128)
                    [:, f0:f1, dt * 128:(dt + 1) * 128], nk * 128)
                b2 = self.C["p_l0b2"][:, lay * 8 + dt: lay * 8 + dt + 1]
                for c in range(NCH):
                    islot, ps, fz = self.slA.next()
                    mm = self.mm_proj_chunk(ps, slab, c, False,
                                            [w_ms, fz, hh_ready],
                                            src=self.OUT, nk=nk)
                    if qr == 0:
                        mev = P.op("v", lambda e, p_=ps, d_=dt, c_=c, b_=b2:
                                   e.scalar_tensor_tensor(
                                       out=self.Y[:, d_ * S + c_ * 512:
                                                  d_ * S + (c_ + 1) * 512],
                                       in0=p_, scalar=b_,
                                       in1=self.Y[:, d_ * S + c_ * 512:
                                                  d_ * S + (c_ + 1) * 512],
                                       op0=ALU.add, op1=ALU.add),
                                   waits=[mm], inc=True)
                    else:
                        mev = P.op("v", lambda e, p_=ps, d_=dt, c_=c:
                                   e.tensor_add(
                                       out=self.Y[:, d_ * S + c_ * 512:
                                                  d_ * S + (c_ + 1) * 512],
                                       in0=p_,
                                       in1=self.Y[:, d_ * S + c_ * 512:
                                                  d_ * S + (c_ + 1) * 512]),
                                   waits=[mm], inc=True)
                    self.slA.consumed(islot, mev)
                WRing.free(fh, P.now("t"))
            hh_free = P.now("t")

    def _dump_y(self):
        P = self.P
        P.barrier()
        for t in range(NT):
            P.dma("y", self.xs(t), self.Y[:, t * S:(t + 1) * S])
        P.barrier()

    def phase_B(self):
        P = self.P
        P.barrier()
        self.emit_ln(src_sbuf=None, dst=self.Y)
        P.barrier()
        if self.dbg == "ln":
            self._dump_y()
            return
        self.load_l0_tables()
        P.barrier()
        for lay in range(self.l0_layers):
            for t in range(NT):
                P.op("s", lambda e, t_=t: e.copy(
                    out=self.XN[:, t_ * S:(t_ + 1) * S],
                    in_=self.Y[:, t_ * S:(t_ + 1) * S]),
                    inc=(t == NT - 1))
            cast_now = P.now("s")
            self._reset_attn_tracking()
            self._dbg_stop = False
            for hg in range(NT):
                self.l0_attn_hg(lay, hg, cast_now)
                if self._dbg_stop:
                    return
            P.barrier()
            if self.dbg == "out0" and lay == 0:
                for t in range(NT):
                    P.op("v", lambda e, t_=t: e.tensor_copy(
                        out=self.Y[:, t_ * S:(t_ + 1) * S],
                        in_=self.OUT[:, t_ * S:(t_ + 1) * S]), inc=True)
                self._dump_y()
                return
            self.l0_wo(lay)
            P.barrier()
            if self.dbg == "attn" and lay == 0:
                self._dump_y()
                return
            self.emit_ln(src_sbuf=self.Y, dst=self.XN, shift=True,
                         gain=True, gcol=lay)
            P.barrier()
            if self.dbg == "ffnln" and lay == 0:
                for t in range(NT):
                    P.op("v", lambda e, t_=t: e.tensor_copy(
                        out=self.Y[:, t_ * S:(t_ + 1) * S],
                        in_=self.XN[:, t_ * S:(t_ + 1) * S]), inc=True)
                self._dump_y()
                return
            self.l0_ffn(lay)
            P.barrier()
        prev = [None, None]
        for t in range(NT):
            buf = self.XS0 if t % 2 == 0 else self.Q2
            ml = P.dma("y", buf[:, :], self.xs(t), waits=[prev[t % 2]])
            ma = P.op("v", lambda e, t_=t, b_=buf: e.tensor_add(
                out=self.Y[:, t_ * S:(t_ + 1) * S],
                in0=self.Y[:, t_ * S:(t_ + 1) * S], in1=b_[:, :]),
                waits=[ml], inc=True)
            prev[t % 2] = ma
            P.dma("y", self.xs(t), self.Y[:, t * S:(t + 1) * S], waits=[ma])
        P.barrier()

    def _reset_attn_tracking(self):
        self._psA_free = None
        self._vtt_free = None
        self._hh_free = None
        self._wbc_read = None
        self._wbc_ms = None
        self._sidx = 0
        self._dst_free = {}
        self._ba_free = None
        self._ba_read = None
        self._bc_free = None
        self._mu_free = None
        self._smf_free = None
        self._smf_read = None
        self._psF_free = None
        self._psT_free = [None, None]
        self._pt_read = [None, None]
        self._rot_free = [None, None]
        self._sm_free = [None] * 4
        self._p0_free = [None] * 4
        self._stat_free = [None] * 4

    def _l1_rope(self, tile_ap, rot_rows, ready):
        P = self.P
        msm_all = []
        mro = None
        for c in range(NCH):
            islot, ps, fz = self.slB.next()
            mro = P.op("t", lambda e, c_=c, p_=ps, rr_=rot_rows: e.matmul(
                p_[0:rr_, :], self.C["c_rt128"][0:rr_, 0:rr_],
                tile_ap[0:rr_, c_ * 512:(c_ + 1) * 512],
                start=True, stop=True),
                waits=[ready, fz], inc=True)
            msm = P.op("v", lambda e, c_=c, p_=ps, rr_=rot_rows:
                       e.tensor_mul(
                           out=self.SMF[0:rr_, c_ * 512:(c_ + 1) * 512],
                           in0=p_[0:rr_, :],
                           in1=self.T1S[0:rr_, c_ * 512:(c_ + 1) * 512]),
                       waits=[mro, self._smf_read], inc=True)
            self.slB.consumed(islot, msm)
            msm_all.append(msm)
        mtc = P.op("v", lambda e, rr_=rot_rows: e.tensor_mul(
            out=tile_ap[0:rr_, :], in0=tile_ap[0:rr_, :],
            in1=self.T1C[0:rr_, :]), waits=[mro], inc=True)
        mad = P.op("v", lambda e, rr_=rot_rows: e.tensor_add(
            out=tile_ap[0:rr_, :], in0=tile_ap[0:rr_, :],
            in1=self.SMF[0:rr_, :]), waits=[mtc] + msm_all, inc=True)
        self._smf_read = mad
        return mad

    def phase_C(self):
        P = self.P
        self.PT1 = self.BE
        self.VTT = self.BC
        P.barrier()
        self.emit_ln(src_sbuf=None, dst=self.XN)
        P.barrier()
        self.load_l1_tables()
        P.barrier()
        self._reset_attn_tracking()
        self._smf_read = None
        wqa, wqa_ms, wqa_fh = self.wring.load(
            self.dram["w_1qa"].rearrange("(k p) e -> p k e", p=128), 512)
        t1_ms = []
        for c in range(NCH):
            islot, ps, fz = self.slB.next()
            mm = None
            for kt in range(NT):
                mm = P.op("t", lambda e, k_=kt, c_=c, p_=ps, w_=wqa:
                          e.matmul(p_[0:64, :], w_[:, k_ * 64:(k_ + 1) * 64],
                                   self.XN[:, k_ * S + c_ * 512:
                                           k_ * S + (c_ + 1) * 512],
                                   start=(k_ == 0), stop=(k_ == NT - 1)),
                          waits=[wqa_ms, fz] if kt == 0 else (),
                          inc=(kt == NT - 1))
            mev = P.op("s", lambda e, c_=c, p_=ps: e.copy(
                out=self.BA[0:64, c_ * 512:(c_ + 1) * 512], in_=p_[0:64, :]),
                waits=[mm], inc=True)
            self.slB.consumed(islot, mev)
            t1_ms.append(mev)
        WRing.free(wqa_fh, P.now("t"))
        wqb = self.BB[0:64, 0:1024]
        wqb_ms = P.dma("g", wqb, self.dram["w_1qb"][:, :])
        q_ms = []
        for et in range(NT):
            evs = []
            for c in range(NCH):
                islot, ps, fz = self.slA.next()
                mm = P.op("t", lambda e, c_=c, p_=ps, e_=et, w_=wqb:
                          e.matmul(p_, w_[:, e_ * 128:(e_ + 1) * 128],
                                   self.BA[0:64, c_ * 512:(c_ + 1) * 512],
                                   start=True, stop=True),
                          waits=[wqb_ms] + t1_ms + [fz], inc=True)
                mev = P.op("v", lambda e, c_=c, p_=ps, e_=et: e.tensor_copy(
                    out=self.QARE[:, e_ * S + c_ * 512:
                                  e_ * S + (c_ + 1) * 512], in_=p_),
                    waits=[mm], inc=True)
                self.slA.consumed(islot, mev)
                evs.append(mev)
            mq = self._l1_rope(self.QARE[:, et * S:(et + 1) * S], 128,
                               evs[-1])
            q_ms.append(mq)
        wkva, wkva_ms, wkva_fh = self.wring.load(
            self.dram["w_1kva"].rearrange("(k p) e -> p k e", p=128), 256)
        t2_ms = []
        for c in range(NCH):
            islot, ps, fz = self.slB.next()
            mm = None
            for kt in range(NT):
                mm = P.op("t", lambda e, k_=kt, c_=c, p_=ps, w_=wkva:
                          e.matmul(p_[0:32, :], w_[:, k_ * 32:(k_ + 1) * 32],
                                   self.XN[:, k_ * S + c_ * 512:
                                           k_ * S + (c_ + 1) * 512],
                                   start=(k_ == 0), stop=(k_ == NT - 1)),
                          waits=[wkva_ms, fz] + t1_ms if kt == 0 else (),
                          inc=(kt == NT - 1))
            mev = P.op("s", lambda e, c_=c, p_=ps: e.copy(
                out=self.BA[0:32, c_ * 512:(c_ + 1) * 512], in_=p_[0:32, :]),
                waits=[mm] + q_ms, inc=True)
            self.slB.consumed(islot, mev)
            t2_ms.append(mev)
        WRing.free(wkva_fh, P.now("t"))
        wk2 = self.BB[0:32, 1024:1536]
        wk2_ms = P.dma("g", wk2, self.dram["w_1kvb_k2"][:, :])
        k_ms = []
        for g in range(4):
            evs = []
            for c in range(NCH):
                islot, ps, fz = self.slA.next()
                mm = P.op("t", lambda e, c_=c, p_=ps, g_=g, w_=wk2:
                          e.matmul(p_, w_[:, g_ * 128:(g_ + 1) * 128],
                                   self.BA[0:32, c_ * 512:(c_ + 1) * 512],
                                   start=True, stop=True),
                          waits=[wk2_ms] + t2_ms + [fz], inc=True)
                mev = P.op("v", lambda e, c_=c, p_=ps, g_=g: e.tensor_copy(
                    out=self.KARE[:, g_ * S + c_ * 512:
                                  g_ * S + (c_ + 1) * 512], in_=p_),
                    waits=[mm], inc=True)
                self.slA.consumed(islot, mev)
                evs.append(mev)
            mk = self._l1_rope(self.KARE[:, g * S:(g + 1) * S], 128,
                               evs[-1])
            k_ms.append(mk)
        wv = self.BB[0:32, 1536:1792]
        wv_ms = P.dma("g", wv, self.dram["w_1kvb_v"][:, :])
        vn_ms = []
        for g in range(4):
            evs = []
            for c in range(NCH):
                islot, ps, fz = self.slB.next()
                mm = P.op("t", lambda e, c_=c, p_=ps, g_=g, w_=wv:
                          e.matmul(p_[0:64, :], w_[:, g_ * 64:(g_ + 1) * 64],
                                   self.BA[0:32, c_ * 512:(c_ + 1) * 512],
                                   start=True, stop=True),
                          waits=[wv_ms] + t2_ms + [fz, self._vtt_free],
                          inc=True)
                mev = P.op("s", lambda e, c_=c, p_=ps: e.copy(
                    out=self.VTT[0:64, c_ * 512:(c_ + 1) * 512],
                    in_=p_[0:64, :]), waits=[mm], inc=True)
                self.slB.consumed(islot, mev)
                evs.append(mev)
            mtr = None
            for jb in range(16):
                mtr = P.op("t", lambda e, j_=jb: e.transpose(
                    self.psT[:, j_ * 64:(j_ + 1) * 64],
                    self.VTT[0:64, j_ * 128:(j_ + 1) * 128],
                    self.C["c_identb"][0:64, 0:64]),
                    waits=evs + [self._psT_free[0], self._pt_read[0]],
                    inc=(jb == 15))
            self._vtt_free = mtr
            mv1 = P.op("v", lambda e, g_=g: e.tensor_copy(
                out=self.VNARE[:, g_ * 2048: (g_ + 1) * 2048].rearrange(
                    "p (b c) -> p b c", c=128)[:, :, 0:64],
                in_=self.psT[:, 0:1024].rearrange("p (b c) -> p b c", c=64)),
                waits=[mtr], inc=True)
            mv2 = P.op("v", lambda e, g_=g: e.tensor_copy(
                out=self.VNARE[:, g_ * 2048: (g_ + 1) * 2048].rearrange(
                    "p (b c) -> p b c", c=128)[:, :, 64:128],
                in_=self.psT[:, 0:1024].rearrange("p (b c) -> p b c", c=64)),
                waits=[mtr], inc=True)
            self._psT_free[0] = mv2
            vn_ms.append(mv2)
        P.barrier()
        for h in range(16):
            self._l1_attn_head(h, q_ms, k_ms, vn_ms)
        P.barrier()
        xprev = [None, None]
        for dt in range(NT):
            slab, w_ms, fh = self.wring.load(
                self.dram["w_1o"].rearrange("(k p) e -> p k e", p=128)
                [:, :, dt * 128:(dt + 1) * 128], 1024)
            buf = self.XS0 if dt % 2 == 0 else self.Q2
            ml = P.dma("y", buf[:, :], self.xs(dt), waits=[xprev[dt % 2]])
            bout = self.C["p_bout"][:, dt:dt + 1]
            mevs = []
            for c in range(NCH):
                islot, ps, fz = self.slA.next()
                mm = self.mm_proj_chunk(ps, slab, c, False, [w_ms, fz],
                                        src=self.OUT)
                mev = P.op("v", lambda e, p_=ps, c_=c, b_=bout, bf_=buf:
                           e.scalar_tensor_tensor(
                               out=bf_[:, c_ * 512:(c_ + 1) * 512],
                               in0=p_, scalar=b_,
                               in1=bf_[:, c_ * 512:(c_ + 1) * 512],
                               op0=ALU.add, op1=ALU.add),
                           waits=[mm, ml], inc=True)
                self.slA.consumed(islot, mev)
                mevs.append(mev)
            mw = P.dma("y", self.xs(dt), buf[:, :], waits=mevs)
            xprev[dt % 2] = mw
            WRing.free(fh, P.now("t"))
        P.barrier()

    def _l1_attn_head(self, h, q_ms, k_ms, vn_ms):
        P = self.P
        qt = h // 2
        hr = 64 * (h % 2)
        g = h // 4
        for ib in range(16):
            L = (ib + 1) * 128
            mm = None
            for c0 in range(0, L, 512):
                w_ = min(512, L - c0)
                mm = P.op("t", lambda e, c0_=c0, w2_=w_, ib_=ib, qt_=qt,
                          hr_=hr, g_=g: e.matmul(
                    self.psA[:, c0_:c0_ + w2_],
                    self.QARE[hr_:hr_ + 64,
                              qt_ * S + ib_ * 128: qt_ * S + ib_ * 128 + 128],
                    self.KARE[hr_:hr_ + 64, g_ * S + c0_: g_ * S + c0_ + w2_],
                    start=True, stop=True),
                    waits=[q_ms[qt], k_ms[g], self._psA_free], inc=True)
            msc = P.op("v", lambda e, L_=L: e.tensor_scalar(
                out=self.SMF[:, 0:L_], in0=self.psA[:, 0:L_],
                scalar1=0.125, scalar2=None, op0=ALU.mult),
                waits=[mm, self._sm_free[0]], inc=True)
            mdg = P.op("v", lambda e, L_=L: e.tensor_add(
                out=self.SMF[:, L_ - 128:L_], in0=self.SMF[:, L_ - 128:L_],
                in1=self.C["c_diag"][:, :]), waits=[msc], inc=True)
            self._psA_free = mdg
            mnm = P.op("v", lambda e, L_=L: e.reduce_max(
                out=self.SMALL[:, 0:1], in_=self.SMF[:, 0:L_],
                axis=AX.X, negate=True),
                waits=[mdg, self._stat_free[0]], inc=True)
            mz0 = P.op("v", lambda e: e.memset(
                self.SMALL[:, 2:3], 0.0), waits=[self._stat_free[0]],
                inc=True)
            mex = P.op("s", lambda e, L_=L: e.activation(
                out=self.BD[:, 0:L_], in_=self.SMF[:, 0:L_],
                func=AF.Exp, bias=self.SMALL[:, 0:1],
                accum_out=self.SMALL[:, 2:3]),
                waits=[mnm, mz0, self._p0_free[0]], inc=True)
            self._sm_free[0] = mex
            miz = P.op("v", lambda e: e.reciprocal(
                out=self.SMALL[:, 4:5], in_=self.SMALL[:, 2:3]),
                waits=[mex], inc=True)
            mpp = P.op("v", lambda e, L_=L: e.tensor_scalar(
                out=self.BD[:, 0:L_], in0=self.BD[:, 0:L_],
                scalar1=self.SMALL[:, 4:5], scalar2=None, op0=ALU.mult),
                waits=[Prog.sync(miz)], inc=True)
            self._stat_free[0] = mpp
            mpt_all = []
            for bi, base in enumerate(range(0, ib + 1, 8)):
                hi = min(base + 8, ib + 1)
                par = bi % 2
                mtr = None
                for jb in range(base, hi):
                    mtr = P.op("t", lambda e, j_=jb, b0_=base, pr_=par:
                               e.transpose(
                        self.psT[:, pr_ * 1024 + (j_ - b0_) * 128:
                                 pr_ * 1024 + (j_ - b0_) * 128 + 128],
                        self.BD[:, j_ * 128:(j_ + 1) * 128],
                        self.C["c_identb"][:, :]),
                        waits=[mpp, self._psT_free[par]], inc=(jb == hi - 1))
                mpt = P.op("s", lambda e, b0_=base, hi_=hi, pr_=par:
                           e.copy(
                    out=self.PT1[:, b0_ * 128: hi_ * 128],
                    in_=self.psT[:, pr_ * 1024:
                                 pr_ * 1024 + (hi_ - b0_) * 128]),
                    waits=[mtr, self._pt_read[par]], inc=True)
                self._psT_free[par] = mpt
                mpt_all.append(mpt)
            self._p0_free[0] = mpt_all[-1]
            mav = None
            for jb in range(ib + 1):
                mav = P.op("t", lambda e, j_=jb, g_=g,
                           st_=(jb == 0), sp_=(jb == ib): e.matmul(
                    self.psF[:, 0:128],
                    self.VNARE[:, g_ * 2048 + j_ * 128:
                               g_ * 2048 + (j_ + 1) * 128],
                    self.PT1[:, j_ * 128:(j_ + 1) * 128],
                    start=st_, stop=sp_),
                    waits=mpt_all + [vn_ms[g], self._psF_free],
                    inc=(jb == ib))
            self._pt_read[0] = mav
            self._pt_read[1] = mav
            mo = P.op("s", lambda e, h_=h, hr_=hr, ib_=ib: e.copy(
                out=self.OUT[hr_:hr_ + 64,
                             (h_ // 2) * S + ib_ * 128:
                             (h_ // 2) * S + ib_ * 128 + 128],
                in_=self.psF[hr_:hr_ + 64, 0:128]), waits=[mav], inc=True)
            self._psF_free = mo

    def phase_D(self):
        P = self.P
        P.barrier()
        self.emit_ln(src_sbuf=None, dst=self.XN)
        P.barrier()
        self._reset_attn_tracking()
        wg, wg_ms, wg_fh = self.wring.load(
            self.dram["w_g"].rearrange("(k p) e -> p k e", p=128), 64)
        bg = self.C["p_bg"][0:8, 0:1]
        lg_ms = []
        for c in range(NCH):
            mm = None
            for kt in range(NT):
                mm = P.op("t", lambda e, k_=kt, c_=c, w_=wg: e.matmul(
                    self.psF[0:8, 0:512], w_[:, k_ * 8:(k_ + 1) * 8],
                    self.XN[:, k_ * S + c_ * 512: k_ * S + (c_ + 1) * 512],
                    start=(k_ == 0), stop=(k_ == NT - 1)),
                    waits=[wg_ms, self._psF_free] if kt == 0 else (),
                    inc=(kt == NT - 1))
            mev = P.op("v", lambda e, c_=c, b_=bg: e.tensor_scalar(
                out=self.MU[0:8, c_ * 512:(c_ + 1) * 512],
                in0=self.psF[0:8, 0:512], scalar1=b_, scalar2=None,
                op0=ALU.add), waits=[mm], inc=True)
            self._psF_free = mev
            lg_ms.append(mev)
        WRing.free(wg_fh, P.now("t"))
        lt_ms = []
        for sbk in range(16):
            mtr = P.op("t", lambda e, s_=sbk: e.transpose(
                self.psF[:, 8:16], self.MU[0:8, s_ * 128:(s_ + 1) * 128],
                self.C["c_identf"][0:8, 0:8]),
                waits=lg_ms + [self._psF_free], inc=True)
            mev = P.op("v", lambda e, s_=sbk: e.tensor_copy(
                out=self.RS[:, s_ * 8:(s_ + 1) * 8],
                in_=self.psF[:, 8:16]), waits=[mtr], inc=True)
            self._psF_free = mev
            lt_ms.append(mev)
        w_ms = []
        for sbk in range(16):
            k0 = sbk * 8
            LT = self.RS[:, k0:k0 + 8]
            m1 = P.op("v", lambda e, L_=LT: e.reduce_max(
                out=self.SMALL[:, 0:1], in_=L_, axis=AX.X, negate=True),
                waits=[lt_ms[sbk], self._stat_free[0]], inc=True)
            m2 = P.op("v", lambda e, L_=LT, k_=k0: e.tensor_scalar(
                out=self.Q1[:, k_:k_ + 8], in0=L_,
                scalar1=self.SMALL[:, 0:1], scalar2=None, op0=ALU.add),
                waits=[Prog.sync(m1)], inc=True)
            m3 = P.op("v", lambda e, k_=k0: e.tensor_single_scalar(
                out=self.Q2[:, k_:k_ + 8], in_=self.Q1[:, k_:k_ + 8],
                scalar=0.0, op=ALU.is_ge), waits=[Prog.sync(m2)], inc=True)
            m4 = P.op("v", lambda e, L_=LT, k_=k0: e.scalar_tensor_tensor(
                out=self.Q2[:, k_:k_ + 8], in0=self.Q2[:, k_:k_ + 8],
                scalar=-1e30, in1=L_, op0=ALU.mult, op1=ALU.add),
                waits=[Prog.sync(m3)], inc=True)
            m5 = P.op("v", lambda e, k_=k0: e.reduce_max(
                out=self.SMALL[:, 1:2], in_=self.Q2[:, k_:k_ + 8],
                axis=AX.X, negate=True), waits=[Prog.sync(m4)], inc=True)
            m6 = P.op("s", lambda e, k_=k0: e.activation(
                out=self.Q1[:, k_:k_ + 8], in_=self.Q1[:, k_:k_ + 8],
                func=AF.Exp), waits=[m2, m3], inc=True)
            m7 = P.op("v", lambda e, L_=LT, k_=k0: e.tensor_scalar(
                out=self.Q2[:, k_:k_ + 8], in0=L_,
                scalar1=self.SMALL[:, 1:2], scalar2=None, op0=ALU.add),
                waits=[Prog.sync(m5), m6], inc=True)
            m8 = P.op("v", lambda e, k_=k0: e.tensor_single_scalar(
                out=self.Q2[:, k_:k_ + 8], in_=self.Q2[:, k_:k_ + 8],
                scalar=0.0, op=ALU.is_ge), waits=[Prog.sync(m7)], inc=True)
            m9 = P.op("v", lambda e: e.tensor_sub(
                out=self.SMALL[:, 2:3], in0=self.SMALL[:, 0:1],
                in1=self.SMALL[:, 1:2]), waits=[Prog.sync(m5)], inc=True)
            m10 = P.op("s", lambda e: e.activation(
                out=self.SMALL[:, 3:4], in_=self.SMALL[:, 2:3],
                func=AF.Exp), waits=[m9], inc=True)
            m11 = P.op("v", lambda e: e.tensor_scalar(
                out=self.SMALL[:, 4:5], in0=self.SMALL[:, 3:4],
                scalar1=1.0, scalar2=None, op0=ALU.add),
                waits=[m10], inc=True)
            m12 = P.op("v", lambda e: e.reciprocal(
                out=self.SMALL[:, 5:6], in_=self.SMALL[:, 4:5]),
                waits=[Prog.sync(m11)], inc=True)
            m13 = P.op("v", lambda e, k_=k0: e.tensor_mul(
                out=self.Q1[:, k_:k_ + 8], in0=self.Q1[:, k_:k_ + 8],
                in1=self.Q2[:, k_:k_ + 8]), waits=[m6, Prog.sync(m8)],
                inc=True)
            m14 = P.op("v", lambda e, k_=k0: e.tensor_scalar(
                out=self.Q1[:, k_:k_ + 8], in0=self.Q1[:, k_:k_ + 8],
                scalar1=self.SMALL[:, 5:6], scalar2=None, op0=ALU.mult),
                waits=[Prog.sync(m12), Prog.sync(m13)], inc=True)
            self._stat_free[0] = m14
            w_ms.append(m14)
        wt_ms = []
        for sbk in range(16):
            mtr = P.op("t", lambda e, s_=sbk: e.transpose(
                self.psF[0:8, 16:144], self.Q1[:, s_ * 8:(s_ + 1) * 8],
                self.C["c_identf"][:, :]),
                waits=[w_ms[sbk], self._psF_free], inc=True)
            mev = P.op("v", lambda e, s_=sbk: e.tensor_copy(
                out=self.MU[0:8, s_ * 128:(s_ + 1) * 128],
                in_=self.psF[0:8, 16:144]), waits=[mtr], inc=True)
            self._psF_free = mev
            wt_ms.append(mev)
        mwb = P.op("s", lambda e: e.copy(
            out=self.BC[0:8, :], in_=self.MU[0:8, :]), waits=wt_ms,
            inc=True)
        mb2c = P.dma("g", self.BA[0:8, 0:1024],
                     self.dram["p_moeb2t"][:, :])
        self._region_ms = {}
        smf_slots = [self.SMF[:, i * 512:(i + 1) * 512] for i in range(4)]
        smf_free = [None] * 4
        for dt in range(NT):
            for c in range(NCH):
                islot, ps, fz = self.slB.next()
                mm = P.op("t", lambda e, d_=dt, c_=c, p_=ps: e.matmul(
                    p_, self.BA[0:8, d_ * 128:(d_ + 1) * 128],
                    self.BC[0:8, c_ * 512:(c_ + 1) * 512],
                    start=True, stop=True),
                    waits=[mb2c, mwb, fz], inc=True)
                sl = self._sidx % 4
                mev = P.op("v", lambda e, p_=ps, s_=smf_slots[sl]:
                           e.tensor_copy(out=s_, in_=p_),
                           waits=[mm, smf_free[sl]], inc=True)
                self.slB.consumed(islot, mev)
                md = P.dma("g", self.xs(dt, c * 512, (c + 1) * 512),
                           smf_slots[sl], waits=[mev], accum=True)
                smf_free[sl] = md
                self._region_ms[(dt, c)] = md
                self._sidx += 1
        for ex in range(8):
            for fh in (0, 1):
                self._moe_expert(ex, fh, smf_slots, smf_free)
        P.barrier()

    def _moe_expert(self, ex, fh2, smf_slots, smf_free):
        P = self.P
        wb_ms = []
        if fh2 == 0:
            for c in range(NCH):
                islot, ps, fz = self.slB.next()
                mm = P.op("t", lambda e, c_=c, p_=ps, x_=ex: e.matmul(
                    p_, self.C["c_sel8"][:, x_ * 128:(x_ + 1) * 128],
                    self.BC[0:8, c_ * 512:(c_ + 1) * 512],
                    start=True, stop=True), waits=[fz, self._wbc_read],
                    inc=True)
                mev = P.op("v", lambda e, c_=c, p_=ps: e.tensor_copy(
                    out=self.BB[:, c_ * 512:(c_ + 1) * 512], in_=p_),
                    waits=[mm], inc=True)
                self.slB.consumed(islot, mev)
                wb_ms.append(mev)
            self._wbc_ms = wb_ms[-1]
        wbc_ms = self._wbc_ms
        for fp in range(8):
            f0 = fh2 * 16 + fp * 2
            slab, w_ms, wfh = self.wring.load(
                self.dram["w_moe1"][ex].rearrange("(k p) f -> p k f", p=128)
                [:, :, f0 * 128:(f0 + 2) * 128], 2048)
            for fl in range(2):
                ft = fp * 2 + fl
                fg = f0 + fl
                b1 = self.C["p_moeb1"][:, ex * 32 + fg: ex * 32 + fg + 1]
                for cg in range(NCH):
                    islot, ps, fz = self.slA.next()
                    mm = None
                    for kt in range(NT):
                        mm = P.op("t", lambda e, k_=kt, c_=cg, p_=ps,
                                  w_=slab, o_=fl * 128: e.matmul(
                            p_, w_[:, k_ * 256 + o_: k_ * 256 + o_ + 128],
                            self.XN[:, k_ * S + c_ * 512:
                                    k_ * S + (c_ + 1) * 512],
                            start=(k_ == 0), stop=(k_ == NT - 1)),
                            waits=([w_ms, fz, self._hh_free]
                                   if kt == 0 else ()),
                            inc=(kt == NT - 1))
                    mg = P.op("s", lambda e, p_=ps, f_=ft, c_=cg, b_=b1:
                              e.activation(
                                  out=self.HHARE[:, f_ * 2048 + c_ * 512:
                                                 f_ * 2048 + (c_ + 1) * 512],
                                  in_=p_, func=AF.Gelu, bias=b_),
                              waits=[mm], inc=True)
                    self.slA.consumed(islot, mg)
            WRing.free(wfh, P.now("t"))
        hh_ready = P.now("s")
        for dt in range(NT):
            slab, w_ms, wfh = self.wring.load(
                self.dram["w_moe2"][ex].rearrange(
                    "(k p) d -> p k d", p=128)
                [:, fh2 * 16:(fh2 + 1) * 16, dt * 128:(dt + 1) * 128],
                2048)
            for cg in range(NCH):
                islot, ps, fz = self.slA.next()
                mm = None
                for kt in range(16):
                    mm = P.op("t", lambda e, k_=kt, p_=ps, c_=cg, w_=slab:
                              e.matmul(
                                  p_, w_[:, k_ * 128: k_ * 128 + 128],
                                  self.HHARE[:, k_ * 2048 + c_ * 512:
                                             k_ * 2048 + (c_ + 1) * 512],
                                  start=(k_ == 0), stop=(k_ == 15)),
                              waits=([w_ms, fz, hh_ready]
                                     if kt == 0 else ()),
                              inc=(kt == 15))
                sl = self._sidx % 4
                mev = P.op("v", lambda e, p_=ps, s_=smf_slots[sl], c_=cg:
                           e.tensor_mul(out=s_, in0=p_,
                                        in1=self.BB[:, c_ * 512:
                                                    (c_ + 1) * 512]),
                           waits=[mm, smf_free[sl], wbc_ms], inc=True)
                self.slA.consumed(islot, mev)
                md = P.dma("g", self.xs(dt, cg * 512, (cg + 1) * 512),
                           smf_slots[sl],
                           waits=[mev, self._region_ms.get((dt, cg))],
                           accum=True)
                smf_free[sl] = md
                self._region_ms[(dt, cg)] = md
                self._sidx += 1
            WRing.free(wfh, P.now("t"))
        self._hh_free = P.now("t")
        self._wbc_read = P.now("v")



_NCORES = 8

def _prep_inputs(inputs):
    f = np.ascontiguousarray
    d = {}
    d["w_zh"] = f(np.stack([inputs["L0_attn_Wz"], inputs["L0_attn_Wh"]]))
    d["p_bz"] = _tile_vec(inputs["L0_attn_bz"], 8)
    d["p_bh"] = _tile_vec(inputs["L0_attn_bh"], 8)
    d["w_l0qkv"] = f(inputs["L0_ffn_Wqkv"])
    d["w_l0o"] = f(inputs["L0_ffn_Wo"])
    w1 = inputs["L0_ffn_W1"]
    w1p = np.zeros((6, D, 5632), dtype=np.float32)
    w1p[:, :, 0:2730] = w1[:, :, 0:2730]
    w1p[:, :, 2816:2816 + 2730] = w1[:, :, 2730:5460]
    d["w_l0w1p"] = w1p
    w2 = inputs["L0_ffn_W2"]
    w2p = np.zeros((6, 2816, D), dtype=np.float32)
    w2p[:, 0:2730, :] = w2
    d["w_l0w2p"] = w2p
    qs, ks = inputs["L0_ffn_q_scale"], inputs["L0_ffn_k_scale"]
    d["p_l0qs"] = f(np.concatenate([qs.T, qs.T], 0).astype(np.float32))
    d["p_l0ks"] = f(np.concatenate([ks.T, ks.T], 0).astype(np.float32))
    d["p_l0g"] = np.concatenate(
        [_tile_vec(inputs["L0_ffn_ln_g"][i], 8) for i in range(6)], 1)
    d["p_l0b"] = np.concatenate(
        [_tile_vec(inputs["L0_ffn_ln_b"][i], 8) for i in range(6)], 1)
    d["p_l0b1"] = np.concatenate(
        [np.concatenate([_tile_vec(inputs["L0_ffn_b1"][i, :2730], 22),
                         _tile_vec(inputs["L0_ffn_b1"][i, 2730:], 22)], 1)
         for i in range(6)], 1)
    d["p_l0b2"] = np.concatenate(
        [_tile_vec(inputs["L0_ffn_b2"][i], 8) for i in range(6)], 1)
    d["w_1qa"] = f(inputs["L1_attn_Wqa"])
    d["w_1qb"] = f(inputs["L1_attn_Wqb"][:, _perm_wqb()])
    d["w_1kva"] = f(inputs["L1_attn_Wkva"])
    kvb = inputs["L1_attn_Wkvb"]
    k2 = np.zeros((32, 512), dtype=np.float32)
    vv = np.zeros((32, 256), dtype=np.float32)
    for g in range(4):
        kcols = [g * 128 + 4 * r if r < 32 else g * 128 + 4 * (r - 32) + 2
                 for r in range(64)]
        for r in range(64):
            k2[:, g * 128 + r] = kvb[:, kcols[r]]
            k2[:, g * 128 + 64 + r] = kvb[:, kcols[r]]
            vv[:, g * 64 + r] = kvb[:, g * 128 + 2 * r + 1]
    d["w_1kvb_k2"] = k2
    d["w_1kvb_v"] = vv
    d["w_1o"] = f(inputs["L1_attn_Wout"])
    d["p_bout"] = _tile_vec(inputs["L1_attn_bout"], 8)
    d["w_moe1"] = f(inputs["L1_ffn_W1"])
    d["w_moe2"] = f(inputs["L1_ffn_W2"])
    d["p_moeb1"] = np.concatenate(
        [_tile_vec(inputs["L1_ffn_b1"][e], 32) for e in range(8)], 1)
    d["p_moeb2t"] = f(inputs["L1_ffn_b2"].astype(np.float32))
    d["w_g"] = f(inputs["L1_ffn_Wg"])
    bg = np.zeros((128, 1), dtype=np.float32)
    bg[0:8, 0] = inputs["L1_ffn_bg"]
    d["p_bg"] = bg
    cst, tbl = _consts()
    for k, v in {**cst, **tbl}.items():
        d[k] = f(v)
    return d

_PHASES = ("A", "B", "C", "D")

def kernel(**inputs):
    mk = MK(phases=_PHASES)
    nc = mk.build()
    shared = _prep_inputs(inputs)
    x = np.ascontiguousarray(inputs["L0_attn_x"].astype(np.float32))
    in_maps = []
    for c in range(_NCORES):
        m = dict(shared)
        m["x_in"] = x[c]
        in_maps.append(m)
    res = run_bass_kernel_spmd(nc, in_maps, core_ids=list(range(_NCORES)))
    return np.stack([res.results[c]["x_out"] for c in range(_NCORES)])
